# version 01
# baseline (speedup 1.0000x reference)
import numpy as np
from contextlib import ExitStack

import concourse.bass as bass
import concourse.mybir as mybir
from concourse.bass_utils import run_bass_kernel_spmd

F32 = mybir.dt.float32
BF16 = mybir.dt.bfloat16
Alu = mybir.AluOpType
Act = mybir.ActivationFunctionType
AX = mybir.AxisListType.X

MAGIC = 1.5 * 2.0**23
N_CORES = 8
S = 4096
E = 1024
H = 16
I = 3072
MYT = 2048
NEC = E // 128
NIC = I // 128

ENGS = ("sync", "gpsimd", "vector", "scalar", "tensor")
V, Sc, T, G, Sy = "vector", "scalar", "tensor", "gpsimd", "sync"


class Builder:

    def __init__(self, nc):
        self.nc = nc
        self.ops = {e: [] for e in ENGS}
        self.sems = {}

    def mksem(self, name):
        h = self.nc.semaphore(name).__enter__()
        self.sems[name] = [h, 0]

    AUTO_CHAIN = {"vector": "sV", "scalar": "sS", "gpsimd": "sG"}

    def op(self, eng, fn, inc=None, k=1):
        ch = self.AUTO_CHAIN.get(eng)
        if ch is not None:
            self.wait(eng, ch)
        if inc is None:
            self.ops[eng].append(fn)
            return None
        rec = self.sems[inc]
        rec[1] += k
        h = rec[0]
        self.ops[eng].append(lambda e, fn=fn, h=h, k=k: fn(e).then_inc(h, k))
        return rec[1]

    def wait(self, eng, sem, val=None):
        h, c = self.sems[sem]
        v = c if val is None else val
        if v > 0:
            self.ops[eng].append(lambda e, h=h, v=v: e.wait_ge(h, v))

    def dma_g(self, out, in_, **kw):
        return self.dma("gpsimd", out, in_, serial=False, **kw)

    def dma(self, eng, out, in_, serial=True, **kw):
        q = {"sync": "dS", "gpsimd": "dG", "scalar": "dA"}[eng]
        n = self.op(eng, lambda e: e.dma_start(out=out, in_=in_, **kw), q, 16)
        if serial:
            self.wait(eng, q, n)
        return n

    def vsync(self, eng):
        sem = {"vector": "sV", "scalar": "sS", "tensor": "sT",
               "gpsimd": "sG"}[eng]
        self.wait(eng, sem)

    def vfiller(self, sb):
        self.op("vector", lambda e: e.tensor_copy(sb["ROW"][0:1, 4095:4096],
                                                  sb["CST"][0:1, 0:1]), "sV")

    def barrier(self, engs=ENGS):
        snap = [(h, c) for (h, c) in self.sems.values() if c > 0]
        for eng in engs:
            for h, v in snap:
                self.ops[eng].append(lambda e, h=h, v=v: e.wait_ge(h, v))


def build_nc():
    nc = bass.Bass("TRN2", target_bir_lowering=False)

    dram = {}
    for nm, shp in [
        ("xseqT", [E, S]), ("x_tok", [S, E]),
        ("wqT", [E, E]), ("wkT", [E, E]), ("wvT", [E, E]),
        ("wu", [I, E]), ("wg", [I, E]), ("wd", [E, I]),
        ("gamma8", [128, NEC]), ("beta8", [128, NEC]),
        ("meta", [1, 2]),
    ]:
        dram[nm] = nc.dram_tensor(nm, shp, F32, kind="ExternalInput").ap()
    out_d = nc.dram_tensor("out", [MYT, E], F32, kind="ExternalOutput").ap()

    sp = {}
    for nm, shp in [
        ("sp_v", [S, H * 65]), ("sp_attn", [E, MYT]), ("sp_sums", [H, MYT]),
        ("sp_aln", [E, MYT]), ("sp_up", [I, MYT]), ("sp_gate", [I, MYT]),
        ("sp_g2", [I, MYT]), ("sp_uq", [I, MYT]), ("sp_o", [I, MYT]),
        ("sp_af", [E, MYT]), ("scr_row", [4, S]), ("sp_st", [32, MYT]),
    ]:
        sp[nm] = nc.dram_tensor(
            nm, shp, BF16 if nm == "sp_v" else F32, kind="Internal").ap()
    sp["cc_in"] = {k: nc.dram_tensor(f"cc{k}_i", [1, w], F32, kind="Internal").ap()
                   for k, w in (("x", 1), ("ug", 2), ("g2", 1), ("o", 1))}
    sp["cc_out"] = {k: nc.dram_tensor(f"cc{k}_o", [1, w], F32, kind="Internal").ap()
                    for k, w in (("x", 1), ("ug", 2), ("g2", 1), ("o", 1))}

    st = ExitStack()
    sb = {}

    def perm(nm, shp, dt):
        sb[nm] = st.enter_context(nc.sbuf_tensor(nm, shp, dt))
        return sb[nm]

    perm("CST", [128, 8], F32)
    perm("SXB", [128, 2], F32)
    perm("SUGB", [128, 4], F32)
    perm("S2B", [128, 2], F32)
    perm("SOB", [128, 2], F32)
    perm("METB", [128, 2], F32)
    perm("AMX", [128, 8], F32)
    perm("IDF", [128, 128], F32)
    perm("IND2", [128, 2], F32)
    perm("IND16", [16, 1024], F32)
    perm("ONES1", [128, 1], F32)
    perm("PIOTA", [128, 1], F32)
    perm("GAM", [128, NEC], F32)
    perm("BET", [128, NEC], F32)
    perm("ROW", [128, 4096], F32)
    perm("BMX", [128, 96], F32)
    perm("BSC", [128, 96], F32)
    perm("BRS", [128, 96], F32)

    TOP = nc.SBUF_PARTITION_SIZE_BYTES
    ZB = 24576

    def at(nm, off, shp, dt):
        end = off + int(np.prod(shp[1:])) * mybir.dt.size(dt)
        assert ZB <= off and end <= TOP, (nm, off, end)
        sb[nm] = nc.alloc_sbuf_tensor_at(nm, shp, dt, offset=off)
        return sb[nm]

    at("WVTS", 40960, [128, NEC * E], BF16)
    at("WKTS", 57344, [128, NEC * E], BF16)
    at("WQTS", 73728, [128, NEC * E], BF16)
    at("XNTA", 90112, [128, NEC * MYT], BF16)
    at("XNTB", 122880, [128, NEC * MYT], BF16)
    at("XC2", 155648, [128, S], F32)
    at("MUBC", 172032, [128, S], F32)
    at("RSBC", 188416, [128, S], F32)
    at("XT2", 204800, [128, 2 * E], F32)
    at("SQT", 212992, [128, E], F32)
    at("STT", 217088, [128, 128], F32)
    at("VEV", 217600, [128, H * 65], BF16)
    at("KF", 155648, [128, NEC * S], BF16)
    at("QF", 40960, [128, NEC * MYT], BF16)
    at("DNE", 122880, [128, 4096], F32)
    at("DFA", 139264, [128, 4096], F32)
    at("SW", 73728, [128, 1024], F32)
    at("PT", 77824, [128, 1024], BF16)
    at("AVE", 79872, [128, 1024], F32)
    at("VSB", 83968, [128, 32 * 65], BF16)
    at("XQT", 155648, [128, NEC * MYT], BF16)
    at("SQ2", 188416, [128, MYT], F32)
    at("ACL", 40960, [128, NEC * MYT], F32)
    at("BBC", 106496, [128, MYT], F32)
    at("C16", 114688, [16, MYT], F32)
    at("A16", 122880, [16, MYT], F32)
    at("SM16", 131072, [16, MYT], F32)
    at("S16A", 139264, [16, MYT], F32)
    at("S16B", 147456, [16, MYT], F32)
    at("S2T", 196608, [2, MYT], F32)
    at("WUT", 40960, [128, NEC * I], BF16)
    at("WGT", 90112, [128, NEC * I], BF16)
    at("FEV", 139264, [128, 8 * 512], F32)
    at("WNA", 196608, [128, 2 * E], F32)
    at("WSC", 204800, [128, 2 * E], F32)
    at("FG", 40960, [128, MYT], F32)
    at("FU", 49152, [128, MYT], F32)
    at("FT1", 57344, [128, MYT], F32)
    at("FGQ", 65536, [128, MYT], F32)
    at("FSQ", 73728, [128, MYT], F32)
    at("FG2", 81920, [128, MYT], F32)
    at("OQT", 40960, [128, NIC * MYT], BF16)
    at("WDT", 139264, [128, NIC * E], BF16)
    at("W3A", 188416, [128, 2 * I], F32)
    at("W3S", 212992, [128, I], F32)
    at("FG4", 188416, [128, MYT], F32)
    at("FT4", 196608, [128, MYT], F32)
    at("ALC", 188416, [128, MYT], F32)
    at("AFE", 196608, [128, MYT], F32)
    at("AFL", 40960, [128, NEC * MYT], F32)
    at("XT6", 106496, [128, 2 * E], F32)
    at("OSB", 114688, [128, 2 * E], F32)

    PS = nc.alloc_psum_tensor("PS", [128, 4096], F32)

    b = Builder(nc)
    for s_ in ("dS", "dG", "dA", "sV", "sS", "sT", "sG", "sCC"):
        b.mksem(s_)

    emit_program(nc, b, sb, PS, dram, out_d, sp)
    _NC_CACHE["builder"] = b

    with nc.Block() as block:
        @block.sync
        def _(e):
            for f in b.ops["sync"]:
                f(e)

        @block.gpsimd
        def _(e):
            for f in b.ops["gpsimd"]:
                f(e)

        @block.vector
        def _(e):
            for f in b.ops["vector"]:
                f(e)

        @block.scalar
        def _(e):
            for f in b.ops["scalar"]:
                f(e)

        @block.tensor
        def _(e):
            for f in b.ops["tensor"]:
                f(e)

    st.close()
    return nc


def emit_program(nc, b, sb, PS, dr, out_d, sp):
    ROW = sb["ROW"]
    XNT = {0: sb["XNTA"], 1: sb["XNTB"]}

    def xnt_cols(ec, c0, c1):
        assert (c0 // MYT) == ((c1 - 1) // MYT)
        half = c0 // MYT
        t = XNT[half]
        return t[:, ec * MYT + c0 % MYT: ec * MYT + c0 % MYT + (c1 - c0)]

    b.op(V, lambda e: e.memset(sb["CST"][:, 0:1], MAGIC), "sV")
    b.op(V, lambda e: e.memset(sb["CST"][:, 1:2], 1e-5), "sV")
    b.op(V, lambda e: e.memset(sb["CST"][:, 2:3], 65536.0), "sV")
    b.op(V, lambda e: e.memset(sb["CST"][:, 3:4], -1920.0 / 4096.0), "sV")
    b.op(V, lambda e: e.memset(sb["CST"][:, 4:5], 1.0 / 4096.0), "sV")
    b.op(V, lambda e: e.memset(sb["ONES1"][:, :], 1.0), "sV")
    b.op(V, lambda e: e.memset(sb["AMX"][:, :], 0.0), "sV")
    b.op(G, lambda e: e.iota(sb["IDF"][:, :], pattern=[[1, 128]], base=0,
                             channel_multiplier=-1,
                             allow_small_or_imprecise_dtypes=True), "sG")
    b.op(G, lambda e: e.iota(sb["PIOTA"][:, :], pattern=[[0, 1]], base=0,
                             channel_multiplier=1,
                             allow_small_or_imprecise_dtypes=True), "sG")
    b.dma_g(sb["METB"][:, :], dr["meta"][0:1, :].partition_broadcast(128))
    b.wait(V, "sG")
    b.op(V, lambda e: e.tensor_scalar(sb["IND2"][:, 0:1], sb["PIOTA"][:, :],
                                      64.0, None, Alu.is_lt), "sV")
    b.op(V, lambda e: e.tensor_scalar(sb["IND2"][:, 1:2], sb["PIOTA"][:, :],
                                      64.0, None, Alu.is_ge), "sV")
    b.op(V, lambda e: e.tensor_scalar(sb["IDF"][:, :], sb["IDF"][:, :],
                                      0.0, None, Alu.is_equal), "sV")
    b.op(G, lambda e: e.iota(sb["IND16"][:, :],
                             pattern=[[2, 8], [1, 2], [0, 64]], base=0,
                             channel_multiplier=0,
                             allow_small_or_imprecise_dtypes=True), "sG")
    b.wait(V, "sG")
    b.op(V, lambda e: e.tensor_tensor(
        sb["IND16"][:, :], sb["IND16"][:, :],
        sb["PIOTA"][0:16, 0:1].broadcast_to([16, 1024]), Alu.is_equal), "sV")
    b.dma(Sy, sb["GAM"][:, :], dr["gamma8"][:, :])
    b.dma(Sy, sb["BET"][:, :], dr["beta8"][:, :])
    for nm, dst in (("wqT", "WQTS"), ("wkT", "WKTS"), ("wvT", "WVTS")):
        for ec in range(NEC):
            b.dma_g(sb[dst][:, ec * E:(ec + 1) * E],
                  dr[nm][ec * 128:(ec + 1) * 128, :])

    def global_scale(col, key, packw, dst, qmaxv):
        b.wait(T, "sV")
        b.op(T, lambda e: e.transpose(PS[0:packw, 1536:1664],
                                      sb["AMX"][:, col:col + packw],
                                      sb["IDF"][:, :]), "sT")
        b.wait(V, "sT")
        b.op(V, lambda e: e.tensor_reduce(ROW[0:packw, 256:257],
                                          PS[0:packw, 1536:1664], AX,
                                          Alu.max), "sV")
        b.wait(G, "sV")
        b.dma_g(sp["cc_in"][key][0:1, :].rearrange("a (p c) -> (a p) c",
                                                    p=packw),
              ROW[0:packw, 256:257])
        b.wait(G, "dG")
        b.op(G, lambda e: e.collective_compute(
            "AllReduce", Alu.max, replica_groups=[list(range(N_CORES))],
            ins=[sp["cc_in"][key].opt()], outs=[sp["cc_out"][key].opt()],
        ), "sCC")
        b.wait(G, "sCC")
        b.dma_g(ROW[0:1, 260:260 + packw], sp["cc_out"][key][0:1, :])
        b.wait(V, "dG")
        b.op(V, lambda e: e.tensor_scalar(
            ROW[0:1, 280:280 + packw], ROW[0:1, 260:260 + packw],
            1.0 / qmaxv, 1e-12, Alu.mult, Alu.max), "sV")
        b.vsync(V)
        b.op(V, lambda e: e.reciprocal(
            ROW[0:1, 288:288 + packw], ROW[0:1, 280:280 + packw]), "sV")
        b.vsync(V)
        for i in range(packw):
            b.op(V, lambda e, i=i: e.tensor_copy(
                ROW[0:1, 300 + 2 * i:301 + 2 * i],
                ROW[0:1, 280 + i:281 + i]), "sV")
            b.op(V, lambda e, i=i: e.tensor_copy(
                ROW[0:1, 301 + 2 * i:302 + 2 * i],
                ROW[0:1, 288 + i:289 + i]), "sV")
        b.wait(G, "sV")
        b.dma_g(sp["scr_row"][2:3, 0:2 * packw], ROW[0:1, 300:300 + 2 * packw])
        b.wait(G, "dG")
        b.dma_g(sb[dst][:, 0:2 * packw],
              sp["scr_row"][2:3, 0:2 * packw].partition_broadcast(128))

    def qdq_chunk(src_ap, dst_ap, s_ap, rs_ap, lo, hi, tmp_ap):
        b.wait(Sc, "sV")
        b.op(Sc, lambda e: e.activation(out=tmp_ap, in_=src_ap,
                                        func=Act.Identity,
                                        bias=sb["CST"][:, 0:1],
                                        scale=rs_ap), "sS")
        b.wait(V, "sS")
        b.op(V, lambda e: e.tensor_scalar(tmp_ap, tmp_ap, MAGIC, lo,
                                          Alu.subtract, Alu.max), "sV")
        b.op(V, lambda e: e.tensor_scalar(dst_ap, tmp_ap, hi, s_ap,
                                          Alu.min, Alu.mult), "sV")
        b.wait(Sc, "sV")

    XT2, SQT, STT, XC2 = sb["XT2"], sb["SQT"], sb["STT"], sb["XC2"]
    for tt in range(32):
        pb = (tt % 2) * E
        if tt >= 2:
            b.wait(Sy, "sV")
            b.wait(Sy, "sS")
        ld = b.dma(Sy, XT2[:, pb:pb + E], dr["x_tok"][tt * 128:(tt + 1) * 128, :])
        b.wait(Sc, "dS")
        b.wait(Sc, "sV")
        b.op(Sc, lambda e, pb=pb: e.activation(
            out=SQT[:, :], in_=XT2[:, pb:pb + E], func=Act.Square), "sS")
        b.wait(V, "dS")
        b.op(V, lambda e, pb=pb, tt=tt: e.tensor_reduce(
            STT[:, tt:tt + 1], XT2[:, pb:pb + E], AX, Alu.add), "sV")
        b.wait(V, "sS")
        b.op(V, lambda e, tt=tt: e.tensor_reduce(
            STT[:, 32 + tt:33 + tt], SQT[:, :], AX, Alu.add), "sV")
    b.vsync(V)
    b.op(V, lambda e: e.tensor_scalar(STT[:, 0:32], STT[:, 0:32],
                                      1.0 / E, None, Alu.mult), "sV")
    b.op(V, lambda e: e.tensor_scalar(STT[:, 32:64], STT[:, 32:64],
                                      1.0 / E, None, Alu.mult), "sV")
    b.vsync(V)
    b.op(V, lambda e: e.tensor_tensor(STT[:, 64:96], STT[:, 0:32],
                                      STT[:, 0:32], Alu.mult), "sV")
    b.vsync(V)
    b.op(V, lambda e: e.tensor_tensor(STT[:, 64:96], STT[:, 32:64],
                                      STT[:, 64:96], Alu.subtract), "sV")
    b.wait(Sc, "sV")
    b.op(Sc, lambda e: e.activation(out=STT[:, 96:128], in_=STT[:, 64:96],
                                    func=Act.Sqrt, bias=sb["CST"][:, 1:2],
                                    scale=1.0), "sS")
    b.wait(V, "sS")
    b.op(V, lambda e: e.reciprocal(STT[:, 96:128], STT[:, 96:128]), "sV")
    b.vsync(V)
    b.wait(T, "sV")
    b.op(T, lambda e: e.transpose(PS[0:32, 0:128], STT[:, 0:32],
                                  sb["IDF"][:, :]), "sT")
    b.op(T, lambda e: e.transpose(PS[0:32, 512:640], STT[:, 96:128],
                                  sb["IDF"][:, :]), "sT")
    b.wait(V, "sT")
    b.op(V, lambda e: e.tensor_copy(ROW[0:32, 0:128], PS[0:32, 0:128]), "sV")
    b.op(V, lambda e: e.tensor_copy(ROW[0:32, 128:256], PS[0:32, 512:640]), "sV")
    b.wait(G, "sV")
    b.dma_g(sp["scr_row"][0:1, :].rearrange("a (p c) -> (a p) c", p=32),
          ROW[0:32, 0:128])
    b.dma_g(sp["scr_row"][1:2, :].rearrange("a (p c) -> (a p) c", p=32),
          ROW[0:32, 128:256])
    b.wait(G, "dG")
    b.dma_g(sb["MUBC"][:, :], sp["scr_row"][0:1, :].partition_broadcast(128))
    b.dma_g(sb["RSBC"][:, :], sp["scr_row"][1:2, :].partition_broadcast(128))
    for ec in range(NEC):
        pb = 0
        if ec >= 1:
            b.wait(Sy, "sV")
        ld = b.dma(Sy, XC2[:, pb:pb + S], dr["xseqT"][ec * 128:(ec + 1) * 128, :])
        b.wait(V, "dS")
        b.wait(V, "dG")
        b.op(V, lambda e, pb=pb: e.tensor_tensor(
            XC2[:, pb:pb + S], XC2[:, pb:pb + S], sb["MUBC"][:, :],
            Alu.subtract), "sV")
        for half in range(2):
            b.op(V, lambda e, pb=pb, ec=ec, half=half: e.tensor_tensor(
                XNT[half][:, ec * MYT:(ec + 1) * MYT],
                XC2[:, pb + half * MYT: pb + (half + 1) * MYT],
                sb["RSBC"][:, half * MYT:(half + 1) * MYT], Alu.mult), "sV")
            b.op(V, lambda e, ec=ec, half=half: e.tensor_reduce(
                STT[:, 64:65], XNT[half][:, ec * MYT:(ec + 1) * MYT], AX,
                Alu.max, apply_absolute_value=True), "sV")
            b.vsync(V)
            b.op(V, lambda e: e.tensor_tensor(
                sb["AMX"][:, 0:1], sb["AMX"][:, 0:1], STT[:, 64:65],
                Alu.max), "sV")
            b.vsync(V)
    global_scale(0, "x", 1, "SXB", 32767.0)

    b.barrier([T])
    VEV = sb["VEV"]
    for tt in range(32):
        half_col = tt % 2
        for half in range(2):
            pt = PS[:, 2048 + half * 512: 2048 + (half + 1) * 512]
            if tt > 0 or half > 0:
                b.wait(T, "sV")
            for ec in range(NEC):
                b.op(T, lambda e, tt=tt, ec=ec, pt=pt, half=half: e.matmul(
                    pt, xnt_cols(ec, tt * 128, (tt + 1) * 128),
                    sb["WVTS"][:, ec * E + half * 512:
                               ec * E + (half + 1) * 512],
                    start=(ec == 0), stop=(ec == NEC - 1)), "sT")
            b.wait(V, "sT")
            b.wait(V, "dG")
            b.op(V, lambda e, pt=pt, half=half: e.tensor_copy(
                VEV[:, half * 8 * 65:(half + 1) * 8 * 65]
                .rearrange("p (h c) -> p h c", c=65)[:, :, 0:64],
                pt.rearrange("p (h c) -> p h c", c=64)), "sV")
        b.op(V, lambda e: e.memset(
            VEV[:, :].rearrange("p (h c) -> p h c", c=65)[:, :, 64:65],
            1.0), "sV")
        b.wait(G, "sV")
        b.dma_g(sp["sp_v"][tt * 128:(tt + 1) * 128, :], VEV[:, :])

    b.barrier([T])
    KF, QF = sb["KF"], sb["QF"]
    for pair in range(NEC):
        for nk in range(8):
            pk = PS[:, 3072 + (nk % 2) * 512: 3072 + (nk % 2) * 512 + 512]
            b.wait(T, "sV")
            for ec in range(NEC):
                b.op(T, lambda e, pk=pk, ec=ec, pair=pair, nk=nk: e.matmul(
                    pk, sb["WKTS"][:, ec * E + pair * 128:
                                   ec * E + (pair + 1) * 128],
                    xnt_cols(ec, nk * 512, (nk + 1) * 512),
                    start=(ec == 0), stop=(ec == NEC - 1)), "sT")
            b.wait(V, "sT")
            b.op(V, lambda e, pk=pk, pair=pair, nk=nk: e.tensor_copy(
                KF[:, pair * S + nk * 512: pair * S + (nk + 1) * 512],
                pk), "sV")
    for pair in range(NEC):
        for nq in range(4):
            pk = PS[:, 3072 + (nq % 2) * 512: 3072 + (nq % 2) * 512 + 512]
            b.wait(T, "sV")
            for ec in range(NEC):
                b.op(T, lambda e, pk=pk, ec=ec, pair=pair, nq=nq: e.matmul(
                    pk, sb["WQTS"][:, ec * E + pair * 128:
                                   ec * E + (pair + 1) * 128],
                    xnt_cols(ec, nq * 512, (nq + 1) * 512),
                    start=(ec == 0), stop=(ec == NEC - 1)), "sT")
            b.wait(V, "sT")
            b.op(V, lambda e, pk=pk, pair=pair, nq=nq: e.tensor_scalar(
                QF[:, pair * MYT + nq * 512: pair * MYT + (nq + 1) * 512],
                pk, 1.0 / 32.0, None, Alu.mult), "sV")

    b.barrier([G])
    b.op(G, lambda e: e.iota(sb["DNE"][:, :], pattern=[[1, 4096]], base=0,
                             channel_multiplier=-1,
                             allow_small_or_imprecise_dtypes=True), "sG")
    b.op(G, lambda e: e.iota(sb["DFA"][:, :], pattern=[[1, 4096]], base=0,
                             channel_multiplier=-1,
                             allow_small_or_imprecise_dtypes=True), "sG")
    b.wait(V, "sG")
    b.wait(V, "dG")
    b.op(V, lambda e: e.tensor_scalar(sb["DFA"][:, :], sb["DFA"][:, :],
                                      sb["METB"][:, 0:1], sb["METB"][:, 1:2],
                                      Alu.mult, Alu.add), "sV")
    b.wait(Sc, "sG")
    b.wait(Sc, "sV")
    b.op(Sc, lambda e: e.activation(out=sb["DNE"][:, :], in_=sb["DNE"][:, :],
                                    func=Act.Abs, bias=sb["CST"][:, 3:4],
                                    scale=sb["CST"][:, 4:5]), "sS")

    b.barrier([T, Sc, G, Sy])
    SW, PT, AVE, VSB = sb["SW"], sb["PT"], sb["AVE"], sb["VSB"]
    DNE, DFA = sb["DNE"], sb["DFA"]
    for head in range(H):
        pair, hh = divmod(head, 2)
        hb = hh * 64
        for qh in range(2):
            vb = 0
            if head * 2 + qh >= 2:
                b.wait(Sy, "sT")
            vld = b.dma(Sy, VSB[:, vb:vb + 32 * 65]
                        .rearrange("p (k c) -> p k c", c=65),
                        sp["sp_v"][:, head * 65:(head + 1) * 65]
                        .rearrange("(k p) c -> p k c", p=128))
            tt_mark = {}
            for kc in range(32):
                ppar = kc % 2
                pscr = PS[:, ppar * 1024:(ppar + 1) * 1024]
                if kc >= 2:
                    b.wait(T, "sV", tt_mark[kc - 2])
                elif kc == 0:
                    b.wait(T, "sV")
                for jj in range(2):
                    b.op(T, lambda e, pscr=pscr, jj=jj, hb=hb, kc=kc, qh=qh,
                         pair=pair: e.matmul(
                             pscr[:, jj * 512:(jj + 1) * 512],
                             KF[hb:hb + 64,
                                pair * S + kc * 128: pair * S + (kc + 1) * 128],
                             QF[hb:hb + 64,
                                pair * MYT + qh * 1024 + jj * 512:
                                pair * MYT + qh * 1024 + (jj + 1) * 512],
                             start=True, stop=True), "sT")
                if kc < 16:
                    doff = qh * 1024 - kc * 128 + 1920
                    dsl = DNE[:, doff:doff + 1024]
                else:
                    doff = qh * 1024 - (kc - 16) * 128 + 1920
                    dsl = DFA[:, doff:doff + 1024]
                b.wait(V, "sT")
                b.wait(V, "sS")
                tt_mark[kc] = b.op(V, lambda e, pscr=pscr, dsl=dsl:
                                   e.tensor_tensor(
                    SW[:, :], pscr, dsl, Alu.mult), "sV")
                b.wait(Sc, "sV")
                b.wait(Sc, "sT")
                b.op(Sc, lambda e: e.activation(
                    out=PT[:, :], in_=SW[:, :], func=Act.Exp), "sS")
                b.wait(T, "sS")
                if kc == 0:
                    b.wait(T, "dS")
                for jj in range(2):
                    b.op(T, lambda e, jj=jj, kc=kc, vb=vb: e.matmul(
                        PS[0:65, 2048 + jj * 512: 2560 + jj * 512],
                        VSB[:, vb + kc * 65: vb + kc * 65 + 65],
                        PT[:, jj * 512:(jj + 1) * 512],
                        start=(kc == 0), stop=(kc == 31)), "sT")
            b.wait(V, "sT")
            b.wait(V, "dG")
            b.op(V, lambda e: e.tensor_copy(AVE[0:65, :],
                                            PS[0:65, 2048:3072]), "sV")
            b.wait(G, "sV")
            b.dma_g(sp["sp_attn"][head * 64:(head + 1) * 64,
                                   qh * 1024:(qh + 1) * 1024], AVE[0:64, :])
            b.dma_g(sp["sp_sums"][head:head + 1,
                                   qh * 1024:(qh + 1) * 1024], AVE[64:65, :])
            b.wait(T, "sV")

    b.barrier()
    XQT, SQ2 = sb["XQT"], sb["SQ2"]
    for ec in range(NEC):
        b.op(Sc, lambda e, ec=ec: e.activation(
            out=SQ2[:, :], in_=sb["XNTA"][:, ec * MYT:(ec + 1) * MYT],
            func=Act.Identity, bias=sb["CST"][:, 0:1],
            scale=sb["SXB"][:, 1:2]), "sS")
        b.wait(V, "sS")
        b.op(V, lambda e: e.tensor_scalar(SQ2[:, :], SQ2[:, :], MAGIC,
                                          -32768.0, Alu.subtract,
                                          Alu.max), "sV")
        b.op(V, lambda e, ec=ec: e.tensor_scalar(
            XQT[:, ec * MYT:(ec + 1) * MYT], SQ2[:, :], 32767.0,
            sb["SXB"][:, 0:1], Alu.min, Alu.mult), "sV")
        b.wait(Sc, "sV")

    b.barrier()
    ACL, BBC = sb["ACL"], sb["BBC"]
    ld = b.dma(Sy, sb["SM16"][:, :], sp["sp_sums"][:, :])
    b.wait(V, "dS")
    b.op(V, lambda e: e.reciprocal(sb["C16"][:, :], sb["SM16"][:, :]), "sV")
    b.vsync(V)
    for ec in range(NEC):
        ld = b.dma(Sy, ACL[:, ec * MYT:(ec + 1) * MYT],
                   sp["sp_attn"][ec * 128:(ec + 1) * 128, :])
        b.wait(Sc, "dS")
        b.wait(Sc, "sT")
        b.op(Sc, lambda e, ec=ec: e.activation(
            out=SQ2[:, :], in_=ACL[:, ec * MYT:(ec + 1) * MYT],
            func=Act.Square), "sS")
        b.wait(T, "dS")
        b.wait(T, "sS")
        b.wait(T, "sV")
        for np_ in range(4):
            b.op(T, lambda e, ec=ec, np_=np_: e.matmul(
                PS[0:2, np_ * 512:(np_ + 1) * 512], sb["IND2"][:, :],
                ACL[:, ec * MYT + np_ * 512: ec * MYT + (np_ + 1) * 512],
                start=True, stop=True), "sT")
            b.op(T, lambda e, np_=np_: e.matmul(
                PS[0:2, 2048 + np_ * 512: 2048 + (np_ + 1) * 512],
                sb["IND2"][:, :], SQ2[:, np_ * 512:(np_ + 1) * 512],
                start=True, stop=True), "sT")
        b.wait(V, "sT")
        b.wait(V, "dG")
        b.op(V, lambda e, ec=ec: e.tensor_copy(
            sb["S2T"][:, :], PS[0:2, 0:2048]), "sV")
        b.wait(G, "sV")
        b.dma_g(sp["sp_st"][2 * ec:2 * ec + 2, :], sb["S2T"][:, :])
        b.wait(V, "dG")
        b.op(V, lambda e, ec=ec: e.tensor_copy(
            sb["S2T"][:, :], PS[0:2, 2048:4096]), "sV")
        b.wait(G, "sV")
        b.dma_g(sp["sp_st"][16 + 2 * ec:18 + 2 * ec, :], sb["S2T"][:, :])
    b.wait(Sy, "dG")
    ldA = b.dma(Sy, sb["S16A"][:, :], sp["sp_st"][0:16, :])
    ldB = b.dma(Sy, sb["S16B"][:, :], sp["sp_st"][16:32, :])
    b.wait(V, "dS")
    b.op(V, lambda e: e.tensor_tensor(sb["S16A"][:, :], sb["S16A"][:, :],
                                      sb["C16"][:, :], Alu.mult), "sV")
    b.op(V, lambda e: e.tensor_tensor(sb["A16"][:, :], sb["C16"][:, :],
                                      sb["C16"][:, :], Alu.mult), "sV")
    b.vsync(V)
    b.op(V, lambda e: e.tensor_tensor(sb["S16B"][:, :], sb["S16B"][:, :],
                                      sb["A16"][:, :], Alu.mult), "sV")
    b.wait(T, "sV")
    for np_ in range(4):
        b.op(T, lambda e, np_=np_: e.matmul(
            PS[0:1, np_ * 512:(np_ + 1) * 512], sb["ONES1"][0:16, :],
            sb["S16A"][:, np_ * 512:(np_ + 1) * 512],
            start=True, stop=True), "sT")
        b.op(T, lambda e, np_=np_: e.matmul(
            PS[0:1, 2048 + np_ * 512: 2048 + (np_ + 1) * 512],
            sb["ONES1"][0:16, :], sb["S16B"][:, np_ * 512:(np_ + 1) * 512],
            start=True, stop=True), "sT")
    b.wait(V, "sT")
    b.op(V, lambda e: e.tensor_scalar(ROW[0:1, 0:2048], PS[0:1, 0:2048],
                                      1.0 / E, None, Alu.mult), "sV")
    b.op(V, lambda e: e.tensor_scalar(ROW[0:1, 2048:4096], PS[0:1, 2048:4096],
                                      1.0 / E, None, Alu.mult), "sV")
    mu = ROW[0:1, 0:2048]
    ex2 = ROW[0:1, 2048:4096]
    musq = sb["S16B"][0:1, 0:2048]
    var = sb["SM16"][0:1, 0:2048]
    sd = sb["S16A"][0:1, 0:2048]
    rr = sb["A16"][0:1, 0:2048]
    bb = sb["S16B"][0:1, 0:2048]
    b.vsync(V)
    b.op(V, lambda e: e.tensor_tensor(musq, mu, mu, Alu.mult), "sV")
    b.vsync(V)
    b.op(V, lambda e: e.tensor_tensor(var, ex2, musq, Alu.subtract), "sV")
    b.wait(Sc, "sV")
    b.op(Sc, lambda e: e.activation(out=sd, in_=var, func=Act.Sqrt,
                                    bias=sb["CST"][0:1, 1:2], scale=1.0), "sS")
    b.wait(V, "sS")
    b.op(V, lambda e: e.reciprocal(rr, sd), "sV")
    b.vsync(V)
    b.op(V, lambda e: e.tensor_tensor(bb, mu, rr, Alu.mult), "sV")
    b.vsync(V)
    b.wait(G, "sV")
    b.dma_g(sp["scr_row"][2:3, 0:2048], bb)
    b.dma_g(sp["scr_row"][3:4, 0:2048], rr)
    b.wait(G, "dG")
    b.dma_g(BBC[:, :], sp["scr_row"][2:3, 0:2048].partition_broadcast(128))
    b.dma_g(sb["SM16"][:, :],
          sp["scr_row"][3:4, 0:2048].partition_broadcast(16))
    b.wait(V, "dG")
    b.op(V, lambda e: e.tensor_tensor(sb["A16"][:, :], sb["C16"][:, :],
                                      sb["SM16"][:, :], Alu.mult), "sV")
    b.vsync(V)
    for ec in range(NEC):
        b.wait(T, "sV")
        for np_ in range(4):
            b.op(T, lambda e, ec=ec, np_=np_: e.matmul(
                PS[:, np_ * 512:(np_ + 1) * 512],
                sb["IND16"][:, ec * 128:(ec + 1) * 128],
                sb["A16"][:, np_ * 512:(np_ + 1) * 512],
                start=True, stop=True), "sT")
        b.wait(V, "sT")
        b.wait(Sc, "sV")
        b.op(V, lambda e, ec=ec: e.tensor_tensor(
            SQ2[:, :], ACL[:, ec * MYT:(ec + 1) * MYT], PS[:, 0:2048],
            Alu.mult), "sV")
        b.op(V, lambda e: e.tensor_tensor(SQ2[:, :], SQ2[:, :], BBC[:, :],
                                          Alu.subtract), "sV")
        b.wait(V, "dG")
        b.op(V, lambda e, ec=ec: e.tensor_scalar(
            ACL[:, ec * MYT:(ec + 1) * MYT], SQ2[:, :],
            sb["GAM"][:, ec:ec + 1], sb["BET"][:, ec:ec + 1],
            Alu.mult, Alu.add), "sV")
        b.wait(G, "sV")
        b.dma_g(sp["sp_aln"][ec * 128:(ec + 1) * 128, :],
              ACL[:, ec * MYT:(ec + 1) * MYT])

    b.barrier()

    def prep_ffn_weight(src, dstT, nchunks, icols, dst_stride, wna, wsc):
        nb = icols // 32
        for c in range(nchunks):
            pb = (c % 2) * icols
            b.wait(Sy, "sT")
            b.wait(Sy, "sV")
            ld = b.dma(Sy, wna[:, pb:pb + icols], src[c * 128:(c + 1) * 128, :])
            b.wait(V, "dS")
            b.wait(V, "sT")
            wna3 = wna[:, pb:pb + icols].rearrange("p (nb k) -> p nb k", k=32)
            b.op(V, lambda e, wna3=wna3, nb=nb: e.tensor_reduce(
                sb["BMX"][:, 0:nb], wna3, AX, Alu.max,
                apply_absolute_value=True), "sV")
            b.vsync(V)
            b.op(V, lambda e, nb=nb: e.tensor_scalar(
                sb["BSC"][:, 0:nb], sb["BMX"][:, 0:nb], 1.0 / 7.0, 1e-12,
                Alu.mult, Alu.max), "sV")
            b.vsync(V)
            b.op(V, lambda e, nb=nb: e.reciprocal(
                sb["BRS"][:, 0:nb], sb["BSC"][:, 0:nb]), "sV")
            b.vsync(V)
            qsc = wsc[:, 0:icols]
            qsc3 = qsc.rearrange("p (nb k) -> p nb k", k=32)
            b.op(V, lambda e, wna3=wna3, qsc3=qsc3, nb=nb: e.tensor_tensor(
                qsc3, wna3,
                sb["BRS"][:, 0:nb].unsqueeze(2).broadcast_to([128, nb, 32]), Alu.mult), "sV")
            b.op(V, lambda e, qsc=qsc: e.tensor_scalar(
                qsc, qsc, MAGIC, MAGIC, Alu.add, Alu.subtract), "sV")
            b.op(V, lambda e, qsc=qsc: e.tensor_scalar(
                qsc, qsc, -8.0, 7.0, Alu.max, Alu.min), "sV")
            b.op(V, lambda e, qsc3=qsc3, wna3=wna3, nb=nb: e.tensor_tensor(
                wna3, qsc3,
                sb["BSC"][:, 0:nb].unsqueeze(2).broadcast_to([128, nb, 32]), Alu.mult), "sV")
            b.wait(T, "sV")
            for s8 in range(icols // 128):
                pp = PS[:, (s8 % 2) * 512:(s8 % 2) * 512 + 128]
                if s8 >= 2:
                    b.wait(T, "sV")
                b.op(T, lambda e, pp=pp, pb=pb, s8=s8, wna=wna: e.transpose(
                    pp, wna[:, pb + s8 * 128: pb + (s8 + 1) * 128],
                    sb["IDF"][:, :]), "sT")
                b.wait(V, "sT")
                b.op(V, lambda e, pp=pp, s8=s8, c=c: e.tensor_copy(
                    sb[dstT][:, s8 * dst_stride + c * 128:
                             s8 * dst_stride + (c + 1) * 128], pp), "sV")

    prep_ffn_weight(dr["wu"], "WUT", NIC, E, I, sb["WNA"], sb["WSC"])
    prep_ffn_weight(dr["wg"], "WGT", NIC, E, I, sb["WNA"], sb["WSC"])

    b.barrier()
    for wt, spd, amcol in (("WUT", sp["sp_up"], 1), ("WGT", sp["sp_gate"], 2)):
        for ic in range(NIC):
            base = (ic % 2) * 4 * 512
            b.wait(T, "sV")
            for ec in range(NEC):
                for nq in range(4):
                    b.op(T, lambda e, base=base, nq=nq, ec=ec, ic=ic, wt=wt:
                         e.matmul(
                             PS[:, base + nq * 512: base + (nq + 1) * 512],
                             sb[wt][:, ec * I + ic * 128:
                                    ec * I + (ic + 1) * 128],
                             sb["XQT"][:, ec * MYT + nq * 512:
                                       ec * MYT + (nq + 1) * 512],
                             start=(ec == 0), stop=(ec == NEC - 1)), "sT")
            fb = (ic % 2) * 2048
            b.wait(V, "sT")
            b.wait(V, "dG")
            for nq in range(4):
                b.op(V, lambda e, base=base, nq=nq, fb=fb: e.tensor_copy(
                    sb["FEV"][:, fb + nq * 512: fb + (nq + 1) * 512],
                    PS[:, base + nq * 512: base + (nq + 1) * 512]), "sV")
            b.op(V, lambda e, fb=fb: e.tensor_reduce(
                ROW[0:128, 384:385], sb["FEV"][:, fb:fb + 2048], AX, Alu.max,
                apply_absolute_value=True), "sV")
            b.vsync(V)
            b.op(V, lambda e, amcol=amcol: e.tensor_tensor(
                sb["AMX"][:, amcol:amcol + 1], sb["AMX"][:, amcol:amcol + 1],
                ROW[0:128, 384:385], Alu.max), "sV")
            b.vsync(V)
            b.wait(G, "sV")
            b.dma_g(spd[ic * 128:(ic + 1) * 128, :], sb["FEV"][:, fb:fb + 2048])
    global_scale(1, "ug", 2, "SUGB", 32767.0)

    b.barrier()
    FG, FU, FT1 = sb["FG"], sb["FU"], sb["FT1"]
    FGQ, FSQ, FG2 = sb["FGQ"], sb["FSQ"], sb["FG2"]
    for ic in range(NIC):
        b.wait(Sy, "sV")
        b.wait(Sy, "sS")
        ldg = b.dma(Sy, FG[:, :], sp["sp_gate"][ic * 128:(ic + 1) * 128, :])
        ldu = b.dma(Sy, FU[:, :], sp["sp_up"][ic * 128:(ic + 1) * 128, :])
        b.wait(Sc, "dS")
        b.wait(V, "dG")
        qdq_chunk(FG[:, :], FGQ[:, :], sb["SUGB"][:, 2:3], sb["SUGB"][:, 3:4],
                  -32768.0, 32767.0, FT1[:, :])
        b.op(Sc, lambda e: e.activation(out=FSQ[:, :], in_=FGQ[:, :],
                                        func=Act.Sigmoid), "sS")
        b.vsync(Sc)
        b.op(Sc, lambda e: e.activation(out=FT1[:, :], in_=FSQ[:, :],
                                        func=Act.Identity,
                                        bias=sb["CST"][:, 0:1],
                                        scale=sb["CST"][:, 2:3]), "sS")
        b.wait(V, "sS")
        b.op(V, lambda e: e.tensor_scalar(FT1[:, :], FT1[:, :], MAGIC, 0.0,
                                          Alu.subtract, Alu.max), "sV")
        b.op(V, lambda e: e.tensor_scalar(FSQ[:, :], FT1[:, :], 65535.0,
                                          1.0 / 65536.0, Alu.min,
                                          Alu.mult), "sV")
        b.wait(V, "dG")
        b.op(V, lambda e: e.tensor_tensor(FG2[:, :], FGQ[:, :], FSQ[:, :],
                                          Alu.mult), "sV")
        b.op(V, lambda e: e.tensor_reduce(ROW[0:128, 384:385], FG2[:, :], AX,
                                          Alu.max,
                                          apply_absolute_value=True), "sV")
        b.vsync(V)
        b.op(V, lambda e: e.tensor_tensor(sb["AMX"][:, 3:4], sb["AMX"][:, 3:4],
                                          ROW[0:128, 384:385], Alu.max), "sV")
        b.vsync(V)
        b.wait(G, "sV")
        b.dma_g(sp["sp_g2"][ic * 128:(ic + 1) * 128, :], FG2[:, :])
        b.wait(Sc, "dS")
        qdq_chunk(FU[:, :], FGQ[:, :], sb["SUGB"][:, 0:1], sb["SUGB"][:, 1:2],
                  -32768.0, 32767.0, FT1[:, :])
        b.wait(G, "sV")
        b.dma_g(sp["sp_uq"][ic * 128:(ic + 1) * 128, :], FGQ[:, :])
    global_scale(3, "g2", 1, "S2B", 32767.0)

    b.barrier()
    for ic in range(NIC):
        b.wait(Sy, "sV")
        b.wait(Sy, "sS")
        ldg = b.dma(Sy, FG[:, :], sp["sp_g2"][ic * 128:(ic + 1) * 128, :])
        ldu = b.dma(Sy, FU[:, :], sp["sp_uq"][ic * 128:(ic + 1) * 128, :])
        b.wait(Sc, "dS")
        b.wait(V, "dG")
        qdq_chunk(FG[:, :], FGQ[:, :], sb["S2B"][:, 0:1], sb["S2B"][:, 1:2],
                  -32768.0, 32767.0, FT1[:, :])
        b.wait(V, "dS")
        b.wait(V, "dG")
        b.op(V, lambda e: e.tensor_tensor(FG2[:, :], FGQ[:, :], FU[:, :],
                                          Alu.mult), "sV")
        b.op(V, lambda e: e.tensor_reduce(ROW[0:128, 384:385], FG2[:, :], AX,
                                          Alu.max,
                                          apply_absolute_value=True), "sV")
        b.op(V, lambda e: e.tensor_tensor(sb["AMX"][:, 4:5], sb["AMX"][:, 4:5],
                                          ROW[0:128, 384:385], Alu.max), "sV")
        b.wait(G, "sV")
        b.dma_g(sp["sp_o"][ic * 128:(ic + 1) * 128, :], FG2[:, :])
    global_scale(4, "o", 1, "SOB", 32767.0)

    b.barrier()
    prep_ffn_weight(dr["wd"], "WDT", NEC, I, E, sb["W3A"], sb["W3S"])

    b.barrier()
    FG4, FT4, OQT = sb["FG4"], sb["FT4"], sb["OQT"]
    for ic in range(NIC):
        b.wait(Sy, "sV")
        b.wait(Sy, "sS")
        ld = b.dma(Sy, FG4[:, :], sp["sp_o"][ic * 128:(ic + 1) * 128, :])
        b.wait(Sc, "dS")
        b.wait(Sc, "sV")
        b.op(Sc, lambda e: e.activation(out=FT4[:, :], in_=FG4[:, :],
                                        func=Act.Identity,
                                        bias=sb["CST"][:, 0:1],
                                        scale=sb["SOB"][:, 1:2]), "sS")
        b.wait(V, "sS")
        b.op(V, lambda e: e.tensor_scalar(FT4[:, :], FT4[:, :], MAGIC,
                                          -32768.0, Alu.subtract,
                                          Alu.max), "sV")
        b.op(V, lambda e, ic=ic: e.tensor_scalar(
            OQT[:, ic * MYT:(ic + 1) * MYT], FT4[:, :], 32767.0,
            sb["SOB"][:, 0:1], Alu.min, Alu.mult), "sV")

    b.barrier()
    for hc in range(NEC):
        base = (hc % 2) * 4 * 512
        b.wait(Sy, "sV")
        ld = b.dma(Sy, sb["ALC"][:, :], sp["sp_aln"][hc * 128:(hc + 1) * 128, :])
        b.wait(T, "sV")
        for ic in range(NIC):
            for nq in range(4):
                b.op(T, lambda e, base=base, nq=nq, ic=ic, hc=hc: e.matmul(
                    PS[:, base + nq * 512: base + (nq + 1) * 512],
                    sb["WDT"][:, ic * E + hc * 128: ic * E + (hc + 1) * 128],
                    OQT[:, ic * MYT + nq * 512: ic * MYT + (nq + 1) * 512],
                    start=(ic == 0), stop=(ic == NIC - 1)), "sT")
        b.wait(V, "sT")
        b.wait(V, "dS")
        b.wait(V, "dG")
        for nq in range(4):
            b.op(V, lambda e, base=base, nq=nq: e.tensor_tensor(
                sb["AFE"][:, nq * 512:(nq + 1) * 512],
                PS[:, base + nq * 512: base + (nq + 1) * 512],
                sb["ALC"][:, nq * 512:(nq + 1) * 512], Alu.add), "sV")
        b.wait(G, "sV")
        b.dma_g(sp["sp_af"][hc * 128:(hc + 1) * 128, :], sb["AFE"][:, :])

    b.barrier()
    AFL, XT6, OSB = sb["AFL"], sb["XT6"], sb["OSB"]
    for ec in range(NEC):
        b.dma(Sy, AFL[:, ec * MYT:(ec + 1) * MYT],
              sp["sp_af"][ec * 128:(ec + 1) * 128, :])
    afl_ld = b.sems["dS"][1]
    for tt in range(16):
        pb = (tt % 2) * E
        base = (tt % 2) * 1024
        if tt >= 2:
            b.wait(Sy, "sV")
        ld = b.dma(Sy, XT6[:, pb:pb + E], dr["x_tok"][tt * 128:(tt + 1) * 128, :])
        b.wait(T, "dS")
        b.wait(T, "sV")
        for hc in range(NEC):
            b.op(T, lambda e, base=base, hc=hc, tt=tt: e.transpose(
                PS[:, base + hc * 128: base + (hc + 1) * 128],
                AFL[:, hc * MYT + tt * 128: hc * MYT + (tt + 1) * 128],
                sb["IDF"][:, :]), "sT")
        b.wait(V, "sT")
        b.wait(V, "dS")
        b.wait(V, "dG")
        b.op(V, lambda e, base=base, pb=pb: e.tensor_tensor(
            OSB[:, pb:pb + E], PS[:, base:base + E], XT6[:, pb:pb + E],
            Alu.add), "sV")
        b.wait(G, "sV")
        b.dma_g(out_d[tt * 128:(tt + 1) * 128, :], OSB[:, pb:pb + E])
    b.wait(G, "dG")


_NC_CACHE = {}


def kernel(L0_attn_x, L0_attn_Wq, L0_attn_Wk, L0_attn_Wv, L0_attn_ln_gamma,
           L0_attn_ln_beta, L0_ffn_w_gate, L0_ffn_w_up, L0_ffn_w_down):
    x = np.ascontiguousarray(np.asarray(L0_attn_x, np.float32))
    wq = np.asarray(L0_attn_Wq, np.float32)
    wk = np.asarray(L0_attn_Wk, np.float32)
    wv = np.asarray(L0_attn_Wv, np.float32)
    gam = np.asarray(L0_attn_ln_gamma, np.float32)
    bet = np.asarray(L0_attn_ln_beta, np.float32)
    wg = np.ascontiguousarray(np.asarray(L0_ffn_w_gate, np.float32))
    wu = np.ascontiguousarray(np.asarray(L0_ffn_w_up, np.float32))
    wd = np.ascontiguousarray(np.asarray(L0_ffn_w_down, np.float32))

    wqT = np.ascontiguousarray(wq.T)
    wkT = np.ascontiguousarray(wk.T)
    wvT = np.ascontiguousarray(wv.T)
    gamma8 = np.ascontiguousarray(gam.reshape(NEC, 128).T)
    beta8 = np.ascontiguousarray(bet.reshape(NEC, 128).T)

    if "nc" not in _NC_CACHE:
        _NC_CACHE["nc"] = build_nc()
    nc = _NC_CACHE["nc"]

    in_maps = []
    for c in range(N_CORES):
        bq, h = divmod(c, 2)
        qoff = h * MYT
        xloc = np.roll(x[bq], -qoff, axis=0)
        if h == 0:
            meta = np.array([[-1.0 / 4096.0, 3968.0 / 4096.0]], np.float32)
        else:
            meta = np.array([[1.0 / 4096.0, 1.0 - 3968.0 / 4096.0]], np.float32)
        in_maps.append({
            "xseqT": np.ascontiguousarray(xloc.T),
            "x_tok": np.ascontiguousarray(xloc),
            "wqT": wqT, "wkT": wkT, "wvT": wvT,
            "wu": wu, "wg": wg, "wd": wd,
            "gamma8": gamma8, "beta8": beta8,
            "meta": meta,
        })

    res = run_bass_kernel_spmd(nc, in_maps, core_ids=list(range(N_CORES)))
    out = np.empty((4, S, E), np.float32)
    for c in range(N_CORES):
        bq, h = divmod(c, 2)
        out[bq, h * MYT:(h + 1) * MYT, :] = res.results[c]["out"]
    return out


# version 02
# speedup vs baseline: 1.0201x; 1.0201x over previous
import numpy as np
from contextlib import ExitStack

import concourse.bass as bass
import concourse.mybir as mybir
from concourse.bass_utils import run_bass_kernel_spmd

F32 = mybir.dt.float32
BF16 = mybir.dt.bfloat16
Alu = mybir.AluOpType
Act = mybir.ActivationFunctionType
AX = mybir.AxisListType.X

MAGIC = 1.5 * 2.0**23
N_CORES = 8
S = 4096
E = 1024
H = 16
I = 3072
MYT = 2048
NEC = E // 128
NIC = I // 128

ENGS = ("sync", "gpsimd", "vector", "scalar", "tensor")
V, Sc, T, G, Sy = "vector", "scalar", "tensor", "gpsimd", "sync"


class Builder:

    def __init__(self, nc):
        self.nc = nc
        self.ops = {e: [] for e in ENGS}
        self.sems = {}

    def mksem(self, name):
        h = self.nc.semaphore(name).__enter__()
        self.sems[name] = [h, 0]

    AUTO_CHAIN = {"vector": "sV", "scalar": "sS", "gpsimd": "sG"}

    def op(self, eng, fn, inc=None, k=1):
        ch = self.AUTO_CHAIN.get(eng)
        if ch is not None:
            self.wait(eng, ch)
        if inc is None:
            self.ops[eng].append(fn)
            return None
        rec = self.sems[inc]
        rec[1] += k
        h = rec[0]
        self.ops[eng].append(lambda e, fn=fn, h=h, k=k: fn(e).then_inc(h, k))
        return rec[1]

    def wait(self, eng, sem, val=None):
        h, c = self.sems[sem]
        v = c if val is None else val
        if v > 0:
            self.ops[eng].append(lambda e, h=h, v=v: e.wait_ge(h, v))

    def dma_g(self, out, in_, **kw):
        return self.dma("gpsimd", out, in_, serial=False, **kw)

    def dmas(self, eng, sem, out, in_, **kw):
        if sem not in self.sems:
            self.mksem(sem)
        return self.op(eng, lambda e: e.dma_start(out=out, in_=in_, **kw),
                       sem, 16)

    def dma(self, eng, out, in_, serial=True, **kw):
        q = {"sync": "dS", "gpsimd": "dG", "scalar": "dA"}[eng]
        n = self.op(eng, lambda e: e.dma_start(out=out, in_=in_, **kw), q, 16)
        if serial:
            self.wait(eng, q, n)
        return n

    def vsync(self, eng):
        sem = {"vector": "sV", "scalar": "sS", "tensor": "sT",
               "gpsimd": "sG"}[eng]
        self.wait(eng, sem)

    def vfiller(self, sb):
        self.op("vector", lambda e: e.tensor_copy(sb["ROW"][0:1, 4095:4096],
                                                  sb["CST"][0:1, 0:1]), "sV")

    def barrier(self, engs=ENGS):
        snap = [(h, c) for (h, c) in self.sems.values() if c > 0]
        for eng in engs:
            for h, v in snap:
                self.ops[eng].append(lambda e, h=h, v=v: e.wait_ge(h, v))


def build_nc():
    nc = bass.Bass("TRN2", target_bir_lowering=False)

    dram = {}
    for nm, shp in [
        ("xseqT", [E, S]), ("x_tok", [S, E]),
        ("wqT", [E, E]), ("wkT", [E, E]), ("wvT", [E, E]),
        ("wu", [I, E]), ("wg", [I, E]), ("wd", [E, I]),
        ("gamma8", [128, NEC]), ("beta8", [128, NEC]),
        ("meta", [1, 2]),
    ]:
        dram[nm] = nc.dram_tensor(nm, shp, F32, kind="ExternalInput").ap()
    out_d = nc.dram_tensor("out", [MYT, E], F32, kind="ExternalOutput").ap()

    sp = {}
    for nm, shp in [
        ("sp_v", [S, H * 65]), ("sp_attn", [E, MYT]), ("sp_sums", [H, MYT]),
        ("sp_aln", [E, MYT]), ("sp_up", [I, MYT]), ("sp_gate", [I, MYT]),
        ("sp_g2", [I, MYT]), ("sp_uq", [I, MYT]), ("sp_o", [I, MYT]),
        ("sp_af", [E, MYT]), ("scr_row", [4, S]), ("sp_st", [32, MYT]),
    ]:
        sp[nm] = nc.dram_tensor(
            nm, shp, BF16 if nm == "sp_v" else F32, kind="Internal").ap()
    sp["cc_in"] = {k: nc.dram_tensor(f"cc{k}_i", [1, w], F32, kind="Internal").ap()
                   for k, w in (("x", 1), ("ug", 2), ("g2", 1), ("o", 1))}
    sp["cc_out"] = {k: nc.dram_tensor(f"cc{k}_o", [1, w], F32, kind="Internal").ap()
                    for k, w in (("x", 1), ("ug", 2), ("g2", 1), ("o", 1))}

    st = ExitStack()
    sb = {}

    def perm(nm, shp, dt):
        sb[nm] = st.enter_context(nc.sbuf_tensor(nm, shp, dt))
        return sb[nm]

    perm("CST", [128, 8], F32)
    perm("SXB", [128, 2], F32)
    perm("SUGB", [128, 4], F32)
    perm("S2B", [128, 2], F32)
    perm("SOB", [128, 2], F32)
    perm("METB", [128, 2], F32)
    perm("AMX", [128, 8], F32)
    perm("IDF", [128, 128], F32)
    perm("IND2", [128, 2], F32)
    perm("IND16", [16, 1024], F32)
    perm("ONES1", [128, 1], F32)
    perm("PIOTA", [128, 1], F32)
    perm("GAM", [128, NEC], F32)
    perm("BET", [128, NEC], F32)
    perm("ROW", [128, 4096], F32)
    perm("BMX", [128, 96], F32)
    perm("BSC", [128, 96], F32)
    perm("BRS", [128, 96], F32)

    TOP = nc.SBUF_PARTITION_SIZE_BYTES
    ZB = 24576

    def at(nm, off, shp, dt):
        end = off + int(np.prod(shp[1:])) * mybir.dt.size(dt)
        assert ZB <= off and end <= TOP, (nm, off, end)
        sb[nm] = nc.alloc_sbuf_tensor_at(nm, shp, dt, offset=off)
        return sb[nm]

    at("WVTS", 40960, [128, NEC * E], BF16)
    at("WKTS", 57344, [128, NEC * E], BF16)
    at("WQTS", 73728, [128, NEC * E], BF16)
    at("XNTA", 90112, [128, NEC * MYT], BF16)
    at("XNTB", 122880, [128, NEC * MYT], BF16)
    at("XC2", 155648, [128, S], F32)
    at("MUBC", 172032, [128, S], F32)
    at("RSBC", 188416, [128, S], F32)
    at("XT2", 204800, [128, 2 * E], F32)
    at("SQT", 212992, [128, E], F32)
    at("STT", 217088, [128, 128], F32)
    at("VEV", 217600, [128, H * 65], BF16)
    at("KF", 155648, [128, NEC * S], BF16)
    at("QF", 40960, [128, NEC * MYT], BF16)
    at("DNE", 122880, [128, 4096], F32)
    at("DFA", 139264, [128, 4096], F32)
    at("SW", 73728, [128, 2 * 1024], F32)
    at("PT", 81920, [128, 2 * 1024], BF16)
    at("AVE", 86016, [128, 1024], F32)
    at("VSB", 221184, [128, 32 * 65], BF16)
    at("XQT", 155648, [128, NEC * MYT], BF16)
    at("SQ2", 188416, [128, MYT], F32)
    at("ACL", 40960, [128, NEC * MYT], F32)
    at("BBC", 106496, [128, MYT], F32)
    at("C16", 114688, [16, MYT], F32)
    at("A16", 122880, [16, MYT], F32)
    at("SM16", 131072, [16, MYT], F32)
    at("S16A", 139264, [16, MYT], F32)
    at("S16B", 147456, [16, MYT], F32)
    at("S2T", 196608, [2, MYT], F32)
    at("WUT", 40960, [128, NEC * I], BF16)
    at("WGT", 90112, [128, NEC * I], BF16)
    at("FEV", 139264, [128, 8 * 512], F32)
    at("WNA", 196608, [128, 2 * E], F32)
    at("WSC", 204800, [128, 2 * E], F32)
    at("FG", 40960, [128, MYT], F32)
    at("FU", 49152, [128, MYT], F32)
    at("FT1", 57344, [128, MYT], F32)
    at("FGQ", 65536, [128, MYT], F32)
    at("FSQ", 73728, [128, MYT], F32)
    at("FG2", 81920, [128, MYT], F32)
    at("OQT", 40960, [128, NIC * MYT], BF16)
    at("WDT", 139264, [128, NIC * E], BF16)
    at("W3A", 188416, [128, 2 * I], F32)
    at("W3S", 212992, [128, I], F32)
    at("FG4", 188416, [128, MYT], F32)
    at("FT4", 196608, [128, MYT], F32)
    at("ALC", 188416, [128, MYT], F32)
    at("AFE", 196608, [128, MYT], F32)
    at("AFL", 40960, [128, NEC * MYT], F32)
    at("XT6", 106496, [128, 2 * E], F32)
    at("OSB", 114688, [128, 2 * E], F32)

    PS = nc.alloc_psum_tensor("PS", [128, 4096], F32)

    b = Builder(nc)
    for s_ in ("dS", "dG", "dA", "sV", "sS", "sT", "sG", "sCC"):
        b.mksem(s_)

    emit_program(nc, b, sb, PS, dram, out_d, sp)
    _NC_CACHE["builder"] = b

    with nc.Block() as block:
        @block.sync
        def _(e):
            for f in b.ops["sync"]:
                f(e)

        @block.gpsimd
        def _(e):
            for f in b.ops["gpsimd"]:
                f(e)

        @block.vector
        def _(e):
            for f in b.ops["vector"]:
                f(e)

        @block.scalar
        def _(e):
            for f in b.ops["scalar"]:
                f(e)

        @block.tensor
        def _(e):
            for f in b.ops["tensor"]:
                f(e)

    st.close()
    return nc


def emit_program(nc, b, sb, PS, dr, out_d, sp):
    ROW = sb["ROW"]
    XNT = {0: sb["XNTA"], 1: sb["XNTB"]}

    def xnt_cols(ec, c0, c1):
        assert (c0 // MYT) == ((c1 - 1) // MYT)
        half = c0 // MYT
        t = XNT[half]
        return t[:, ec * MYT + c0 % MYT: ec * MYT + c0 % MYT + (c1 - c0)]

    b.op(V, lambda e: e.memset(sb["CST"][:, 0:1], MAGIC), "sV")
    b.op(V, lambda e: e.memset(sb["CST"][:, 1:2], 1e-5), "sV")
    b.op(V, lambda e: e.memset(sb["CST"][:, 2:3], 65536.0), "sV")
    b.op(V, lambda e: e.memset(sb["CST"][:, 3:4], -1920.0 / 4096.0), "sV")
    b.op(V, lambda e: e.memset(sb["CST"][:, 4:5], 1.0 / 4096.0), "sV")
    b.op(V, lambda e: e.memset(sb["ONES1"][:, :], 1.0), "sV")
    b.op(V, lambda e: e.memset(sb["AMX"][:, :], 0.0), "sV")
    b.op(G, lambda e: e.iota(sb["IDF"][:, :], pattern=[[1, 128]], base=0,
                             channel_multiplier=-1,
                             allow_small_or_imprecise_dtypes=True), "sG")
    b.op(G, lambda e: e.iota(sb["PIOTA"][:, :], pattern=[[0, 1]], base=0,
                             channel_multiplier=1,
                             allow_small_or_imprecise_dtypes=True), "sG")
    b.dma_g(sb["METB"][:, :], dr["meta"][0:1, :].partition_broadcast(128))
    b.wait(V, "sG")
    b.op(V, lambda e: e.tensor_scalar(sb["IND2"][:, 0:1], sb["PIOTA"][:, :],
                                      64.0, None, Alu.is_lt), "sV")
    b.op(V, lambda e: e.tensor_scalar(sb["IND2"][:, 1:2], sb["PIOTA"][:, :],
                                      64.0, None, Alu.is_ge), "sV")
    b.op(V, lambda e: e.tensor_scalar(sb["IDF"][:, :], sb["IDF"][:, :],
                                      0.0, None, Alu.is_equal), "sV")
    b.op(G, lambda e: e.iota(sb["IND16"][:, :],
                             pattern=[[2, 8], [1, 2], [0, 64]], base=0,
                             channel_multiplier=0,
                             allow_small_or_imprecise_dtypes=True), "sG")
    b.wait(V, "sG")
    b.op(V, lambda e: e.tensor_tensor(
        sb["IND16"][:, :], sb["IND16"][:, :],
        sb["PIOTA"][0:16, 0:1].broadcast_to([16, 1024]), Alu.is_equal), "sV")
    b.dma(Sy, sb["GAM"][:, :], dr["gamma8"][:, :])
    b.dma(Sy, sb["BET"][:, :], dr["beta8"][:, :])
    for nm, dst in (("wqT", "WQTS"), ("wkT", "WKTS"), ("wvT", "WVTS")):
        for ec in range(NEC):
            b.dma_g(sb[dst][:, ec * E:(ec + 1) * E],
                  dr[nm][ec * 128:(ec + 1) * 128, :])

    def global_scale(col, key, packw, dst, qmaxv):
        b.wait(T, "sV")
        b.op(T, lambda e: e.transpose(PS[0:packw, 1536:1664],
                                      sb["AMX"][:, col:col + packw],
                                      sb["IDF"][:, :]), "sT")
        b.wait(V, "sT")
        b.op(V, lambda e: e.tensor_reduce(ROW[0:packw, 256:257],
                                          PS[0:packw, 1536:1664], AX,
                                          Alu.max), "sV")
        b.wait(G, "sV")
        b.dma_g(sp["cc_in"][key][0:1, :].rearrange("a (p c) -> (a p) c",
                                                    p=packw),
              ROW[0:packw, 256:257])
        b.wait(G, "dG")
        b.op(G, lambda e: e.collective_compute(
            "AllReduce", Alu.max, replica_groups=[list(range(N_CORES))],
            ins=[sp["cc_in"][key].opt()], outs=[sp["cc_out"][key].opt()],
        ), "sCC")
        b.wait(G, "sCC")
        b.dma_g(ROW[0:1, 260:260 + packw], sp["cc_out"][key][0:1, :])
        b.wait(V, "dG")
        b.op(V, lambda e: e.tensor_scalar(
            ROW[0:1, 280:280 + packw], ROW[0:1, 260:260 + packw],
            1.0 / qmaxv, 1e-12, Alu.mult, Alu.max), "sV")
        b.vsync(V)
        b.op(V, lambda e: e.reciprocal(
            ROW[0:1, 288:288 + packw], ROW[0:1, 280:280 + packw]), "sV")
        b.vsync(V)
        for i in range(packw):
            b.op(V, lambda e, i=i: e.tensor_copy(
                ROW[0:1, 300 + 2 * i:301 + 2 * i],
                ROW[0:1, 280 + i:281 + i]), "sV")
            b.op(V, lambda e, i=i: e.tensor_copy(
                ROW[0:1, 301 + 2 * i:302 + 2 * i],
                ROW[0:1, 288 + i:289 + i]), "sV")
        b.wait(G, "sV")
        b.dma_g(sp["scr_row"][2:3, 0:2 * packw], ROW[0:1, 300:300 + 2 * packw])
        b.wait(G, "dG")
        b.dma_g(sb[dst][:, 0:2 * packw],
              sp["scr_row"][2:3, 0:2 * packw].partition_broadcast(128))

    def qdq_chunk(src_ap, dst_ap, s_ap, rs_ap, lo, hi, tmp_ap):
        b.wait(Sc, "sV")
        b.op(Sc, lambda e: e.activation(out=tmp_ap, in_=src_ap,
                                        func=Act.Identity,
                                        bias=sb["CST"][:, 0:1],
                                        scale=rs_ap), "sS")
        b.wait(V, "sS")
        b.op(V, lambda e: e.tensor_scalar(tmp_ap, tmp_ap, MAGIC, lo,
                                          Alu.subtract, Alu.max), "sV")
        b.op(V, lambda e: e.tensor_scalar(dst_ap, tmp_ap, hi, s_ap,
                                          Alu.min, Alu.mult), "sV")
        b.wait(Sc, "sV")

    XT2, SQT, STT, XC2 = sb["XT2"], sb["SQT"], sb["STT"], sb["XC2"]
    for tt in range(32):
        pb = (tt % 2) * E
        if tt >= 2:
            b.wait(Sy, "sV")
            b.wait(Sy, "sS")
        ld = b.dma(Sy, XT2[:, pb:pb + E], dr["x_tok"][tt * 128:(tt + 1) * 128, :])
        b.wait(Sc, "dS")
        b.wait(Sc, "sV")
        b.op(Sc, lambda e, pb=pb: e.activation(
            out=SQT[:, :], in_=XT2[:, pb:pb + E], func=Act.Square), "sS")
        b.wait(V, "dS")
        b.op(V, lambda e, pb=pb, tt=tt: e.tensor_reduce(
            STT[:, tt:tt + 1], XT2[:, pb:pb + E], AX, Alu.add), "sV")
        b.wait(V, "sS")
        b.op(V, lambda e, tt=tt: e.tensor_reduce(
            STT[:, 32 + tt:33 + tt], SQT[:, :], AX, Alu.add), "sV")
    b.vsync(V)
    b.op(V, lambda e: e.tensor_scalar(STT[:, 0:32], STT[:, 0:32],
                                      1.0 / E, None, Alu.mult), "sV")
    b.op(V, lambda e: e.tensor_scalar(STT[:, 32:64], STT[:, 32:64],
                                      1.0 / E, None, Alu.mult), "sV")
    b.vsync(V)
    b.op(V, lambda e: e.tensor_tensor(STT[:, 64:96], STT[:, 0:32],
                                      STT[:, 0:32], Alu.mult), "sV")
    b.vsync(V)
    b.op(V, lambda e: e.tensor_tensor(STT[:, 64:96], STT[:, 32:64],
                                      STT[:, 64:96], Alu.subtract), "sV")
    b.wait(Sc, "sV")
    b.op(Sc, lambda e: e.activation(out=STT[:, 96:128], in_=STT[:, 64:96],
                                    func=Act.Sqrt, bias=sb["CST"][:, 1:2],
                                    scale=1.0), "sS")
    b.wait(V, "sS")
    b.op(V, lambda e: e.reciprocal(STT[:, 96:128], STT[:, 96:128]), "sV")
    b.vsync(V)
    b.wait(T, "sV")
    b.op(T, lambda e: e.transpose(PS[0:32, 0:128], STT[:, 0:32],
                                  sb["IDF"][:, :]), "sT")
    b.op(T, lambda e: e.transpose(PS[0:32, 512:640], STT[:, 96:128],
                                  sb["IDF"][:, :]), "sT")
    b.wait(V, "sT")
    b.op(V, lambda e: e.tensor_copy(ROW[0:32, 0:128], PS[0:32, 0:128]), "sV")
    b.op(V, lambda e: e.tensor_copy(ROW[0:32, 128:256], PS[0:32, 512:640]), "sV")
    b.wait(G, "sV")
    b.dma_g(sp["scr_row"][0:1, :].rearrange("a (p c) -> (a p) c", p=32),
          ROW[0:32, 0:128])
    b.dma_g(sp["scr_row"][1:2, :].rearrange("a (p c) -> (a p) c", p=32),
          ROW[0:32, 128:256])
    b.wait(G, "dG")
    b.dma_g(sb["MUBC"][:, :], sp["scr_row"][0:1, :].partition_broadcast(128))
    b.dma_g(sb["RSBC"][:, :], sp["scr_row"][1:2, :].partition_broadcast(128))
    for ec in range(NEC):
        pb = 0
        if ec >= 1:
            b.wait(Sy, "sV")
        ld = b.dma(Sy, XC2[:, pb:pb + S], dr["xseqT"][ec * 128:(ec + 1) * 128, :])
        b.wait(V, "dS")
        b.wait(V, "dG")
        b.op(V, lambda e, pb=pb: e.tensor_tensor(
            XC2[:, pb:pb + S], XC2[:, pb:pb + S], sb["MUBC"][:, :],
            Alu.subtract), "sV")
        for half in range(2):
            b.op(V, lambda e, pb=pb, ec=ec, half=half: e.tensor_tensor(
                XNT[half][:, ec * MYT:(ec + 1) * MYT],
                XC2[:, pb + half * MYT: pb + (half + 1) * MYT],
                sb["RSBC"][:, half * MYT:(half + 1) * MYT], Alu.mult), "sV")
            b.op(V, lambda e, ec=ec, half=half: e.tensor_reduce(
                STT[:, 64:65], XNT[half][:, ec * MYT:(ec + 1) * MYT], AX,
                Alu.max, apply_absolute_value=True), "sV")
            b.vsync(V)
            b.op(V, lambda e: e.tensor_tensor(
                sb["AMX"][:, 0:1], sb["AMX"][:, 0:1], STT[:, 64:65],
                Alu.max), "sV")
            b.vsync(V)
    global_scale(0, "x", 1, "SXB", 32767.0)

    b.barrier([T])
    VEV = sb["VEV"]
    for tt in range(32):
        half_col = tt % 2
        for half in range(2):
            pt = PS[:, 2048 + half * 512: 2048 + (half + 1) * 512]
            if tt > 0 or half > 0:
                b.wait(T, "sV")
            for ec in range(NEC):
                b.op(T, lambda e, tt=tt, ec=ec, pt=pt, half=half: e.matmul(
                    pt, xnt_cols(ec, tt * 128, (tt + 1) * 128),
                    sb["WVTS"][:, ec * E + half * 512:
                               ec * E + (half + 1) * 512],
                    start=(ec == 0), stop=(ec == NEC - 1)), "sT")
            b.wait(V, "sT")
            b.wait(V, "dG")
            b.op(V, lambda e, pt=pt, half=half: e.tensor_copy(
                VEV[:, half * 8 * 65:(half + 1) * 8 * 65]
                .rearrange("p (h c) -> p h c", c=65)[:, :, 0:64],
                pt.rearrange("p (h c) -> p h c", c=64)), "sV")
        b.op(V, lambda e: e.memset(
            VEV[:, :].rearrange("p (h c) -> p h c", c=65)[:, :, 64:65],
            1.0), "sV")
        b.wait(G, "sV")
        b.dma_g(sp["sp_v"][tt * 128:(tt + 1) * 128, :], VEV[:, :])

    b.barrier([T])
    KF, QF = sb["KF"], sb["QF"]
    for pair in range(NEC):
        for nk in range(8):
            pk = PS[:, 3072 + (nk % 2) * 512: 3072 + (nk % 2) * 512 + 512]
            b.wait(T, "sV")
            for ec in range(NEC):
                b.op(T, lambda e, pk=pk, ec=ec, pair=pair, nk=nk: e.matmul(
                    pk, sb["WKTS"][:, ec * E + pair * 128:
                                   ec * E + (pair + 1) * 128],
                    xnt_cols(ec, nk * 512, (nk + 1) * 512),
                    start=(ec == 0), stop=(ec == NEC - 1)), "sT")
            b.wait(V, "sT")
            b.op(V, lambda e, pk=pk, pair=pair, nk=nk: e.tensor_copy(
                KF[:, pair * S + nk * 512: pair * S + (nk + 1) * 512],
                pk), "sV")
    for pair in range(NEC):
        for nq in range(4):
            pk = PS[:, 3072 + (nq % 2) * 512: 3072 + (nq % 2) * 512 + 512]
            b.wait(T, "sV")
            for ec in range(NEC):
                b.op(T, lambda e, pk=pk, ec=ec, pair=pair, nq=nq: e.matmul(
                    pk, sb["WQTS"][:, ec * E + pair * 128:
                                   ec * E + (pair + 1) * 128],
                    xnt_cols(ec, nq * 512, (nq + 1) * 512),
                    start=(ec == 0), stop=(ec == NEC - 1)), "sT")
            b.wait(V, "sT")
            b.op(V, lambda e, pk=pk, pair=pair, nq=nq: e.tensor_scalar(
                QF[:, pair * MYT + nq * 512: pair * MYT + (nq + 1) * 512],
                pk, 1.0 / 32.0, None, Alu.mult), "sV")

    b.barrier([G])
    b.op(G, lambda e: e.iota(sb["DNE"][:, :], pattern=[[1, 4096]], base=0,
                             channel_multiplier=-1,
                             allow_small_or_imprecise_dtypes=True), "sG")
    b.op(G, lambda e: e.iota(sb["DFA"][:, :], pattern=[[1, 4096]], base=0,
                             channel_multiplier=-1,
                             allow_small_or_imprecise_dtypes=True), "sG")
    b.wait(V, "sG")
    b.wait(V, "dG")
    b.op(V, lambda e: e.tensor_scalar(sb["DFA"][:, :], sb["DFA"][:, :],
                                      sb["METB"][:, 0:1], sb["METB"][:, 1:2],
                                      Alu.mult, Alu.add), "sV")
    b.wait(Sc, "sG")
    b.wait(Sc, "sV")
    b.op(Sc, lambda e: e.activation(out=sb["DNE"][:, :], in_=sb["DNE"][:, :],
                                    func=Act.Abs, bias=sb["CST"][:, 3:4],
                                    scale=sb["CST"][:, 4:5]), "sS")

    b.barrier([T, Sc, G, Sy])
    SW, PT, AVE, VSB = sb["SW"], sb["PT"], sb["AVE"], sb["VSB"]
    DNE, DFA = sb["DNE"], sb["DFA"]
    exp_mark = {}
    av_mark = {}
    tt_mark = {}
    evac_mark = {}
    spill_mark = {}
    it = -1
    for head in range(H):
        pair, hh = divmod(head, 2)
        hb = hh * 64
        for qh in range(2):
            it += 1
            if it >= 1:
                b.wait(Sy, "sT", av_mark[(it - 1, 31)])
            vld = b.dmas(Sy, "dVS", VSB[:, :]
                         .rearrange("p (k c) -> p k c", c=65),
                         sp["sp_v"][:, head * 65:(head + 1) * 65]
                         .rearrange("(k p) c -> p k c", p=128))
            for kc in range(32):
                ppar = kc % 2
                pscr = PS[:, ppar * 1024:(ppar + 1) * 1024]
                swb = SW[:, ppar * 1024:(ppar + 1) * 1024]
                ptb = PT[:, ppar * 1024:(ppar + 1) * 1024]
                key = (it, kc - 2) if kc >= 2 else (it - 1, 30 + kc)
                if key in tt_mark:
                    b.wait(T, "sV", tt_mark[key])
                for jj in range(2):
                    b.op(T, lambda e, pscr=pscr, jj=jj, hb=hb, kc=kc, qh=qh,
                         pair=pair: e.matmul(
                             pscr[:, jj * 512:(jj + 1) * 512],
                             KF[hb:hb + 64,
                                pair * S + kc * 128: pair * S + (kc + 1) * 128],
                             QF[hb:hb + 64,
                                pair * MYT + qh * 1024 + jj * 512:
                                pair * MYT + qh * 1024 + (jj + 1) * 512],
                             start=True, stop=True), "sT")
                sc_mark = b.sems["sT"][1]
                if kc < 16:
                    doff = qh * 1024 - kc * 128 + 1920
                    dsl = DNE[:, doff:doff + 1024]
                else:
                    doff = qh * 1024 - (kc - 16) * 128 + 1920
                    dsl = DFA[:, doff:doff + 1024]
                b.wait(V, "sT", sc_mark)
                if key in exp_mark:
                    b.wait(V, "sS", exp_mark[key])
                tt_mark[(it, kc)] = b.op(
                    V, lambda e, pscr=pscr, dsl=dsl, swb=swb: e.tensor_tensor(
                        swb, pscr, dsl, Alu.mult), "sV")
                b.wait(Sc, "sV", tt_mark[(it, kc)])
                if key in av_mark:
                    b.wait(Sc, "sT", av_mark[key])
                exp_mark[(it, kc)] = b.op(
                    Sc, lambda e, swb=swb, ptb=ptb: e.activation(
                        out=ptb, in_=swb, func=Act.Exp), "sS")
                b.wait(T, "sS", exp_mark[(it, kc)])
                if kc == 0:
                    b.wait(T, "dVS", vld)
                    if (it - 1, "evac") in evac_mark:
                        b.wait(T, "sV", evac_mark[(it - 1, "evac")])
                for jj in range(2):
                    b.op(T, lambda e, jj=jj, kc=kc, ptb=ptb: e.matmul(
                        PS[0:65, 2048 + jj * 512: 2560 + jj * 512],
                        VSB[:, kc * 65: kc * 65 + 65],
                        ptb[:, jj * 512:(jj + 1) * 512],
                        start=(kc == 0), stop=(kc == 31)), "sT")
                av_mark[(it, kc)] = b.sems["sT"][1]
            b.wait(V, "sT", av_mark[(it, 31)])
            if (it - 1) >= 0:
                b.wait(V, "dA1")
                b.wait(V, "dA2")
            evac_mark[(it, "evac")] = b.op(
                V, lambda e: e.tensor_copy(AVE[0:65, :],
                                           PS[0:65, 2048:3072]), "sV")
            b.wait(Sc, "sV", evac_mark[(it, "evac")])
            b.dmas(Sc, "dA1", sp["sp_attn"][head * 64:(head + 1) * 64,
                                            qh * 1024:(qh + 1) * 1024],
                   AVE[0:64, :])
            b.dmas(Sc, "dA2", sp["sp_sums"][head:head + 1,
                                            qh * 1024:(qh + 1) * 1024],
                   AVE[64:65, :])

    b.barrier()
    XQT, SQ2 = sb["XQT"], sb["SQ2"]
    for ec in range(NEC):
        b.op(Sc, lambda e, ec=ec: e.activation(
            out=SQ2[:, :], in_=sb["XNTA"][:, ec * MYT:(ec + 1) * MYT],
            func=Act.Identity, bias=sb["CST"][:, 0:1],
            scale=sb["SXB"][:, 1:2]), "sS")
        b.wait(V, "sS")
        b.op(V, lambda e: e.tensor_scalar(SQ2[:, :], SQ2[:, :], MAGIC,
                                          -32768.0, Alu.subtract,
                                          Alu.max), "sV")
        b.op(V, lambda e, ec=ec: e.tensor_scalar(
            XQT[:, ec * MYT:(ec + 1) * MYT], SQ2[:, :], 32767.0,
            sb["SXB"][:, 0:1], Alu.min, Alu.mult), "sV")
        b.wait(Sc, "sV")

    b.barrier()
    ACL, BBC = sb["ACL"], sb["BBC"]
    ld = b.dma(Sy, sb["SM16"][:, :], sp["sp_sums"][:, :])
    b.wait(V, "dS")
    b.op(V, lambda e: e.reciprocal(sb["C16"][:, :], sb["SM16"][:, :]), "sV")
    b.vsync(V)
    for ec in range(NEC):
        ld = b.dma(Sy, ACL[:, ec * MYT:(ec + 1) * MYT],
                   sp["sp_attn"][ec * 128:(ec + 1) * 128, :])
        b.wait(Sc, "dS")
        b.wait(Sc, "sT")
        b.op(Sc, lambda e, ec=ec: e.activation(
            out=SQ2[:, :], in_=ACL[:, ec * MYT:(ec + 1) * MYT],
            func=Act.Square), "sS")
        b.wait(T, "dS")
        b.wait(T, "sS")
        b.wait(T, "sV")
        for np_ in range(4):
            b.op(T, lambda e, ec=ec, np_=np_: e.matmul(
                PS[0:2, np_ * 512:(np_ + 1) * 512], sb["IND2"][:, :],
                ACL[:, ec * MYT + np_ * 512: ec * MYT + (np_ + 1) * 512],
                start=True, stop=True), "sT")
            b.op(T, lambda e, np_=np_: e.matmul(
                PS[0:2, 2048 + np_ * 512: 2048 + (np_ + 1) * 512],
                sb["IND2"][:, :], SQ2[:, np_ * 512:(np_ + 1) * 512],
                start=True, stop=True), "sT")
        b.wait(V, "sT")
        b.wait(V, "dG")
        b.op(V, lambda e, ec=ec: e.tensor_copy(
            sb["S2T"][:, :], PS[0:2, 0:2048]), "sV")
        b.wait(G, "sV")
        b.dma_g(sp["sp_st"][2 * ec:2 * ec + 2, :], sb["S2T"][:, :])
        b.wait(V, "dG")
        b.op(V, lambda e, ec=ec: e.tensor_copy(
            sb["S2T"][:, :], PS[0:2, 2048:4096]), "sV")
        b.wait(G, "sV")
        b.dma_g(sp["sp_st"][16 + 2 * ec:18 + 2 * ec, :], sb["S2T"][:, :])
    b.wait(Sy, "dG")
    ldA = b.dma(Sy, sb["S16A"][:, :], sp["sp_st"][0:16, :])
    ldB = b.dma(Sy, sb["S16B"][:, :], sp["sp_st"][16:32, :])
    b.wait(V, "dS")
    b.op(V, lambda e: e.tensor_tensor(sb["S16A"][:, :], sb["S16A"][:, :],
                                      sb["C16"][:, :], Alu.mult), "sV")
    b.op(V, lambda e: e.tensor_tensor(sb["A16"][:, :], sb["C16"][:, :],
                                      sb["C16"][:, :], Alu.mult), "sV")
    b.vsync(V)
    b.op(V, lambda e: e.tensor_tensor(sb["S16B"][:, :], sb["S16B"][:, :],
                                      sb["A16"][:, :], Alu.mult), "sV")
    b.wait(T, "sV")
    for np_ in range(4):
        b.op(T, lambda e, np_=np_: e.matmul(
            PS[0:1, np_ * 512:(np_ + 1) * 512], sb["ONES1"][0:16, :],
            sb["S16A"][:, np_ * 512:(np_ + 1) * 512],
            start=True, stop=True), "sT")
        b.op(T, lambda e, np_=np_: e.matmul(
            PS[0:1, 2048 + np_ * 512: 2048 + (np_ + 1) * 512],
            sb["ONES1"][0:16, :], sb["S16B"][:, np_ * 512:(np_ + 1) * 512],
            start=True, stop=True), "sT")
    b.wait(V, "sT")
    b.op(V, lambda e: e.tensor_scalar(ROW[0:1, 0:2048], PS[0:1, 0:2048],
                                      1.0 / E, None, Alu.mult), "sV")
    b.op(V, lambda e: e.tensor_scalar(ROW[0:1, 2048:4096], PS[0:1, 2048:4096],
                                      1.0 / E, None, Alu.mult), "sV")
    mu = ROW[0:1, 0:2048]
    ex2 = ROW[0:1, 2048:4096]
    musq = sb["S16B"][0:1, 0:2048]
    var = sb["SM16"][0:1, 0:2048]
    sd = sb["S16A"][0:1, 0:2048]
    rr = sb["A16"][0:1, 0:2048]
    bb = sb["S16B"][0:1, 0:2048]
    b.vsync(V)
    b.op(V, lambda e: e.tensor_tensor(musq, mu, mu, Alu.mult), "sV")
    b.vsync(V)
    b.op(V, lambda e: e.tensor_tensor(var, ex2, musq, Alu.subtract), "sV")
    b.wait(Sc, "sV")
    b.op(Sc, lambda e: e.activation(out=sd, in_=var, func=Act.Sqrt,
                                    bias=sb["CST"][0:1, 1:2], scale=1.0), "sS")
    b.wait(V, "sS")
    b.op(V, lambda e: e.reciprocal(rr, sd), "sV")
    b.vsync(V)
    b.op(V, lambda e: e.tensor_tensor(bb, mu, rr, Alu.mult), "sV")
    b.vsync(V)
    b.wait(G, "sV")
    b.dma_g(sp["scr_row"][2:3, 0:2048], bb)
    b.dma_g(sp["scr_row"][3:4, 0:2048], rr)
    b.wait(G, "dG")
    b.dma_g(BBC[:, :], sp["scr_row"][2:3, 0:2048].partition_broadcast(128))
    b.dma_g(sb["SM16"][:, :],
          sp["scr_row"][3:4, 0:2048].partition_broadcast(16))
    b.wait(V, "dG")
    b.op(V, lambda e: e.tensor_tensor(sb["A16"][:, :], sb["C16"][:, :],
                                      sb["SM16"][:, :], Alu.mult), "sV")
    b.vsync(V)
    for ec in range(NEC):
        b.wait(T, "sV")
        for np_ in range(4):
            b.op(T, lambda e, ec=ec, np_=np_: e.matmul(
                PS[:, np_ * 512:(np_ + 1) * 512],
                sb["IND16"][:, ec * 128:(ec + 1) * 128],
                sb["A16"][:, np_ * 512:(np_ + 1) * 512],
                start=True, stop=True), "sT")
        b.wait(V, "sT")
        b.wait(Sc, "sV")
        b.op(V, lambda e, ec=ec: e.tensor_tensor(
            SQ2[:, :], ACL[:, ec * MYT:(ec + 1) * MYT], PS[:, 0:2048],
            Alu.mult), "sV")
        b.op(V, lambda e: e.tensor_tensor(SQ2[:, :], SQ2[:, :], BBC[:, :],
                                          Alu.subtract), "sV")
        b.wait(V, "dG")
        b.op(V, lambda e, ec=ec: e.tensor_scalar(
            ACL[:, ec * MYT:(ec + 1) * MYT], SQ2[:, :],
            sb["GAM"][:, ec:ec + 1], sb["BET"][:, ec:ec + 1],
            Alu.mult, Alu.add), "sV")
        b.wait(G, "sV")
        b.dma_g(sp["sp_aln"][ec * 128:(ec + 1) * 128, :],
              ACL[:, ec * MYT:(ec + 1) * MYT])

    b.barrier()

    def prep_ffn_weight(src, dstT, nchunks, icols, dst_stride, wna, wsc):
        nb = icols // 32
        for c in range(nchunks):
            pb = (c % 2) * icols
            b.wait(Sy, "sT")
            b.wait(Sy, "sV")
            ld = b.dma(Sy, wna[:, pb:pb + icols], src[c * 128:(c + 1) * 128, :])
            b.wait(V, "dS")
            b.wait(V, "sT")
            wna3 = wna[:, pb:pb + icols].rearrange("p (nb k) -> p nb k", k=32)
            b.op(V, lambda e, wna3=wna3, nb=nb: e.tensor_reduce(
                sb["BMX"][:, 0:nb], wna3, AX, Alu.max,
                apply_absolute_value=True), "sV")
            b.vsync(V)
            b.op(V, lambda e, nb=nb: e.tensor_scalar(
                sb["BSC"][:, 0:nb], sb["BMX"][:, 0:nb], 1.0 / 7.0, 1e-12,
                Alu.mult, Alu.max), "sV")
            b.vsync(V)
            b.op(V, lambda e, nb=nb: e.reciprocal(
                sb["BRS"][:, 0:nb], sb["BSC"][:, 0:nb]), "sV")
            b.vsync(V)
            qsc = wsc[:, 0:icols]
            qsc3 = qsc.rearrange("p (nb k) -> p nb k", k=32)
            b.op(V, lambda e, wna3=wna3, qsc3=qsc3, nb=nb: e.tensor_tensor(
                qsc3, wna3,
                sb["BRS"][:, 0:nb].unsqueeze(2).broadcast_to([128, nb, 32]), Alu.mult), "sV")
            b.op(V, lambda e, qsc=qsc: e.tensor_scalar(
                qsc, qsc, MAGIC, MAGIC, Alu.add, Alu.subtract), "sV")
            b.op(V, lambda e, qsc=qsc: e.tensor_scalar(
                qsc, qsc, -8.0, 7.0, Alu.max, Alu.min), "sV")
            b.op(V, lambda e, qsc3=qsc3, wna3=wna3, nb=nb: e.tensor_tensor(
                wna3, qsc3,
                sb["BSC"][:, 0:nb].unsqueeze(2).broadcast_to([128, nb, 32]), Alu.mult), "sV")
            b.wait(T, "sV")
            for s8 in range(icols // 128):
                pp = PS[:, (s8 % 2) * 512:(s8 % 2) * 512 + 128]
                if s8 >= 2:
                    b.wait(T, "sV")
                b.op(T, lambda e, pp=pp, pb=pb, s8=s8, wna=wna: e.transpose(
                    pp, wna[:, pb + s8 * 128: pb + (s8 + 1) * 128],
                    sb["IDF"][:, :]), "sT")
                b.wait(V, "sT")
                b.op(V, lambda e, pp=pp, s8=s8, c=c: e.tensor_copy(
                    sb[dstT][:, s8 * dst_stride + c * 128:
                             s8 * dst_stride + (c + 1) * 128], pp), "sV")

    prep_ffn_weight(dr["wu"], "WUT", NIC, E, I, sb["WNA"], sb["WSC"])
    prep_ffn_weight(dr["wg"], "WGT", NIC, E, I, sb["WNA"], sb["WSC"])

    b.barrier()
    for wt, spd, amcol in (("WUT", sp["sp_up"], 1), ("WGT", sp["sp_gate"], 2)):
        for ic in range(NIC):
            base = (ic % 2) * 4 * 512
            b.wait(T, "sV")
            for ec in range(NEC):
                for nq in range(4):
                    b.op(T, lambda e, base=base, nq=nq, ec=ec, ic=ic, wt=wt:
                         e.matmul(
                             PS[:, base + nq * 512: base + (nq + 1) * 512],
                             sb[wt][:, ec * I + ic * 128:
                                    ec * I + (ic + 1) * 128],
                             sb["XQT"][:, ec * MYT + nq * 512:
                                       ec * MYT + (nq + 1) * 512],
                             start=(ec == 0), stop=(ec == NEC - 1)), "sT")
            fb = (ic % 2) * 2048
            b.wait(V, "sT")
            b.wait(V, "dF%d" % (ic % 2) if ("dF%d" % (ic % 2)) in b.sems else "sT")
            for nq in range(4):
                b.op(V, lambda e, base=base, nq=nq, fb=fb: e.tensor_copy(
                    sb["FEV"][:, fb + nq * 512: fb + (nq + 1) * 512],
                    PS[:, base + nq * 512: base + (nq + 1) * 512]), "sV")
            b.op(V, lambda e, fb=fb: e.tensor_reduce(
                ROW[0:128, 384:385], sb["FEV"][:, fb:fb + 2048], AX, Alu.max,
                apply_absolute_value=True), "sV")
            b.vsync(V)
            b.op(V, lambda e, amcol=amcol: e.tensor_tensor(
                sb["AMX"][:, amcol:amcol + 1], sb["AMX"][:, amcol:amcol + 1],
                ROW[0:128, 384:385], Alu.max), "sV")
            b.vsync(V)
            b.wait(Sc, "sV")
            b.dmas(Sc, "dF%d" % (ic % 2), spd[ic * 128:(ic + 1) * 128, :],
                   sb["FEV"][:, fb:fb + 2048])
    global_scale(1, "ug", 2, "SUGB", 32767.0)

    b.barrier()
    FG, FU, FT1 = sb["FG"], sb["FU"], sb["FT1"]
    FGQ, FSQ, FG2 = sb["FGQ"], sb["FSQ"], sb["FG2"]
    for ic in range(NIC):
        b.wait(Sy, "sV")
        b.wait(Sy, "sS")
        ldg = b.dmas(Sy, "dLG", FG[:, :],
                     sp["sp_gate"][ic * 128:(ic + 1) * 128, :])
        ldu = b.dmas(Sy, "dLU", FU[:, :],
                     sp["sp_up"][ic * 128:(ic + 1) * 128, :])
        b.wait(Sc, "dLG", ldg)
        if "dUQ" in b.sems:
            b.wait(V, "dUQ")
        qdq_chunk(FG[:, :], FGQ[:, :], sb["SUGB"][:, 2:3], sb["SUGB"][:, 3:4],
                  -32768.0, 32767.0, FT1[:, :])
        b.op(Sc, lambda e: e.activation(out=FSQ[:, :], in_=FGQ[:, :],
                                        func=Act.Sigmoid), "sS")
        b.vsync(Sc)
        b.op(Sc, lambda e: e.activation(out=FT1[:, :], in_=FSQ[:, :],
                                        func=Act.Identity,
                                        bias=sb["CST"][:, 0:1],
                                        scale=sb["CST"][:, 2:3]), "sS")
        b.wait(V, "sS")
        b.op(V, lambda e: e.tensor_scalar(FT1[:, :], FT1[:, :], MAGIC, 0.0,
                                          Alu.subtract, Alu.max), "sV")
        b.op(V, lambda e: e.tensor_scalar(FSQ[:, :], FT1[:, :], 65535.0,
                                          1.0 / 65536.0, Alu.min,
                                          Alu.mult), "sV")
        if "dG2" in b.sems:
            b.wait(V, "dG2")
        b.op(V, lambda e: e.tensor_tensor(FG2[:, :], FGQ[:, :], FSQ[:, :],
                                          Alu.mult), "sV")
        b.op(V, lambda e: e.tensor_reduce(ROW[0:128, 384:385], FG2[:, :], AX,
                                          Alu.max,
                                          apply_absolute_value=True), "sV")
        b.vsync(V)
        b.op(V, lambda e: e.tensor_tensor(sb["AMX"][:, 3:4], sb["AMX"][:, 3:4],
                                          ROW[0:128, 384:385], Alu.max), "sV")
        b.vsync(V)
        b.wait(Sc, "sV")
        b.dmas(Sc, "dG2", sp["sp_g2"][ic * 128:(ic + 1) * 128, :], FG2[:, :])
        b.wait(Sc, "dLU", ldu)
        qdq_chunk(FU[:, :], FGQ[:, :], sb["SUGB"][:, 0:1], sb["SUGB"][:, 1:2],
                  -32768.0, 32767.0, FT1[:, :])
        b.wait(Sc, "sV")
        b.dmas(Sc, "dUQ", sp["sp_uq"][ic * 128:(ic + 1) * 128, :], FGQ[:, :])
    global_scale(3, "g2", 1, "S2B", 32767.0)

    b.barrier()
    for ic in range(NIC):
        b.wait(Sy, "sV")
        b.wait(Sy, "sS")
        ldg = b.dmas(Sy, "dLG", FG[:, :],
                     sp["sp_g2"][ic * 128:(ic + 1) * 128, :])
        ldu = b.dmas(Sy, "dLU", FU[:, :],
                     sp["sp_uq"][ic * 128:(ic + 1) * 128, :])
        b.wait(Sc, "dLG", ldg)
        qdq_chunk(FG[:, :], FGQ[:, :], sb["S2B"][:, 0:1], sb["S2B"][:, 1:2],
                  -32768.0, 32767.0, FT1[:, :])
        b.wait(V, "dLU", ldu)
        if "dO" in b.sems:
            b.wait(V, "dO")
        b.op(V, lambda e: e.tensor_tensor(FG2[:, :], FGQ[:, :], FU[:, :],
                                          Alu.mult), "sV")
        b.op(V, lambda e: e.tensor_reduce(ROW[0:128, 384:385], FG2[:, :], AX,
                                          Alu.max,
                                          apply_absolute_value=True), "sV")
        b.op(V, lambda e: e.tensor_tensor(sb["AMX"][:, 4:5], sb["AMX"][:, 4:5],
                                          ROW[0:128, 384:385], Alu.max), "sV")
        b.wait(Sc, "sV")
        b.dmas(Sc, "dO", sp["sp_o"][ic * 128:(ic + 1) * 128, :], FG2[:, :])
    global_scale(4, "o", 1, "SOB", 32767.0)

    b.barrier()
    prep_ffn_weight(dr["wd"], "WDT", NEC, I, E, sb["W3A"], sb["W3S"])

    b.barrier()
    FG4, FT4, OQT = sb["FG4"], sb["FT4"], sb["OQT"]
    for ic in range(NIC):
        b.wait(Sy, "sV")
        b.wait(Sy, "sS")
        ld = b.dmas(Sy, "dLG", FG4[:, :],
                    sp["sp_o"][ic * 128:(ic + 1) * 128, :])
        b.wait(Sc, "dLG", ld)
        b.wait(Sc, "sV")
        b.op(Sc, lambda e: e.activation(out=FT4[:, :], in_=FG4[:, :],
                                        func=Act.Identity,
                                        bias=sb["CST"][:, 0:1],
                                        scale=sb["SOB"][:, 1:2]), "sS")
        b.wait(V, "sS")
        b.op(V, lambda e: e.tensor_scalar(FT4[:, :], FT4[:, :], MAGIC,
                                          -32768.0, Alu.subtract,
                                          Alu.max), "sV")
        b.op(V, lambda e, ic=ic: e.tensor_scalar(
            OQT[:, ic * MYT:(ic + 1) * MYT], FT4[:, :], 32767.0,
            sb["SOB"][:, 0:1], Alu.min, Alu.mult), "sV")

    b.barrier()
    for hc in range(NEC):
        base = (hc % 2) * 4 * 512
        b.wait(Sy, "sV")
        ld = b.dmas(Sy, "dAL", sb["ALC"][:, :],
                    sp["sp_aln"][hc * 128:(hc + 1) * 128, :])
        b.wait(T, "sV")
        for ic in range(NIC):
            for nq in range(4):
                b.op(T, lambda e, base=base, nq=nq, ic=ic, hc=hc: e.matmul(
                    PS[:, base + nq * 512: base + (nq + 1) * 512],
                    sb["WDT"][:, ic * E + hc * 128: ic * E + (hc + 1) * 128],
                    OQT[:, ic * MYT + nq * 512: ic * MYT + (nq + 1) * 512],
                    start=(ic == 0), stop=(ic == NIC - 1)), "sT")
        b.wait(V, "sT")
        b.wait(V, "dAL", ld)
        if "dAF" in b.sems:
            b.wait(V, "dAF")
        for nq in range(4):
            b.op(V, lambda e, base=base, nq=nq: e.tensor_tensor(
                sb["AFE"][:, nq * 512:(nq + 1) * 512],
                PS[:, base + nq * 512: base + (nq + 1) * 512],
                sb["ALC"][:, nq * 512:(nq + 1) * 512], Alu.add), "sV")
        b.wait(Sc, "sV")
        b.dmas(Sc, "dAF", sp["sp_af"][hc * 128:(hc + 1) * 128, :],
               sb["AFE"][:, :])

    b.barrier()
    AFL, XT6, OSB = sb["AFL"], sb["XT6"], sb["OSB"]
    for ec in range(NEC):
        b.dmas(Sy, "dAFL", AFL[:, ec * MYT:(ec + 1) * MYT],
               sp["sp_af"][ec * 128:(ec + 1) * 128, :])
    afl_ld = b.sems["dAFL"][1]
    for tt in range(16):
        pb = (tt % 2) * E
        base = (tt % 2) * 1024
        if tt >= 2:
            b.wait(Sy, "sV")
        ld = b.dma(Sy, XT6[:, pb:pb + E], dr["x_tok"][tt * 128:(tt + 1) * 128, :])
        b.wait(T, "dS")
        b.wait(T, "sV")
        for hc in range(NEC):
            b.op(T, lambda e, base=base, hc=hc, tt=tt: e.transpose(
                PS[:, base + hc * 128: base + (hc + 1) * 128],
                AFL[:, hc * MYT + tt * 128: hc * MYT + (tt + 1) * 128],
                sb["IDF"][:, :]), "sT")
        b.wait(V, "sT")
        b.wait(V, "dS")
        b.wait(V, "dG")
        b.op(V, lambda e, base=base, pb=pb: e.tensor_tensor(
            OSB[:, pb:pb + E], PS[:, base:base + E], XT6[:, pb:pb + E],
            Alu.add), "sV")
        b.wait(G, "sV")
        b.dma_g(out_d[tt * 128:(tt + 1) * 128, :], OSB[:, pb:pb + E])
    b.wait(G, "dG")


_NC_CACHE = {}


def kernel(L0_attn_x, L0_attn_Wq, L0_attn_Wk, L0_attn_Wv, L0_attn_ln_gamma,
           L0_attn_ln_beta, L0_ffn_w_gate, L0_ffn_w_up, L0_ffn_w_down):
    x = np.ascontiguousarray(np.asarray(L0_attn_x, np.float32))
    wq = np.asarray(L0_attn_Wq, np.float32)
    wk = np.asarray(L0_attn_Wk, np.float32)
    wv = np.asarray(L0_attn_Wv, np.float32)
    gam = np.asarray(L0_attn_ln_gamma, np.float32)
    bet = np.asarray(L0_attn_ln_beta, np.float32)
    wg = np.ascontiguousarray(np.asarray(L0_ffn_w_gate, np.float32))
    wu = np.ascontiguousarray(np.asarray(L0_ffn_w_up, np.float32))
    wd = np.ascontiguousarray(np.asarray(L0_ffn_w_down, np.float32))

    wqT = np.ascontiguousarray(wq.T)
    wkT = np.ascontiguousarray(wk.T)
    wvT = np.ascontiguousarray(wv.T)
    gamma8 = np.ascontiguousarray(gam.reshape(NEC, 128).T)
    beta8 = np.ascontiguousarray(bet.reshape(NEC, 128).T)

    if "nc" not in _NC_CACHE:
        _NC_CACHE["nc"] = build_nc()
    nc = _NC_CACHE["nc"]

    in_maps = []
    for c in range(N_CORES):
        bq, h = divmod(c, 2)
        qoff = h * MYT
        xloc = np.roll(x[bq], -qoff, axis=0)
        if h == 0:
            meta = np.array([[-1.0 / 4096.0, 3968.0 / 4096.0]], np.float32)
        else:
            meta = np.array([[1.0 / 4096.0, 1.0 - 3968.0 / 4096.0]], np.float32)
        in_maps.append({
            "xseqT": np.ascontiguousarray(xloc.T),
            "x_tok": np.ascontiguousarray(xloc),
            "wqT": wqT, "wkT": wkT, "wvT": wvT,
            "wu": wu, "wg": wg, "wd": wd,
            "gamma8": gamma8, "beta8": beta8,
            "meta": meta,
        })

    res = run_bass_kernel_spmd(nc, in_maps, core_ids=list(range(N_CORES)))
    out = np.empty((4, S, E), np.float32)
    for c in range(N_CORES):
        bq, h = divmod(c, 2)
        out[bq, h * MYT:(h + 1) * MYT, :] = res.results[c]["out"]
    return out


# version 04
# speedup vs baseline: 1.4428x; 1.4144x over previous
import numpy as np
from contextlib import ExitStack

import concourse.bass as bass
import concourse.mybir as mybir
from concourse.bass_utils import run_bass_kernel_spmd

F32 = mybir.dt.float32
BF16 = mybir.dt.bfloat16
Alu = mybir.AluOpType
Act = mybir.ActivationFunctionType
AX = mybir.AxisListType.X

MAGIC = 1.5 * 2.0**23
N_CORES = 8
S = 4096
E = 1024
H = 16
I = 3072
MYT = 2048
NEC = E // 128
NIC = I // 128

ENGS = ("sync", "gpsimd", "vector", "scalar", "tensor")
V, Sc, T, G, Sy = "vector", "scalar", "tensor", "gpsimd", "sync"


class Builder:

    def __init__(self, nc):
        self.nc = nc
        self.ops = {e: [] for e in ENGS}
        self.sems = {}

    def mksem(self, name):
        h = self.nc.semaphore(name).__enter__()
        self.sems[name] = [h, 0]

    AUTO_CHAIN = {"vector": "sV", "scalar": "sS", "gpsimd": "sG"}

    def op(self, eng, fn, inc=None, k=1):
        ch = self.AUTO_CHAIN.get(eng)
        if ch is not None:
            self.wait(eng, ch)
        if inc is None:
            self.ops[eng].append(fn)
            return None
        rec = self.sems[inc]
        rec[1] += k
        h = rec[0]
        self.ops[eng].append(lambda e, fn=fn, h=h, k=k: fn(e).then_inc(h, k))
        return rec[1]

    def wait(self, eng, sem, val=None):
        h, c = self.sems[sem]
        v = c if val is None else val
        if v > 0:
            self.ops[eng].append(lambda e, h=h, v=v: e.wait_ge(h, v))

    def dma_g(self, out, in_, **kw):
        return self.dma("gpsimd", out, in_, serial=False, **kw)

    def dmas(self, eng, sem, out, in_, **kw):
        if sem not in self.sems:
            self.mksem(sem)
        return self.op(eng, lambda e: e.dma_start(out=out, in_=in_, **kw),
                       sem, 16)

    def dma(self, eng, out, in_, serial=True, **kw):
        q = {"sync": "dS", "gpsimd": "dG", "scalar": "dA"}[eng]
        n = self.op(eng, lambda e: e.dma_start(out=out, in_=in_, **kw), q, 16)
        if serial:
            self.wait(eng, q, n)
        return n

    def vsync(self, eng):
        sem = {"vector": "sV", "scalar": "sS", "tensor": "sT",
               "gpsimd": "sG"}[eng]
        self.wait(eng, sem)

    def vfiller(self, sb):
        self.op("vector", lambda e: e.tensor_copy(sb["ROW"][0:1, 4095:4096],
                                                  sb["CST"][0:1, 0:1]), "sV")

    def barrier(self, engs=ENGS):
        snap = [(h, c) for (h, c) in self.sems.values() if c > 0]
        for eng in engs:
            for h, v in snap:
                self.ops[eng].append(lambda e, h=h, v=v: e.wait_ge(h, v))


def build_nc():
    nc = bass.Bass("TRN2", target_bir_lowering=False)

    dram = {}
    for nm, shp in [
        ("xseqT", [E, S]), ("x_tok", [S, E]),
        ("wqT", [E, E]), ("wkT", [E, E]), ("wvT", [E, E]),
        ("wu", [I, E]), ("wg", [I, E]), ("wd", [E, I]),
        ("gamma8", [128, NEC]), ("beta8", [128, NEC]),
        ("meta", [1, 2]),
    ]:
        dram[nm] = nc.dram_tensor(nm, shp, F32, kind="ExternalInput").ap()
    out_d = nc.dram_tensor("out", [MYT, E], F32, kind="ExternalOutput").ap()

    sp = {}
    for nm, shp in [
        ("sp_v", [S, H * 65]), ("sp_attn", [E, MYT]), ("sp_sums", [H, MYT]),
        ("sp_aln", [E, MYT]), ("sp_up", [I, MYT]), ("sp_gate", [I, MYT]),
        ("sp_g2", [I, MYT]), ("sp_uq", [I, MYT]), ("sp_o", [I, MYT]),
        ("sp_af", [E, MYT]), ("scr_row", [4, S]), ("sp_st", [32, MYT]),
    ]:
        sp[nm] = nc.dram_tensor(
            nm, shp, BF16 if nm == "sp_v" else F32, kind="Internal").ap()
    sp["cc_in"] = {k: nc.dram_tensor(f"cc{k}_i", [1, w], F32, kind="Internal").ap()
                   for k, w in (("x", 1), ("ug", 2), ("g2", 1), ("o", 1))}
    sp["cc_out"] = {k: nc.dram_tensor(f"cc{k}_o", [1, w], F32, kind="Internal").ap()
                    for k, w in (("x", 1), ("ug", 2), ("g2", 1), ("o", 1))}

    st = ExitStack()
    sb = {}

    def perm(nm, shp, dt):
        sb[nm] = st.enter_context(nc.sbuf_tensor(nm, shp, dt))
        return sb[nm]

    perm("CST", [128, 8], F32)
    perm("SXB", [128, 2], F32)
    perm("SUGB", [128, 4], F32)
    perm("S2B", [128, 2], F32)
    perm("SOB", [128, 2], F32)
    perm("METB", [128, 2], F32)
    perm("AMX", [128, 8], F32)
    perm("IDF", [128, 128], F32)
    perm("IND2", [128, 2], F32)
    perm("IND16", [16, 1024], F32)
    perm("ONES1", [128, 1], F32)
    perm("PIOTA", [128, 1], F32)
    perm("GAM", [128, NEC], F32)
    perm("BET", [128, NEC], F32)
    perm("ROW", [128, 4096], F32)
    perm("BMX", [128, 96], F32)
    perm("BSC", [128, 96], F32)
    perm("BRS", [128, 96], F32)

    TOP = nc.SBUF_PARTITION_SIZE_BYTES
    ZB = 24576

    def at(nm, off, shp, dt):
        end = off + int(np.prod(shp[1:])) * mybir.dt.size(dt)
        assert ZB <= off and end <= TOP, (nm, off, end)
        sb[nm] = nc.alloc_sbuf_tensor_at(nm, shp, dt, offset=off)
        return sb[nm]

    at("WVTS", 40960, [128, NEC * E], BF16)
    at("WKTS", 57344, [128, NEC * E], BF16)
    at("WQTS", 73728, [128, NEC * E], BF16)
    at("XNTA", 90112, [128, NEC * MYT], BF16)
    at("XNTB", 122880, [128, NEC * MYT], BF16)
    at("XC2", 155648, [128, S], F32)
    at("MUBC", 172032, [128, S], F32)
    at("RSBC", 188416, [128, S], F32)
    at("XT2", 204800, [128, 2 * E], F32)
    at("SQT", 212992, [128, E], F32)
    at("STT", 217088, [128, 128], F32)
    at("VEV", 217600, [128, H * 65], BF16)
    at("KF", 155648, [128, NEC * S], BF16)
    at("QF", 40960, [128, NEC * MYT], BF16)
    at("DNE", 122880, [128, 4096], F32)
    at("DFA", 139264, [128, 4096], F32)
    at("SW", 73728, [128, 2 * 1024], F32)
    at("PT", 81920, [128, 2 * 1024], BF16)
    at("AVE", 86016, [128, 1024], F32)
    at("VSB", 221184, [128, 32 * 65], BF16)
    at("XQT", 155648, [128, NEC * MYT], BF16)
    at("SQ2", 188416, [128, MYT], F32)
    at("ACL", 40960, [128, NEC * MYT], F32)
    at("BBC", 106496, [128, MYT], F32)
    at("C16", 114688, [16, MYT], F32)
    at("A16", 122880, [16, MYT], F32)
    at("SM16", 131072, [16, MYT], F32)
    at("S16A", 139264, [16, MYT], F32)
    at("S16B", 147456, [16, MYT], F32)
    at("S2T", 196608, [2, MYT], F32)
    at("WUT", 40960, [128, NEC * I], BF16)
    at("WGT", 90112, [128, NEC * I], BF16)
    at("FEV", 139264, [128, 8 * 512], F32)
    at("WNA", 196608, [128, 2 * E], F32)
    at("WSC", 204800, [128, 2 * E], F32)
    at("FG", 40960, [128, MYT], F32)
    at("FU", 49152, [128, MYT], F32)
    at("FT1", 57344, [128, MYT], F32)
    at("FGQ", 65536, [128, MYT], F32)
    at("FSQ", 73728, [128, MYT], F32)
    at("FG2", 81920, [128, MYT], F32)
    at("OQT", 40960, [128, NIC * MYT], BF16)
    at("WDT", 139264, [128, NIC * E], BF16)
    at("W3A", 188416, [128, 2 * I], F32)
    at("W3S", 212992, [128, I], F32)
    at("FG4", 188416, [128, MYT], F32)
    at("FT4", 196608, [128, MYT], F32)
    at("ALC", 188416, [128, MYT], F32)
    at("AFE", 196608, [128, MYT], F32)
    at("AFL", 40960, [128, NEC * MYT], F32)
    at("XT6", 106496, [128, 2 * E], F32)
    at("OSB", 114688, [128, 2 * E], F32)

    PS = nc.alloc_psum_tensor("PS", [128, 4096], F32)

    b = Builder(nc)
    for s_ in ("dS", "dG", "dA", "sV", "sS", "sT", "sG", "sCC"):
        b.mksem(s_)

    emit_program(nc, b, sb, PS, dram, out_d, sp)
    _NC_CACHE["builder"] = b

    with nc.Block() as block:
        @block.sync
        def _(e):
            for f in b.ops["sync"]:
                f(e)

        @block.gpsimd
        def _(e):
            for f in b.ops["gpsimd"]:
                f(e)

        @block.vector
        def _(e):
            for f in b.ops["vector"]:
                f(e)

        @block.scalar
        def _(e):
            for f in b.ops["scalar"]:
                f(e)

        @block.tensor
        def _(e):
            for f in b.ops["tensor"]:
                f(e)

    st.close()
    return nc


def emit_program(nc, b, sb, PS, dr, out_d, sp):
    ROW = sb["ROW"]
    XNT = {0: sb["XNTA"], 1: sb["XNTB"]}

    def xnt_cols(ec, c0, c1):
        assert (c0 // MYT) == ((c1 - 1) // MYT)
        half = c0 // MYT
        t = XNT[half]
        return t[:, ec * MYT + c0 % MYT: ec * MYT + c0 % MYT + (c1 - c0)]

    b.op(V, lambda e: e.memset(sb["CST"][:, 0:1], MAGIC), "sV")
    b.op(V, lambda e: e.memset(sb["CST"][:, 1:2], 1e-5), "sV")
    b.op(V, lambda e: e.memset(sb["CST"][:, 2:3], 65536.0), "sV")
    b.op(V, lambda e: e.memset(sb["CST"][:, 3:4], -1920.0 / 4096.0), "sV")
    b.op(V, lambda e: e.memset(sb["CST"][:, 4:5], 1.0 / 4096.0), "sV")
    b.op(V, lambda e: e.memset(sb["ONES1"][:, :], 1.0), "sV")
    b.op(V, lambda e: e.memset(sb["AMX"][:, :], 0.0), "sV")
    b.op(G, lambda e: e.iota(sb["IDF"][:, :], pattern=[[1, 128]], base=0,
                             channel_multiplier=-1,
                             allow_small_or_imprecise_dtypes=True), "sG")
    b.op(G, lambda e: e.iota(sb["PIOTA"][:, :], pattern=[[0, 1]], base=0,
                             channel_multiplier=1,
                             allow_small_or_imprecise_dtypes=True), "sG")
    b.dma_g(sb["METB"][:, :], dr["meta"][0:1, :].partition_broadcast(128))
    b.wait(V, "sG")
    b.op(V, lambda e: e.tensor_scalar(sb["IND2"][:, 0:1], sb["PIOTA"][:, :],
                                      64.0, None, Alu.is_lt), "sV")
    b.op(V, lambda e: e.tensor_scalar(sb["IND2"][:, 1:2], sb["PIOTA"][:, :],
                                      64.0, None, Alu.is_ge), "sV")
    b.op(V, lambda e: e.tensor_scalar(sb["IDF"][:, :], sb["IDF"][:, :],
                                      0.0, None, Alu.is_equal), "sV")
    b.op(G, lambda e: e.iota(sb["IND16"][:, :],
                             pattern=[[2, 8], [1, 2], [0, 64]], base=0,
                             channel_multiplier=0,
                             allow_small_or_imprecise_dtypes=True), "sG")
    b.wait(V, "sG")
    b.op(V, lambda e: e.tensor_tensor(
        sb["IND16"][:, :], sb["IND16"][:, :],
        sb["PIOTA"][0:16, 0:1].broadcast_to([16, 1024]), Alu.is_equal), "sV")
    b.dma(Sy, sb["GAM"][:, :], dr["gamma8"][:, :])
    b.dma(Sy, sb["BET"][:, :], dr["beta8"][:, :])
    for nm, dst in (("wqT", "WQTS"), ("wkT", "WKTS"), ("wvT", "WVTS")):
        for ec in range(NEC):
            b.dma_g(sb[dst][:, ec * E:(ec + 1) * E],
                  dr[nm][ec * 128:(ec + 1) * 128, :])

    def global_scale(col, key, packw, dst, qmaxv):
        b.wait(T, "sV")
        b.op(T, lambda e: e.transpose(PS[0:packw, 1536:1664],
                                      sb["AMX"][:, col:col + packw],
                                      sb["IDF"][:, :]), "sT")
        b.wait(V, "sT")
        b.op(V, lambda e: e.tensor_reduce(ROW[0:packw, 256:257],
                                          PS[0:packw, 1536:1664], AX,
                                          Alu.max), "sV")
        b.wait(G, "sV")
        b.dma_g(sp["cc_in"][key][0:1, :].rearrange("a (p c) -> (a p) c",
                                                    p=packw),
              ROW[0:packw, 256:257])
        b.wait(G, "dG")
        b.op(G, lambda e: e.collective_compute(
            "AllReduce", Alu.max, replica_groups=[list(range(N_CORES))],
            ins=[sp["cc_in"][key].opt()], outs=[sp["cc_out"][key].opt()],
        ), "sCC")
        b.wait(G, "sCC")
        b.dma_g(ROW[0:1, 260:260 + packw], sp["cc_out"][key][0:1, :])
        b.wait(V, "dG")
        b.op(V, lambda e: e.tensor_scalar(
            ROW[0:1, 280:280 + packw], ROW[0:1, 260:260 + packw],
            1.0 / qmaxv, 1e-12, Alu.mult, Alu.max), "sV")
        b.vsync(V)
        b.op(V, lambda e: e.reciprocal(
            ROW[0:1, 288:288 + packw], ROW[0:1, 280:280 + packw]), "sV")
        b.vsync(V)
        for i in range(packw):
            b.op(V, lambda e, i=i: e.tensor_copy(
                ROW[0:1, 300 + 2 * i:301 + 2 * i],
                ROW[0:1, 280 + i:281 + i]), "sV")
            b.op(V, lambda e, i=i: e.tensor_copy(
                ROW[0:1, 301 + 2 * i:302 + 2 * i],
                ROW[0:1, 288 + i:289 + i]), "sV")
        b.wait(G, "sV")
        b.dma_g(sp["scr_row"][2:3, 0:2 * packw], ROW[0:1, 300:300 + 2 * packw])
        b.wait(G, "dG")
        b.dma_g(sb[dst][:, 0:2 * packw],
              sp["scr_row"][2:3, 0:2 * packw].partition_broadcast(128))

    def qdq_chunk(src_ap, dst_ap, s_ap, rs_ap, lo, hi, tmp_ap):
        b.wait(Sc, "sV")
        b.op(Sc, lambda e: e.activation(out=tmp_ap, in_=src_ap,
                                        func=Act.Identity,
                                        bias=sb["CST"][:, 0:1],
                                        scale=rs_ap), "sS")
        b.wait(V, "sS")
        b.op(V, lambda e: e.tensor_scalar(tmp_ap, tmp_ap, MAGIC, lo,
                                          Alu.subtract, Alu.max), "sV")
        b.op(V, lambda e: e.tensor_scalar(dst_ap, tmp_ap, hi, s_ap,
                                          Alu.min, Alu.mult), "sV")
        b.wait(Sc, "sV")

    XT2, SQT, STT, XC2 = sb["XT2"], sb["SQT"], sb["STT"], sb["XC2"]
    for tt in range(32):
        pb = (tt % 2) * E
        if tt >= 2:
            b.wait(Sy, "sV")
            b.wait(Sy, "sS")
        ld = b.dma(Sy, XT2[:, pb:pb + E], dr["x_tok"][tt * 128:(tt + 1) * 128, :])
        b.wait(Sc, "dS")
        b.wait(Sc, "sV")
        b.op(Sc, lambda e, pb=pb: e.activation(
            out=SQT[:, :], in_=XT2[:, pb:pb + E], func=Act.Square), "sS")
        b.wait(V, "dS")
        b.op(V, lambda e, pb=pb, tt=tt: e.tensor_reduce(
            STT[:, tt:tt + 1], XT2[:, pb:pb + E], AX, Alu.add), "sV")
        b.wait(V, "sS")
        b.op(V, lambda e, tt=tt: e.tensor_reduce(
            STT[:, 32 + tt:33 + tt], SQT[:, :], AX, Alu.add), "sV")
    b.vsync(V)
    b.op(V, lambda e: e.tensor_scalar(STT[:, 0:32], STT[:, 0:32],
                                      1.0 / E, None, Alu.mult), "sV")
    b.op(V, lambda e: e.tensor_scalar(STT[:, 32:64], STT[:, 32:64],
                                      1.0 / E, None, Alu.mult), "sV")
    b.vsync(V)
    b.op(V, lambda e: e.tensor_tensor(STT[:, 64:96], STT[:, 0:32],
                                      STT[:, 0:32], Alu.mult), "sV")
    b.vsync(V)
    b.op(V, lambda e: e.tensor_tensor(STT[:, 64:96], STT[:, 32:64],
                                      STT[:, 64:96], Alu.subtract), "sV")
    b.wait(Sc, "sV")
    b.op(Sc, lambda e: e.activation(out=STT[:, 96:128], in_=STT[:, 64:96],
                                    func=Act.Sqrt, bias=sb["CST"][:, 1:2],
                                    scale=1.0), "sS")
    b.wait(V, "sS")
    b.op(V, lambda e: e.reciprocal(STT[:, 96:128], STT[:, 96:128]), "sV")
    b.vsync(V)
    b.wait(T, "sV")
    b.op(T, lambda e: e.transpose(PS[0:32, 0:128], STT[:, 0:32],
                                  sb["IDF"][:, :]), "sT")
    b.op(T, lambda e: e.transpose(PS[0:32, 512:640], STT[:, 96:128],
                                  sb["IDF"][:, :]), "sT")
    b.wait(V, "sT")
    b.op(V, lambda e: e.tensor_copy(ROW[0:32, 0:128], PS[0:32, 0:128]), "sV")
    b.op(V, lambda e: e.tensor_copy(ROW[0:32, 128:256], PS[0:32, 512:640]), "sV")
    b.wait(G, "sV")
    b.dma_g(sp["scr_row"][0:1, :].rearrange("a (p c) -> (a p) c", p=32),
          ROW[0:32, 0:128])
    b.dma_g(sp["scr_row"][1:2, :].rearrange("a (p c) -> (a p) c", p=32),
          ROW[0:32, 128:256])
    b.wait(G, "dG")
    b.dma_g(sb["MUBC"][:, :], sp["scr_row"][0:1, :].partition_broadcast(128))
    b.dma_g(sb["RSBC"][:, :], sp["scr_row"][1:2, :].partition_broadcast(128))
    for ec in range(NEC):
        pb = 0
        if ec >= 1:
            b.wait(Sy, "sV")
        ld = b.dma(Sy, XC2[:, pb:pb + S], dr["xseqT"][ec * 128:(ec + 1) * 128, :])
        b.wait(V, "dS")
        b.wait(V, "dG")
        b.op(V, lambda e, pb=pb: e.tensor_tensor(
            XC2[:, pb:pb + S], XC2[:, pb:pb + S], sb["MUBC"][:, :],
            Alu.subtract), "sV")
        for half in range(2):
            b.op(V, lambda e, pb=pb, ec=ec, half=half: e.tensor_tensor(
                XNT[half][:, ec * MYT:(ec + 1) * MYT],
                XC2[:, pb + half * MYT: pb + (half + 1) * MYT],
                sb["RSBC"][:, half * MYT:(half + 1) * MYT], Alu.mult), "sV")
            b.op(V, lambda e, ec=ec, half=half: e.tensor_reduce(
                STT[:, 64:65], XNT[half][:, ec * MYT:(ec + 1) * MYT], AX,
                Alu.max, apply_absolute_value=True), "sV")
            b.vsync(V)
            b.op(V, lambda e: e.tensor_tensor(
                sb["AMX"][:, 0:1], sb["AMX"][:, 0:1], STT[:, 64:65],
                Alu.max), "sV")
            b.vsync(V)
    global_scale(0, "x", 1, "SXB", 32767.0)

    b.barrier([T])
    VEV = sb["VEV"]
    for tt in range(32):
        half_col = tt % 2
        for half in range(2):
            pt = PS[:, 2048 + half * 512: 2048 + (half + 1) * 512]
            if tt > 0 or half > 0:
                b.wait(T, "sV")
            for ec in range(NEC):
                b.op(T, lambda e, tt=tt, ec=ec, pt=pt, half=half: e.matmul(
                    pt, xnt_cols(ec, tt * 128, (tt + 1) * 128),
                    sb["WVTS"][:, ec * E + half * 512:
                               ec * E + (half + 1) * 512],
                    start=(ec == 0), stop=(ec == NEC - 1)), "sT")
            b.wait(V, "sT")
            b.wait(V, "dG")
            b.op(V, lambda e, pt=pt, half=half: e.tensor_copy(
                VEV[:, half * 8 * 65:(half + 1) * 8 * 65]
                .rearrange("p (h c) -> p h c", c=65)[:, :, 0:64],
                pt.rearrange("p (h c) -> p h c", c=64)), "sV")
        b.op(V, lambda e: e.memset(
            VEV[:, :].rearrange("p (h c) -> p h c", c=65)[:, :, 64:65],
            1.0), "sV")
        b.wait(G, "sV")
        b.dma_g(sp["sp_v"][tt * 128:(tt + 1) * 128, :], VEV[:, :])

    b.barrier([T])
    KF, QF = sb["KF"], sb["QF"]
    for pair in range(NEC):
        for nk in range(8):
            pk = PS[:, 3072 + (nk % 2) * 512: 3072 + (nk % 2) * 512 + 512]
            b.wait(T, "sV")
            for ec in range(NEC):
                b.op(T, lambda e, pk=pk, ec=ec, pair=pair, nk=nk: e.matmul(
                    pk, sb["WKTS"][:, ec * E + pair * 128:
                                   ec * E + (pair + 1) * 128],
                    xnt_cols(ec, nk * 512, (nk + 1) * 512),
                    start=(ec == 0), stop=(ec == NEC - 1)), "sT")
            b.wait(V, "sT")
            b.op(V, lambda e, pk=pk, pair=pair, nk=nk: e.tensor_copy(
                KF[:, pair * S + nk * 512: pair * S + (nk + 1) * 512],
                pk), "sV")
    for pair in range(NEC):
        for nq in range(4):
            pk = PS[:, 3072 + (nq % 2) * 512: 3072 + (nq % 2) * 512 + 512]
            b.wait(T, "sV")
            for ec in range(NEC):
                b.op(T, lambda e, pk=pk, ec=ec, pair=pair, nq=nq: e.matmul(
                    pk, sb["WQTS"][:, ec * E + pair * 128:
                                   ec * E + (pair + 1) * 128],
                    xnt_cols(ec, nq * 512, (nq + 1) * 512),
                    start=(ec == 0), stop=(ec == NEC - 1)), "sT")
            b.wait(V, "sT")
            b.op(V, lambda e, pk=pk, pair=pair, nq=nq: e.tensor_scalar(
                QF[:, pair * MYT + nq * 512: pair * MYT + (nq + 1) * 512],
                pk, 1.0 / 32.0, None, Alu.mult), "sV")

    b.barrier([G])
    b.op(G, lambda e: e.iota(sb["DNE"][:, :], pattern=[[1, 4096]], base=0,
                             channel_multiplier=-1,
                             allow_small_or_imprecise_dtypes=True), "sG")
    b.op(G, lambda e: e.iota(sb["DFA"][:, :], pattern=[[1, 4096]], base=0,
                             channel_multiplier=-1,
                             allow_small_or_imprecise_dtypes=True), "sG")
    b.wait(V, "sG")
    b.wait(V, "dG")
    b.op(V, lambda e: e.tensor_scalar(sb["DFA"][:, :], sb["DFA"][:, :],
                                      sb["METB"][:, 0:1], sb["METB"][:, 1:2],
                                      Alu.mult, Alu.add), "sV")
    b.wait(Sc, "sG")
    b.wait(Sc, "sV")
    b.op(Sc, lambda e: e.activation(out=sb["DNE"][:, :], in_=sb["DNE"][:, :],
                                    func=Act.Abs, bias=sb["CST"][:, 3:4],
                                    scale=sb["CST"][:, 4:5]), "sS")

    b.barrier([T, Sc, G, Sy])
    SW, PT, AVE, VSB = sb["SW"], sb["PT"], sb["AVE"], sb["VSB"]
    DNE, DFA = sb["DNE"], sb["DFA"]
    RCOL = (0, 1024, 3072)
    sc_mark, tt_mark, exp_mark, av_mark, evac_mark = {}, {}, {}, {}, {}
    spill_mark = {}

    def prevkey(it, k, back, n=32):
        kk = k - back
        return (it, kk) if kk >= 0 else (it - 1, kk + n)

    it = -1
    for head in range(H):
        pair, hh = divmod(head, 2)
        hb = hh * 64
        for qh in range(2):
            it += 1
            vb = 0
            if (it - 1, 31) in av_mark:
                b.wait(Sy, "sT", av_mark[(it - 1, 31)])
            vld = b.dmas(Sy, "dVS", VSB[:, vb:vb + 2080]
                         .rearrange("p (k c) -> p k c", c=65),
                         sp["sp_v"][:, head * 65:(head + 1) * 65]
                         .rearrange("(k p) c -> p k c", p=128))
            for step in range(34):
                k_sc, k_tt, k_av = step, step - 1, step - 2
                if k_sc < 32:
                    kc = k_sc
                    pscr = PS[:, RCOL[kc % 3]:RCOL[kc % 3] + 1024]
                    key = prevkey(it, kc, 3)
                    if key in tt_mark:
                        b.wait(T, "sV", tt_mark[key])
                    for jj in range(2):
                        b.op(T, lambda e, pscr=pscr, jj=jj, hb=hb, kc=kc,
                             qh=qh, pair=pair: e.matmul(
                                 pscr[:, jj * 512:(jj + 1) * 512],
                                 KF[hb:hb + 64, pair * S + kc * 128:
                                    pair * S + (kc + 1) * 128],
                                 QF[hb:hb + 64,
                                    pair * MYT + qh * 1024 + jj * 512:
                                    pair * MYT + qh * 1024 + (jj + 1) * 512],
                                 start=True, stop=True), "sT")
                    sc_mark[(it, kc)] = b.sems["sT"][1]
                if 0 <= k_tt < 32:
                    kc = k_tt
                    pscr = PS[:, RCOL[kc % 3]:RCOL[kc % 3] + 1024]
                    swb = SW[:, (kc % 2) * 1024:(kc % 2) * 1024 + 1024]
                    ptb = PT[:, (kc % 2) * 1024:(kc % 2) * 1024 + 1024]
                    if kc < 16:
                        doff = qh * 1024 - kc * 128 + 1920
                        dsl = DNE[:, doff:doff + 1024]
                    else:
                        doff = qh * 1024 - (kc - 16) * 128 + 1920
                        dsl = DFA[:, doff:doff + 1024]
                    b.wait(V, "sT", sc_mark[(it, kc)])
                    key = prevkey(it, kc, 2)
                    if key in exp_mark:
                        b.wait(V, "sS", exp_mark[key])
                    if kc == 0 and "dA1" in b.sems:
                        pass
                    tt_mark[(it, kc)] = b.op(
                        V, lambda e, pscr=pscr, dsl=dsl, swb=swb:
                        e.tensor_tensor(swb, pscr, dsl, Alu.mult), "sV")
                    b.wait(Sc, "sV", tt_mark[(it, kc)])
                    if key in av_mark:
                        b.wait(Sc, "sT", av_mark[key])
                    exp_mark[(it, kc)] = b.op(
                        Sc, lambda e, swb=swb, ptb=ptb: e.activation(
                            out=ptb, in_=swb, func=Act.Exp), "sS")
                if 0 <= k_av < 32:
                    kc = k_av
                    ptb = PT[:, (kc % 2) * 1024:(kc % 2) * 1024 + 1024]
                    b.wait(T, "sS", exp_mark[(it, kc)])
                    if kc == 0:
                        b.wait(T, "dVS", vld)
                        if (it - 1, "evac") in evac_mark:
                            b.wait(T, "sV", evac_mark[(it - 1, "evac")])
                    for jj in range(2):
                        b.op(T, lambda e, jj=jj, kc=kc, ptb=ptb, vb=vb:
                             e.matmul(
                                 PS[0:65, 2048 + jj * 512: 2560 + jj * 512],
                                 VSB[:, vb + kc * 65: vb + kc * 65 + 65],
                                 ptb[:, jj * 512:(jj + 1) * 512],
                                 start=(kc == 0), stop=(kc == 31)), "sT")
                    av_mark[(it, kc)] = b.sems["sT"][1]
            b.wait(V, "sT", av_mark[(it, 31)])
            if it >= 1:
                b.wait(V, "dA1")
                b.wait(V, "dA2")
            evac_mark[(it, "evac")] = b.op(
                V, lambda e: e.tensor_copy(AVE[0:65, :],
                                           PS[0:65, 2048:3072]), "sV")
            b.wait(Sc, "sV", evac_mark[(it, "evac")])
            b.dmas(Sc, "dA1", sp["sp_attn"][head * 64:(head + 1) * 64,
                                            qh * 1024:(qh + 1) * 1024],
                   AVE[0:64, :])
            b.dmas(Sc, "dA2", sp["sp_sums"][head:head + 1,
                                            qh * 1024:(qh + 1) * 1024],
                   AVE[64:65, :])

    b.barrier()
    XQT, SQ2 = sb["XQT"], sb["SQ2"]
    for ec in range(NEC):
        b.op(Sc, lambda e, ec=ec: e.activation(
            out=SQ2[:, :], in_=sb["XNTA"][:, ec * MYT:(ec + 1) * MYT],
            func=Act.Identity, bias=sb["CST"][:, 0:1],
            scale=sb["SXB"][:, 1:2]), "sS")
        b.wait(V, "sS")
        b.op(V, lambda e: e.tensor_scalar(SQ2[:, :], SQ2[:, :], MAGIC,
                                          -32768.0, Alu.subtract,
                                          Alu.max), "sV")
        b.op(V, lambda e, ec=ec: e.tensor_scalar(
            XQT[:, ec * MYT:(ec + 1) * MYT], SQ2[:, :], 32767.0,
            sb["SXB"][:, 0:1], Alu.min, Alu.mult), "sV")
        b.wait(Sc, "sV")

    b.barrier()
    ACL, BBC = sb["ACL"], sb["BBC"]
    ld = b.dma(Sy, sb["SM16"][:, :], sp["sp_sums"][:, :])
    b.wait(V, "dS")
    b.op(V, lambda e: e.reciprocal(sb["C16"][:, :], sb["SM16"][:, :]), "sV")
    b.vsync(V)
    for ec in range(NEC):
        ld = b.dma(Sy, ACL[:, ec * MYT:(ec + 1) * MYT],
                   sp["sp_attn"][ec * 128:(ec + 1) * 128, :])
        b.wait(Sc, "dS")
        b.wait(Sc, "sT")
        b.op(Sc, lambda e, ec=ec: e.activation(
            out=SQ2[:, :], in_=ACL[:, ec * MYT:(ec + 1) * MYT],
            func=Act.Square), "sS")
        b.wait(T, "dS")
        b.wait(T, "sS")
        b.wait(T, "sV")
        for np_ in range(4):
            b.op(T, lambda e, ec=ec, np_=np_: e.matmul(
                PS[0:2, np_ * 512:(np_ + 1) * 512], sb["IND2"][:, :],
                ACL[:, ec * MYT + np_ * 512: ec * MYT + (np_ + 1) * 512],
                start=True, stop=True), "sT")
            b.op(T, lambda e, np_=np_: e.matmul(
                PS[0:2, 2048 + np_ * 512: 2048 + (np_ + 1) * 512],
                sb["IND2"][:, :], SQ2[:, np_ * 512:(np_ + 1) * 512],
                start=True, stop=True), "sT")
        b.wait(V, "sT")
        b.wait(V, "dG")
        b.op(V, lambda e, ec=ec: e.tensor_copy(
            sb["S2T"][:, :], PS[0:2, 0:2048]), "sV")
        b.wait(G, "sV")
        b.dma_g(sp["sp_st"][2 * ec:2 * ec + 2, :], sb["S2T"][:, :])
        b.wait(V, "dG")
        b.op(V, lambda e, ec=ec: e.tensor_copy(
            sb["S2T"][:, :], PS[0:2, 2048:4096]), "sV")
        b.wait(G, "sV")
        b.dma_g(sp["sp_st"][16 + 2 * ec:18 + 2 * ec, :], sb["S2T"][:, :])
    b.wait(Sy, "dG")
    ldA = b.dma(Sy, sb["S16A"][:, :], sp["sp_st"][0:16, :])
    ldB = b.dma(Sy, sb["S16B"][:, :], sp["sp_st"][16:32, :])
    b.wait(V, "dS")
    b.op(V, lambda e: e.tensor_tensor(sb["S16A"][:, :], sb["S16A"][:, :],
                                      sb["C16"][:, :], Alu.mult), "sV")
    b.op(V, lambda e: e.tensor_tensor(sb["A16"][:, :], sb["C16"][:, :],
                                      sb["C16"][:, :], Alu.mult), "sV")
    b.vsync(V)
    b.op(V, lambda e: e.tensor_tensor(sb["S16B"][:, :], sb["S16B"][:, :],
                                      sb["A16"][:, :], Alu.mult), "sV")
    b.wait(T, "sV")
    for np_ in range(4):
        b.op(T, lambda e, np_=np_: e.matmul(
            PS[0:1, np_ * 512:(np_ + 1) * 512], sb["ONES1"][0:16, :],
            sb["S16A"][:, np_ * 512:(np_ + 1) * 512],
            start=True, stop=True), "sT")
        b.op(T, lambda e, np_=np_: e.matmul(
            PS[0:1, 2048 + np_ * 512: 2048 + (np_ + 1) * 512],
            sb["ONES1"][0:16, :], sb["S16B"][:, np_ * 512:(np_ + 1) * 512],
            start=True, stop=True), "sT")
    b.wait(V, "sT")
    b.op(V, lambda e: e.tensor_scalar(ROW[0:1, 0:2048], PS[0:1, 0:2048],
                                      1.0 / E, None, Alu.mult), "sV")
    b.op(V, lambda e: e.tensor_scalar(ROW[0:1, 2048:4096], PS[0:1, 2048:4096],
                                      1.0 / E, None, Alu.mult), "sV")
    mu = ROW[0:1, 0:2048]
    ex2 = ROW[0:1, 2048:4096]
    musq = sb["S16B"][0:1, 0:2048]
    var = sb["SM16"][0:1, 0:2048]
    sd = sb["S16A"][0:1, 0:2048]
    rr = sb["A16"][0:1, 0:2048]
    bb = sb["S16B"][0:1, 0:2048]
    b.vsync(V)
    b.op(V, lambda e: e.tensor_tensor(musq, mu, mu, Alu.mult), "sV")
    b.vsync(V)
    b.op(V, lambda e: e.tensor_tensor(var, ex2, musq, Alu.subtract), "sV")
    b.wait(Sc, "sV")
    b.op(Sc, lambda e: e.activation(out=sd, in_=var, func=Act.Sqrt,
                                    bias=sb["CST"][0:1, 1:2], scale=1.0), "sS")
    b.wait(V, "sS")
    b.op(V, lambda e: e.reciprocal(rr, sd), "sV")
    b.vsync(V)
    b.op(V, lambda e: e.tensor_tensor(bb, mu, rr, Alu.mult), "sV")
    b.vsync(V)
    b.wait(G, "sV")
    b.dma_g(sp["scr_row"][2:3, 0:2048], bb)
    b.dma_g(sp["scr_row"][3:4, 0:2048], rr)
    b.wait(G, "dG")
    b.dma_g(BBC[:, :], sp["scr_row"][2:3, 0:2048].partition_broadcast(128))
    b.dma_g(sb["SM16"][:, :],
          sp["scr_row"][3:4, 0:2048].partition_broadcast(16))
    b.wait(V, "dG")
    b.op(V, lambda e: e.tensor_tensor(sb["A16"][:, :], sb["C16"][:, :],
                                      sb["SM16"][:, :], Alu.mult), "sV")
    b.vsync(V)
    for ec in range(NEC):
        b.wait(T, "sV")
        for np_ in range(4):
            b.op(T, lambda e, ec=ec, np_=np_: e.matmul(
                PS[:, np_ * 512:(np_ + 1) * 512],
                sb["IND16"][:, ec * 128:(ec + 1) * 128],
                sb["A16"][:, np_ * 512:(np_ + 1) * 512],
                start=True, stop=True), "sT")
        b.wait(V, "sT")
        b.wait(Sc, "sV")
        b.op(V, lambda e, ec=ec: e.tensor_tensor(
            SQ2[:, :], ACL[:, ec * MYT:(ec + 1) * MYT], PS[:, 0:2048],
            Alu.mult), "sV")
        b.op(V, lambda e: e.tensor_tensor(SQ2[:, :], SQ2[:, :], BBC[:, :],
                                          Alu.subtract), "sV")
        b.wait(V, "dG")
        b.op(V, lambda e, ec=ec: e.tensor_scalar(
            ACL[:, ec * MYT:(ec + 1) * MYT], SQ2[:, :],
            sb["GAM"][:, ec:ec + 1], sb["BET"][:, ec:ec + 1],
            Alu.mult, Alu.add), "sV")
        b.wait(G, "sV")
        b.dma_g(sp["sp_aln"][ec * 128:(ec + 1) * 128, :],
              ACL[:, ec * MYT:(ec + 1) * MYT])

    b.barrier()

    def prep_ffn_weight(src, dstT, nchunks, icols, dst_stride, wna, wsc):
        nb = icols // 32
        for c in range(nchunks):
            pb = (c % 2) * icols
            b.wait(Sy, "sT")
            b.wait(Sy, "sV")
            ld = b.dma(Sy, wna[:, pb:pb + icols], src[c * 128:(c + 1) * 128, :])
            b.wait(V, "dS")
            b.wait(V, "sT")
            wna3 = wna[:, pb:pb + icols].rearrange("p (nb k) -> p nb k", k=32)
            b.op(V, lambda e, wna3=wna3, nb=nb: e.tensor_reduce(
                sb["BMX"][:, 0:nb], wna3, AX, Alu.max,
                apply_absolute_value=True), "sV")
            b.vsync(V)
            b.op(V, lambda e, nb=nb: e.tensor_scalar(
                sb["BSC"][:, 0:nb], sb["BMX"][:, 0:nb], 1.0 / 7.0, 1e-12,
                Alu.mult, Alu.max), "sV")
            b.vsync(V)
            b.op(V, lambda e, nb=nb: e.reciprocal(
                sb["BRS"][:, 0:nb], sb["BSC"][:, 0:nb]), "sV")
            b.vsync(V)
            qsc = wsc[:, 0:icols]
            qsc3 = qsc.rearrange("p (nb k) -> p nb k", k=32)
            b.op(V, lambda e, wna3=wna3, qsc3=qsc3, nb=nb: e.tensor_tensor(
                qsc3, wna3,
                sb["BRS"][:, 0:nb].unsqueeze(2).broadcast_to([128, nb, 32]), Alu.mult), "sV")
            b.op(V, lambda e, qsc=qsc: e.tensor_scalar(
                qsc, qsc, MAGIC, MAGIC, Alu.add, Alu.subtract), "sV")
            b.op(V, lambda e, qsc=qsc: e.tensor_scalar(
                qsc, qsc, -8.0, 7.0, Alu.max, Alu.min), "sV")
            b.op(V, lambda e, qsc3=qsc3, wna3=wna3, nb=nb: e.tensor_tensor(
                wna3, qsc3,
                sb["BSC"][:, 0:nb].unsqueeze(2).broadcast_to([128, nb, 32]), Alu.mult), "sV")
            b.wait(T, "sV")
            for s8 in range(icols // 128):
                pp = PS[:, (s8 % 2) * 512:(s8 % 2) * 512 + 128]
                if s8 >= 2:
                    b.wait(T, "sV")
                b.op(T, lambda e, pp=pp, pb=pb, s8=s8, wna=wna: e.transpose(
                    pp, wna[:, pb + s8 * 128: pb + (s8 + 1) * 128],
                    sb["IDF"][:, :]), "sT")
                b.wait(V, "sT")
                b.op(V, lambda e, pp=pp, s8=s8, c=c: e.tensor_copy(
                    sb[dstT][:, s8 * dst_stride + c * 128:
                             s8 * dst_stride + (c + 1) * 128], pp), "sV")

    prep_ffn_weight(dr["wu"], "WUT", NIC, E, I, sb["WNA"], sb["WSC"])
    prep_ffn_weight(dr["wg"], "WGT", NIC, E, I, sb["WNA"], sb["WSC"])

    b.barrier()
    for wt, spd, amcol in (("WUT", sp["sp_up"], 1), ("WGT", sp["sp_gate"], 2)):
        for ic in range(NIC):
            base = (ic % 2) * 4 * 512
            b.wait(T, "sV")
            for ec in range(NEC):
                for nq in range(4):
                    b.op(T, lambda e, base=base, nq=nq, ec=ec, ic=ic, wt=wt:
                         e.matmul(
                             PS[:, base + nq * 512: base + (nq + 1) * 512],
                             sb[wt][:, ec * I + ic * 128:
                                    ec * I + (ic + 1) * 128],
                             sb["XQT"][:, ec * MYT + nq * 512:
                                       ec * MYT + (nq + 1) * 512],
                             start=(ec == 0), stop=(ec == NEC - 1)), "sT")
            fb = (ic % 2) * 2048
            b.wait(V, "sT")
            b.wait(V, "dF%d" % (ic % 2) if ("dF%d" % (ic % 2)) in b.sems else "sT")
            for nq in range(4):
                b.op(V, lambda e, base=base, nq=nq, fb=fb: e.tensor_copy(
                    sb["FEV"][:, fb + nq * 512: fb + (nq + 1) * 512],
                    PS[:, base + nq * 512: base + (nq + 1) * 512]), "sV")
            b.op(V, lambda e, fb=fb: e.tensor_reduce(
                ROW[0:128, 384:385], sb["FEV"][:, fb:fb + 2048], AX, Alu.max,
                apply_absolute_value=True), "sV")
            b.vsync(V)
            b.op(V, lambda e, amcol=amcol: e.tensor_tensor(
                sb["AMX"][:, amcol:amcol + 1], sb["AMX"][:, amcol:amcol + 1],
                ROW[0:128, 384:385], Alu.max), "sV")
            b.vsync(V)
            b.wait(Sc, "sV")
            b.dmas(Sc, "dF%d" % (ic % 2), spd[ic * 128:(ic + 1) * 128, :],
                   sb["FEV"][:, fb:fb + 2048])
    global_scale(1, "ug", 2, "SUGB", 32767.0)

    b.barrier()
    FG, FU, FT1 = sb["FG"], sb["FU"], sb["FT1"]
    FGQ, FSQ, FG2 = sb["FGQ"], sb["FSQ"], sb["FG2"]
    for ic in range(NIC):
        b.wait(Sy, "sV")
        b.wait(Sy, "sS")
        ldg = b.dmas(Sy, "dLG", FG[:, :],
                     sp["sp_gate"][ic * 128:(ic + 1) * 128, :])
        ldu = b.dmas(Sy, "dLU", FU[:, :],
                     sp["sp_up"][ic * 128:(ic + 1) * 128, :])
        b.wait(Sc, "dLG", ldg)
        if "dUQ" in b.sems:
            b.wait(V, "dUQ")
        qdq_chunk(FG[:, :], FGQ[:, :], sb["SUGB"][:, 2:3], sb["SUGB"][:, 3:4],
                  -32768.0, 32767.0, FT1[:, :])
        b.op(Sc, lambda e: e.activation(out=FSQ[:, :], in_=FGQ[:, :],
                                        func=Act.Sigmoid), "sS")
        b.vsync(Sc)
        b.op(Sc, lambda e: e.activation(out=FT1[:, :], in_=FSQ[:, :],
                                        func=Act.Identity,
                                        bias=sb["CST"][:, 0:1],
                                        scale=sb["CST"][:, 2:3]), "sS")
        b.wait(V, "sS")
        b.op(V, lambda e: e.tensor_scalar(FT1[:, :], FT1[:, :], MAGIC, 0.0,
                                          Alu.subtract, Alu.max), "sV")
        b.op(V, lambda e: e.tensor_scalar(FSQ[:, :], FT1[:, :], 65535.0,
                                          1.0 / 65536.0, Alu.min,
                                          Alu.mult), "sV")
        if "dG2" in b.sems:
            b.wait(V, "dG2")
        b.op(V, lambda e: e.tensor_tensor(FG2[:, :], FGQ[:, :], FSQ[:, :],
                                          Alu.mult), "sV")
        b.op(V, lambda e: e.tensor_reduce(ROW[0:128, 384:385], FG2[:, :], AX,
                                          Alu.max,
                                          apply_absolute_value=True), "sV")
        b.vsync(V)
        b.op(V, lambda e: e.tensor_tensor(sb["AMX"][:, 3:4], sb["AMX"][:, 3:4],
                                          ROW[0:128, 384:385], Alu.max), "sV")
        b.vsync(V)
        b.wait(Sc, "sV")
        b.dmas(Sc, "dG2", sp["sp_g2"][ic * 128:(ic + 1) * 128, :], FG2[:, :])
        b.wait(Sc, "dLU", ldu)
        qdq_chunk(FU[:, :], FGQ[:, :], sb["SUGB"][:, 0:1], sb["SUGB"][:, 1:2],
                  -32768.0, 32767.0, FT1[:, :])
        b.wait(Sc, "sV")
        b.dmas(Sc, "dUQ", sp["sp_uq"][ic * 128:(ic + 1) * 128, :], FGQ[:, :])
    global_scale(3, "g2", 1, "S2B", 32767.0)

    b.barrier()
    for ic in range(NIC):
        b.wait(Sy, "sV")
        b.wait(Sy, "sS")
        ldg = b.dmas(Sy, "dLG", FG[:, :],
                     sp["sp_g2"][ic * 128:(ic + 1) * 128, :])
        ldu = b.dmas(Sy, "dLU", FU[:, :],
                     sp["sp_uq"][ic * 128:(ic + 1) * 128, :])
        b.wait(Sc, "dLG", ldg)
        qdq_chunk(FG[:, :], FGQ[:, :], sb["S2B"][:, 0:1], sb["S2B"][:, 1:2],
                  -32768.0, 32767.0, FT1[:, :])
        b.wait(V, "dLU", ldu)
        if "dO" in b.sems:
            b.wait(V, "dO")
        b.op(V, lambda e: e.tensor_tensor(FG2[:, :], FGQ[:, :], FU[:, :],
                                          Alu.mult), "sV")
        b.op(V, lambda e: e.tensor_reduce(ROW[0:128, 384:385], FG2[:, :], AX,
                                          Alu.max,
                                          apply_absolute_value=True), "sV")
        b.op(V, lambda e: e.tensor_tensor(sb["AMX"][:, 4:5], sb["AMX"][:, 4:5],
                                          ROW[0:128, 384:385], Alu.max), "sV")
        b.wait(Sc, "sV")
        b.dmas(Sc, "dO", sp["sp_o"][ic * 128:(ic + 1) * 128, :], FG2[:, :])
    global_scale(4, "o", 1, "SOB", 32767.0)

    b.barrier()
    prep_ffn_weight(dr["wd"], "WDT", NEC, I, E, sb["W3A"], sb["W3S"])

    b.barrier()
    FG4, FT4, OQT = sb["FG4"], sb["FT4"], sb["OQT"]
    for ic in range(NIC):
        b.wait(Sy, "sV")
        b.wait(Sy, "sS")
        ld = b.dmas(Sy, "dLG", FG4[:, :],
                    sp["sp_o"][ic * 128:(ic + 1) * 128, :])
        b.wait(Sc, "dLG", ld)
        b.wait(Sc, "sV")
        b.op(Sc, lambda e: e.activation(out=FT4[:, :], in_=FG4[:, :],
                                        func=Act.Identity,
                                        bias=sb["CST"][:, 0:1],
                                        scale=sb["SOB"][:, 1:2]), "sS")
        b.wait(V, "sS")
        b.op(V, lambda e: e.tensor_scalar(FT4[:, :], FT4[:, :], MAGIC,
                                          -32768.0, Alu.subtract,
                                          Alu.max), "sV")
        b.op(V, lambda e, ic=ic: e.tensor_scalar(
            OQT[:, ic * MYT:(ic + 1) * MYT], FT4[:, :], 32767.0,
            sb["SOB"][:, 0:1], Alu.min, Alu.mult), "sV")

    b.barrier()
    for hc in range(NEC):
        base = (hc % 2) * 4 * 512
        b.wait(Sy, "sV")
        ld = b.dmas(Sy, "dAL", sb["ALC"][:, :],
                    sp["sp_aln"][hc * 128:(hc + 1) * 128, :])
        b.wait(T, "sV")
        for ic in range(NIC):
            for nq in range(4):
                b.op(T, lambda e, base=base, nq=nq, ic=ic, hc=hc: e.matmul(
                    PS[:, base + nq * 512: base + (nq + 1) * 512],
                    sb["WDT"][:, ic * E + hc * 128: ic * E + (hc + 1) * 128],
                    OQT[:, ic * MYT + nq * 512: ic * MYT + (nq + 1) * 512],
                    start=(ic == 0), stop=(ic == NIC - 1)), "sT")
        b.wait(V, "sT")
        b.wait(V, "dAL", ld)
        if "dAF" in b.sems:
            b.wait(V, "dAF")
        for nq in range(4):
            b.op(V, lambda e, base=base, nq=nq: e.tensor_tensor(
                sb["AFE"][:, nq * 512:(nq + 1) * 512],
                PS[:, base + nq * 512: base + (nq + 1) * 512],
                sb["ALC"][:, nq * 512:(nq + 1) * 512], Alu.add), "sV")
        b.wait(Sc, "sV")
        b.dmas(Sc, "dAF", sp["sp_af"][hc * 128:(hc + 1) * 128, :],
               sb["AFE"][:, :])

    b.barrier()
    AFL, XT6, OSB = sb["AFL"], sb["XT6"], sb["OSB"]
    for ec in range(NEC):
        b.dmas(Sy, "dAFL", AFL[:, ec * MYT:(ec + 1) * MYT],
               sp["sp_af"][ec * 128:(ec + 1) * 128, :])
    afl_ld = b.sems["dAFL"][1]
    for tt in range(16):
        pb = (tt % 2) * E
        base = (tt % 2) * 1024
        if tt >= 2:
            b.wait(Sy, "sV")
        ld = b.dma(Sy, XT6[:, pb:pb + E], dr["x_tok"][tt * 128:(tt + 1) * 128, :])
        b.wait(T, "dS")
        b.wait(T, "sV")
        for hc in range(NEC):
            b.op(T, lambda e, base=base, hc=hc, tt=tt: e.transpose(
                PS[:, base + hc * 128: base + (hc + 1) * 128],
                AFL[:, hc * MYT + tt * 128: hc * MYT + (tt + 1) * 128],
                sb["IDF"][:, :]), "sT")
        b.wait(V, "sT")
        b.wait(V, "dS")
        b.wait(V, "dG")
        b.op(V, lambda e, base=base, pb=pb: e.tensor_tensor(
            OSB[:, pb:pb + E], PS[:, base:base + E], XT6[:, pb:pb + E],
            Alu.add), "sV")
        b.wait(G, "sV")
        b.dma_g(out_d[tt * 128:(tt + 1) * 128, :], OSB[:, pb:pb + E])
    b.wait(G, "dG")


_NC_CACHE = {}


def kernel(L0_attn_x, L0_attn_Wq, L0_attn_Wk, L0_attn_Wv, L0_attn_ln_gamma,
           L0_attn_ln_beta, L0_ffn_w_gate, L0_ffn_w_up, L0_ffn_w_down):
    x = np.ascontiguousarray(np.asarray(L0_attn_x, np.float32))
    wq = np.asarray(L0_attn_Wq, np.float32)
    wk = np.asarray(L0_attn_Wk, np.float32)
    wv = np.asarray(L0_attn_Wv, np.float32)
    gam = np.asarray(L0_attn_ln_gamma, np.float32)
    bet = np.asarray(L0_attn_ln_beta, np.float32)
    wg = np.ascontiguousarray(np.asarray(L0_ffn_w_gate, np.float32))
    wu = np.ascontiguousarray(np.asarray(L0_ffn_w_up, np.float32))
    wd = np.ascontiguousarray(np.asarray(L0_ffn_w_down, np.float32))

    wqT = np.ascontiguousarray(wq.T)
    wkT = np.ascontiguousarray(wk.T)
    wvT = np.ascontiguousarray(wv.T)
    gamma8 = np.ascontiguousarray(gam.reshape(NEC, 128).T)
    beta8 = np.ascontiguousarray(bet.reshape(NEC, 128).T)

    if "nc" not in _NC_CACHE:
        _NC_CACHE["nc"] = build_nc()
    nc = _NC_CACHE["nc"]

    in_maps = []
    for c in range(N_CORES):
        bq, h = divmod(c, 2)
        qoff = h * MYT
        xloc = np.roll(x[bq], -qoff, axis=0)
        if h == 0:
            meta = np.array([[-1.0 / 4096.0, 3968.0 / 4096.0]], np.float32)
        else:
            meta = np.array([[1.0 / 4096.0, 1.0 - 3968.0 / 4096.0]], np.float32)
        in_maps.append({
            "xseqT": np.ascontiguousarray(xloc.T),
            "x_tok": np.ascontiguousarray(xloc),
            "wqT": wqT, "wkT": wkT, "wvT": wvT,
            "wu": wu, "wg": wg, "wd": wd,
            "gamma8": gamma8, "beta8": beta8,
            "meta": meta,
        })

    res = run_bass_kernel_spmd(nc, in_maps, core_ids=list(range(N_CORES)))
    out = np.empty((4, S, E), np.float32)
    for c in range(N_CORES):
        bq, h = divmod(c, 2)
        out[bq, h * MYT:(h + 1) * MYT, :] = res.results[c]["out"]
    return out


# version 09
# speedup vs baseline: 1.4875x; 1.0310x over previous
import numpy as np
from contextlib import ExitStack

import concourse.bass as bass
import concourse.mybir as mybir
from concourse.bass_utils import run_bass_kernel_spmd

F32 = mybir.dt.float32
BF16 = mybir.dt.bfloat16
Alu = mybir.AluOpType
Act = mybir.ActivationFunctionType
AX = mybir.AxisListType.X

MAGIC = 1.5 * 2.0**23
N_CORES = 8
S = 4096
E = 1024
H = 16
I = 3072
MYT = 2048
NEC = E // 128
NIC = I // 128

ENGS = ("sync", "gpsimd", "vector", "scalar", "tensor")
V, Sc, T, G, Sy = "vector", "scalar", "tensor", "gpsimd", "sync"


class Builder:

    def __init__(self, nc):
        self.nc = nc
        self.ops = {e: [] for e in ENGS}
        self.sems = {}

    def mksem(self, name):
        h = self.nc.semaphore(name).__enter__()
        self.sems[name] = [h, 0]

    AUTO_CHAIN = {"vector": "sV", "scalar": "sS", "gpsimd": "sG"}

    def op(self, eng, fn, inc=None, k=1):
        ch = self.AUTO_CHAIN.get(eng)
        if ch is not None:
            self.wait(eng, ch)
        if inc is None:
            self.ops[eng].append(fn)
            return None
        rec = self.sems[inc]
        rec[1] += k
        h = rec[0]
        self.ops[eng].append(lambda e, fn=fn, h=h, k=k: fn(e).then_inc(h, k))
        return rec[1]

    def wait(self, eng, sem, val=None):
        h, c = self.sems[sem]
        v = c if val is None else val
        if v > 0:
            self.ops[eng].append(lambda e, h=h, v=v: e.wait_ge(h, v))

    def dma_g(self, out, in_, **kw):
        return self.dma("gpsimd", out, in_, serial=False, **kw)

    def dmas(self, eng, sem, out, in_, **kw):
        if sem not in self.sems:
            self.mksem(sem)
        return self.op(eng, lambda e: e.dma_start(out=out, in_=in_, **kw),
                       sem, 16)

    def dma(self, eng, out, in_, serial=True, **kw):
        q = {"sync": "dS", "gpsimd": "dG", "scalar": "dA"}[eng]
        n = self.op(eng, lambda e: e.dma_start(out=out, in_=in_, **kw), q, 16)
        if serial:
            self.wait(eng, q, n)
        return n

    def vsync(self, eng):
        sem = {"vector": "sV", "scalar": "sS", "tensor": "sT",
               "gpsimd": "sG"}[eng]
        self.wait(eng, sem)

    def vfiller(self, sb):
        self.op("vector", lambda e: e.tensor_copy(sb["ROW"][0:1, 4095:4096],
                                                  sb["CST"][0:1, 0:1]), "sV")

    def barrier(self, engs=ENGS):
        snap = [(h, c) for (h, c) in self.sems.values() if c > 0]
        for eng in engs:
            for h, v in snap:
                self.ops[eng].append(lambda e, h=h, v=v: e.wait_ge(h, v))


def build_nc():
    nc = bass.Bass("TRN2", target_bir_lowering=False)

    dram = {}
    for nm, shp in [
        ("xseqT", [E, S]), ("x_tok", [S, E]),
        ("wqT", [E, E]), ("wkT", [E, E]), ("wvT", [E, E]),
        ("wu", [I, E]), ("wg", [I, E]), ("wd", [E, I]),
        ("gamma8", [128, NEC]), ("beta8", [128, NEC]),
        ("meta", [1, 2]),
    ]:
        dram[nm] = nc.dram_tensor(nm, shp, F32, kind="ExternalInput").ap()
    out_d = nc.dram_tensor("out", [MYT, E], F32, kind="ExternalOutput").ap()

    sp = {}
    for nm, shp in [
        ("sp_v", [S, H * 65]), ("sp_attn", [E, MYT]), ("sp_sums", [H, MYT]),
        ("sp_aln", [E, MYT]), ("sp_up", [I, MYT]), ("sp_gate", [I, MYT]),
        ("sp_g2", [I, MYT]), ("sp_uq", [I, MYT]), ("sp_o", [I, MYT]),
        ("sp_af", [E, MYT]), ("scr_row", [4, S]), ("sp_st", [32, MYT]),
    ]:
        sp[nm] = nc.dram_tensor(
            nm, shp, BF16 if nm == "sp_v" else F32, kind="Internal").ap()
    sp["cc_in"] = {k: nc.dram_tensor(f"cc{k}_i", [1, w], F32, kind="Internal").ap()
                   for k, w in (("x", 1), ("ug", 2), ("g2", 1), ("o", 1))}
    sp["cc_out"] = {k: nc.dram_tensor(f"cc{k}_o", [1, w], F32, kind="Internal").ap()
                    for k, w in (("x", 1), ("ug", 2), ("g2", 1), ("o", 1))}

    st = ExitStack()
    sb = {}

    def perm(nm, shp, dt):
        sb[nm] = st.enter_context(nc.sbuf_tensor(nm, shp, dt))
        return sb[nm]

    perm("CST", [128, 8], F32)
    perm("SXB", [128, 2], F32)
    perm("SUGB", [128, 4], F32)
    perm("S2B", [128, 2], F32)
    perm("SOB", [128, 2], F32)
    perm("METB", [128, 2], F32)
    perm("AMX", [128, 8], F32)
    perm("IDF", [128, 128], F32)
    perm("IND2", [128, 2], F32)
    perm("IND16", [16, 1024], F32)
    perm("ONES1", [128, 1], F32)
    perm("PIOTA", [128, 1], F32)
    perm("GAM", [128, NEC], F32)
    perm("BET", [128, NEC], F32)
    perm("ROW", [128, 4096], F32)
    perm("BMX", [128, 96], F32)
    perm("BSC", [128, 96], F32)
    perm("BRS", [128, 96], F32)

    TOP = nc.SBUF_PARTITION_SIZE_BYTES
    ZB = 24576

    def at(nm, off, shp, dt):
        end = off + int(np.prod(shp[1:])) * mybir.dt.size(dt)
        assert ZB <= off and end <= TOP, (nm, off, end)
        sb[nm] = nc.alloc_sbuf_tensor_at(nm, shp, dt, offset=off)
        return sb[nm]

    at("WVTS", 40960, [128, NEC * E], BF16)
    at("WKTS", 57344, [128, NEC * E], BF16)
    at("WQTS", 73728, [128, NEC * E], BF16)
    at("XNTA", 90112, [128, NEC * MYT], BF16)
    at("XNTB", 122880, [128, NEC * MYT], BF16)
    at("XC2", 155648, [128, S], F32)
    at("XC2B", 204800, [128, S], F32)
    at("MUBC", 172032, [128, S], F32)
    at("RSBC", 188416, [128, S], F32)
    at("XT2", 204800, [128, 2 * E], F32)
    at("SQT", 212992, [128, E], F32)
    at("STT", 217088, [128, 128], F32)
    at("VEV", 217600, [128, H * 65], BF16)
    at("KF", 155648, [128, NEC * S], BF16)
    at("QF", 40960, [128, NEC * MYT], BF16)
    at("DNE", 122880, [128, 4096], F32)
    at("DFA", 139264, [128, 4096], F32)
    at("SW", 73728, [128, 2 * 1024], F32)
    at("PT", 81920, [128, 2 * 1024], BF16)
    at("AVE", 86016, [128, 1024], F32)
    at("VSB", 221184, [128, 32 * 65], BF16)
    at("XQT", 155648, [128, NEC * MYT], BF16)
    at("SQ2", 188416, [128, MYT], F32)
    at("ACL", 40960, [128, NEC * MYT], F32)
    at("BBC", 106496, [128, MYT], F32)
    at("C16", 114688, [16, MYT], F32)
    at("A16", 122880, [16, MYT], F32)
    at("SM16", 131072, [16, MYT], F32)
    at("S16A", 139264, [16, MYT], F32)
    at("S16B", 147456, [16, MYT], F32)
    at("S2T", 196608, [2, MYT], F32)
    at("WUT", 40960, [128, NEC * I], BF16)
    at("WGT", 90112, [128, NEC * I], BF16)
    at("FEV", 139264, [128, 8 * 512], F32)
    at("WNA", 196608, [128, 2 * E], F32)
    at("WSC", 204800, [128, 2 * E], F32)
    at("FG", 40960, [128, 2 * MYT], F32)
    at("FU", 57344, [128, 2 * MYT], F32)
    at("FT1", 73728, [128, MYT], F32)
    at("FGQ", 81920, [128, MYT], F32)
    at("FSQ", 90112, [128, MYT], F32)
    at("FG2", 98304, [128, MYT], F32)
    at("OQT", 40960, [128, NIC * MYT], BF16)
    at("WDT", 139264, [128, NIC * E], BF16)
    at("W3A", 188416, [128, 2 * I], F32)
    at("W3S", 212992, [128, I], F32)
    at("FG4", 188416, [128, 2 * MYT], F32)
    at("FT4", 204800, [128, MYT], F32)
    at("ALC", 188416, [128, MYT], F32)
    at("AFE", 196608, [128, MYT], F32)
    at("AFL", 40960, [128, NEC * MYT], F32)
    at("XT6", 106496, [128, 2 * E], F32)
    at("OSB", 114688, [128, 2 * E], F32)

    PS = nc.alloc_psum_tensor("PS", [128, 4096], F32)

    b = Builder(nc)
    for s_ in ("dS", "dG", "dA", "sV", "sS", "sT", "sG", "sCC"):
        b.mksem(s_)

    emit_program(nc, b, sb, PS, dram, out_d, sp)

    with nc.Block() as block:
        @block.sync
        def _(e):
            for f in b.ops["sync"]:
                f(e)

        @block.gpsimd
        def _(e):
            for f in b.ops["gpsimd"]:
                f(e)

        @block.vector
        def _(e):
            for f in b.ops["vector"]:
                f(e)

        @block.scalar
        def _(e):
            for f in b.ops["scalar"]:
                f(e)

        @block.tensor
        def _(e):
            for f in b.ops["tensor"]:
                f(e)

    st.close()
    return nc


def emit_program(nc, b, sb, PS, dr, out_d, sp):
    ROW = sb["ROW"]
    XNT = {0: sb["XNTA"], 1: sb["XNTB"]}

    def xnt_cols(ec, c0, c1):
        assert (c0 // MYT) == ((c1 - 1) // MYT)
        half = c0 // MYT
        t = XNT[half]
        return t[:, ec * MYT + c0 % MYT: ec * MYT + c0 % MYT + (c1 - c0)]

    b.op(V, lambda e: e.memset(sb["CST"][:, 0:1], MAGIC), "sV")
    b.op(V, lambda e: e.memset(sb["CST"][:, 1:2], 1e-5), "sV")
    b.op(V, lambda e: e.memset(sb["CST"][:, 2:3], 65536.0), "sV")
    b.op(V, lambda e: e.memset(sb["CST"][:, 3:4], -1920.0 / 4096.0), "sV")
    b.op(V, lambda e: e.memset(sb["CST"][:, 4:5], 1.0 / 4096.0), "sV")
    b.op(V, lambda e: e.memset(sb["ONES1"][:, :], 1.0), "sV")
    b.op(V, lambda e: e.memset(sb["AMX"][:, :], 0.0), "sV")
    b.op(G, lambda e: e.iota(sb["IDF"][:, :], pattern=[[1, 128]], base=0,
                             channel_multiplier=-1,
                             allow_small_or_imprecise_dtypes=True), "sG")
    b.op(G, lambda e: e.iota(sb["PIOTA"][:, :], pattern=[[0, 1]], base=0,
                             channel_multiplier=1,
                             allow_small_or_imprecise_dtypes=True), "sG")
    b.dma_g(sb["METB"][:, :], dr["meta"][0:1, :].partition_broadcast(128))
    b.wait(V, "sG")
    b.op(V, lambda e: e.tensor_scalar(sb["IND2"][:, 0:1], sb["PIOTA"][:, :],
                                      64.0, None, Alu.is_lt), "sV")
    b.op(V, lambda e: e.tensor_scalar(sb["IND2"][:, 1:2], sb["PIOTA"][:, :],
                                      64.0, None, Alu.is_ge), "sV")
    b.op(V, lambda e: e.tensor_scalar(sb["IDF"][:, :], sb["IDF"][:, :],
                                      0.0, None, Alu.is_equal), "sV")
    b.op(G, lambda e: e.iota(sb["IND16"][:, :],
                             pattern=[[2, 8], [1, 2], [0, 64]], base=0,
                             channel_multiplier=0,
                             allow_small_or_imprecise_dtypes=True), "sG")
    b.wait(V, "sG")
    b.op(V, lambda e: e.tensor_tensor(
        sb["IND16"][:, :], sb["IND16"][:, :],
        sb["PIOTA"][0:16, 0:1].broadcast_to([16, 1024]), Alu.is_equal), "sV")
    b.dma(Sy, sb["GAM"][:, :], dr["gamma8"][:, :])
    b.dma(Sy, sb["BET"][:, :], dr["beta8"][:, :])
    for wi, (nm, dst) in enumerate((("wqT", "WQTS"), ("wkT", "WKTS"),
                                    ("wvT", "WVTS"))):
        for ec in range(NEC):
            b.dmas(G, "dW%d" % wi, sb[dst][:, ec * E:(ec + 1) * E],
                   dr[nm][ec * 128:(ec + 1) * 128, :])

    def global_scale(col, key, packw, dst, qmaxv):
        b.wait(T, "sV")
        b.wait(T, "sS")
        b.op(T, lambda e: e.transpose(PS[0:packw, 1536:1664],
                                      sb["AMX"][:, col:col + packw],
                                      sb["IDF"][:, :]), "sT")
        b.wait(V, "sT")
        b.op(V, lambda e: e.tensor_reduce(ROW[0:packw, 256:257],
                                          PS[0:packw, 1536:1664], AX,
                                          Alu.max), "sV")
        b.wait(G, "sV")
        b.dma_g(sp["cc_in"][key][0:1, :].rearrange("a (p c) -> (a p) c",
                                                    p=packw),
              ROW[0:packw, 256:257])
        b.wait(G, "dG")
        b.op(G, lambda e: e.collective_compute(
            "AllReduce", Alu.max, replica_groups=[list(range(N_CORES))],
            ins=[sp["cc_in"][key].opt()], outs=[sp["cc_out"][key].opt()],
        ), "sCC")
        b.wait(G, "sCC")
        b.dma_g(ROW[0:1, 260:260 + packw], sp["cc_out"][key][0:1, :])
        b.wait(V, "dG")
        b.op(V, lambda e: e.tensor_scalar(
            ROW[0:1, 280:280 + packw], ROW[0:1, 260:260 + packw],
            1.0 / qmaxv, 1e-12, Alu.mult, Alu.max), "sV")
        b.vsync(V)
        b.op(V, lambda e: e.reciprocal(
            ROW[0:1, 288:288 + packw], ROW[0:1, 280:280 + packw]), "sV")
        b.vsync(V)
        for i in range(packw):
            b.op(V, lambda e, i=i: e.tensor_copy(
                ROW[0:1, 300 + 2 * i:301 + 2 * i],
                ROW[0:1, 280 + i:281 + i]), "sV")
            b.op(V, lambda e, i=i: e.tensor_copy(
                ROW[0:1, 301 + 2 * i:302 + 2 * i],
                ROW[0:1, 288 + i:289 + i]), "sV")
        b.wait(G, "sV")
        b.dma_g(sp["scr_row"][2:3, 0:2 * packw], ROW[0:1, 300:300 + 2 * packw])
        b.wait(G, "dG")
        b.dma_g(sb[dst][:, 0:2 * packw],
              sp["scr_row"][2:3, 0:2 * packw].partition_broadcast(128))

    def qdq_chunk(src_ap, dst_ap, s_ap, rs_ap, lo, hi, tmp_ap):
        b.wait(Sc, "sV")
        b.op(Sc, lambda e: e.activation(out=tmp_ap, in_=src_ap,
                                        func=Act.Identity,
                                        bias=sb["CST"][:, 0:1],
                                        scale=rs_ap), "sS")
        b.wait(V, "sS")
        b.op(V, lambda e: e.tensor_scalar(tmp_ap, tmp_ap, MAGIC, lo,
                                          Alu.subtract, Alu.max), "sV")
        b.op(V, lambda e: e.tensor_scalar(dst_ap, tmp_ap, hi, s_ap,
                                          Alu.min, Alu.mult), "sV")
        b.wait(Sc, "sV")

    XT2, SQT, STT, XC2 = sb["XT2"], sb["SQT"], sb["STT"], sb["XC2"]
    for tt in range(32):
        pb = (tt % 2) * E
        if tt >= 2:
            b.wait(Sy, "sV")
            b.wait(Sy, "sS")
        ld = b.dmas(Sy, "dXT%d" % (tt % 2), XT2[:, pb:pb + E],
                    dr["x_tok"][tt * 128:(tt + 1) * 128, :])
        b.wait(Sc, "dXT%d" % (tt % 2), ld)
        b.wait(Sc, "sV")
        b.op(Sc, lambda e, pb=pb: e.activation(
            out=SQT[:, :], in_=XT2[:, pb:pb + E], func=Act.Square), "sS")
        b.wait(V, "dXT%d" % (tt % 2), ld)
        b.op(V, lambda e, pb=pb, tt=tt: e.tensor_reduce(
            STT[:, tt:tt + 1], XT2[:, pb:pb + E], AX, Alu.add), "sV")
        b.wait(V, "sS")
        b.op(V, lambda e, tt=tt: e.tensor_reduce(
            STT[:, 32 + tt:33 + tt], SQT[:, :], AX, Alu.add), "sV")
    b.vsync(V)
    b.op(V, lambda e: e.tensor_scalar(STT[:, 0:32], STT[:, 0:32],
                                      1.0 / E, None, Alu.mult), "sV")
    b.op(V, lambda e: e.tensor_scalar(STT[:, 32:64], STT[:, 32:64],
                                      1.0 / E, None, Alu.mult), "sV")
    b.vsync(V)
    b.op(V, lambda e: e.tensor_tensor(STT[:, 64:96], STT[:, 0:32],
                                      STT[:, 0:32], Alu.mult), "sV")
    b.vsync(V)
    b.op(V, lambda e: e.tensor_tensor(STT[:, 64:96], STT[:, 32:64],
                                      STT[:, 64:96], Alu.subtract), "sV")
    b.wait(Sc, "sV")
    b.op(Sc, lambda e: e.activation(out=STT[:, 96:128], in_=STT[:, 64:96],
                                    func=Act.Sqrt, bias=sb["CST"][:, 1:2],
                                    scale=1.0), "sS")
    b.wait(V, "sS")
    b.op(V, lambda e: e.reciprocal(STT[:, 96:128], STT[:, 96:128]), "sV")
    b.vsync(V)
    b.wait(T, "sV")
    b.op(T, lambda e: e.transpose(PS[0:32, 0:128], STT[:, 0:32],
                                  sb["IDF"][:, :]), "sT")
    b.op(T, lambda e: e.transpose(PS[0:32, 512:640], STT[:, 96:128],
                                  sb["IDF"][:, :]), "sT")
    b.wait(V, "sT")
    b.op(V, lambda e: e.tensor_copy(ROW[0:32, 0:128], PS[0:32, 0:128]), "sV")
    b.op(V, lambda e: e.tensor_copy(ROW[0:32, 128:256], PS[0:32, 512:640]), "sV")
    b.wait(G, "sV")
    b.dma_g(sp["scr_row"][0:1, :].rearrange("a (p c) -> (a p) c", p=32),
          ROW[0:32, 0:128])
    b.dma_g(sp["scr_row"][1:2, :].rearrange("a (p c) -> (a p) c", p=32),
          ROW[0:32, 128:256])
    b.wait(G, "dG")
    b.dma_g(sb["MUBC"][:, :], sp["scr_row"][0:1, :].partition_broadcast(128))
    b.dma_g(sb["RSBC"][:, :], sp["scr_row"][1:2, :].partition_broadcast(128))
    b.barrier([Sy])
    for ec in range(NEC):
        xcb = sb["XC2"] if ec % 2 == 0 else sb["XC2B"]
        pb = 0
        if ec >= 2:
            b.wait(Sy, "sV")
        ld = b.dmas(Sy, "dXC%d" % (ec % 2), xcb[:, :],
                    dr["xseqT"][ec * 128:(ec + 1) * 128, :])
        b.wait(V, "dXC%d" % (ec % 2), ld)
        b.wait(V, "dG")
        b.op(V, lambda e, xcb=xcb: e.tensor_tensor(
            xcb[:, :], xcb[:, :], sb["MUBC"][:, :], Alu.subtract), "sV")
        for half in range(2):
            b.op(V, lambda e, xcb=xcb, ec=ec, half=half: e.tensor_tensor(
                XNT[half][:, ec * MYT:(ec + 1) * MYT],
                xcb[:, half * MYT:(half + 1) * MYT],
                sb["RSBC"][:, half * MYT:(half + 1) * MYT], Alu.mult), "sV")
            b.op(V, lambda e, ec=ec, half=half: e.tensor_reduce(
                ROW[0:128, 386:387], XNT[half][:, ec * MYT:(ec + 1) * MYT],
                AX, Alu.max, apply_absolute_value=True), "sV")
            b.op(V, lambda e: e.tensor_tensor(
                sb["AMX"][:, 0:1], sb["AMX"][:, 0:1], ROW[0:128, 386:387],
                Alu.max), "sV")
    global_scale(0, "x", 1, "SXB", 32767.0)

    b.barrier([T])
    VEV = sb["VEV"]
    for tt in range(32):
        half_col = tt % 2
        for half in range(2):
            pt = PS[:, 2048 + half * 512: 2048 + (half + 1) * 512]
            if tt > 0 or half > 0:
                b.wait(T, "sV")
            for ec in range(NEC):
                b.op(T, lambda e, tt=tt, ec=ec, pt=pt, half=half: e.matmul(
                    pt, xnt_cols(ec, tt * 128, (tt + 1) * 128),
                    sb["WVTS"][:, ec * E + half * 512:
                               ec * E + (half + 1) * 512],
                    start=(ec == 0), stop=(ec == NEC - 1)), "sT")
            b.wait(V, "sT")
            b.wait(V, "dG")
            b.op(V, lambda e, pt=pt, half=half: e.tensor_copy(
                VEV[:, half * 8 * 65:(half + 1) * 8 * 65]
                .rearrange("p (h c) -> p h c", c=65)[:, :, 0:64],
                pt.rearrange("p (h c) -> p h c", c=64)), "sV")
        b.op(V, lambda e: e.memset(
            VEV[:, :].rearrange("p (h c) -> p h c", c=65)[:, :, 64:65],
            1.0), "sV")
        b.wait(G, "sV")
        b.dma_g(sp["sp_v"][tt * 128:(tt + 1) * 128, :], VEV[:, :])

    b.barrier([T])
    KF, QF = sb["KF"], sb["QF"]
    for pair in range(NEC):
        for nk in range(8):
            pk = PS[:, 3072 + (nk % 2) * 512: 3072 + (nk % 2) * 512 + 512]
            b.wait(T, "sV")
            for ec in range(NEC):
                b.op(T, lambda e, pk=pk, ec=ec, pair=pair, nk=nk: e.matmul(
                    pk, sb["WKTS"][:, ec * E + pair * 128:
                                   ec * E + (pair + 1) * 128],
                    xnt_cols(ec, nk * 512, (nk + 1) * 512),
                    start=(ec == 0), stop=(ec == NEC - 1)), "sT")
            b.wait(V, "sT")
            b.op(V, lambda e, pk=pk, pair=pair, nk=nk: e.tensor_copy(
                KF[:, pair * S + nk * 512: pair * S + (nk + 1) * 512],
                pk), "sV")
    for pair in range(NEC):
        for nq in range(4):
            pk = PS[:, 3072 + (nq % 2) * 512: 3072 + (nq % 2) * 512 + 512]
            b.wait(T, "sV")
            for ec in range(NEC):
                b.op(T, lambda e, pk=pk, ec=ec, pair=pair, nq=nq: e.matmul(
                    pk, sb["WQTS"][:, ec * E + pair * 128:
                                   ec * E + (pair + 1) * 128],
                    xnt_cols(ec, nq * 512, (nq + 1) * 512),
                    start=(ec == 0), stop=(ec == NEC - 1)), "sT")
            b.wait(V, "sT")
            b.op(V, lambda e, pk=pk, pair=pair, nq=nq: e.tensor_scalar(
                QF[:, pair * MYT + nq * 512: pair * MYT + (nq + 1) * 512],
                pk, 1.0 / 32.0, None, Alu.mult), "sV")

    b.barrier([G])
    b.op(G, lambda e: e.iota(sb["DNE"][:, :], pattern=[[1, 4096]], base=0,
                             channel_multiplier=-1,
                             allow_small_or_imprecise_dtypes=True), "sG")
    b.op(G, lambda e: e.iota(sb["DFA"][:, :], pattern=[[1, 4096]], base=0,
                             channel_multiplier=-1,
                             allow_small_or_imprecise_dtypes=True), "sG")
    b.wait(V, "sG")
    b.wait(V, "dG")
    b.op(V, lambda e: e.tensor_scalar(sb["DFA"][:, :], sb["DFA"][:, :],
                                      sb["METB"][:, 0:1], sb["METB"][:, 1:2],
                                      Alu.mult, Alu.add), "sV")
    b.wait(Sc, "sG")
    b.wait(Sc, "sV")
    b.op(Sc, lambda e: e.activation(out=sb["DNE"][:, :], in_=sb["DNE"][:, :],
                                    func=Act.Abs, bias=sb["CST"][:, 3:4],
                                    scale=sb["CST"][:, 4:5]), "sS")

    b.barrier([T, Sc, G, Sy])
    SW, PT, AVE, VSB = sb["SW"], sb["PT"], sb["AVE"], sb["VSB"]
    DNE, DFA = sb["DNE"], sb["DFA"]
    RCOL = (0, 1024, 3072)
    sc_mark, tt_mark, exp_mark, av_mark, evac_mark = {}, {}, {}, {}, {}
    spill_mark = {}

    def prevkey(it, k, back, n=32):
        kk = k - back
        return (it, kk) if kk >= 0 else (it - 1, kk + n)

    it = -1
    for head in range(H):
        pair, hh = divmod(head, 2)
        hb = hh * 64
        for qh in range(2):
            it += 1
            vb = 0
            if (it - 1, 31) in av_mark:
                b.wait(Sy, "sT", av_mark[(it - 1, 31)])
            vld = b.dmas(Sy, "dVS", VSB[:, vb:vb + 2080]
                         .rearrange("p (k c) -> p k c", c=65),
                         sp["sp_v"][:, head * 65:(head + 1) * 65]
                         .rearrange("(k p) c -> p k c", p=128))
            for step in range(34):
                k_sc, k_tt, k_av = step, step - 1, step - 2
                if k_sc < 32:
                    kc = k_sc
                    pscr = PS[:, RCOL[kc % 3]:RCOL[kc % 3] + 1024]
                    key = prevkey(it, kc, 3)
                    if key in tt_mark:
                        b.wait(T, "sV", tt_mark[key])
                    for jj in range(2):
                        b.op(T, lambda e, pscr=pscr, jj=jj, hb=hb, kc=kc,
                             qh=qh, pair=pair: e.matmul(
                                 pscr[:, jj * 512:(jj + 1) * 512],
                                 KF[hb:hb + 64, pair * S + kc * 128:
                                    pair * S + (kc + 1) * 128],
                                 QF[hb:hb + 64,
                                    pair * MYT + qh * 1024 + jj * 512:
                                    pair * MYT + qh * 1024 + (jj + 1) * 512],
                                 start=True, stop=True), "sT")
                    sc_mark[(it, kc)] = b.sems["sT"][1]
                if 0 <= k_tt < 32:
                    kc = k_tt
                    pscr = PS[:, RCOL[kc % 3]:RCOL[kc % 3] + 1024]
                    swb = SW[:, (kc % 2) * 1024:(kc % 2) * 1024 + 1024]
                    ptb = PT[:, (kc % 2) * 1024:(kc % 2) * 1024 + 1024]
                    if kc < 16:
                        doff = qh * 1024 - kc * 128 + 1920
                        dsl = DNE[:, doff:doff + 1024]
                    else:
                        doff = qh * 1024 - (kc - 16) * 128 + 1920
                        dsl = DFA[:, doff:doff + 1024]
                    b.wait(V, "sT", sc_mark[(it, kc)])
                    key = prevkey(it, kc, 2)
                    if key in exp_mark:
                        b.wait(V, "sS", exp_mark[key])
                    if kc == 0 and "dA1" in b.sems:
                        pass
                    tt_mark[(it, kc)] = b.op(
                        V, lambda e, pscr=pscr, dsl=dsl, swb=swb:
                        e.tensor_tensor(swb, pscr, dsl, Alu.mult), "sV")
                    b.wait(Sc, "sV", tt_mark[(it, kc)])
                    if key in av_mark:
                        b.wait(Sc, "sT", av_mark[key])
                    exp_mark[(it, kc)] = b.op(
                        Sc, lambda e, swb=swb, ptb=ptb: e.activation(
                            out=ptb, in_=swb, func=Act.Exp), "sS")
                if 0 <= k_av < 32:
                    kc = k_av
                    ptb = PT[:, (kc % 2) * 1024:(kc % 2) * 1024 + 1024]
                    b.wait(T, "sS", exp_mark[(it, kc)])
                    if kc == 0:
                        b.wait(T, "dVS", vld)
                        if (it - 1, "evac") in evac_mark:
                            b.wait(T, "sV", evac_mark[(it - 1, "evac")])
                    for jj in range(2):
                        b.op(T, lambda e, jj=jj, kc=kc, ptb=ptb, vb=vb:
                             e.matmul(
                                 PS[0:65, 2048 + jj * 512: 2560 + jj * 512],
                                 VSB[:, vb + kc * 65: vb + kc * 65 + 65],
                                 ptb[:, jj * 512:(jj + 1) * 512],
                                 start=(kc == 0), stop=(kc == 31)), "sT")
                    av_mark[(it, kc)] = b.sems["sT"][1]
            b.wait(V, "sT", av_mark[(it, 31)])
            if it >= 1:
                b.wait(V, "dA1")
                b.wait(V, "dA2")
            evac_mark[(it, "evac")] = b.op(
                V, lambda e: e.tensor_copy(AVE[0:65, :],
                                           PS[0:65, 2048:3072]), "sV")
            b.wait(Sc, "sV", evac_mark[(it, "evac")])
            b.dmas(Sc, "dA1", sp["sp_attn"][head * 64:(head + 1) * 64,
                                            qh * 1024:(qh + 1) * 1024],
                   AVE[0:64, :])
            b.dmas(Sc, "dA2", sp["sp_sums"][head:head + 1,
                                            qh * 1024:(qh + 1) * 1024],
                   AVE[64:65, :])

    b.barrier()
    XQT, SQ2 = sb["XQT"], sb["SQ2"]
    for ec in range(NEC):
        b.op(Sc, lambda e, ec=ec: e.activation(
            out=SQ2[:, :], in_=sb["XNTA"][:, ec * MYT:(ec + 1) * MYT],
            func=Act.Identity, bias=sb["CST"][:, 0:1],
            scale=sb["SXB"][:, 1:2]), "sS")
        b.wait(V, "sS")
        b.op(V, lambda e: e.tensor_scalar(SQ2[:, :], SQ2[:, :], MAGIC,
                                          -32768.0, Alu.subtract,
                                          Alu.max), "sV")
        b.op(V, lambda e, ec=ec: e.tensor_scalar(
            XQT[:, ec * MYT:(ec + 1) * MYT], SQ2[:, :], 32767.0,
            sb["SXB"][:, 0:1], Alu.min, Alu.mult), "sV")
        b.wait(Sc, "sV")

    b.barrier()
    ACL, BBC = sb["ACL"], sb["BBC"]
    ld = b.dma(Sy, sb["SM16"][:, :], sp["sp_sums"][:, :])
    b.wait(V, "dS")
    b.op(V, lambda e: e.reciprocal(sb["C16"][:, :], sb["SM16"][:, :]), "sV")
    b.vsync(V)
    for ec in range(NEC):
        ld = b.dma(Sy, ACL[:, ec * MYT:(ec + 1) * MYT],
                   sp["sp_attn"][ec * 128:(ec + 1) * 128, :])
        b.wait(Sc, "dS")
        b.wait(Sc, "sT")
        b.op(Sc, lambda e, ec=ec: e.activation(
            out=SQ2[:, :], in_=ACL[:, ec * MYT:(ec + 1) * MYT],
            func=Act.Square), "sS")
        b.wait(T, "dS")
        b.wait(T, "sS")
        b.wait(T, "sV")
        for np_ in range(4):
            b.op(T, lambda e, ec=ec, np_=np_: e.matmul(
                PS[0:2, np_ * 512:(np_ + 1) * 512], sb["IND2"][:, :],
                ACL[:, ec * MYT + np_ * 512: ec * MYT + (np_ + 1) * 512],
                start=True, stop=True), "sT")
            b.op(T, lambda e, np_=np_: e.matmul(
                PS[0:2, 2048 + np_ * 512: 2048 + (np_ + 1) * 512],
                sb["IND2"][:, :], SQ2[:, np_ * 512:(np_ + 1) * 512],
                start=True, stop=True), "sT")
        b.wait(V, "sT")
        b.wait(V, "dG")
        b.op(V, lambda e, ec=ec: e.tensor_copy(
            sb["S2T"][:, :], PS[0:2, 0:2048]), "sV")
        b.wait(G, "sV")
        b.dma_g(sp["sp_st"][2 * ec:2 * ec + 2, :], sb["S2T"][:, :])
        b.wait(V, "dG")
        b.op(V, lambda e, ec=ec: e.tensor_copy(
            sb["S2T"][:, :], PS[0:2, 2048:4096]), "sV")
        b.wait(G, "sV")
        b.dma_g(sp["sp_st"][16 + 2 * ec:18 + 2 * ec, :], sb["S2T"][:, :])
    b.wait(Sy, "dG")
    ldA = b.dma(Sy, sb["S16A"][:, :], sp["sp_st"][0:16, :])
    ldB = b.dma(Sy, sb["S16B"][:, :], sp["sp_st"][16:32, :])
    b.wait(V, "dS")
    b.op(V, lambda e: e.tensor_tensor(sb["S16A"][:, :], sb["S16A"][:, :],
                                      sb["C16"][:, :], Alu.mult), "sV")
    b.op(V, lambda e: e.tensor_tensor(sb["A16"][:, :], sb["C16"][:, :],
                                      sb["C16"][:, :], Alu.mult), "sV")
    b.vsync(V)
    b.op(V, lambda e: e.tensor_tensor(sb["S16B"][:, :], sb["S16B"][:, :],
                                      sb["A16"][:, :], Alu.mult), "sV")
    b.wait(T, "sV")
    for np_ in range(4):
        b.op(T, lambda e, np_=np_: e.matmul(
            PS[0:1, np_ * 512:(np_ + 1) * 512], sb["ONES1"][0:16, :],
            sb["S16A"][:, np_ * 512:(np_ + 1) * 512],
            start=True, stop=True), "sT")
        b.op(T, lambda e, np_=np_: e.matmul(
            PS[0:1, 2048 + np_ * 512: 2048 + (np_ + 1) * 512],
            sb["ONES1"][0:16, :], sb["S16B"][:, np_ * 512:(np_ + 1) * 512],
            start=True, stop=True), "sT")
    b.wait(V, "sT")
    b.op(V, lambda e: e.tensor_scalar(ROW[0:1, 0:2048], PS[0:1, 0:2048],
                                      1.0 / E, None, Alu.mult), "sV")
    b.op(V, lambda e: e.tensor_scalar(ROW[0:1, 2048:4096], PS[0:1, 2048:4096],
                                      1.0 / E, None, Alu.mult), "sV")
    mu = ROW[0:1, 0:2048]
    ex2 = ROW[0:1, 2048:4096]
    musq = sb["S16B"][0:1, 0:2048]
    var = sb["SM16"][0:1, 0:2048]
    sd = sb["S16A"][0:1, 0:2048]
    rr = sb["A16"][0:1, 0:2048]
    bb = sb["S16B"][0:1, 0:2048]
    b.vsync(V)
    b.op(V, lambda e: e.tensor_tensor(musq, mu, mu, Alu.mult), "sV")
    b.vsync(V)
    b.op(V, lambda e: e.tensor_tensor(var, ex2, musq, Alu.subtract), "sV")
    b.wait(Sc, "sV")
    b.op(Sc, lambda e: e.activation(out=sd, in_=var, func=Act.Sqrt,
                                    bias=sb["CST"][0:1, 1:2], scale=1.0), "sS")
    b.wait(V, "sS")
    b.op(V, lambda e: e.reciprocal(rr, sd), "sV")
    b.vsync(V)
    b.op(V, lambda e: e.tensor_tensor(bb, mu, rr, Alu.mult), "sV")
    b.vsync(V)
    b.wait(G, "sV")
    b.dma_g(sp["scr_row"][2:3, 0:2048], bb)
    b.dma_g(sp["scr_row"][3:4, 0:2048], rr)
    b.wait(G, "dG")
    b.dma_g(BBC[:, :], sp["scr_row"][2:3, 0:2048].partition_broadcast(128))
    b.dma_g(sb["SM16"][:, :],
          sp["scr_row"][3:4, 0:2048].partition_broadcast(16))
    b.wait(V, "dG")
    b.op(V, lambda e: e.tensor_tensor(sb["A16"][:, :], sb["C16"][:, :],
                                      sb["SM16"][:, :], Alu.mult), "sV")
    b.vsync(V)
    for ec in range(NEC):
        b.wait(T, "sV")
        for np_ in range(4):
            b.op(T, lambda e, ec=ec, np_=np_: e.matmul(
                PS[:, np_ * 512:(np_ + 1) * 512],
                sb["IND16"][:, ec * 128:(ec + 1) * 128],
                sb["A16"][:, np_ * 512:(np_ + 1) * 512],
                start=True, stop=True), "sT")
        b.wait(V, "sT")
        b.wait(Sc, "sV")
        b.op(V, lambda e, ec=ec: e.tensor_tensor(
            SQ2[:, :], ACL[:, ec * MYT:(ec + 1) * MYT], PS[:, 0:2048],
            Alu.mult), "sV")
        b.op(V, lambda e: e.tensor_tensor(SQ2[:, :], SQ2[:, :], BBC[:, :],
                                          Alu.subtract), "sV")
        b.wait(V, "dG")
        b.op(V, lambda e, ec=ec: e.tensor_scalar(
            ACL[:, ec * MYT:(ec + 1) * MYT], SQ2[:, :],
            sb["GAM"][:, ec:ec + 1], sb["BET"][:, ec:ec + 1],
            Alu.mult, Alu.add), "sV")
        b.wait(G, "sV")
        b.dma_g(sp["sp_aln"][ec * 128:(ec + 1) * 128, :],
              ACL[:, ec * MYT:(ec + 1) * MYT])

    b.barrier()

    def prep_ffn_weight(src, dstT, nchunks, icols, dst_stride, wna, wsc):
        nb = icols // 32
        for c in range(nchunks):
            pb = (c % 2) * icols
            b.wait(Sy, "sT")
            b.wait(Sy, "sV")
            ld = b.dma(Sy, wna[:, pb:pb + icols], src[c * 128:(c + 1) * 128, :])
            b.wait(V, "dS")
            b.wait(V, "sT")
            wna3 = wna[:, pb:pb + icols].rearrange("p (nb k) -> p nb k", k=32)
            b.op(V, lambda e, wna3=wna3, nb=nb: e.tensor_reduce(
                sb["BMX"][:, 0:nb], wna3, AX, Alu.max,
                apply_absolute_value=True), "sV")
            b.vsync(V)
            b.op(V, lambda e, nb=nb: e.tensor_scalar(
                sb["BSC"][:, 0:nb], sb["BMX"][:, 0:nb], 1.0 / 7.0, 1e-12,
                Alu.mult, Alu.max), "sV")
            b.vsync(V)
            b.op(V, lambda e, nb=nb: e.reciprocal(
                sb["BRS"][:, 0:nb], sb["BSC"][:, 0:nb]), "sV")
            b.vsync(V)
            qsc = wsc[:, 0:icols]
            qsc3 = qsc.rearrange("p (nb k) -> p nb k", k=32)
            b.op(V, lambda e, wna3=wna3, qsc3=qsc3, nb=nb: e.tensor_tensor(
                qsc3, wna3,
                sb["BRS"][:, 0:nb].unsqueeze(2).broadcast_to([128, nb, 32]), Alu.mult), "sV")
            b.op(V, lambda e, qsc=qsc: e.tensor_scalar(
                qsc, qsc, MAGIC, MAGIC, Alu.add, Alu.subtract), "sV")
            b.op(V, lambda e, qsc=qsc: e.tensor_scalar(
                qsc, qsc, -8.0, 7.0, Alu.max, Alu.min), "sV")
            b.op(V, lambda e, qsc3=qsc3, wna3=wna3, nb=nb: e.tensor_tensor(
                wna3, qsc3,
                sb["BSC"][:, 0:nb].unsqueeze(2).broadcast_to([128, nb, 32]), Alu.mult), "sV")
            b.wait(T, "sV")
            for s8 in range(icols // 128):
                pp = PS[:, (s8 % 2) * 512:(s8 % 2) * 512 + 128]
                if s8 >= 2:
                    b.wait(T, "sV")
                b.op(T, lambda e, pp=pp, pb=pb, s8=s8, wna=wna: e.transpose(
                    pp, wna[:, pb + s8 * 128: pb + (s8 + 1) * 128],
                    sb["IDF"][:, :]), "sT")
                b.wait(V, "sT")
                b.op(V, lambda e, pp=pp, s8=s8, c=c: e.tensor_copy(
                    sb[dstT][:, s8 * dst_stride + c * 128:
                             s8 * dst_stride + (c + 1) * 128], pp), "sV")

    prep_ffn_weight(dr["wu"], "WUT", NIC, E, I, sb["WNA"], sb["WSC"])
    prep_ffn_weight(dr["wg"], "WGT", NIC, E, I, sb["WNA"], sb["WSC"])

    b.barrier()
    for wt, spd, amcol in (("WUT", sp["sp_up"], 1), ("WGT", sp["sp_gate"], 2)):
        for ic in range(NIC):
            base = (ic % 2) * 4 * 512
            b.wait(T, "sV")
            for ec in range(NEC):
                for nq in range(4):
                    b.op(T, lambda e, base=base, nq=nq, ec=ec, ic=ic, wt=wt:
                         e.matmul(
                             PS[:, base + nq * 512: base + (nq + 1) * 512],
                             sb[wt][:, ec * I + ic * 128:
                                    ec * I + (ic + 1) * 128],
                             sb["XQT"][:, ec * MYT + nq * 512:
                                       ec * MYT + (nq + 1) * 512],
                             start=(ec == 0), stop=(ec == NEC - 1)), "sT")
            fb = (ic % 2) * 2048
            b.wait(V, "sT")
            b.wait(V, "dF%d" % (ic % 2) if ("dF%d" % (ic % 2)) in b.sems else "sT")
            for nq in range(4):
                b.op(V, lambda e, base=base, nq=nq, fb=fb: e.tensor_copy(
                    sb["FEV"][:, fb + nq * 512: fb + (nq + 1) * 512],
                    PS[:, base + nq * 512: base + (nq + 1) * 512]), "sV")
            b.op(V, lambda e, fb=fb: e.tensor_reduce(
                ROW[0:128, 384:385], sb["FEV"][:, fb:fb + 2048], AX, Alu.max,
                apply_absolute_value=True), "sV")
            b.vsync(V)
            b.op(V, lambda e, amcol=amcol: e.tensor_tensor(
                sb["AMX"][:, amcol:amcol + 1], sb["AMX"][:, amcol:amcol + 1],
                ROW[0:128, 384:385], Alu.max), "sV")
            b.vsync(V)
            b.wait(Sc, "sV")
            b.dmas(Sc, "dF%d" % (ic % 2), spd[ic * 128:(ic + 1) * 128, :],
                   sb["FEV"][:, fb:fb + 2048])
    global_scale(1, "ug", 2, "SUGB", 32767.0)

    b.barrier()
    FG, FU, FT1 = sb["FG"], sb["FU"], sb["FT1"]
    FGQ, FSQ, FG2 = sb["FGQ"], sb["FSQ"], sb["FG2"]
    f2m = {}
    for ic in range(NIC):
        par = ic % 2
        fgb = FG[:, par * MYT:(par + 1) * MYT]
        fub = FU[:, par * MYT:(par + 1) * MYT]
        if (ic - 2) in f2m:
            b.wait(Sy, "sS", f2m[ic - 2][0])
            b.wait(Sy, "sS", f2m[ic - 2][1])
        ldg = b.dmas(Sy, "dLG%d" % par, fgb,
                     sp["sp_gate"][ic * 128:(ic + 1) * 128, :])
        ldu = b.dmas(Sy, "dLU%d" % par, fub,
                     sp["sp_up"][ic * 128:(ic + 1) * 128, :])
        b.wait(Sc, "dLG%d" % par, ldg)
        if "dUQ" in b.sems:
            b.wait(V, "dUQ")
        qdq_chunk(fgb, FGQ[:, :], sb["SUGB"][:, 2:3], sb["SUGB"][:, 3:4],
                  -32768.0, 32767.0, FT1[:, :])
        fg_done = b.sems["sS"][1]
        b.op(Sc, lambda e: e.activation(out=FSQ[:, :], in_=FGQ[:, :],
                                        func=Act.Sigmoid), "sS")
        b.op(Sc, lambda e: e.activation(out=FT1[:, :], in_=FSQ[:, :],
                                        func=Act.Identity,
                                        bias=sb["CST"][:, 0:1],
                                        scale=sb["CST"][:, 2:3]), "sS")
        b.wait(V, "sS")
        b.op(V, lambda e: e.tensor_scalar(FT1[:, :], FT1[:, :], MAGIC, 0.0,
                                          Alu.subtract, Alu.max), "sV")
        b.op(V, lambda e: e.tensor_scalar(FSQ[:, :], FT1[:, :], 65535.0,
                                          1.0 / 65536.0, Alu.min,
                                          Alu.mult), "sV")
        if "dG2" in b.sems:
            b.wait(V, "dG2")
        b.op(V, lambda e: e.tensor_tensor(FG2[:, :], FGQ[:, :], FSQ[:, :],
                                          Alu.mult), "sV")
        b.op(V, lambda e: e.tensor_reduce(ROW[0:128, 384:385], FG2[:, :], AX,
                                          Alu.max,
                                          apply_absolute_value=True), "sV")
        b.op(V, lambda e: e.tensor_tensor(sb["AMX"][:, 3:4], sb["AMX"][:, 3:4],
                                          ROW[0:128, 384:385], Alu.max), "sV")
        b.wait(Sc, "sV")
        b.dmas(Sc, "dG2", sp["sp_g2"][ic * 128:(ic + 1) * 128, :], FG2[:, :])
        b.wait(Sc, "dLU%d" % par, ldu)
        qdq_chunk(fub, FGQ[:, :], sb["SUGB"][:, 0:1], sb["SUGB"][:, 1:2],
                  -32768.0, 32767.0, FT1[:, :])
        fu_done = b.sems["sS"][1]
        b.wait(Sc, "sV")
        b.dmas(Sc, "dUQ", sp["sp_uq"][ic * 128:(ic + 1) * 128, :], FGQ[:, :])
        f2m[ic] = (fg_done, fu_done)
    global_scale(3, "g2", 1, "S2B", 32767.0)

    b.barrier()
    f3m = {}
    for ic in range(NIC):
        par = ic % 2
        fgb = FG[:, par * MYT:(par + 1) * MYT]
        fub = FU[:, par * MYT:(par + 1) * MYT]
        if (ic - 2) in f3m:
            b.wait(Sy, "sS", f3m[ic - 2][0])
            b.wait(Sy, "sV", f3m[ic - 2][1])
        ldg = b.dmas(Sy, "dLG%d" % par, fgb,
                     sp["sp_g2"][ic * 128:(ic + 1) * 128, :])
        ldu = b.dmas(Sy, "dLU%d" % par, fub,
                     sp["sp_uq"][ic * 128:(ic + 1) * 128, :])
        b.wait(Sc, "dLG%d" % par, ldg)
        qdq_chunk(fgb, FGQ[:, :], sb["S2B"][:, 0:1], sb["S2B"][:, 1:2],
                  -32768.0, 32767.0, FT1[:, :])
        fg_done = b.sems["sS"][1]
        b.wait(V, "dLU%d" % par, ldu)
        if "dO" in b.sems:
            b.wait(V, "dO")
        b.op(V, lambda e, fub=fub: e.tensor_tensor(
            FG2[:, :], FGQ[:, :], fub, Alu.mult), "sV")
        fu_done = b.sems["sV"][1]
        b.op(V, lambda e: e.tensor_reduce(ROW[0:128, 384:385], FG2[:, :], AX,
                                          Alu.max,
                                          apply_absolute_value=True), "sV")
        b.op(V, lambda e: e.tensor_tensor(sb["AMX"][:, 4:5], sb["AMX"][:, 4:5],
                                          ROW[0:128, 384:385], Alu.max), "sV")
        b.wait(Sc, "sV")
        b.dmas(Sc, "dO", sp["sp_o"][ic * 128:(ic + 1) * 128, :], FG2[:, :])
        f3m[ic] = (fg_done, fu_done)
    global_scale(4, "o", 1, "SOB", 32767.0)

    b.barrier()
    prep_ffn_weight(dr["wd"], "WDT", NEC, I, E, sb["W3A"], sb["W3S"])

    b.barrier()
    FG4, FT4, OQT = sb["FG4"], sb["FT4"], sb["OQT"]
    f4m = {}
    for ic in range(NIC):
        par = ic % 2
        fb4 = FG4[:, par * MYT:(par + 1) * MYT]
        if (ic - 2) in f4m:
            b.wait(Sy, "sS", f4m[ic - 2])
        ld = b.dmas(Sy, "dLG%d" % par, fb4,
                    sp["sp_o"][ic * 128:(ic + 1) * 128, :])
        b.wait(Sc, "dLG%d" % par, ld)
        b.wait(Sc, "sV")
        b.op(Sc, lambda e, fb4=fb4: e.activation(out=FT4[:, :], in_=fb4,
                                                 func=Act.Identity,
                                                 bias=sb["CST"][:, 0:1],
                                                 scale=sb["SOB"][:, 1:2]), "sS")
        f4m[ic] = b.sems["sS"][1]
        b.wait(V, "sS")
        b.op(V, lambda e: e.tensor_scalar(FT4[:, :], FT4[:, :], MAGIC,
                                          -32768.0, Alu.subtract,
                                          Alu.max), "sV")
        b.op(V, lambda e, ic=ic: e.tensor_scalar(
            OQT[:, ic * MYT:(ic + 1) * MYT], FT4[:, :], 32767.0,
            sb["SOB"][:, 0:1], Alu.min, Alu.mult), "sV")

    b.barrier()
    for hc in range(NEC):
        base = (hc % 2) * 4 * 512
        b.wait(Sy, "sV")
        ld = b.dmas(Sy, "dAL", sb["ALC"][:, :],
                    sp["sp_aln"][hc * 128:(hc + 1) * 128, :])
        b.wait(T, "sV")
        for ic in range(NIC):
            for nq in range(4):
                b.op(T, lambda e, base=base, nq=nq, ic=ic, hc=hc: e.matmul(
                    PS[:, base + nq * 512: base + (nq + 1) * 512],
                    sb["WDT"][:, ic * E + hc * 128: ic * E + (hc + 1) * 128],
                    OQT[:, ic * MYT + nq * 512: ic * MYT + (nq + 1) * 512],
                    start=(ic == 0), stop=(ic == NIC - 1)), "sT")
        b.wait(V, "sT")
        b.wait(V, "dAL", ld)
        if "dAF" in b.sems:
            b.wait(V, "dAF")
        for nq in range(4):
            b.op(V, lambda e, base=base, nq=nq: e.tensor_tensor(
                sb["AFE"][:, nq * 512:(nq + 1) * 512],
                PS[:, base + nq * 512: base + (nq + 1) * 512],
                sb["ALC"][:, nq * 512:(nq + 1) * 512], Alu.add), "sV")
        b.wait(Sc, "sV")
        b.dmas(Sc, "dAF", sp["sp_af"][hc * 128:(hc + 1) * 128, :],
               sb["AFE"][:, :])

    b.barrier()
    AFL, XT6, OSB = sb["AFL"], sb["XT6"], sb["OSB"]
    for ec in range(NEC):
        b.dmas(Sy, "dAFL", AFL[:, ec * MYT:(ec + 1) * MYT],
               sp["sp_af"][ec * 128:(ec + 1) * 128, :])
    afl_ld = b.sems["dAFL"][1]
    f6m = {}
    for tt in range(16):
        pb = (tt % 2) * E
        base = (tt % 2) * 1024
        if (tt - 2) in f6m:
            b.wait(Sy, "sV", f6m[tt - 2])
        ld = b.dmas(Sy, "dX6%d" % (tt % 2), XT6[:, pb:pb + E],
                    dr["x_tok"][tt * 128:(tt + 1) * 128, :])
        b.wait(T, "dAFL", afl_ld)
        if (tt - 2) in f6m:
            b.wait(T, "sV", f6m[tt - 2])
        for hc in range(NEC):
            b.op(T, lambda e, base=base, hc=hc, tt=tt: e.transpose(
                PS[:, base + hc * 128: base + (hc + 1) * 128],
                AFL[:, hc * MYT + tt * 128: hc * MYT + (tt + 1) * 128],
                sb["IDF"][:, :]), "sT")
        b.wait(V, "sT")
        b.wait(V, "dX6%d" % (tt % 2), ld)
        if ("dOU%d" % (tt % 2)) in b.sems:
            b.wait(V, "dOU%d" % (tt % 2))
        f6m[tt] = b.op(V, lambda e, base=base, pb=pb: e.tensor_tensor(
            OSB[:, pb:pb + E], PS[:, base:base + E], XT6[:, pb:pb + E],
            Alu.add), "sV")
        b.wait(Sc, "sV", f6m[tt])
        b.dmas(Sc, "dOU%d" % (tt % 2), out_d[tt * 128:(tt + 1) * 128, :],
               OSB[:, pb:pb + E])
    b.wait(G, "dOU0")
    b.wait(G, "dOU1")
    b.wait(G, "dG")


_NC_CACHE = {}


def kernel(L0_attn_x, L0_attn_Wq, L0_attn_Wk, L0_attn_Wv, L0_attn_ln_gamma,
           L0_attn_ln_beta, L0_ffn_w_gate, L0_ffn_w_up, L0_ffn_w_down):
    x = np.ascontiguousarray(np.asarray(L0_attn_x, np.float32))
    wq = np.asarray(L0_attn_Wq, np.float32)
    wk = np.asarray(L0_attn_Wk, np.float32)
    wv = np.asarray(L0_attn_Wv, np.float32)
    gam = np.asarray(L0_attn_ln_gamma, np.float32)
    bet = np.asarray(L0_attn_ln_beta, np.float32)
    wg = np.ascontiguousarray(np.asarray(L0_ffn_w_gate, np.float32))
    wu = np.ascontiguousarray(np.asarray(L0_ffn_w_up, np.float32))
    wd = np.ascontiguousarray(np.asarray(L0_ffn_w_down, np.float32))

    wqT = np.ascontiguousarray(wq.T)
    wkT = np.ascontiguousarray(wk.T)
    wvT = np.ascontiguousarray(wv.T)
    gamma8 = np.ascontiguousarray(gam.reshape(NEC, 128).T)
    beta8 = np.ascontiguousarray(bet.reshape(NEC, 128).T)

    if "nc" not in _NC_CACHE:
        _NC_CACHE["nc"] = build_nc()
    nc = _NC_CACHE["nc"]

    in_maps = []
    for c in range(N_CORES):
        bq, h = divmod(c, 2)
        qoff = h * MYT
        xloc = np.roll(x[bq], -qoff, axis=0)
        if h == 0:
            meta = np.array([[-1.0 / 4096.0, 3968.0 / 4096.0]], np.float32)
        else:
            meta = np.array([[1.0 / 4096.0, 1.0 - 3968.0 / 4096.0]], np.float32)
        in_maps.append({
            "xseqT": np.ascontiguousarray(xloc.T),
            "x_tok": np.ascontiguousarray(xloc),
            "wqT": wqT, "wkT": wkT, "wvT": wvT,
            "wu": wu, "wg": wg, "wd": wd,
            "gamma8": gamma8, "beta8": beta8,
            "meta": meta,
        })

    res = run_bass_kernel_spmd(nc, in_maps, core_ids=list(range(N_CORES)))
    out = np.empty((4, S, E), np.float32)
    for c in range(N_CORES):
        bq, h = divmod(c, 2)
        out[bq, h * MYT:(h + 1) * MYT, :] = res.results[c]["out"]
    return out


# version 11
# speedup vs baseline: 1.5922x; 1.0704x over previous
import numpy as np
from contextlib import ExitStack

import concourse.bass as bass
import concourse.mybir as mybir
from concourse.bass_utils import run_bass_kernel_spmd

F32 = mybir.dt.float32
BF16 = mybir.dt.bfloat16
Alu = mybir.AluOpType
Act = mybir.ActivationFunctionType
AX = mybir.AxisListType.X

MAGIC = 1.5 * 2.0**23
N_CORES = 8
S = 4096
E = 1024
H = 16
I = 3072
MYT = 2048
NEC = E // 128
NIC = I // 128

ENGS = ("sync", "gpsimd", "vector", "scalar", "tensor")
V, Sc, T, G, Sy = "vector", "scalar", "tensor", "gpsimd", "sync"


class Builder:

    def __init__(self, nc):
        self.nc = nc
        self.ops = {e: [] for e in ENGS}
        self.sems = {}

    def mksem(self, name):
        h = self.nc.semaphore(name).__enter__()
        self.sems[name] = [h, 0]

    AUTO_CHAIN = {"vector": "sV", "scalar": "sS", "gpsimd": "sG"}

    def op(self, eng, fn, inc=None, k=1):
        ch = self.AUTO_CHAIN.get(eng)
        if ch is not None:
            self.wait(eng, ch)
        if inc is None:
            self.ops[eng].append(fn)
            return None
        rec = self.sems[inc]
        rec[1] += k
        h = rec[0]
        self.ops[eng].append(lambda e, fn=fn, h=h, k=k: fn(e).then_inc(h, k))
        return rec[1]

    def wait(self, eng, sem, val=None):
        h, c = self.sems[sem]
        v = c if val is None else val
        if v > 0:
            self.ops[eng].append(lambda e, h=h, v=v: e.wait_ge(h, v))

    def dma_g(self, out, in_, **kw):
        return self.dma("gpsimd", out, in_, serial=False, **kw)

    def dmas(self, eng, sem, out, in_, **kw):
        if sem not in self.sems:
            self.mksem(sem)
        return self.op(eng, lambda e: e.dma_start(out=out, in_=in_, **kw),
                       sem, 16)

    def dma(self, eng, out, in_, serial=True, **kw):
        q = {"sync": "dS", "gpsimd": "dG", "scalar": "dA"}[eng]
        n = self.op(eng, lambda e: e.dma_start(out=out, in_=in_, **kw), q, 16)
        if serial:
            self.wait(eng, q, n)
        return n

    def vsync(self, eng):
        sem = {"vector": "sV", "scalar": "sS", "tensor": "sT",
               "gpsimd": "sG"}[eng]
        self.wait(eng, sem)

    def vfiller(self, sb):
        self.op("vector", lambda e: e.tensor_copy(sb["ROW"][0:1, 4095:4096],
                                                  sb["CST"][0:1, 0:1]), "sV")

    def barrier(self, engs=ENGS):
        snap = [(h, c) for (h, c) in self.sems.values() if c > 0]
        for eng in engs:
            for h, v in snap:
                self.ops[eng].append(lambda e, h=h, v=v: e.wait_ge(h, v))


def build_nc():
    nc = bass.Bass("TRN2", target_bir_lowering=False)

    dram = {}
    for nm, shp in [
        ("xseqT", [E, S]), ("x_tok", [S, E]),
        ("wqT", [E, E]), ("wkT", [E, E]), ("wvT", [E, E]),
        ("wu", [I, E]), ("wg", [I, E]), ("wd", [E, I]),
        ("gamma8", [128, NEC]), ("beta8", [128, NEC]),
        ("meta", [1, 2]),
    ]:
        dram[nm] = nc.dram_tensor(nm, shp, F32, kind="ExternalInput").ap()
    out_d = nc.dram_tensor("out", [MYT, E], F32, kind="ExternalOutput").ap()

    sp = {}
    for nm, shp in [
        ("sp_v", [S, H * 65]), ("sp_attn", [E, MYT]), ("sp_sums", [H, MYT]),
        ("sp_aln", [E, MYT]), ("sp_up", [I, MYT]), ("sp_gate", [I, MYT]),
        ("sp_g2", [I, MYT]), ("sp_uq", [I, MYT]), ("sp_o", [I, MYT]),
        ("sp_af", [E, MYT]), ("scr_row", [4, S]), ("sp_st", [32, MYT]),
    ]:
        sp[nm] = nc.dram_tensor(
            nm, shp, BF16 if nm == "sp_v" else F32, kind="Internal").ap()
    sp["cc_in"] = {k: nc.dram_tensor(f"cc{k}_i", [1, w], F32, kind="Internal").ap()
                   for k, w in (("x", 1), ("ug", 2), ("g2", 1), ("o", 1))}
    sp["cc_out"] = {k: nc.dram_tensor(f"cc{k}_o", [1, w], F32, kind="Internal").ap()
                    for k, w in (("x", 1), ("ug", 2), ("g2", 1), ("o", 1))}

    st = ExitStack()
    sb = {}

    def perm(nm, shp, dt):
        sb[nm] = st.enter_context(nc.sbuf_tensor(nm, shp, dt))
        return sb[nm]

    perm("CST", [128, 8], F32)
    perm("SXB", [128, 2], F32)
    perm("SUGB", [128, 4], F32)
    perm("S2B", [128, 2], F32)
    perm("SOB", [128, 2], F32)
    perm("METB", [128, 2], F32)
    perm("AMX", [128, 8], F32)
    perm("IDF", [128, 128], F32)
    perm("IND2", [128, 2], F32)
    perm("IND16", [16, 1024], F32)
    perm("ONES1", [128, 1], F32)
    perm("PIOTA", [128, 1], F32)
    perm("GAM", [128, NEC], F32)
    perm("BET", [128, NEC], F32)
    perm("ROW", [128, 4096], F32)
    perm("BMX", [128, 96], F32)
    perm("BSC", [128, 96], F32)
    perm("BRS", [128, 96], F32)

    TOP = nc.SBUF_PARTITION_SIZE_BYTES
    ZB = 24576

    def at(nm, off, shp, dt):
        end = off + int(np.prod(shp[1:])) * mybir.dt.size(dt)
        assert ZB <= off and end <= TOP, (nm, off, end)
        sb[nm] = nc.alloc_sbuf_tensor_at(nm, shp, dt, offset=off)
        return sb[nm]

    at("WVTS", 40960, [128, NEC * E], BF16)
    at("WKTS", 57344, [128, NEC * E], BF16)
    at("WQTS", 73728, [128, NEC * E], BF16)
    at("XNTA", 90112, [128, NEC * MYT], BF16)
    at("XNTB", 122880, [128, NEC * MYT], BF16)
    at("XC2", 155648, [128, S], F32)
    at("XC2B", 204800, [128, S], F32)
    at("MUBC", 172032, [128, S], F32)
    at("RSBC", 188416, [128, S], F32)
    at("XT2", 204800, [128, 2 * E], F32)
    at("SQT", 212992, [128, E], F32)
    at("STT", 217088, [128, 128], F32)
    at("VEV", 217600, [128, H * 65], BF16)
    at("KF", 155648, [128, NEC * S], BF16)
    at("QF", 40960, [128, NEC * MYT], BF16)
    at("DNE", 122880, [128, 4096], F32)
    at("DFA", 139264, [128, 4096], F32)
    at("SW", 73728, [128, 2 * 1024], F32)
    at("SQ2E", 196608, [128, MYT], F32)
    at("PT", 81920, [128, 2 * 1024], BF16)
    at("AVE", 86016, [128, 1024], F32)
    at("VSB", 221184, [128, 32 * 65], BF16)
    at("XQT", 155648, [128, NEC * MYT], BF16)
    at("SQ2", 188416, [128, MYT], F32)
    at("ACL", 40960, [128, NEC * MYT], F32)
    at("BBC", 106496, [128, MYT], F32)
    at("C16", 114688, [16, MYT], F32)
    at("A16", 122880, [16, MYT], F32)
    at("SM16", 131072, [16, MYT], F32)
    at("S16A", 139264, [16, MYT], F32)
    at("S16B", 147456, [16, MYT], F32)
    at("S2T", 196608, [2, MYT], F32)
    at("WUT", 40960, [128, NEC * I], BF16)
    at("WGT", 90112, [128, NEC * I], BF16)
    at("FEV", 139264, [128, 8 * 512], F32)
    at("WNA", 196608, [128, 2 * E], F32)
    at("WSC", 204800, [128, 2 * E], F32)
    at("FG", 40960, [128, 2 * MYT], F32)
    at("FU", 57344, [128, 2 * MYT], F32)
    at("FT1", 73728, [128, 2 * MYT], F32)
    at("FGQ", 90112, [128, 2 * MYT], F32)
    at("FSQ", 106496, [128, 2 * MYT], F32)
    at("FG2", 122880, [128, 2 * MYT], F32)
    at("OQT", 40960, [128, NIC * MYT], BF16)
    at("WDT", 139264, [128, NIC * E], BF16)
    at("W3A", 188416, [128, 2 * I], F32)
    at("W3S", 212992, [128, I], F32)
    at("FG4", 188416, [128, 2 * MYT], F32)
    at("FT4", 204800, [128, MYT], F32)
    at("ALC", 188416, [128, MYT], F32)
    at("AFE", 196608, [128, MYT], F32)
    at("AFL", 40960, [128, NEC * MYT], F32)
    at("XT6", 106496, [128, 2 * E], F32)
    at("OSB", 114688, [128, 2 * E], F32)

    PS = nc.alloc_psum_tensor("PS", [128, 4096], F32)

    b = Builder(nc)
    for s_ in ("dS", "dG", "dA", "sV", "sS", "sT", "sG", "sCC"):
        b.mksem(s_)

    emit_program(nc, b, sb, PS, dram, out_d, sp)

    with nc.Block() as block:
        @block.sync
        def _(e):
            for f in b.ops["sync"]:
                f(e)

        @block.gpsimd
        def _(e):
            for f in b.ops["gpsimd"]:
                f(e)

        @block.vector
        def _(e):
            for f in b.ops["vector"]:
                f(e)

        @block.scalar
        def _(e):
            for f in b.ops["scalar"]:
                f(e)

        @block.tensor
        def _(e):
            for f in b.ops["tensor"]:
                f(e)

    st.close()
    return nc


def emit_program(nc, b, sb, PS, dr, out_d, sp):
    ROW = sb["ROW"]
    XNT = {0: sb["XNTA"], 1: sb["XNTB"]}

    def xnt_cols(ec, c0, c1):
        assert (c0 // MYT) == ((c1 - 1) // MYT)
        half = c0 // MYT
        t = XNT[half]
        return t[:, ec * MYT + c0 % MYT: ec * MYT + c0 % MYT + (c1 - c0)]

    b.op(V, lambda e: e.memset(sb["CST"][:, 0:1], MAGIC), "sV")
    b.op(V, lambda e: e.memset(sb["CST"][:, 1:2], 1e-5), "sV")
    b.op(V, lambda e: e.memset(sb["CST"][:, 2:3], 65536.0), "sV")
    b.op(V, lambda e: e.memset(sb["CST"][:, 3:4], -1920.0 / 4096.0), "sV")
    b.op(V, lambda e: e.memset(sb["CST"][:, 4:5], 1.0 / 4096.0), "sV")
    b.op(V, lambda e: e.memset(sb["ONES1"][:, :], 1.0), "sV")
    b.op(V, lambda e: e.memset(sb["AMX"][:, :], 0.0), "sV")
    b.op(G, lambda e: e.iota(sb["IDF"][:, :], pattern=[[1, 128]], base=0,
                             channel_multiplier=-1,
                             allow_small_or_imprecise_dtypes=True), "sG")
    b.op(G, lambda e: e.iota(sb["PIOTA"][:, :], pattern=[[0, 1]], base=0,
                             channel_multiplier=1,
                             allow_small_or_imprecise_dtypes=True), "sG")
    b.dma_g(sb["METB"][:, :], dr["meta"][0:1, :].partition_broadcast(128))
    b.wait(V, "sG")
    b.op(V, lambda e: e.tensor_scalar(sb["IND2"][:, 0:1], sb["PIOTA"][:, :],
                                      64.0, None, Alu.is_lt), "sV")
    b.op(V, lambda e: e.tensor_scalar(sb["IND2"][:, 1:2], sb["PIOTA"][:, :],
                                      64.0, None, Alu.is_ge), "sV")
    b.op(V, lambda e: e.tensor_scalar(sb["IDF"][:, :], sb["IDF"][:, :],
                                      0.0, None, Alu.is_equal), "sV")
    b.op(G, lambda e: e.iota(sb["IND16"][:, :],
                             pattern=[[2, 8], [1, 2], [0, 64]], base=0,
                             channel_multiplier=0,
                             allow_small_or_imprecise_dtypes=True), "sG")
    b.wait(V, "sG")
    b.op(V, lambda e: e.tensor_tensor(
        sb["IND16"][:, :], sb["IND16"][:, :],
        sb["PIOTA"][0:16, 0:1].broadcast_to([16, 1024]), Alu.is_equal), "sV")
    b.dma(Sy, sb["GAM"][:, :], dr["gamma8"][:, :])
    b.dma(Sy, sb["BET"][:, :], dr["beta8"][:, :])
    for wi, (nm, dst) in enumerate((("wqT", "WQTS"), ("wkT", "WKTS"),
                                    ("wvT", "WVTS"))):
        for ec in range(NEC):
            b.dmas(G, "dW%d" % wi, sb[dst][:, ec * E:(ec + 1) * E],
                   dr[nm][ec * 128:(ec + 1) * 128, :])

    def global_scale(col, key, packw, dst, qmaxv):
        b.wait(T, "sV")
        b.wait(T, "sS")
        b.op(T, lambda e: e.transpose(PS[0:packw, 1536:1664],
                                      sb["AMX"][:, col:col + packw],
                                      sb["IDF"][:, :]), "sT")
        b.wait(V, "sT")
        b.op(V, lambda e: e.tensor_reduce(ROW[0:packw, 256:257],
                                          PS[0:packw, 1536:1664], AX,
                                          Alu.max), "sV")
        b.wait(G, "sV")
        b.dma_g(sp["cc_in"][key][0:1, :].rearrange("a (p c) -> (a p) c",
                                                    p=packw),
              ROW[0:packw, 256:257])
        b.wait(G, "dG")
        b.op(G, lambda e: e.collective_compute(
            "AllReduce", Alu.max, replica_groups=[list(range(N_CORES))],
            ins=[sp["cc_in"][key].opt()], outs=[sp["cc_out"][key].opt()],
        ), "sCC")
        b.wait(G, "sCC")
        b.dma_g(ROW[0:1, 260:260 + packw], sp["cc_out"][key][0:1, :])
        b.wait(V, "dG")
        b.op(V, lambda e: e.tensor_scalar(
            ROW[0:1, 280:280 + packw], ROW[0:1, 260:260 + packw],
            1.0 / qmaxv, 1e-12, Alu.mult, Alu.max), "sV")
        b.vsync(V)
        b.op(V, lambda e: e.reciprocal(
            ROW[0:1, 288:288 + packw], ROW[0:1, 280:280 + packw]), "sV")
        b.vsync(V)
        for i in range(packw):
            b.op(V, lambda e, i=i: e.tensor_copy(
                ROW[0:1, 300 + 2 * i:301 + 2 * i],
                ROW[0:1, 280 + i:281 + i]), "sV")
            b.op(V, lambda e, i=i: e.tensor_copy(
                ROW[0:1, 301 + 2 * i:302 + 2 * i],
                ROW[0:1, 288 + i:289 + i]), "sV")
        b.wait(G, "sV")
        b.dma_g(sp["scr_row"][2:3, 0:2 * packw], ROW[0:1, 300:300 + 2 * packw])
        b.wait(G, "dG")
        b.dma_g(sb[dst][:, 0:2 * packw],
              sp["scr_row"][2:3, 0:2 * packw].partition_broadcast(128))

    def qdq_chunk(src_ap, dst_ap, s_ap, rs_ap, lo, hi, tmp_ap, war=-1):
        if war < 0:
            b.wait(Sc, "sV")
        elif war > 0:
            b.wait(Sc, "sV", war)
        b.op(Sc, lambda e: e.activation(out=tmp_ap, in_=src_ap,
                                        func=Act.Identity,
                                        bias=sb["CST"][:, 0:1],
                                        scale=rs_ap), "sS")
        b.wait(V, "sS")
        b.op(V, lambda e: e.tensor_scalar(tmp_ap, tmp_ap, MAGIC, lo,
                                          Alu.subtract, Alu.max), "sV")
        b.op(V, lambda e: e.tensor_scalar(dst_ap, tmp_ap, hi, s_ap,
                                          Alu.min, Alu.mult), "sV")
        b.wait(Sc, "sV")

    XT2, SQT, STT, XC2 = sb["XT2"], sb["SQT"], sb["STT"], sb["XC2"]
    for tt in range(32):
        pb = (tt % 2) * E
        if tt >= 2:
            b.wait(Sy, "sV")
            b.wait(Sy, "sS")
        ld = b.dmas(Sy, "dXT%d" % (tt % 2), XT2[:, pb:pb + E],
                    dr["x_tok"][tt * 128:(tt + 1) * 128, :])
        b.wait(Sc, "dXT%d" % (tt % 2), ld)
        b.wait(Sc, "sV")
        b.op(Sc, lambda e, pb=pb: e.activation(
            out=SQT[:, :], in_=XT2[:, pb:pb + E], func=Act.Square), "sS")
        b.wait(V, "dXT%d" % (tt % 2), ld)
        b.op(V, lambda e, pb=pb, tt=tt: e.tensor_reduce(
            STT[:, tt:tt + 1], XT2[:, pb:pb + E], AX, Alu.add), "sV")
        b.wait(V, "sS")
        b.op(V, lambda e, tt=tt: e.tensor_reduce(
            STT[:, 32 + tt:33 + tt], SQT[:, :], AX, Alu.add), "sV")
    b.vsync(V)
    b.op(V, lambda e: e.tensor_scalar(STT[:, 0:32], STT[:, 0:32],
                                      1.0 / E, None, Alu.mult), "sV")
    b.op(V, lambda e: e.tensor_scalar(STT[:, 32:64], STT[:, 32:64],
                                      1.0 / E, None, Alu.mult), "sV")
    b.vsync(V)
    b.op(V, lambda e: e.tensor_tensor(STT[:, 64:96], STT[:, 0:32],
                                      STT[:, 0:32], Alu.mult), "sV")
    b.vsync(V)
    b.op(V, lambda e: e.tensor_tensor(STT[:, 64:96], STT[:, 32:64],
                                      STT[:, 64:96], Alu.subtract), "sV")
    b.wait(Sc, "sV")
    b.op(Sc, lambda e: e.activation(out=STT[:, 96:128], in_=STT[:, 64:96],
                                    func=Act.Sqrt, bias=sb["CST"][:, 1:2],
                                    scale=1.0), "sS")
    b.wait(V, "sS")
    b.op(V, lambda e: e.reciprocal(STT[:, 96:128], STT[:, 96:128]), "sV")
    b.vsync(V)
    b.wait(T, "sV")
    b.op(T, lambda e: e.transpose(PS[0:32, 0:128], STT[:, 0:32],
                                  sb["IDF"][:, :]), "sT")
    b.op(T, lambda e: e.transpose(PS[0:32, 512:640], STT[:, 96:128],
                                  sb["IDF"][:, :]), "sT")
    b.wait(V, "sT")
    b.op(V, lambda e: e.tensor_copy(ROW[0:32, 0:128], PS[0:32, 0:128]), "sV")
    b.op(V, lambda e: e.tensor_copy(ROW[0:32, 128:256], PS[0:32, 512:640]), "sV")
    b.wait(G, "sV")
    b.dma_g(sp["scr_row"][0:1, :].rearrange("a (p c) -> (a p) c", p=32),
          ROW[0:32, 0:128])
    b.dma_g(sp["scr_row"][1:2, :].rearrange("a (p c) -> (a p) c", p=32),
          ROW[0:32, 128:256])
    b.wait(G, "dG")
    b.dma_g(sb["MUBC"][:, :], sp["scr_row"][0:1, :].partition_broadcast(128))
    b.dma_g(sb["RSBC"][:, :], sp["scr_row"][1:2, :].partition_broadcast(128))
    b.barrier([Sy])
    for ec in range(NEC):
        xcb = sb["XC2"] if ec % 2 == 0 else sb["XC2B"]
        pb = 0
        if ec >= 2:
            b.wait(Sy, "sV")
        ld = b.dmas(Sy, "dXC%d" % (ec % 2), xcb[:, :],
                    dr["xseqT"][ec * 128:(ec + 1) * 128, :])
        b.wait(V, "dXC%d" % (ec % 2), ld)
        b.wait(V, "dG")
        b.op(V, lambda e, xcb=xcb: e.tensor_tensor(
            xcb[:, :], xcb[:, :], sb["MUBC"][:, :], Alu.subtract), "sV")
        for half in range(2):
            b.op(V, lambda e, xcb=xcb, ec=ec, half=half: e.tensor_tensor(
                XNT[half][:, ec * MYT:(ec + 1) * MYT],
                xcb[:, half * MYT:(half + 1) * MYT],
                sb["RSBC"][:, half * MYT:(half + 1) * MYT], Alu.mult), "sV")
            b.op(V, lambda e, ec=ec, half=half: e.tensor_reduce(
                ROW[0:128, 386:387], XNT[half][:, ec * MYT:(ec + 1) * MYT],
                AX, Alu.max, apply_absolute_value=True), "sV")
            b.op(V, lambda e: e.tensor_tensor(
                sb["AMX"][:, 0:1], sb["AMX"][:, 0:1], ROW[0:128, 386:387],
                Alu.max), "sV")
    global_scale(0, "x", 1, "SXB", 32767.0)

    b.barrier([T])
    VEV = sb["VEV"]
    vem = {}
    for tt in range(32):
        for half in range(2):
            slot = tt * 2 + half
            pt = PS[:, 2048 + (slot % 4) * 512: 2048 + (slot % 4) * 512 + 512]
            if (slot - 4) in vem:
                b.wait(T, "sV", vem[slot - 4])
            for ec in range(NEC):
                b.op(T, lambda e, tt=tt, ec=ec, pt=pt, half=half: e.matmul(
                    pt, xnt_cols(ec, tt * 128, (tt + 1) * 128),
                    sb["WVTS"][:, ec * E + half * 512:
                               ec * E + (half + 1) * 512],
                    start=(ec == 0), stop=(ec == NEC - 1)), "sT")
            b.wait(V, "sT")
            if (slot - 2) in vem:
                b.wait(V, "dSv" if "dSv" in b.sems else "sT")
            vem[slot] = b.op(V, lambda e, pt=pt, half=half: e.tensor_copy(
                VEV[:, half * 8 * 65:(half + 1) * 8 * 65]
                .rearrange("p (h c) -> p h c", c=65)[:, :, 0:64],
                pt.rearrange("p (h c) -> p h c", c=64)), "sV")
        b.op(V, lambda e: e.memset(
            VEV[:, :].rearrange("p (h c) -> p h c", c=65)[:, :, 64:65],
            1.0), "sV")
        mm = b.sems["sV"][1]
        b.wait(Sc, "sV", mm)
        b.dmas(Sc, "dSv", sp["sp_v"][tt * 128:(tt + 1) * 128, :], VEV[:, :])

    b.barrier([T])
    KF, QF = sb["KF"], sb["QF"]
    kqm = {}
    slot = 0
    for pair in range(NEC):
        for nk in range(8):
            pk = PS[:, 3072 + (slot % 2) * 512: 3072 + (slot % 2) * 512 + 512]
            if (slot - 2) in kqm:
                b.wait(T, "sV", kqm[slot - 2])
            for ec in range(NEC):
                b.op(T, lambda e, pk=pk, ec=ec, pair=pair, nk=nk: e.matmul(
                    pk, sb["WKTS"][:, ec * E + pair * 128:
                                   ec * E + (pair + 1) * 128],
                    xnt_cols(ec, nk * 512, (nk + 1) * 512),
                    start=(ec == 0), stop=(ec == NEC - 1)), "sT")
            b.wait(V, "sT")
            kqm[slot] = b.op(V, lambda e, pk=pk, pair=pair, nk=nk:
                             e.tensor_copy(
                KF[:, pair * S + nk * 512: pair * S + (nk + 1) * 512],
                pk), "sV")
            slot += 1
    for pair in range(NEC):
        for nq in range(4):
            pk = PS[:, 3072 + (slot % 2) * 512: 3072 + (slot % 2) * 512 + 512]
            if (slot - 2) in kqm:
                b.wait(T, "sV", kqm[slot - 2])
            for ec in range(NEC):
                b.op(T, lambda e, pk=pk, ec=ec, pair=pair, nq=nq: e.matmul(
                    pk, sb["WQTS"][:, ec * E + pair * 128:
                                   ec * E + (pair + 1) * 128],
                    xnt_cols(ec, nq * 512, (nq + 1) * 512),
                    start=(ec == 0), stop=(ec == NEC - 1)), "sT")
            b.wait(V, "sT")
            kqm[slot] = b.op(V, lambda e, pk=pk, pair=pair, nq=nq:
                             e.tensor_scalar(
                QF[:, pair * MYT + nq * 512: pair * MYT + (nq + 1) * 512],
                pk, 1.0 / 32.0, None, Alu.mult), "sV")
            slot += 1

    b.barrier([G])
    b.op(G, lambda e: e.iota(sb["DNE"][:, :], pattern=[[1, 4096]], base=0,
                             channel_multiplier=-1,
                             allow_small_or_imprecise_dtypes=True), "sG")
    b.op(G, lambda e: e.iota(sb["DFA"][:, :], pattern=[[1, 4096]], base=0,
                             channel_multiplier=-1,
                             allow_small_or_imprecise_dtypes=True), "sG")
    b.wait(V, "sG")
    b.wait(V, "dG")
    b.op(V, lambda e: e.tensor_scalar(sb["DFA"][:, :], sb["DFA"][:, :],
                                      sb["METB"][:, 0:1], sb["METB"][:, 1:2],
                                      Alu.mult, Alu.add), "sV")
    b.wait(Sc, "sG")
    b.wait(Sc, "sV")
    b.op(Sc, lambda e: e.activation(out=sb["DNE"][:, :], in_=sb["DNE"][:, :],
                                    func=Act.Abs, bias=sb["CST"][:, 3:4],
                                    scale=sb["CST"][:, 4:5]), "sS")

    b.barrier([T, Sc, G, Sy])
    SW, PT, AVE, VSB = sb["SW"], sb["PT"], sb["AVE"], sb["VSB"]
    DNE, DFA = sb["DNE"], sb["DFA"]
    RCOL = (0, 1024, 3072)
    sc_mark, tt_mark, exp_mark, av_mark, evac_mark = {}, {}, {}, {}, {}
    spill_mark = {}

    def prevkey(it, k, back, n=32):
        kk = k - back
        return (it, kk) if kk >= 0 else (it - 1, kk + n)

    it = -1
    for head in range(H):
        pair, hh = divmod(head, 2)
        hb = hh * 64
        for qh in range(2):
            it += 1
            vb = 0
            if (it - 1, 31) in av_mark:
                b.wait(Sy, "sT", av_mark[(it - 1, 31)])
            vld = b.dmas(Sy, "dVS", VSB[:, vb:vb + 2080]
                         .rearrange("p (k c) -> p k c", c=65),
                         sp["sp_v"][:, head * 65:(head + 1) * 65]
                         .rearrange("(k p) c -> p k c", p=128))
            for step in range(34):
                k_sc, k_tt, k_av = step, step - 1, step - 2
                if k_sc < 32:
                    kc = k_sc
                    pscr = PS[:, RCOL[kc % 3]:RCOL[kc % 3] + 1024]
                    key = prevkey(it, kc, 3)
                    if key in tt_mark:
                        b.wait(T, "sV", tt_mark[key])
                    for jj in range(2):
                        b.op(T, lambda e, pscr=pscr, jj=jj, hb=hb, kc=kc,
                             qh=qh, pair=pair: e.matmul(
                                 pscr[:, jj * 512:(jj + 1) * 512],
                                 KF[hb:hb + 64, pair * S + kc * 128:
                                    pair * S + (kc + 1) * 128],
                                 QF[hb:hb + 64,
                                    pair * MYT + qh * 1024 + jj * 512:
                                    pair * MYT + qh * 1024 + (jj + 1) * 512],
                                 start=True, stop=True), "sT")
                    sc_mark[(it, kc)] = b.sems["sT"][1]
                if 0 <= k_tt < 32:
                    kc = k_tt
                    pscr = PS[:, RCOL[kc % 3]:RCOL[kc % 3] + 1024]
                    swb = SW[:, (kc % 2) * 1024:(kc % 2) * 1024 + 1024]
                    ptb = PT[:, (kc % 2) * 1024:(kc % 2) * 1024 + 1024]
                    if kc < 16:
                        doff = qh * 1024 - kc * 128 + 1920
                        dsl = DNE[:, doff:doff + 1024]
                    else:
                        doff = qh * 1024 - (kc - 16) * 128 + 1920
                        dsl = DFA[:, doff:doff + 1024]
                    b.wait(V, "sT", sc_mark[(it, kc)])
                    key = prevkey(it, kc, 2)
                    if key in exp_mark:
                        b.wait(V, "sS", exp_mark[key])
                    if kc == 0 and "dA1" in b.sems:
                        pass
                    tt_mark[(it, kc)] = b.op(
                        V, lambda e, pscr=pscr, dsl=dsl, swb=swb:
                        e.tensor_tensor(swb, pscr, dsl, Alu.mult), "sV")
                    b.wait(Sc, "sV", tt_mark[(it, kc)])
                    if key in av_mark:
                        b.wait(Sc, "sT", av_mark[key])
                    exp_mark[(it, kc)] = b.op(
                        Sc, lambda e, swb=swb, ptb=ptb: e.activation(
                            out=ptb, in_=swb, func=Act.Exp), "sS")
                if 0 <= k_av < 32:
                    kc = k_av
                    ptb = PT[:, (kc % 2) * 1024:(kc % 2) * 1024 + 1024]
                    b.wait(T, "sS", exp_mark[(it, kc)])
                    if kc == 0:
                        b.wait(T, "dVS", vld)
                        if (it - 1, "evac") in evac_mark:
                            b.wait(T, "sV", evac_mark[(it - 1, "evac")])
                    for jj in range(2):
                        b.op(T, lambda e, jj=jj, kc=kc, ptb=ptb, vb=vb:
                             e.matmul(
                                 PS[0:65, 2048 + jj * 512: 2560 + jj * 512],
                                 VSB[:, vb + kc * 65: vb + kc * 65 + 65],
                                 ptb[:, jj * 512:(jj + 1) * 512],
                                 start=(kc == 0), stop=(kc == 31)), "sT")
                    av_mark[(it, kc)] = b.sems["sT"][1]
            b.wait(V, "sT", av_mark[(it, 31)])
            if it >= 1:
                b.wait(V, "dA1")
                b.wait(V, "dA2")
            evac_mark[(it, "evac")] = b.op(
                V, lambda e: e.tensor_copy(AVE[0:65, :],
                                           PS[0:65, 2048:3072]), "sV")
            b.wait(Sc, "sV", evac_mark[(it, "evac")])
            b.dmas(Sc, "dA1", sp["sp_attn"][head * 64:(head + 1) * 64,
                                            qh * 1024:(qh + 1) * 1024],
                   AVE[0:64, :])
            b.dmas(Sc, "dA2", sp["sp_sums"][head:head + 1,
                                            qh * 1024:(qh + 1) * 1024],
                   AVE[64:65, :])

    b.barrier()
    XQT, SQ2 = sb["XQT"], sb["SQ2"]
    SQ2E = sb["SQ2E"]
    for ec in range(NEC):
        b.op(Sc, lambda e, ec=ec: e.activation(
            out=SQ2E[:, :], in_=sb["XNTA"][:, ec * MYT:(ec + 1) * MYT],
            func=Act.Identity, bias=sb["CST"][:, 0:1],
            scale=sb["SXB"][:, 1:2]), "sS")
        b.wait(V, "sS")
        b.op(V, lambda e: e.tensor_scalar(SQ2E[:, :], SQ2E[:, :], MAGIC,
                                          -32768.0, Alu.subtract,
                                          Alu.max), "sV")
        b.op(V, lambda e, ec=ec: e.tensor_scalar(
            XQT[:, ec * MYT:(ec + 1) * MYT], SQ2E[:, :], 32767.0,
            sb["SXB"][:, 0:1], Alu.min, Alu.mult), "sV")
        b.wait(Sc, "sV")

    e_s, e_v = b.sems["sS"][1], b.sems["sV"][1]
    b.wait(Sy, "sS", e_s)
    b.wait(Sy, "sV", e_v)
    ACL, BBC = sb["ACL"], sb["BBC"]
    ld = b.dma(Sy, sb["SM16"][:, :], sp["sp_sums"][:, :])
    b.wait(V, "dS")
    b.op(V, lambda e: e.reciprocal(sb["C16"][:, :], sb["SM16"][:, :]), "sV")
    b.vsync(V)
    for ec in range(NEC):
        ld = b.dma(Sy, ACL[:, ec * MYT:(ec + 1) * MYT],
                   sp["sp_attn"][ec * 128:(ec + 1) * 128, :])
        b.wait(Sc, "dS")
        b.wait(Sc, "sT")
        b.op(Sc, lambda e, ec=ec: e.activation(
            out=SQ2[:, :], in_=ACL[:, ec * MYT:(ec + 1) * MYT],
            func=Act.Square), "sS")
        b.wait(T, "dS")
        b.wait(T, "sS")
        b.wait(T, "sV")
        for np_ in range(4):
            b.op(T, lambda e, ec=ec, np_=np_: e.matmul(
                PS[0:2, np_ * 512:(np_ + 1) * 512], sb["IND2"][:, :],
                ACL[:, ec * MYT + np_ * 512: ec * MYT + (np_ + 1) * 512],
                start=True, stop=True), "sT")
            b.op(T, lambda e, np_=np_: e.matmul(
                PS[0:2, 2048 + np_ * 512: 2048 + (np_ + 1) * 512],
                sb["IND2"][:, :], SQ2[:, np_ * 512:(np_ + 1) * 512],
                start=True, stop=True), "sT")
        b.wait(V, "sT")
        b.wait(V, "dG")
        b.op(V, lambda e, ec=ec: e.tensor_copy(
            sb["S2T"][:, :], PS[0:2, 0:2048]), "sV")
        b.wait(G, "sV")
        b.dma_g(sp["sp_st"][2 * ec:2 * ec + 2, :], sb["S2T"][:, :])
        b.wait(V, "dG")
        b.op(V, lambda e, ec=ec: e.tensor_copy(
            sb["S2T"][:, :], PS[0:2, 2048:4096]), "sV")
        b.wait(G, "sV")
        b.dma_g(sp["sp_st"][16 + 2 * ec:18 + 2 * ec, :], sb["S2T"][:, :])
    b.wait(Sy, "dG")
    ldA = b.dma(Sy, sb["S16A"][:, :], sp["sp_st"][0:16, :])
    ldB = b.dma(Sy, sb["S16B"][:, :], sp["sp_st"][16:32, :])
    b.wait(V, "dS")
    b.op(V, lambda e: e.tensor_tensor(sb["S16A"][:, :], sb["S16A"][:, :],
                                      sb["C16"][:, :], Alu.mult), "sV")
    b.op(V, lambda e: e.tensor_tensor(sb["A16"][:, :], sb["C16"][:, :],
                                      sb["C16"][:, :], Alu.mult), "sV")
    b.vsync(V)
    b.op(V, lambda e: e.tensor_tensor(sb["S16B"][:, :], sb["S16B"][:, :],
                                      sb["A16"][:, :], Alu.mult), "sV")
    b.wait(T, "sV")
    for np_ in range(4):
        b.op(T, lambda e, np_=np_: e.matmul(
            PS[0:1, np_ * 512:(np_ + 1) * 512], sb["ONES1"][0:16, :],
            sb["S16A"][:, np_ * 512:(np_ + 1) * 512],
            start=True, stop=True), "sT")
        b.op(T, lambda e, np_=np_: e.matmul(
            PS[0:1, 2048 + np_ * 512: 2048 + (np_ + 1) * 512],
            sb["ONES1"][0:16, :], sb["S16B"][:, np_ * 512:(np_ + 1) * 512],
            start=True, stop=True), "sT")
    b.wait(V, "sT")
    b.op(V, lambda e: e.tensor_scalar(ROW[0:1, 0:2048], PS[0:1, 0:2048],
                                      1.0 / E, None, Alu.mult), "sV")
    b.op(V, lambda e: e.tensor_scalar(ROW[0:1, 2048:4096], PS[0:1, 2048:4096],
                                      1.0 / E, None, Alu.mult), "sV")
    mu = ROW[0:1, 0:2048]
    ex2 = ROW[0:1, 2048:4096]
    musq = sb["S16B"][0:1, 0:2048]
    var = sb["SM16"][0:1, 0:2048]
    sd = sb["S16A"][0:1, 0:2048]
    rr = sb["A16"][0:1, 0:2048]
    bb = sb["S16B"][0:1, 0:2048]
    b.vsync(V)
    b.op(V, lambda e: e.tensor_tensor(musq, mu, mu, Alu.mult), "sV")
    b.vsync(V)
    b.op(V, lambda e: e.tensor_tensor(var, ex2, musq, Alu.subtract), "sV")
    b.wait(Sc, "sV")
    b.op(Sc, lambda e: e.activation(out=sd, in_=var, func=Act.Sqrt,
                                    bias=sb["CST"][0:1, 1:2], scale=1.0), "sS")
    b.wait(V, "sS")
    b.op(V, lambda e: e.reciprocal(rr, sd), "sV")
    b.vsync(V)
    b.op(V, lambda e: e.tensor_tensor(bb, mu, rr, Alu.mult), "sV")
    b.vsync(V)
    b.wait(G, "sV")
    b.dma_g(sp["scr_row"][2:3, 0:2048], bb)
    b.dma_g(sp["scr_row"][3:4, 0:2048], rr)
    b.wait(G, "dG")
    b.dma_g(BBC[:, :], sp["scr_row"][2:3, 0:2048].partition_broadcast(128))
    b.dma_g(sb["SM16"][:, :],
          sp["scr_row"][3:4, 0:2048].partition_broadcast(16))
    b.wait(V, "dG")
    b.op(V, lambda e: e.tensor_tensor(sb["A16"][:, :], sb["C16"][:, :],
                                      sb["SM16"][:, :], Alu.mult), "sV")
    b.vsync(V)
    for ec in range(NEC):
        b.wait(T, "sV")
        for np_ in range(4):
            b.op(T, lambda e, ec=ec, np_=np_: e.matmul(
                PS[:, np_ * 512:(np_ + 1) * 512],
                sb["IND16"][:, ec * 128:(ec + 1) * 128],
                sb["A16"][:, np_ * 512:(np_ + 1) * 512],
                start=True, stop=True), "sT")
        b.wait(V, "sT")
        b.wait(Sc, "sV")
        b.op(V, lambda e, ec=ec: e.tensor_tensor(
            SQ2[:, :], ACL[:, ec * MYT:(ec + 1) * MYT], PS[:, 0:2048],
            Alu.mult), "sV")
        b.op(V, lambda e: e.tensor_tensor(SQ2[:, :], SQ2[:, :], BBC[:, :],
                                          Alu.subtract), "sV")
        b.wait(V, "dG")
        b.op(V, lambda e, ec=ec: e.tensor_scalar(
            ACL[:, ec * MYT:(ec + 1) * MYT], SQ2[:, :],
            sb["GAM"][:, ec:ec + 1], sb["BET"][:, ec:ec + 1],
            Alu.mult, Alu.add), "sV")
        b.wait(G, "sV")
        b.dma_g(sp["sp_aln"][ec * 128:(ec + 1) * 128, :],
              ACL[:, ec * MYT:(ec + 1) * MYT])

    b.barrier()

    def prep_ffn_weight(src, dstT, nchunks, icols, dst_stride, wna, wsc):
        nb = icols // 32
        for c in range(nchunks):
            pb = (c % 2) * icols
            b.wait(Sy, "sT")
            b.wait(Sy, "sV")
            ld = b.dma(Sy, wna[:, pb:pb + icols], src[c * 128:(c + 1) * 128, :])
            b.wait(V, "dS")
            b.wait(V, "sT")
            wna3 = wna[:, pb:pb + icols].rearrange("p (nb k) -> p nb k", k=32)
            b.op(V, lambda e, wna3=wna3, nb=nb: e.tensor_reduce(
                sb["BMX"][:, 0:nb], wna3, AX, Alu.max,
                apply_absolute_value=True), "sV")
            b.vsync(V)
            b.op(V, lambda e, nb=nb: e.tensor_scalar(
                sb["BSC"][:, 0:nb], sb["BMX"][:, 0:nb], 1.0 / 7.0, 1e-12,
                Alu.mult, Alu.max), "sV")
            b.vsync(V)
            b.op(V, lambda e, nb=nb: e.reciprocal(
                sb["BRS"][:, 0:nb], sb["BSC"][:, 0:nb]), "sV")
            b.vsync(V)
            qsc = wsc[:, 0:icols]
            qsc3 = qsc.rearrange("p (nb k) -> p nb k", k=32)
            b.op(V, lambda e, wna3=wna3, qsc3=qsc3, nb=nb: e.tensor_tensor(
                qsc3, wna3,
                sb["BRS"][:, 0:nb].unsqueeze(2).broadcast_to([128, nb, 32]), Alu.mult), "sV")
            b.op(V, lambda e, qsc=qsc: e.tensor_scalar(
                qsc, qsc, MAGIC, MAGIC, Alu.add, Alu.subtract), "sV")
            b.op(V, lambda e, qsc=qsc: e.tensor_scalar(
                qsc, qsc, -8.0, 7.0, Alu.max, Alu.min), "sV")
            b.op(V, lambda e, qsc3=qsc3, wna3=wna3, nb=nb: e.tensor_tensor(
                wna3, qsc3,
                sb["BSC"][:, 0:nb].unsqueeze(2).broadcast_to([128, nb, 32]), Alu.mult), "sV")
            b.wait(T, "sV")
            for s8 in range(icols // 128):
                pp = PS[:, (s8 % 2) * 512:(s8 % 2) * 512 + 128]
                if s8 >= 2:
                    b.wait(T, "sV")
                b.op(T, lambda e, pp=pp, pb=pb, s8=s8, wna=wna: e.transpose(
                    pp, wna[:, pb + s8 * 128: pb + (s8 + 1) * 128],
                    sb["IDF"][:, :]), "sT")
                b.wait(V, "sT")
                b.op(V, lambda e, pp=pp, s8=s8, c=c: e.tensor_copy(
                    sb[dstT][:, s8 * dst_stride + c * 128:
                             s8 * dst_stride + (c + 1) * 128], pp), "sV")

    prep_ffn_weight(dr["wu"], "WUT", NIC, E, I, sb["WNA"], sb["WSC"])
    prep_ffn_weight(dr["wg"], "WGT", NIC, E, I, sb["WNA"], sb["WSC"])

    b.barrier()
    for wt, spd, amcol in (("WUT", sp["sp_up"], 1), ("WGT", sp["sp_gate"], 2)):
        for ic in range(NIC):
            base = (ic % 2) * 4 * 512
            b.wait(T, "sV")
            for ec in range(NEC):
                for nq in range(4):
                    b.op(T, lambda e, base=base, nq=nq, ec=ec, ic=ic, wt=wt:
                         e.matmul(
                             PS[:, base + nq * 512: base + (nq + 1) * 512],
                             sb[wt][:, ec * I + ic * 128:
                                    ec * I + (ic + 1) * 128],
                             sb["XQT"][:, ec * MYT + nq * 512:
                                       ec * MYT + (nq + 1) * 512],
                             start=(ec == 0), stop=(ec == NEC - 1)), "sT")
            fb = (ic % 2) * 2048
            b.wait(V, "sT")
            b.wait(V, "dF%d" % (ic % 2) if ("dF%d" % (ic % 2)) in b.sems else "sT")
            for nq in range(4):
                b.op(V, lambda e, base=base, nq=nq, fb=fb: e.tensor_copy(
                    sb["FEV"][:, fb + nq * 512: fb + (nq + 1) * 512],
                    PS[:, base + nq * 512: base + (nq + 1) * 512]), "sV")
            b.op(V, lambda e, fb=fb: e.tensor_reduce(
                ROW[0:128, 384:385], sb["FEV"][:, fb:fb + 2048], AX, Alu.max,
                apply_absolute_value=True), "sV")
            b.vsync(V)
            b.op(V, lambda e, amcol=amcol: e.tensor_tensor(
                sb["AMX"][:, amcol:amcol + 1], sb["AMX"][:, amcol:amcol + 1],
                ROW[0:128, 384:385], Alu.max), "sV")
            b.vsync(V)
            b.wait(Sc, "sV")
            b.dmas(Sc, "dF%d" % (ic % 2), spd[ic * 128:(ic + 1) * 128, :],
                   sb["FEV"][:, fb:fb + 2048])
    global_scale(1, "ug", 2, "SUGB", 32767.0)

    b.barrier()
    FG, FU, FT1 = sb["FG"], sb["FU"], sb["FT1"]
    FGQ, FSQ, FG2 = sb["FGQ"], sb["FSQ"], sb["FG2"]
    f2m = {}
    vend = {}
    for ic in range(NIC):
        par = ic % 2
        fgb = FG[:, par * MYT:(par + 1) * MYT]
        fub = FU[:, par * MYT:(par + 1) * MYT]
        ft = FT1[:, par * MYT:(par + 1) * MYT]
        fq = FGQ[:, par * MYT:(par + 1) * MYT]
        fs = FSQ[:, par * MYT:(par + 1) * MYT]
        f2 = FG2[:, par * MYT:(par + 1) * MYT]
        if (ic - 2) in f2m:
            b.wait(Sy, "sS", f2m[ic - 2][0])
            b.wait(Sy, "sS", f2m[ic - 2][1])
        ldg = b.dmas(Sy, "dLG%d" % par, fgb,
                     sp["sp_gate"][ic * 128:(ic + 1) * 128, :])
        ldu = b.dmas(Sy, "dLU%d" % par, fub,
                     sp["sp_up"][ic * 128:(ic + 1) * 128, :])
        b.wait(Sc, "dLG%d" % par, ldg)
        if ("dUQ%d" % par) in b.sems:
            b.wait(Sc, "dUQ%d" % par)
        qdq_chunk(fgb, fq, sb["SUGB"][:, 2:3], sb["SUGB"][:, 3:4],
                  -32768.0, 32767.0, ft, war=vend.get(ic - 2, 0))
        fg_done = b.sems["sS"][1]
        b.op(Sc, lambda e, fq=fq, fs=fs: e.activation(
            out=fs, in_=fq, func=Act.Sigmoid), "sS")
        b.op(Sc, lambda e, fs=fs, ft=ft: e.activation(
            out=ft, in_=fs, func=Act.Identity, bias=sb["CST"][:, 0:1],
            scale=sb["CST"][:, 2:3]), "sS")
        b.wait(V, "sS")
        b.op(V, lambda e, ft=ft: e.tensor_scalar(
            ft, ft, MAGIC, 0.0, Alu.subtract, Alu.max), "sV")
        b.op(V, lambda e, fs=fs, ft=ft: e.tensor_scalar(
            fs, ft, 65535.0, 1.0 / 65536.0, Alu.min, Alu.mult), "sV")
        if ("dG2%d" % par) in b.sems:
            b.wait(V, "dG2%d" % par)
        b.op(V, lambda e, f2=f2, fq=fq, fs=fs: e.tensor_tensor(
            f2, fq, fs, Alu.mult), "sV")
        b.op(V, lambda e, f2=f2: e.tensor_reduce(
            ROW[0:128, 384:385], f2, AX, Alu.max,
            apply_absolute_value=True), "sV")
        b.op(V, lambda e: e.tensor_tensor(sb["AMX"][:, 3:4], sb["AMX"][:, 3:4],
                                          ROW[0:128, 384:385], Alu.max), "sV")
        g2k = b.sems["sV"][1]
        b.wait(Sc, "sV", g2k)
        b.dmas(Sc, "dG2%d" % par, sp["sp_g2"][ic * 128:(ic + 1) * 128, :], f2)
        b.wait(Sc, "dLU%d" % par, ldu)
        qdq_chunk(fub, fq, sb["SUGB"][:, 0:1], sb["SUGB"][:, 1:2],
                  -32768.0, 32767.0, ft, war=g2k)
        fu_done = b.sems["sS"][1]
        uqk = b.sems["sV"][1]
        b.wait(Sc, "sV", uqk)
        b.dmas(Sc, "dUQ%d" % par, sp["sp_uq"][ic * 128:(ic + 1) * 128, :], fq)
        vend[ic] = uqk
        f2m[ic] = (fg_done, fu_done)
    global_scale(3, "g2", 1, "S2B", 32767.0)

    b.barrier()
    f3m = {}
    vend3 = {}
    for ic in range(NIC):
        par = ic % 2
        fgb = FG[:, par * MYT:(par + 1) * MYT]
        fub = FU[:, par * MYT:(par + 1) * MYT]
        ft = FT1[:, par * MYT:(par + 1) * MYT]
        fq = FGQ[:, par * MYT:(par + 1) * MYT]
        f2 = FG2[:, par * MYT:(par + 1) * MYT]
        if (ic - 2) in f3m:
            b.wait(Sy, "sS", f3m[ic - 2][0])
            b.wait(Sy, "sV", f3m[ic - 2][1])
        ldg = b.dmas(Sy, "dLG%d" % par, fgb,
                     sp["sp_g2"][ic * 128:(ic + 1) * 128, :])
        ldu = b.dmas(Sy, "dLU%d" % par, fub,
                     sp["sp_uq"][ic * 128:(ic + 1) * 128, :])
        b.wait(Sc, "dLG%d" % par, ldg)
        qdq_chunk(fgb, fq, sb["S2B"][:, 0:1], sb["S2B"][:, 1:2],
                  -32768.0, 32767.0, ft, war=vend3.get(ic - 2, 0))
        fg_done = b.sems["sS"][1]
        b.wait(V, "dLU%d" % par, ldu)
        if ("dO%d" % par) in b.sems:
            b.wait(V, "dO%d" % par)
        b.op(V, lambda e, f2=f2, fq=fq, fub=fub: e.tensor_tensor(
            f2, fq, fub, Alu.mult), "sV")
        fu_done = b.sems["sV"][1]
        b.op(V, lambda e, f2=f2: e.tensor_reduce(
            ROW[0:128, 384:385], f2, AX, Alu.max,
            apply_absolute_value=True), "sV")
        b.op(V, lambda e: e.tensor_tensor(sb["AMX"][:, 4:5], sb["AMX"][:, 4:5],
                                          ROW[0:128, 384:385], Alu.max), "sV")
        ok_ = b.sems["sV"][1]
        b.wait(Sc, "sV", ok_)
        b.dmas(Sc, "dO%d" % par, sp["sp_o"][ic * 128:(ic + 1) * 128, :], f2)
        vend3[ic] = ok_
        f3m[ic] = (fg_done, fu_done)
    global_scale(4, "o", 1, "SOB", 32767.0)

    b.barrier()
    prep_ffn_weight(dr["wd"], "WDT", NEC, I, E, sb["W3A"], sb["W3S"])

    b.barrier()
    FG4, FT4, OQT = sb["FG4"], sb["FT4"], sb["OQT"]
    f4m = {}
    v4 = {}
    for ic in range(NIC):
        par = ic % 2
        fb4 = FG4[:, par * MYT:(par + 1) * MYT]
        if (ic - 2) in f4m:
            b.wait(Sy, "sS", f4m[ic - 2])
        ld = b.dmas(Sy, "dLG%d" % par, fb4,
                    sp["sp_o"][ic * 128:(ic + 1) * 128, :])
        b.wait(Sc, "dLG%d" % par, ld)
        if (ic - 1) in v4:
            b.wait(Sc, "sV", v4[ic - 1])
        b.op(Sc, lambda e, fb4=fb4: e.activation(out=FT4[:, :], in_=fb4,
                                                 func=Act.Identity,
                                                 bias=sb["CST"][:, 0:1],
                                                 scale=sb["SOB"][:, 1:2]), "sS")
        f4m[ic] = b.sems["sS"][1]
        b.wait(V, "sS")
        b.op(V, lambda e: e.tensor_scalar(FT4[:, :], FT4[:, :], MAGIC,
                                          -32768.0, Alu.subtract,
                                          Alu.max), "sV")
        b.op(V, lambda e, ic=ic: e.tensor_scalar(
            OQT[:, ic * MYT:(ic + 1) * MYT], FT4[:, :], 32767.0,
            sb["SOB"][:, 0:1], Alu.min, Alu.mult), "sV")
        v4[ic] = b.sems["sV"][1]

    b.barrier()
    for hc in range(NEC):
        base = (hc % 2) * 4 * 512
        b.wait(Sy, "sV")
        ld = b.dmas(Sy, "dAL", sb["ALC"][:, :],
                    sp["sp_aln"][hc * 128:(hc + 1) * 128, :])
        b.wait(T, "sV")
        for ic in range(NIC):
            for nq in range(4):
                b.op(T, lambda e, base=base, nq=nq, ic=ic, hc=hc: e.matmul(
                    PS[:, base + nq * 512: base + (nq + 1) * 512],
                    sb["WDT"][:, ic * E + hc * 128: ic * E + (hc + 1) * 128],
                    OQT[:, ic * MYT + nq * 512: ic * MYT + (nq + 1) * 512],
                    start=(ic == 0), stop=(ic == NIC - 1)), "sT")
        b.wait(V, "sT")
        b.wait(V, "dAL", ld)
        if "dAF" in b.sems:
            b.wait(V, "dAF")
        for nq in range(4):
            b.op(V, lambda e, base=base, nq=nq: e.tensor_tensor(
                sb["AFE"][:, nq * 512:(nq + 1) * 512],
                PS[:, base + nq * 512: base + (nq + 1) * 512],
                sb["ALC"][:, nq * 512:(nq + 1) * 512], Alu.add), "sV")
        b.wait(Sc, "sV")
        b.dmas(Sc, "dAF", sp["sp_af"][hc * 128:(hc + 1) * 128, :],
               sb["AFE"][:, :])

    b.barrier()
    AFL, XT6, OSB = sb["AFL"], sb["XT6"], sb["OSB"]
    for ec in range(NEC):
        b.dmas(Sy, "dAFL", AFL[:, ec * MYT:(ec + 1) * MYT],
               sp["sp_af"][ec * 128:(ec + 1) * 128, :])
    afl_ld = b.sems["dAFL"][1]
    f6m = {}
    for tt in range(16):
        pb = (tt % 2) * E
        base = (tt % 2) * 1024
        if (tt - 2) in f6m:
            b.wait(Sy, "sV", f6m[tt - 2])
        ld = b.dmas(Sy, "dX6%d" % (tt % 2), XT6[:, pb:pb + E],
                    dr["x_tok"][tt * 128:(tt + 1) * 128, :])
        b.wait(T, "dAFL", afl_ld)
        if (tt - 2) in f6m:
            b.wait(T, "sV", f6m[tt - 2])
        for hc in range(NEC):
            b.op(T, lambda e, base=base, hc=hc, tt=tt: e.transpose(
                PS[:, base + hc * 128: base + (hc + 1) * 128],
                AFL[:, hc * MYT + tt * 128: hc * MYT + (tt + 1) * 128],
                sb["IDF"][:, :]), "sT")
        b.wait(V, "sT")
        b.wait(V, "dX6%d" % (tt % 2), ld)
        if ("dOU%d" % (tt % 2)) in b.sems:
            b.wait(V, "dOU%d" % (tt % 2))
        f6m[tt] = b.op(V, lambda e, base=base, pb=pb: e.tensor_tensor(
            OSB[:, pb:pb + E], PS[:, base:base + E], XT6[:, pb:pb + E],
            Alu.add), "sV")
        b.wait(Sc, "sV", f6m[tt])
        b.dmas(Sc, "dOU%d" % (tt % 2), out_d[tt * 128:(tt + 1) * 128, :],
               OSB[:, pb:pb + E])
    b.wait(G, "dOU0")
    b.wait(G, "dOU1")
    b.wait(G, "dG")


_NC_CACHE = {}


def kernel(L0_attn_x, L0_attn_Wq, L0_attn_Wk, L0_attn_Wv, L0_attn_ln_gamma,
           L0_attn_ln_beta, L0_ffn_w_gate, L0_ffn_w_up, L0_ffn_w_down):
    x = np.ascontiguousarray(np.asarray(L0_attn_x, np.float32))
    wq = np.asarray(L0_attn_Wq, np.float32)
    wk = np.asarray(L0_attn_Wk, np.float32)
    wv = np.asarray(L0_attn_Wv, np.float32)
    gam = np.asarray(L0_attn_ln_gamma, np.float32)
    bet = np.asarray(L0_attn_ln_beta, np.float32)
    wg = np.ascontiguousarray(np.asarray(L0_ffn_w_gate, np.float32))
    wu = np.ascontiguousarray(np.asarray(L0_ffn_w_up, np.float32))
    wd = np.ascontiguousarray(np.asarray(L0_ffn_w_down, np.float32))

    wqT = np.ascontiguousarray(wq.T)
    wkT = np.ascontiguousarray(wk.T)
    wvT = np.ascontiguousarray(wv.T)
    gamma8 = np.ascontiguousarray(gam.reshape(NEC, 128).T)
    beta8 = np.ascontiguousarray(bet.reshape(NEC, 128).T)

    if "nc" not in _NC_CACHE:
        _NC_CACHE["nc"] = build_nc()
    nc = _NC_CACHE["nc"]

    in_maps = []
    for c in range(N_CORES):
        bq, h = divmod(c, 2)
        qoff = h * MYT
        xloc = np.roll(x[bq], -qoff, axis=0)
        if h == 0:
            meta = np.array([[-1.0 / 4096.0, 3968.0 / 4096.0]], np.float32)
        else:
            meta = np.array([[1.0 / 4096.0, 1.0 - 3968.0 / 4096.0]], np.float32)
        in_maps.append({
            "xseqT": np.ascontiguousarray(xloc.T),
            "x_tok": np.ascontiguousarray(xloc),
            "wqT": wqT, "wkT": wkT, "wvT": wvT,
            "wu": wu, "wg": wg, "wd": wd,
            "gamma8": gamma8, "beta8": beta8,
            "meta": meta,
        })

    res = run_bass_kernel_spmd(nc, in_maps, core_ids=list(range(N_CORES)))
    out = np.empty((4, S, E), np.float32)
    for c in range(N_CORES):
        bq, h = divmod(c, 2)
        out[bq, h * MYT:(h + 1) * MYT, :] = res.results[c]["out"]
    return out


# version 12
# speedup vs baseline: 1.6427x; 1.0317x over previous
import numpy as np
from contextlib import ExitStack

import concourse.bass as bass
import concourse.mybir as mybir
from concourse.bass_utils import run_bass_kernel_spmd

F32 = mybir.dt.float32
BF16 = mybir.dt.bfloat16
Alu = mybir.AluOpType
Act = mybir.ActivationFunctionType
AX = mybir.AxisListType.X

MAGIC = 1.5 * 2.0**23
N_CORES = 8
S = 4096
E = 1024
H = 16
I = 3072
MYT = 2048
NEC = E // 128
NIC = I // 128

ENGS = ("sync", "gpsimd", "vector", "scalar", "tensor")
V, Sc, T, G, Sy = "vector", "scalar", "tensor", "gpsimd", "sync"


class Builder:

    def __init__(self, nc):
        self.nc = nc
        self.ops = {e: [] for e in ENGS}
        self.sems = {}

    def mksem(self, name):
        h = self.nc.semaphore(name).__enter__()
        self.sems[name] = [h, 0]

    AUTO_CHAIN = {"vector": "sV", "scalar": "sS", "gpsimd": "sG"}

    def op(self, eng, fn, inc=None, k=1):
        ch = self.AUTO_CHAIN.get(eng)
        if ch is not None:
            self.wait(eng, ch)
        if inc is None:
            self.ops[eng].append(fn)
            return None
        rec = self.sems[inc]
        rec[1] += k
        h = rec[0]
        self.ops[eng].append(lambda e, fn=fn, h=h, k=k: fn(e).then_inc(h, k))
        return rec[1]

    def wait(self, eng, sem, val=None):
        h, c = self.sems[sem]
        v = c if val is None else val
        if v > 0:
            self.ops[eng].append(lambda e, h=h, v=v: e.wait_ge(h, v))

    def dma_g(self, out, in_, **kw):
        return self.dma("gpsimd", out, in_, serial=False, **kw)

    def dmas(self, eng, sem, out, in_, **kw):
        if sem not in self.sems:
            self.mksem(sem)
        return self.op(eng, lambda e: e.dma_start(out=out, in_=in_, **kw),
                       sem, 16)

    def dma(self, eng, out, in_, serial=True, **kw):
        q = {"sync": "dS", "gpsimd": "dG", "scalar": "dA"}[eng]
        n = self.op(eng, lambda e: e.dma_start(out=out, in_=in_, **kw), q, 16)
        if serial:
            self.wait(eng, q, n)
        return n

    def vsync(self, eng):
        sem = {"vector": "sV", "scalar": "sS", "tensor": "sT",
               "gpsimd": "sG"}[eng]
        self.wait(eng, sem)

    def vfiller(self, sb):
        self.op("vector", lambda e: e.tensor_copy(sb["ROW"][0:1, 4095:4096],
                                                  sb["CST"][0:1, 0:1]), "sV")

    def barrier(self, engs=ENGS):
        snap = [(h, c) for (h, c) in self.sems.values() if c > 0]
        for eng in engs:
            for h, v in snap:
                self.ops[eng].append(lambda e, h=h, v=v: e.wait_ge(h, v))


def build_nc():
    nc = bass.Bass("TRN2", target_bir_lowering=False)

    dram = {}
    for nm, shp in [
        ("xseqT", [E, S]), ("x_tok", [S, E]),
        ("wqT", [E, E]), ("wkT", [E, E]), ("wvT", [E, E]),
        ("wu", [I, E]), ("wg", [I, E]), ("wd", [E, I]),
        ("gamma8", [128, NEC]), ("beta8", [128, NEC]),
        ("meta", [1, 2]),
    ]:
        dram[nm] = nc.dram_tensor(nm, shp, F32, kind="ExternalInput").ap()
    out_d = nc.dram_tensor("out", [MYT, E], F32, kind="ExternalOutput").ap()

    sp = {}
    for nm, shp in [
        ("sp_v", [S, H * 65]), ("sp_attn", [E, MYT]), ("sp_sums", [H, MYT]),
        ("sp_aln", [E, MYT]), ("sp_up", [I, MYT]), ("sp_gate", [I, MYT]),
        ("sp_g2", [I, MYT]), ("sp_uq", [I, MYT]), ("sp_o", [I, MYT]),
        ("sp_af", [E, MYT]), ("scr_row", [4, S]), ("sp_st", [32, MYT]),
    ]:
        sp[nm] = nc.dram_tensor(
            nm, shp, BF16 if nm == "sp_v" else F32, kind="Internal").ap()
    sp["cc_in"] = {k: nc.dram_tensor(f"cc{k}_i", [1, w], F32, kind="Internal").ap()
                   for k, w in (("x", 1), ("ug", 2), ("g2", 1), ("o", 1))}
    sp["cc_out"] = {k: nc.dram_tensor(f"cc{k}_o", [1, w], F32, kind="Internal").ap()
                    for k, w in (("x", 1), ("ug", 2), ("g2", 1), ("o", 1))}

    st = ExitStack()
    sb = {}

    def perm(nm, shp, dt):
        sb[nm] = st.enter_context(nc.sbuf_tensor(nm, shp, dt))
        return sb[nm]

    perm("CST", [128, 8], F32)
    perm("SXB", [128, 2], F32)
    perm("SUGB", [128, 4], F32)
    perm("S2B", [128, 2], F32)
    perm("SOB", [128, 2], F32)
    perm("METB", [128, 2], F32)
    perm("AMX", [128, 8], F32)
    perm("IDF", [128, 128], F32)
    perm("IND2", [128, 2], F32)
    perm("IND16", [16, 1024], F32)
    perm("ONES1", [128, 1], F32)
    perm("PIOTA", [128, 1], F32)
    perm("GAM", [128, NEC], F32)
    perm("BET", [128, NEC], F32)
    perm("ROW", [128, 4096], F32)
    perm("BMX", [128, 96], F32)
    perm("BSC", [128, 96], F32)
    perm("BRS", [128, 96], F32)

    TOP = nc.SBUF_PARTITION_SIZE_BYTES
    ZB = 24576

    def at(nm, off, shp, dt):
        end = off + int(np.prod(shp[1:])) * mybir.dt.size(dt)
        assert ZB <= off and end <= TOP, (nm, off, end)
        sb[nm] = nc.alloc_sbuf_tensor_at(nm, shp, dt, offset=off)
        return sb[nm]

    at("WVTS", 40960, [128, NEC * E], BF16)
    at("WKTS", 57344, [128, NEC * E], BF16)
    at("WQTS", 73728, [128, NEC * E], BF16)
    at("XNTA", 90112, [128, NEC * MYT], BF16)
    at("XNTB", 122880, [128, NEC * MYT], BF16)
    at("XC2", 155648, [128, S], F32)
    at("XC2B", 204800, [128, S], F32)
    at("MUBC", 172032, [128, S], F32)
    at("RSBC", 188416, [128, S], F32)
    at("XT2", 204800, [128, 2 * E], F32)
    at("SQT", 212992, [128, E], F32)
    at("STT", 217088, [128, 128], F32)
    at("VEV", 217600, [128, H * 65], BF16)
    at("KF", 155648, [128, NEC * S], BF16)
    at("QF", 40960, [128, NEC * MYT], BF16)
    at("DNE", 122880, [128, 4096], F32)
    at("DFA", 139264, [128, 4096], F32)
    at("SW", 73728, [128, 2 * 1024], F32)
    at("SQ2E", 196608, [128, MYT], F32)
    at("PT", 81920, [128, 2 * 1024], BF16)
    at("AVE", 86016, [128, 1024], F32)
    at("VSB", 221184, [128, 32 * 65], BF16)
    at("XQT", 155648, [128, NEC * MYT], BF16)
    at("SQ2", 188416, [128, MYT], F32)
    at("ACL", 40960, [128, NEC * MYT], F32)
    at("BBC", 106496, [128, MYT], F32)
    at("C16", 114688, [16, MYT], F32)
    at("A16", 122880, [16, MYT], F32)
    at("SM16", 131072, [16, MYT], F32)
    at("S16A", 139264, [16, MYT], F32)
    at("S16B", 147456, [16, MYT], F32)
    at("S2T", 196608, [2, MYT], F32)
    at("WUT", 40960, [128, NEC * I], BF16)
    at("WGT", 90112, [128, NEC * I], BF16)
    at("FEV", 139264, [128, 8 * 512], F32)
    at("WNA", 196608, [128, 2 * E], F32)
    at("WSC", 204800, [128, 2 * E], F32)
    at("FG", 40960, [128, 2 * MYT], F32)
    at("FU", 57344, [128, 2 * MYT], F32)
    at("FT1", 73728, [128, 2 * MYT], F32)
    at("FGQ", 90112, [128, 2 * MYT], F32)
    at("FSQ", 106496, [128, 2 * MYT], F32)
    at("FG2", 122880, [128, 2 * MYT], F32)
    at("OQT", 40960, [128, NIC * MYT], BF16)
    at("WDT", 139264, [128, NIC * E], BF16)
    at("W3A", 188416, [128, 2 * I], F32)
    at("W3S", 212992, [128, I], F32)
    at("FG4", 188416, [128, 2 * MYT], F32)
    at("FT4", 204800, [128, MYT], F32)
    at("ALC", 188416, [128, MYT], F32)
    at("AFE", 196608, [128, MYT], F32)
    at("AFL", 40960, [128, NEC * MYT], F32)
    at("XT6", 106496, [128, 2 * E], F32)
    at("OSB", 114688, [128, 2 * E], F32)

    PS = nc.alloc_psum_tensor("PS", [128, 4096], F32)

    b = Builder(nc)
    for s_ in ("dS", "dG", "dA", "sV", "sS", "sT", "sG", "sCC"):
        b.mksem(s_)

    emit_program(nc, b, sb, PS, dram, out_d, sp)

    with nc.Block() as block:
        @block.sync
        def _(e):
            for f in b.ops["sync"]:
                f(e)

        @block.gpsimd
        def _(e):
            for f in b.ops["gpsimd"]:
                f(e)

        @block.vector
        def _(e):
            for f in b.ops["vector"]:
                f(e)

        @block.scalar
        def _(e):
            for f in b.ops["scalar"]:
                f(e)

        @block.tensor
        def _(e):
            for f in b.ops["tensor"]:
                f(e)

    st.close()
    return nc


def emit_program(nc, b, sb, PS, dr, out_d, sp):
    ROW = sb["ROW"]
    XNT = {0: sb["XNTA"], 1: sb["XNTB"]}

    def xnt_cols(ec, c0, c1):
        assert (c0 // MYT) == ((c1 - 1) // MYT)
        half = c0 // MYT
        t = XNT[half]
        return t[:, ec * MYT + c0 % MYT: ec * MYT + c0 % MYT + (c1 - c0)]

    b.op(V, lambda e: e.memset(sb["CST"][:, 0:1], MAGIC), "sV")
    b.op(V, lambda e: e.memset(sb["CST"][:, 1:2], 1e-5), "sV")
    b.op(V, lambda e: e.memset(sb["CST"][:, 2:3], 65536.0), "sV")
    b.op(V, lambda e: e.memset(sb["CST"][:, 3:4], -1920.0 / 4096.0), "sV")
    b.op(V, lambda e: e.memset(sb["CST"][:, 4:5], 1.0 / 4096.0), "sV")
    b.op(V, lambda e: e.memset(sb["ONES1"][:, :], 1.0), "sV")
    b.op(V, lambda e: e.memset(sb["AMX"][:, :], 0.0), "sV")
    b.op(G, lambda e: e.iota(sb["IDF"][:, :], pattern=[[1, 128]], base=0,
                             channel_multiplier=-1,
                             allow_small_or_imprecise_dtypes=True), "sG")
    b.op(G, lambda e: e.iota(sb["PIOTA"][:, :], pattern=[[0, 1]], base=0,
                             channel_multiplier=1,
                             allow_small_or_imprecise_dtypes=True), "sG")
    b.dma_g(sb["METB"][:, :], dr["meta"][0:1, :].partition_broadcast(128))
    b.wait(V, "sG")
    b.op(V, lambda e: e.tensor_scalar(sb["IND2"][:, 0:1], sb["PIOTA"][:, :],
                                      64.0, None, Alu.is_lt), "sV")
    b.op(V, lambda e: e.tensor_scalar(sb["IND2"][:, 1:2], sb["PIOTA"][:, :],
                                      64.0, None, Alu.is_ge), "sV")
    b.op(V, lambda e: e.tensor_scalar(sb["IDF"][:, :], sb["IDF"][:, :],
                                      0.0, None, Alu.is_equal), "sV")
    b.op(G, lambda e: e.iota(sb["IND16"][:, :],
                             pattern=[[2, 8], [1, 2], [0, 64]], base=0,
                             channel_multiplier=0,
                             allow_small_or_imprecise_dtypes=True), "sG")
    b.wait(V, "sG")
    b.op(V, lambda e: e.tensor_tensor(
        sb["IND16"][:, :], sb["IND16"][:, :],
        sb["PIOTA"][0:16, 0:1].broadcast_to([16, 1024]), Alu.is_equal), "sV")
    b.dma(Sy, sb["GAM"][:, :], dr["gamma8"][:, :])
    b.dma(Sy, sb["BET"][:, :], dr["beta8"][:, :])
    for wi, (nm, dst) in enumerate((("wqT", "WQTS"), ("wkT", "WKTS"),
                                    ("wvT", "WVTS"))):
        for ec in range(NEC):
            b.dmas(G, "dW%d" % wi, sb[dst][:, ec * E:(ec + 1) * E],
                   dr[nm][ec * 128:(ec + 1) * 128, :])

    def global_scale(col, key, packw, dst, qmaxv):
        b.wait(T, "sV")
        b.wait(T, "sS")
        b.op(T, lambda e: e.transpose(PS[0:packw, 1536:1664],
                                      sb["AMX"][:, col:col + packw],
                                      sb["IDF"][:, :]), "sT")
        b.wait(V, "sT")
        b.op(V, lambda e: e.tensor_reduce(ROW[0:packw, 256:257],
                                          PS[0:packw, 1536:1664], AX,
                                          Alu.max), "sV")
        b.wait(G, "sV")
        b.dma_g(sp["cc_in"][key][0:1, :].rearrange("a (p c) -> (a p) c",
                                                    p=packw),
              ROW[0:packw, 256:257])
        b.wait(G, "dG")
        b.op(G, lambda e: e.collective_compute(
            "AllReduce", Alu.max, replica_groups=[list(range(N_CORES))],
            ins=[sp["cc_in"][key].opt()], outs=[sp["cc_out"][key].opt()],
        ), "sCC")
        b.wait(G, "sCC")
        b.dma_g(ROW[0:1, 260:260 + packw], sp["cc_out"][key][0:1, :])
        b.wait(V, "dG")
        b.op(V, lambda e: e.tensor_scalar(
            ROW[0:1, 280:280 + packw], ROW[0:1, 260:260 + packw],
            1.0 / qmaxv, 1e-12, Alu.mult, Alu.max), "sV")
        b.vsync(V)
        b.op(V, lambda e: e.reciprocal(
            ROW[0:1, 288:288 + packw], ROW[0:1, 280:280 + packw]), "sV")
        b.vsync(V)
        for i in range(packw):
            b.op(V, lambda e, i=i: e.tensor_copy(
                ROW[0:1, 300 + 2 * i:301 + 2 * i],
                ROW[0:1, 280 + i:281 + i]), "sV")
            b.op(V, lambda e, i=i: e.tensor_copy(
                ROW[0:1, 301 + 2 * i:302 + 2 * i],
                ROW[0:1, 288 + i:289 + i]), "sV")
        b.wait(G, "sV")
        b.dma_g(sp["scr_row"][2:3, 0:2 * packw], ROW[0:1, 300:300 + 2 * packw])
        b.wait(G, "dG")
        b.dma_g(sb[dst][:, 0:2 * packw],
              sp["scr_row"][2:3, 0:2 * packw].partition_broadcast(128))

    def qdq_chunk(src_ap, dst_ap, s_ap, rs_ap, lo, hi, tmp_ap, war=-1):
        if war < 0:
            b.wait(Sc, "sV")
        elif war > 0:
            b.wait(Sc, "sV", war)
        b.op(Sc, lambda e: e.activation(out=tmp_ap, in_=src_ap,
                                        func=Act.Identity,
                                        bias=sb["CST"][:, 0:1],
                                        scale=rs_ap), "sS")
        b.wait(V, "sS")
        b.op(V, lambda e: e.tensor_scalar(tmp_ap, tmp_ap, MAGIC, lo,
                                          Alu.subtract, Alu.max), "sV")
        b.op(V, lambda e: e.tensor_scalar(dst_ap, tmp_ap, hi, s_ap,
                                          Alu.min, Alu.mult), "sV")
        b.wait(Sc, "sV")

    XT2, SQT, STT, XC2 = sb["XT2"], sb["SQT"], sb["STT"], sb["XC2"]
    for tt in range(32):
        pb = (tt % 2) * E
        if tt >= 2:
            b.wait(Sy, "sV")
            b.wait(Sy, "sS")
        ld = b.dmas(Sy, "dXT%d" % (tt % 2), XT2[:, pb:pb + E],
                    dr["x_tok"][tt * 128:(tt + 1) * 128, :])
        b.wait(Sc, "dXT%d" % (tt % 2), ld)
        b.wait(Sc, "sV")
        b.op(Sc, lambda e, pb=pb: e.activation(
            out=SQT[:, :], in_=XT2[:, pb:pb + E], func=Act.Square), "sS")
        b.wait(V, "dXT%d" % (tt % 2), ld)
        b.op(V, lambda e, pb=pb, tt=tt: e.tensor_reduce(
            STT[:, tt:tt + 1], XT2[:, pb:pb + E], AX, Alu.add), "sV")
        b.wait(V, "sS")
        b.op(V, lambda e, tt=tt: e.tensor_reduce(
            STT[:, 32 + tt:33 + tt], SQT[:, :], AX, Alu.add), "sV")
    b.vsync(V)
    b.op(V, lambda e: e.tensor_scalar(STT[:, 0:32], STT[:, 0:32],
                                      1.0 / E, None, Alu.mult), "sV")
    b.op(V, lambda e: e.tensor_scalar(STT[:, 32:64], STT[:, 32:64],
                                      1.0 / E, None, Alu.mult), "sV")
    b.vsync(V)
    b.op(V, lambda e: e.tensor_tensor(STT[:, 64:96], STT[:, 0:32],
                                      STT[:, 0:32], Alu.mult), "sV")
    b.vsync(V)
    b.op(V, lambda e: e.tensor_tensor(STT[:, 64:96], STT[:, 32:64],
                                      STT[:, 64:96], Alu.subtract), "sV")
    b.wait(Sc, "sV")
    b.op(Sc, lambda e: e.activation(out=STT[:, 96:128], in_=STT[:, 64:96],
                                    func=Act.Sqrt, bias=sb["CST"][:, 1:2],
                                    scale=1.0), "sS")
    b.wait(V, "sS")
    b.op(V, lambda e: e.reciprocal(STT[:, 96:128], STT[:, 96:128]), "sV")
    b.vsync(V)
    b.wait(T, "sV")
    b.op(T, lambda e: e.transpose(PS[0:32, 0:128], STT[:, 0:32],
                                  sb["IDF"][:, :]), "sT")
    b.op(T, lambda e: e.transpose(PS[0:32, 512:640], STT[:, 96:128],
                                  sb["IDF"][:, :]), "sT")
    b.wait(V, "sT")
    b.op(V, lambda e: e.tensor_copy(ROW[0:32, 0:128], PS[0:32, 0:128]), "sV")
    b.op(V, lambda e: e.tensor_copy(ROW[0:32, 128:256], PS[0:32, 512:640]), "sV")
    b.wait(G, "sV")
    b.dma_g(sp["scr_row"][0:1, :].rearrange("a (p c) -> (a p) c", p=32),
          ROW[0:32, 0:128])
    b.dma_g(sp["scr_row"][1:2, :].rearrange("a (p c) -> (a p) c", p=32),
          ROW[0:32, 128:256])
    b.wait(G, "dG")
    b.dma_g(sb["MUBC"][:, :], sp["scr_row"][0:1, :].partition_broadcast(128))
    b.dma_g(sb["RSBC"][:, :], sp["scr_row"][1:2, :].partition_broadcast(128))
    b.barrier([Sy])
    for ec in range(NEC):
        xcb = sb["XC2"] if ec % 2 == 0 else sb["XC2B"]
        pb = 0
        if ec >= 2:
            b.wait(Sy, "sV")
        ld = b.dmas(Sy, "dXC%d" % (ec % 2), xcb[:, :],
                    dr["xseqT"][ec * 128:(ec + 1) * 128, :])
        b.wait(V, "dXC%d" % (ec % 2), ld)
        b.wait(V, "dG")
        b.op(V, lambda e, xcb=xcb: e.tensor_tensor(
            xcb[:, :], xcb[:, :], sb["MUBC"][:, :], Alu.subtract), "sV")
        for half in range(2):
            b.op(V, lambda e, xcb=xcb, ec=ec, half=half: e.tensor_tensor(
                XNT[half][:, ec * MYT:(ec + 1) * MYT],
                xcb[:, half * MYT:(half + 1) * MYT],
                sb["RSBC"][:, half * MYT:(half + 1) * MYT], Alu.mult), "sV")
            b.op(V, lambda e, ec=ec, half=half: e.tensor_reduce(
                ROW[0:128, 386:387], XNT[half][:, ec * MYT:(ec + 1) * MYT],
                AX, Alu.max, apply_absolute_value=True), "sV")
            b.op(V, lambda e: e.tensor_tensor(
                sb["AMX"][:, 0:1], sb["AMX"][:, 0:1], ROW[0:128, 386:387],
                Alu.max), "sV")
    global_scale(0, "x", 1, "SXB", 32767.0)

    b.barrier([T])
    VEV = sb["VEV"]
    vem = {}
    for tt in range(32):
        for half in range(2):
            slot = tt * 2 + half
            pt = PS[:, 2048 + (slot % 4) * 512: 2048 + (slot % 4) * 512 + 512]
            if (slot - 4) in vem:
                b.wait(T, "sV", vem[slot - 4])
            for ec in range(NEC):
                b.op(T, lambda e, tt=tt, ec=ec, pt=pt, half=half: e.matmul(
                    pt, xnt_cols(ec, tt * 128, (tt + 1) * 128),
                    sb["WVTS"][:, ec * E + half * 512:
                               ec * E + (half + 1) * 512],
                    start=(ec == 0), stop=(ec == NEC - 1)), "sT")
            b.wait(V, "sT")
            if (slot - 2) in vem:
                b.wait(V, "dSv" if "dSv" in b.sems else "sT")
            vem[slot] = b.op(V, lambda e, pt=pt, half=half: e.tensor_copy(
                VEV[:, half * 8 * 65:(half + 1) * 8 * 65]
                .rearrange("p (h c) -> p h c", c=65)[:, :, 0:64],
                pt.rearrange("p (h c) -> p h c", c=64)), "sV")
        b.op(V, lambda e: e.memset(
            VEV[:, :].rearrange("p (h c) -> p h c", c=65)[:, :, 64:65],
            1.0), "sV")
        mm = b.sems["sV"][1]
        b.wait(Sc, "sV", mm)
        b.dmas(Sc, "dSv", sp["sp_v"][tt * 128:(tt + 1) * 128, :], VEV[:, :])

    b.barrier([T])
    KF, QF = sb["KF"], sb["QF"]
    kqm = {}
    slot = 0
    for pair in range(NEC):
        for nk in range(8):
            pk = PS[:, 3072 + (slot % 2) * 512: 3072 + (slot % 2) * 512 + 512]
            if (slot - 2) in kqm:
                b.wait(T, "sV", kqm[slot - 2])
            for ec in range(NEC):
                b.op(T, lambda e, pk=pk, ec=ec, pair=pair, nk=nk: e.matmul(
                    pk, sb["WKTS"][:, ec * E + pair * 128:
                                   ec * E + (pair + 1) * 128],
                    xnt_cols(ec, nk * 512, (nk + 1) * 512),
                    start=(ec == 0), stop=(ec == NEC - 1)), "sT")
            b.wait(V, "sT")
            kqm[slot] = b.op(V, lambda e, pk=pk, pair=pair, nk=nk:
                             e.tensor_copy(
                KF[:, pair * S + nk * 512: pair * S + (nk + 1) * 512],
                pk), "sV")
            slot += 1
    for pair in range(NEC):
        for nq in range(4):
            pk = PS[:, 3072 + (slot % 2) * 512: 3072 + (slot % 2) * 512 + 512]
            if (slot - 2) in kqm:
                b.wait(T, "sV", kqm[slot - 2])
            for ec in range(NEC):
                b.op(T, lambda e, pk=pk, ec=ec, pair=pair, nq=nq: e.matmul(
                    pk, sb["WQTS"][:, ec * E + pair * 128:
                                   ec * E + (pair + 1) * 128],
                    xnt_cols(ec, nq * 512, (nq + 1) * 512),
                    start=(ec == 0), stop=(ec == NEC - 1)), "sT")
            b.wait(V, "sT")
            kqm[slot] = b.op(V, lambda e, pk=pk, pair=pair, nq=nq:
                             e.tensor_scalar(
                QF[:, pair * MYT + nq * 512: pair * MYT + (nq + 1) * 512],
                pk, 1.0 / 32.0, None, Alu.mult), "sV")
            slot += 1

    b.barrier([G])
    b.op(G, lambda e: e.iota(sb["DNE"][:, :], pattern=[[1, 4096]], base=0,
                             channel_multiplier=-1,
                             allow_small_or_imprecise_dtypes=True), "sG")
    b.op(G, lambda e: e.iota(sb["DFA"][:, :], pattern=[[1, 4096]], base=0,
                             channel_multiplier=-1,
                             allow_small_or_imprecise_dtypes=True), "sG")
    b.wait(V, "sG")
    b.wait(V, "dG")
    b.op(V, lambda e: e.tensor_scalar(sb["DFA"][:, :], sb["DFA"][:, :],
                                      sb["METB"][:, 0:1], sb["METB"][:, 1:2],
                                      Alu.mult, Alu.add), "sV")
    b.wait(Sc, "sG")
    b.wait(Sc, "sV")
    b.op(Sc, lambda e: e.activation(out=sb["DNE"][:, :], in_=sb["DNE"][:, :],
                                    func=Act.Abs, bias=sb["CST"][:, 3:4],
                                    scale=sb["CST"][:, 4:5]), "sS")

    b.barrier([T, Sc, G, Sy])
    SW, PT, AVE, VSB = sb["SW"], sb["PT"], sb["AVE"], sb["VSB"]
    DNE, DFA = sb["DNE"], sb["DFA"]
    RCOL = (0, 1024, 3072)
    sc_mark, tt_mark, exp_mark, av_mark, evac_mark = {}, {}, {}, {}, {}
    spill_mark = {}

    def prevkey(it, k, back, n=32):
        kk = k - back
        return (it, kk) if kk >= 0 else (it - 1, kk + n)

    it = -1
    for head in range(H):
        pair, hh = divmod(head, 2)
        hb = hh * 64
        for qh in range(2):
            it += 1
            vb = 0
            if (it - 1, 31) in av_mark:
                b.wait(Sy, "sT", av_mark[(it - 1, 31)])
            vld = b.dmas(Sy, "dVS", VSB[:, vb:vb + 2080]
                         .rearrange("p (k c) -> p k c", c=65),
                         sp["sp_v"][:, head * 65:(head + 1) * 65]
                         .rearrange("(k p) c -> p k c", p=128))
            for step in range(34):
                k_sc, k_tt, k_av = step, step - 1, step - 2
                if k_sc < 32:
                    kc = k_sc
                    pscr = PS[:, RCOL[kc % 3]:RCOL[kc % 3] + 1024]
                    key = prevkey(it, kc, 3)
                    if key in tt_mark:
                        b.wait(T, "sV", tt_mark[key])
                    for jj in range(2):
                        b.op(T, lambda e, pscr=pscr, jj=jj, hb=hb, kc=kc,
                             qh=qh, pair=pair: e.matmul(
                                 pscr[:, jj * 512:(jj + 1) * 512],
                                 KF[hb:hb + 64, pair * S + kc * 128:
                                    pair * S + (kc + 1) * 128],
                                 QF[hb:hb + 64,
                                    pair * MYT + qh * 1024 + jj * 512:
                                    pair * MYT + qh * 1024 + (jj + 1) * 512],
                                 start=True, stop=True), "sT")
                    sc_mark[(it, kc)] = b.sems["sT"][1]
                if 0 <= k_tt < 32:
                    kc = k_tt
                    pscr = PS[:, RCOL[kc % 3]:RCOL[kc % 3] + 1024]
                    swb = SW[:, (kc % 2) * 1024:(kc % 2) * 1024 + 1024]
                    ptb = PT[:, (kc % 2) * 1024:(kc % 2) * 1024 + 1024]
                    if kc < 16:
                        doff = qh * 1024 - kc * 128 + 1920
                        dsl = DNE[:, doff:doff + 1024]
                    else:
                        doff = qh * 1024 - (kc - 16) * 128 + 1920
                        dsl = DFA[:, doff:doff + 1024]
                    b.wait(V, "sT", sc_mark[(it, kc)])
                    key = prevkey(it, kc, 2)
                    if key in exp_mark:
                        b.wait(V, "sS", exp_mark[key])
                    if kc == 0 and "dA1" in b.sems:
                        pass
                    tt_mark[(it, kc)] = b.op(
                        V, lambda e, pscr=pscr, dsl=dsl, swb=swb:
                        e.tensor_tensor(swb, pscr, dsl, Alu.mult), "sV")
                    b.wait(Sc, "sV", tt_mark[(it, kc)])
                    if key in av_mark:
                        b.wait(Sc, "sT", av_mark[key])
                    exp_mark[(it, kc)] = b.op(
                        Sc, lambda e, swb=swb, ptb=ptb: e.activation(
                            out=ptb, in_=swb, func=Act.Exp), "sS")
                if 0 <= k_av < 32:
                    kc = k_av
                    ptb = PT[:, (kc % 2) * 1024:(kc % 2) * 1024 + 1024]
                    b.wait(T, "sS", exp_mark[(it, kc)])
                    if kc == 0:
                        b.wait(T, "dVS", vld)
                        if (it - 1, "evac") in evac_mark:
                            b.wait(T, "sV", evac_mark[(it - 1, "evac")])
                    for jj in range(2):
                        b.op(T, lambda e, jj=jj, kc=kc, ptb=ptb, vb=vb:
                             e.matmul(
                                 PS[0:65, 2048 + jj * 512: 2560 + jj * 512],
                                 VSB[:, vb + kc * 65: vb + kc * 65 + 65],
                                 ptb[:, jj * 512:(jj + 1) * 512],
                                 start=(kc == 0), stop=(kc == 31)), "sT")
                    av_mark[(it, kc)] = b.sems["sT"][1]
            b.wait(V, "sT", av_mark[(it, 31)])
            if it >= 1:
                b.wait(V, "dA1")
                b.wait(V, "dA2")
            evac_mark[(it, "evac")] = b.op(
                V, lambda e: e.tensor_copy(AVE[0:65, :],
                                           PS[0:65, 2048:3072]), "sV")
            b.wait(Sc, "sV", evac_mark[(it, "evac")])
            b.dmas(Sc, "dA1", sp["sp_attn"][head * 64:(head + 1) * 64,
                                            qh * 1024:(qh + 1) * 1024],
                   AVE[0:64, :])
            b.dmas(Sc, "dA2", sp["sp_sums"][head:head + 1,
                                            qh * 1024:(qh + 1) * 1024],
                   AVE[64:65, :])

    b.barrier()
    XQT, SQ2 = sb["XQT"], sb["SQ2"]
    SQ2E = sb["SQ2E"]
    for ec in range(NEC):
        b.op(Sc, lambda e, ec=ec: e.activation(
            out=SQ2E[:, :], in_=sb["XNTA"][:, ec * MYT:(ec + 1) * MYT],
            func=Act.Identity, bias=sb["CST"][:, 0:1],
            scale=sb["SXB"][:, 1:2]), "sS")
        b.wait(V, "sS")
        b.op(V, lambda e: e.tensor_scalar(SQ2E[:, :], SQ2E[:, :], MAGIC,
                                          -32768.0, Alu.subtract,
                                          Alu.max), "sV")
        b.op(V, lambda e, ec=ec: e.tensor_scalar(
            XQT[:, ec * MYT:(ec + 1) * MYT], SQ2E[:, :], 32767.0,
            sb["SXB"][:, 0:1], Alu.min, Alu.mult), "sV")
        b.wait(Sc, "sV")

    e_s, e_v = b.sems["sS"][1], b.sems["sV"][1]
    b.wait(Sy, "sS", e_s)
    b.wait(Sy, "sV", e_v)
    ACL, BBC = sb["ACL"], sb["BBC"]
    ld = b.dma(Sy, sb["SM16"][:, :], sp["sp_sums"][:, :])
    b.wait(V, "dS")
    b.op(V, lambda e: e.reciprocal(sb["C16"][:, :], sb["SM16"][:, :]), "sV")
    b.vsync(V)
    for ec in range(NEC):
        ld = b.dma(Sy, ACL[:, ec * MYT:(ec + 1) * MYT],
                   sp["sp_attn"][ec * 128:(ec + 1) * 128, :])
        b.wait(Sc, "dS")
        b.wait(Sc, "sT")
        b.op(Sc, lambda e, ec=ec: e.activation(
            out=SQ2[:, :], in_=ACL[:, ec * MYT:(ec + 1) * MYT],
            func=Act.Square), "sS")
        b.wait(T, "dS")
        b.wait(T, "sS")
        b.wait(T, "sV")
        for np_ in range(4):
            b.op(T, lambda e, ec=ec, np_=np_: e.matmul(
                PS[0:2, np_ * 512:(np_ + 1) * 512], sb["IND2"][:, :],
                ACL[:, ec * MYT + np_ * 512: ec * MYT + (np_ + 1) * 512],
                start=True, stop=True), "sT")
            b.op(T, lambda e, np_=np_: e.matmul(
                PS[0:2, 2048 + np_ * 512: 2048 + (np_ + 1) * 512],
                sb["IND2"][:, :], SQ2[:, np_ * 512:(np_ + 1) * 512],
                start=True, stop=True), "sT")
        b.wait(V, "sT")
        b.wait(V, "dG")
        b.op(V, lambda e, ec=ec: e.tensor_copy(
            sb["S2T"][:, :], PS[0:2, 0:2048]), "sV")
        b.wait(G, "sV")
        b.dma_g(sp["sp_st"][2 * ec:2 * ec + 2, :], sb["S2T"][:, :])
        b.wait(V, "dG")
        b.op(V, lambda e, ec=ec: e.tensor_copy(
            sb["S2T"][:, :], PS[0:2, 2048:4096]), "sV")
        b.wait(G, "sV")
        b.dma_g(sp["sp_st"][16 + 2 * ec:18 + 2 * ec, :], sb["S2T"][:, :])
    b.wait(Sy, "dG")
    ldA = b.dma(Sy, sb["S16A"][:, :], sp["sp_st"][0:16, :])
    ldB = b.dma(Sy, sb["S16B"][:, :], sp["sp_st"][16:32, :])
    b.wait(V, "dS")
    b.op(V, lambda e: e.tensor_tensor(sb["S16A"][:, :], sb["S16A"][:, :],
                                      sb["C16"][:, :], Alu.mult), "sV")
    b.op(V, lambda e: e.tensor_tensor(sb["A16"][:, :], sb["C16"][:, :],
                                      sb["C16"][:, :], Alu.mult), "sV")
    b.vsync(V)
    b.op(V, lambda e: e.tensor_tensor(sb["S16B"][:, :], sb["S16B"][:, :],
                                      sb["A16"][:, :], Alu.mult), "sV")
    b.wait(T, "sV")
    for np_ in range(4):
        b.op(T, lambda e, np_=np_: e.matmul(
            PS[0:1, np_ * 512:(np_ + 1) * 512], sb["ONES1"][0:16, :],
            sb["S16A"][:, np_ * 512:(np_ + 1) * 512],
            start=True, stop=True), "sT")
        b.op(T, lambda e, np_=np_: e.matmul(
            PS[0:1, 2048 + np_ * 512: 2048 + (np_ + 1) * 512],
            sb["ONES1"][0:16, :], sb["S16B"][:, np_ * 512:(np_ + 1) * 512],
            start=True, stop=True), "sT")
    b.wait(V, "sT")
    b.op(V, lambda e: e.tensor_scalar(ROW[0:1, 0:2048], PS[0:1, 0:2048],
                                      1.0 / E, None, Alu.mult), "sV")
    b.op(V, lambda e: e.tensor_scalar(ROW[0:1, 2048:4096], PS[0:1, 2048:4096],
                                      1.0 / E, None, Alu.mult), "sV")
    mu = ROW[0:1, 0:2048]
    ex2 = ROW[0:1, 2048:4096]
    musq = sb["S16B"][0:1, 0:2048]
    var = sb["SM16"][0:1, 0:2048]
    sd = sb["S16A"][0:1, 0:2048]
    rr = sb["A16"][0:1, 0:2048]
    bb = sb["S16B"][0:1, 0:2048]
    b.vsync(V)
    b.op(V, lambda e: e.tensor_tensor(musq, mu, mu, Alu.mult), "sV")
    b.vsync(V)
    b.op(V, lambda e: e.tensor_tensor(var, ex2, musq, Alu.subtract), "sV")
    b.wait(Sc, "sV")
    b.op(Sc, lambda e: e.activation(out=sd, in_=var, func=Act.Sqrt,
                                    bias=sb["CST"][0:1, 1:2], scale=1.0), "sS")
    b.wait(V, "sS")
    b.op(V, lambda e: e.reciprocal(rr, sd), "sV")
    b.vsync(V)
    b.op(V, lambda e: e.tensor_tensor(bb, mu, rr, Alu.mult), "sV")
    b.vsync(V)
    b.wait(G, "sV")
    b.dma_g(sp["scr_row"][2:3, 0:2048], bb)
    b.dma_g(sp["scr_row"][3:4, 0:2048], rr)
    b.wait(G, "dG")
    b.dma_g(BBC[:, :], sp["scr_row"][2:3, 0:2048].partition_broadcast(128))
    b.dma_g(sb["SM16"][:, :],
          sp["scr_row"][3:4, 0:2048].partition_broadcast(16))
    b.wait(V, "dG")
    b.op(V, lambda e: e.tensor_tensor(sb["A16"][:, :], sb["C16"][:, :],
                                      sb["SM16"][:, :], Alu.mult), "sV")
    b.vsync(V)
    for ec in range(NEC):
        b.wait(T, "sV")
        for np_ in range(4):
            b.op(T, lambda e, ec=ec, np_=np_: e.matmul(
                PS[:, np_ * 512:(np_ + 1) * 512],
                sb["IND16"][:, ec * 128:(ec + 1) * 128],
                sb["A16"][:, np_ * 512:(np_ + 1) * 512],
                start=True, stop=True), "sT")
        b.wait(V, "sT")
        b.wait(Sc, "sV")
        b.op(V, lambda e, ec=ec: e.tensor_tensor(
            SQ2[:, :], ACL[:, ec * MYT:(ec + 1) * MYT], PS[:, 0:2048],
            Alu.mult), "sV")
        b.op(V, lambda e: e.tensor_tensor(SQ2[:, :], SQ2[:, :], BBC[:, :],
                                          Alu.subtract), "sV")
        b.wait(V, "dG")
        b.op(V, lambda e, ec=ec: e.tensor_scalar(
            ACL[:, ec * MYT:(ec + 1) * MYT], SQ2[:, :],
            sb["GAM"][:, ec:ec + 1], sb["BET"][:, ec:ec + 1],
            Alu.mult, Alu.add), "sV")
        b.wait(G, "sV")
        b.dma_g(sp["sp_aln"][ec * 128:(ec + 1) * 128, :],
              ACL[:, ec * MYT:(ec + 1) * MYT])

    b.barrier()

    prep_marks = {}

    def prep_ffn_weight(src, dstT, nchunks, icols, dst_stride, wna, wsc):
        nb = icols // 32
        for c in range(nchunks):
            pb = (c % 2) * icols
            key = (id(wna), c % 2)
            if key in prep_marks:
                b.wait(Sy, "sT", prep_marks[key][0])
                b.wait(Sy, "sV", prep_marks[key][1])
            elif c < 2:
                b.wait(Sy, "sT")
                b.wait(Sy, "sV")
            ld = b.dmas(Sy, "dWN%d" % (c % 2), wna[:, pb:pb + icols],
                        src[c * 128:(c + 1) * 128, :])
            b.wait(V, "dWN%d" % (c % 2), ld)
            wna3 = wna[:, pb:pb + icols].rearrange("p (nb k) -> p nb k", k=32)
            b.op(V, lambda e, wna3=wna3, nb=nb: e.tensor_reduce(
                sb["BMX"][:, 0:nb], wna3, AX, Alu.max,
                apply_absolute_value=True), "sV")
            b.vsync(V)
            b.op(V, lambda e, nb=nb: e.tensor_scalar(
                sb["BSC"][:, 0:nb], sb["BMX"][:, 0:nb], 1.0 / 7.0, 1e-12,
                Alu.mult, Alu.max), "sV")
            b.vsync(V)
            b.op(V, lambda e, nb=nb: e.reciprocal(
                sb["BRS"][:, 0:nb], sb["BSC"][:, 0:nb]), "sV")
            b.vsync(V)
            qsc = wsc[:, 0:icols]
            qsc3 = qsc.rearrange("p (nb k) -> p nb k", k=32)
            b.op(V, lambda e, wna3=wna3, qsc3=qsc3, nb=nb: e.tensor_tensor(
                qsc3, wna3,
                sb["BRS"][:, 0:nb].unsqueeze(2).broadcast_to([128, nb, 32]), Alu.mult), "sV")
            b.op(V, lambda e, qsc=qsc: e.tensor_scalar(
                qsc, qsc, MAGIC, MAGIC, Alu.add, Alu.subtract), "sV")
            b.op(V, lambda e, qsc=qsc: e.tensor_scalar(
                qsc, qsc, -8.0, 7.0, Alu.max, Alu.min), "sV")
            b.op(V, lambda e, qsc3=qsc3, wna3=wna3, nb=nb: e.tensor_tensor(
                wna3, qsc3,
                sb["BSC"][:, 0:nb].unsqueeze(2).broadcast_to([128, nb, 32]), Alu.mult), "sV")
            b.wait(T, "sV")
            for s8 in range(icols // 128):
                pp = PS[:, (s8 % 2) * 512:(s8 % 2) * 512 + 128]
                if s8 >= 2:
                    b.wait(T, "sV")
                b.op(T, lambda e, pp=pp, pb=pb, s8=s8, wna=wna: e.transpose(
                    pp, wna[:, pb + s8 * 128: pb + (s8 + 1) * 128],
                    sb["IDF"][:, :]), "sT")
                b.wait(V, "sT")
                b.op(V, lambda e, pp=pp, s8=s8, c=c: e.tensor_copy(
                    sb[dstT][:, s8 * dst_stride + c * 128:
                             s8 * dst_stride + (c + 1) * 128], pp), "sV")
            prep_marks[(id(wna), c % 2)] = (b.sems["sT"][1], b.sems["sV"][1])

    prep_ffn_weight(dr["wu"], "WUT", NIC, E, I, sb["WNA"], sb["WSC"])
    prep_ffn_weight(dr["wg"], "WGT", NIC, E, I, sb["WNA"], sb["WSC"])

    b.barrier()
    for wt, spd, amcol in (("WUT", sp["sp_up"], 1), ("WGT", sp["sp_gate"], 2)):
        for ic in range(NIC):
            base = (ic % 2) * 4 * 512
            b.wait(T, "sV")
            for ec in range(NEC):
                for nq in range(4):
                    b.op(T, lambda e, base=base, nq=nq, ec=ec, ic=ic, wt=wt:
                         e.matmul(
                             PS[:, base + nq * 512: base + (nq + 1) * 512],
                             sb[wt][:, ec * I + ic * 128:
                                    ec * I + (ic + 1) * 128],
                             sb["XQT"][:, ec * MYT + nq * 512:
                                       ec * MYT + (nq + 1) * 512],
                             start=(ec == 0), stop=(ec == NEC - 1)), "sT")
            fb = (ic % 2) * 2048
            b.wait(V, "sT")
            b.wait(V, "dF%d" % (ic % 2) if ("dF%d" % (ic % 2)) in b.sems else "sT")
            for nq in range(4):
                b.op(V, lambda e, base=base, nq=nq, fb=fb: e.tensor_copy(
                    sb["FEV"][:, fb + nq * 512: fb + (nq + 1) * 512],
                    PS[:, base + nq * 512: base + (nq + 1) * 512]), "sV")
            b.op(V, lambda e, fb=fb: e.tensor_reduce(
                ROW[0:128, 384:385], sb["FEV"][:, fb:fb + 2048], AX, Alu.max,
                apply_absolute_value=True), "sV")
            b.vsync(V)
            b.op(V, lambda e, amcol=amcol: e.tensor_tensor(
                sb["AMX"][:, amcol:amcol + 1], sb["AMX"][:, amcol:amcol + 1],
                ROW[0:128, 384:385], Alu.max), "sV")
            b.vsync(V)
            b.wait(Sc, "sV")
            b.dmas(Sc, "dF%d" % (ic % 2), spd[ic * 128:(ic + 1) * 128, :],
                   sb["FEV"][:, fb:fb + 2048])
    global_scale(1, "ug", 2, "SUGB", 32767.0)

    b.barrier()
    FG, FU, FT1 = sb["FG"], sb["FU"], sb["FT1"]
    FGQ, FSQ, FG2 = sb["FGQ"], sb["FSQ"], sb["FG2"]
    f2m = {}
    vend = {}
    for ic in range(NIC):
        par = ic % 2
        fgb = FG[:, par * MYT:(par + 1) * MYT]
        fub = FU[:, par * MYT:(par + 1) * MYT]
        ft = FT1[:, par * MYT:(par + 1) * MYT]
        fq = FGQ[:, par * MYT:(par + 1) * MYT]
        fs = FSQ[:, par * MYT:(par + 1) * MYT]
        f2 = FG2[:, par * MYT:(par + 1) * MYT]
        if (ic - 2) in f2m:
            b.wait(Sy, "sS", f2m[ic - 2][0])
            b.wait(Sy, "sS", f2m[ic - 2][1])
        ldg = b.dmas(Sy, "dLG%d" % par, fgb,
                     sp["sp_gate"][ic * 128:(ic + 1) * 128, :])
        ldu = b.dmas(Sy, "dLU%d" % par, fub,
                     sp["sp_up"][ic * 128:(ic + 1) * 128, :])
        b.wait(Sc, "dLG%d" % par, ldg)
        if ("dUQ%d" % par) in b.sems:
            b.wait(Sc, "dUQ%d" % par)
        qdq_chunk(fgb, fq, sb["SUGB"][:, 2:3], sb["SUGB"][:, 3:4],
                  -32768.0, 32767.0, ft, war=vend.get(ic - 2, 0))
        fg_done = b.sems["sS"][1]
        b.op(Sc, lambda e, fq=fq, fs=fs: e.activation(
            out=fs, in_=fq, func=Act.Sigmoid), "sS")
        b.op(Sc, lambda e, fs=fs, ft=ft: e.activation(
            out=ft, in_=fs, func=Act.Identity, bias=sb["CST"][:, 0:1],
            scale=sb["CST"][:, 2:3]), "sS")
        b.wait(V, "sS")
        b.op(V, lambda e, ft=ft: e.tensor_scalar(
            ft, ft, MAGIC, 0.0, Alu.subtract, Alu.max), "sV")
        b.op(V, lambda e, fs=fs, ft=ft: e.tensor_scalar(
            fs, ft, 65535.0, 1.0 / 65536.0, Alu.min, Alu.mult), "sV")
        if ("dG2%d" % par) in b.sems:
            b.wait(V, "dG2%d" % par)
        b.op(V, lambda e, f2=f2, fq=fq, fs=fs: e.tensor_tensor(
            f2, fq, fs, Alu.mult), "sV")
        b.op(V, lambda e, f2=f2: e.tensor_reduce(
            ROW[0:128, 384:385], f2, AX, Alu.max,
            apply_absolute_value=True), "sV")
        b.op(V, lambda e: e.tensor_tensor(sb["AMX"][:, 3:4], sb["AMX"][:, 3:4],
                                          ROW[0:128, 384:385], Alu.max), "sV")
        g2k = b.sems["sV"][1]
        b.wait(Sc, "sV", g2k)
        b.dmas(Sc, "dG2%d" % par, sp["sp_g2"][ic * 128:(ic + 1) * 128, :], f2)
        b.wait(Sc, "dLU%d" % par, ldu)
        qdq_chunk(fub, fq, sb["SUGB"][:, 0:1], sb["SUGB"][:, 1:2],
                  -32768.0, 32767.0, ft, war=g2k)
        fu_done = b.sems["sS"][1]
        uqk = b.sems["sV"][1]
        b.wait(Sc, "sV", uqk)
        b.dmas(Sc, "dUQ%d" % par, sp["sp_uq"][ic * 128:(ic + 1) * 128, :], fq)
        vend[ic] = uqk
        f2m[ic] = (fg_done, fu_done)
    global_scale(3, "g2", 1, "S2B", 32767.0)

    b.barrier()
    f3m = {}
    vend3 = {}
    for ic in range(NIC):
        par = ic % 2
        fgb = FG[:, par * MYT:(par + 1) * MYT]
        fub = FU[:, par * MYT:(par + 1) * MYT]
        ft = FT1[:, par * MYT:(par + 1) * MYT]
        fq = FGQ[:, par * MYT:(par + 1) * MYT]
        f2 = FG2[:, par * MYT:(par + 1) * MYT]
        if (ic - 2) in f3m:
            b.wait(Sy, "sS", f3m[ic - 2][0])
            b.wait(Sy, "sV", f3m[ic - 2][1])
        ldg = b.dmas(Sy, "dLG%d" % par, fgb,
                     sp["sp_g2"][ic * 128:(ic + 1) * 128, :])
        ldu = b.dmas(Sy, "dLU%d" % par, fub,
                     sp["sp_uq"][ic * 128:(ic + 1) * 128, :])
        b.wait(Sc, "dLG%d" % par, ldg)
        qdq_chunk(fgb, fq, sb["S2B"][:, 0:1], sb["S2B"][:, 1:2],
                  -32768.0, 32767.0, ft, war=vend3.get(ic - 2, 0))
        fg_done = b.sems["sS"][1]
        b.wait(V, "dLU%d" % par, ldu)
        if ("dO%d" % par) in b.sems:
            b.wait(V, "dO%d" % par)
        b.op(V, lambda e, f2=f2, fq=fq, fub=fub: e.tensor_tensor(
            f2, fq, fub, Alu.mult), "sV")
        fu_done = b.sems["sV"][1]
        b.op(V, lambda e, f2=f2: e.tensor_reduce(
            ROW[0:128, 384:385], f2, AX, Alu.max,
            apply_absolute_value=True), "sV")
        b.op(V, lambda e: e.tensor_tensor(sb["AMX"][:, 4:5], sb["AMX"][:, 4:5],
                                          ROW[0:128, 384:385], Alu.max), "sV")
        ok_ = b.sems["sV"][1]
        b.wait(Sc, "sV", ok_)
        b.dmas(Sc, "dO%d" % par, sp["sp_o"][ic * 128:(ic + 1) * 128, :], f2)
        vend3[ic] = ok_
        f3m[ic] = (fg_done, fu_done)
    global_scale(4, "o", 1, "SOB", 32767.0)

    b.barrier()
    prep_ffn_weight(dr["wd"], "WDT", NEC, I, E, sb["W3A"], sb["W3S"])

    b.barrier()
    FG4, FT4, OQT = sb["FG4"], sb["FT4"], sb["OQT"]
    f4m = {}
    v4 = {}
    for ic in range(NIC):
        par = ic % 2
        fb4 = FG4[:, par * MYT:(par + 1) * MYT]
        if (ic - 2) in f4m:
            b.wait(Sy, "sS", f4m[ic - 2])
        ld = b.dmas(Sy, "dLG%d" % par, fb4,
                    sp["sp_o"][ic * 128:(ic + 1) * 128, :])
        b.wait(Sc, "dLG%d" % par, ld)
        if (ic - 1) in v4:
            b.wait(Sc, "sV", v4[ic - 1])
        b.op(Sc, lambda e, fb4=fb4: e.activation(out=FT4[:, :], in_=fb4,
                                                 func=Act.Identity,
                                                 bias=sb["CST"][:, 0:1],
                                                 scale=sb["SOB"][:, 1:2]), "sS")
        f4m[ic] = b.sems["sS"][1]
        b.wait(V, "sS")
        b.op(V, lambda e: e.tensor_scalar(FT4[:, :], FT4[:, :], MAGIC,
                                          -32768.0, Alu.subtract,
                                          Alu.max), "sV")
        b.op(V, lambda e, ic=ic: e.tensor_scalar(
            OQT[:, ic * MYT:(ic + 1) * MYT], FT4[:, :], 32767.0,
            sb["SOB"][:, 0:1], Alu.min, Alu.mult), "sV")
        v4[ic] = b.sems["sV"][1]

    b.barrier()
    for hc in range(NEC):
        base = (hc % 2) * 4 * 512
        b.wait(Sy, "sV")
        ld = b.dmas(Sy, "dAL", sb["ALC"][:, :],
                    sp["sp_aln"][hc * 128:(hc + 1) * 128, :])
        b.wait(T, "sV")
        for ic in range(NIC):
            for nq in range(4):
                b.op(T, lambda e, base=base, nq=nq, ic=ic, hc=hc: e.matmul(
                    PS[:, base + nq * 512: base + (nq + 1) * 512],
                    sb["WDT"][:, ic * E + hc * 128: ic * E + (hc + 1) * 128],
                    OQT[:, ic * MYT + nq * 512: ic * MYT + (nq + 1) * 512],
                    start=(ic == 0), stop=(ic == NIC - 1)), "sT")
        b.wait(V, "sT")
        b.wait(V, "dAL", ld)
        if "dAF" in b.sems:
            b.wait(V, "dAF")
        for nq in range(4):
            b.op(V, lambda e, base=base, nq=nq: e.tensor_tensor(
                sb["AFE"][:, nq * 512:(nq + 1) * 512],
                PS[:, base + nq * 512: base + (nq + 1) * 512],
                sb["ALC"][:, nq * 512:(nq + 1) * 512], Alu.add), "sV")
        b.wait(Sc, "sV")
        b.dmas(Sc, "dAF", sp["sp_af"][hc * 128:(hc + 1) * 128, :],
               sb["AFE"][:, :])

    b.barrier()
    AFL, XT6, OSB = sb["AFL"], sb["XT6"], sb["OSB"]
    for ec in range(NEC):
        b.dmas(Sy, "dAFL", AFL[:, ec * MYT:(ec + 1) * MYT],
               sp["sp_af"][ec * 128:(ec + 1) * 128, :])
    afl_ld = b.sems["dAFL"][1]
    f6m = {}
    for tt in range(16):
        pb = (tt % 2) * E
        base = (tt % 2) * 1024
        if (tt - 2) in f6m:
            b.wait(Sy, "sV", f6m[tt - 2])
        ld = b.dmas(Sy, "dX6%d" % (tt % 2), XT6[:, pb:pb + E],
                    dr["x_tok"][tt * 128:(tt + 1) * 128, :])
        b.wait(T, "dAFL", afl_ld)
        if (tt - 2) in f6m:
            b.wait(T, "sV", f6m[tt - 2])
        for hc in range(NEC):
            b.op(T, lambda e, base=base, hc=hc, tt=tt: e.transpose(
                PS[:, base + hc * 128: base + (hc + 1) * 128],
                AFL[:, hc * MYT + tt * 128: hc * MYT + (tt + 1) * 128],
                sb["IDF"][:, :]), "sT")
        b.wait(V, "sT")
        b.wait(V, "dX6%d" % (tt % 2), ld)
        if ("dOU%d" % (tt % 2)) in b.sems:
            b.wait(V, "dOU%d" % (tt % 2))
        f6m[tt] = b.op(V, lambda e, base=base, pb=pb: e.tensor_tensor(
            OSB[:, pb:pb + E], PS[:, base:base + E], XT6[:, pb:pb + E],
            Alu.add), "sV")
        b.wait(Sc, "sV", f6m[tt])
        b.dmas(Sc, "dOU%d" % (tt % 2), out_d[tt * 128:(tt + 1) * 128, :],
               OSB[:, pb:pb + E])
    b.wait(G, "dOU0")
    b.wait(G, "dOU1")
    b.wait(G, "dG")


_NC_CACHE = {}


def kernel(L0_attn_x, L0_attn_Wq, L0_attn_Wk, L0_attn_Wv, L0_attn_ln_gamma,
           L0_attn_ln_beta, L0_ffn_w_gate, L0_ffn_w_up, L0_ffn_w_down):
    x = np.ascontiguousarray(np.asarray(L0_attn_x, np.float32))
    wq = np.asarray(L0_attn_Wq, np.float32)
    wk = np.asarray(L0_attn_Wk, np.float32)
    wv = np.asarray(L0_attn_Wv, np.float32)
    gam = np.asarray(L0_attn_ln_gamma, np.float32)
    bet = np.asarray(L0_attn_ln_beta, np.float32)
    wg = np.ascontiguousarray(np.asarray(L0_ffn_w_gate, np.float32))
    wu = np.ascontiguousarray(np.asarray(L0_ffn_w_up, np.float32))
    wd = np.ascontiguousarray(np.asarray(L0_ffn_w_down, np.float32))

    wqT = np.ascontiguousarray(wq.T)
    wkT = np.ascontiguousarray(wk.T)
    wvT = np.ascontiguousarray(wv.T)
    gamma8 = np.ascontiguousarray(gam.reshape(NEC, 128).T)
    beta8 = np.ascontiguousarray(bet.reshape(NEC, 128).T)

    if "nc" not in _NC_CACHE:
        _NC_CACHE["nc"] = build_nc()
    nc = _NC_CACHE["nc"]

    in_maps = []
    for c in range(N_CORES):
        bq, h = divmod(c, 2)
        qoff = h * MYT
        xloc = np.roll(x[bq], -qoff, axis=0)
        if h == 0:
            meta = np.array([[-1.0 / 4096.0, 3968.0 / 4096.0]], np.float32)
        else:
            meta = np.array([[1.0 / 4096.0, 1.0 - 3968.0 / 4096.0]], np.float32)
        in_maps.append({
            "xseqT": np.ascontiguousarray(xloc.T),
            "x_tok": np.ascontiguousarray(xloc),
            "wqT": wqT, "wkT": wkT, "wvT": wvT,
            "wu": wu, "wg": wg, "wd": wd,
            "gamma8": gamma8, "beta8": beta8,
            "meta": meta,
        })

    res = run_bass_kernel_spmd(nc, in_maps, core_ids=list(range(N_CORES)))
    out = np.empty((4, S, E), np.float32)
    for c in range(N_CORES):
        bq, h = divmod(c, 2)
        out[bq, h * MYT:(h + 1) * MYT, :] = res.results[c]["out"]
    return out


# version 14
# speedup vs baseline: 1.6893x; 1.0284x over previous
import numpy as np
from contextlib import ExitStack

import concourse.bass as bass
import concourse.mybir as mybir
from concourse.bass_utils import run_bass_kernel_spmd

F32 = mybir.dt.float32
BF16 = mybir.dt.bfloat16
Alu = mybir.AluOpType
Act = mybir.ActivationFunctionType
AX = mybir.AxisListType.X

MAGIC = 1.5 * 2.0**23
N_CORES = 8
S = 4096
E = 1024
H = 16
I = 3072
MYT = 2048
NEC = E // 128
NIC = I // 128

ENGS = ("sync", "gpsimd", "vector", "scalar", "tensor")
V, Sc, T, G, Sy = "vector", "scalar", "tensor", "gpsimd", "sync"


class Builder:

    def __init__(self, nc):
        self.nc = nc
        self.ops = {e: [] for e in ENGS}
        self.sems = {}

    def mksem(self, name):
        h = self.nc.semaphore(name).__enter__()
        self.sems[name] = [h, 0]

    AUTO_CHAIN = {"vector": "sV", "scalar": "sS", "gpsimd": "sG"}

    def op(self, eng, fn, inc=None, k=1):
        ch = self.AUTO_CHAIN.get(eng)
        if ch is not None:
            self.wait(eng, ch)
        if inc is None:
            self.ops[eng].append(fn)
            return None
        rec = self.sems[inc]
        rec[1] += k
        h = rec[0]
        self.ops[eng].append(lambda e, fn=fn, h=h, k=k: fn(e).then_inc(h, k))
        return rec[1]

    def wait(self, eng, sem, val=None):
        h, c = self.sems[sem]
        v = c if val is None else val
        if v > 0:
            self.ops[eng].append(lambda e, h=h, v=v: e.wait_ge(h, v))

    def dma_g(self, out, in_, **kw):
        return self.dma("gpsimd", out, in_, serial=False, **kw)

    def dmas(self, eng, sem, out, in_, **kw):
        if sem not in self.sems:
            self.mksem(sem)
        return self.op(eng, lambda e: e.dma_start(out=out, in_=in_, **kw),
                       sem, 16)

    def dma(self, eng, out, in_, serial=True, **kw):
        q = {"sync": "dS", "gpsimd": "dG", "scalar": "dA"}[eng]
        n = self.op(eng, lambda e: e.dma_start(out=out, in_=in_, **kw), q, 16)
        if serial:
            self.wait(eng, q, n)
        return n

    def vsync(self, eng):
        sem = {"vector": "sV", "scalar": "sS", "tensor": "sT",
               "gpsimd": "sG"}[eng]
        self.wait(eng, sem)

    def vfiller(self, sb):
        self.op("vector", lambda e: e.tensor_copy(sb["ROW"][0:1, 4095:4096],
                                                  sb["CST"][0:1, 0:1]), "sV")

    def barrier(self, engs=ENGS):
        snap = [(h, c) for (h, c) in self.sems.values() if c > 0]
        for eng in engs:
            for h, v in snap:
                self.ops[eng].append(lambda e, h=h, v=v: e.wait_ge(h, v))


def build_nc():
    nc = bass.Bass("TRN2", target_bir_lowering=False)

    dram = {}
    for nm, shp in [
        ("xseqT", [E, S]), ("x_tok", [S, E]),
        ("wqT", [E, E]), ("wkT", [E, E]), ("wvT", [E, E]),
        ("wu", [I, E]), ("wg", [I, E]), ("wd", [E, I]),
        ("gamma8", [128, NEC]), ("beta8", [128, NEC]),
        ("meta", [1, 2]),
    ]:
        dram[nm] = nc.dram_tensor(nm, shp, F32, kind="ExternalInput").ap()
    out_d = nc.dram_tensor("out", [MYT, E], F32, kind="ExternalOutput").ap()

    sp = {}
    for nm, shp in [
        ("sp_v", [S, H * 65]), ("sp_attn", [E, MYT]), ("sp_sums", [H, MYT]),
        ("sp_aln", [E, MYT]), ("sp_up", [I, MYT]), ("sp_gate", [I, MYT]),
        ("sp_g2", [I, MYT]), ("sp_uq", [I, MYT]), ("sp_o", [I, MYT]),
        ("sp_af", [E, MYT]), ("scr_row", [4, S]), ("sp_st", [32, MYT]),
    ]:
        sp[nm] = nc.dram_tensor(
            nm, shp, BF16 if nm == "sp_v" else F32, kind="Internal").ap()
    sp["cc_in"] = {k: nc.dram_tensor(f"cc{k}_i", [1, w], F32, kind="Internal").ap()
                   for k, w in (("x", 1), ("ug", 2), ("g2", 1), ("o", 1))}
    sp["cc_out"] = {k: nc.dram_tensor(f"cc{k}_o", [1, w], F32, kind="Internal").ap()
                    for k, w in (("x", 1), ("ug", 2), ("g2", 1), ("o", 1))}

    st = ExitStack()
    sb = {}

    def perm(nm, shp, dt):
        sb[nm] = st.enter_context(nc.sbuf_tensor(nm, shp, dt))
        return sb[nm]

    perm("CST", [128, 8], F32)
    perm("SXB", [128, 2], F32)
    perm("SUGB", [128, 4], F32)
    perm("S2B", [128, 2], F32)
    perm("SOB", [128, 2], F32)
    perm("METB", [128, 2], F32)
    perm("AMX", [128, 8], F32)
    perm("IDF", [128, 128], F32)
    perm("IND2", [128, 2], F32)
    perm("IND16", [16, 1024], F32)
    perm("ONES1", [128, 1], F32)
    perm("PIOTA", [128, 1], F32)
    perm("GAM", [128, NEC], F32)
    perm("BET", [128, NEC], F32)
    perm("ROW", [128, 4096], F32)
    perm("BMX", [128, 96], F32)
    perm("BSC", [128, 96], F32)
    perm("BRS", [128, 96], F32)

    TOP = nc.SBUF_PARTITION_SIZE_BYTES
    ZB = 24576

    def at(nm, off, shp, dt):
        end = off + int(np.prod(shp[1:])) * mybir.dt.size(dt)
        assert ZB <= off and end <= TOP, (nm, off, end)
        sb[nm] = nc.alloc_sbuf_tensor_at(nm, shp, dt, offset=off)
        return sb[nm]

    at("WVTS", 40960, [128, NEC * E], BF16)
    at("WKTS", 57344, [128, NEC * E], BF16)
    at("WQTS", 73728, [128, NEC * E], BF16)
    at("XTS", 90112, [128, 4 * E], F32)
    at("XNTA", 90112, [128, NEC * MYT], BF16)
    at("XNTB", 122880, [128, NEC * MYT], BF16)
    at("XC2", 155648, [128, S], F32)
    at("XC2B", 204800, [128, S], F32)
    at("MUBC", 172032, [128, S], F32)
    at("RSBC", 188416, [128, S], F32)
    at("XT2", 204800, [128, 2 * E], F32)
    at("SQT", 212992, [128, E], F32)
    at("STT", 217088, [128, 128], F32)
    at("VEV", 217600, [128, H * 65], BF16)
    at("KF", 155648, [128, NEC * S], BF16)
    at("QF", 40960, [128, NEC * MYT], BF16)
    at("DNE", 122880, [128, 4096], F32)
    at("DFA", 139264, [128, 4096], F32)
    at("SW", 73728, [128, 2 * 1024], F32)
    at("SQ2E", 196608, [128, MYT], F32)
    at("PT", 81920, [128, 2 * 1024], BF16)
    at("AVE", 86016, [128, 1024], F32)
    at("VSB", 221184, [128, 32 * 65], BF16)
    at("XQT", 155648, [128, NEC * MYT], BF16)
    at("SQ2", 188416, [128, MYT], F32)
    at("ACL", 40960, [128, NEC * MYT], F32)
    at("BBC", 106496, [128, MYT], F32)
    at("C16", 114688, [16, MYT], F32)
    at("A16", 122880, [16, MYT], F32)
    at("SM16", 131072, [16, MYT], F32)
    at("S16A", 139264, [16, MYT], F32)
    at("S16B", 147456, [16, MYT], F32)
    at("S2T", 196608, [2, MYT], F32)
    at("WUT", 40960, [128, NEC * I], BF16)
    at("WGT", 90112, [128, NEC * I], BF16)
    at("FEV", 139264, [128, 8 * 512], F32)
    at("WNA", 196608, [128, 2 * E], F32)
    at("WSC", 204800, [128, 2 * E], F32)
    at("FG", 40960, [128, 2 * MYT], F32)
    at("FU", 57344, [128, 2 * MYT], F32)
    at("FT1", 73728, [128, 2 * MYT], F32)
    at("FGQ", 90112, [128, 2 * MYT], F32)
    at("FSQ", 106496, [128, 2 * MYT], F32)
    at("FG2", 122880, [128, 2 * MYT], F32)
    at("FGX", 139264, [128, 2 * MYT], F32)
    at("FUX", 155648, [128, 2 * MYT], F32)
    at("OQT", 40960, [128, NIC * MYT], BF16)
    at("WDT", 139264, [128, NIC * E], BF16)
    at("W3A", 188416, [128, 2 * I], F32)
    at("W3S", 212992, [128, I], F32)
    at("FG4", 188416, [128, 2 * MYT], F32)
    at("FT4", 204800, [128, MYT], F32)
    at("FG4X", 212992, [128, MYT], F32)
    at("ALC", 188416, [128, MYT], F32)
    at("AFE", 196608, [128, MYT], F32)
    at("ALCB", 204800, [128, MYT], F32)
    at("AFL", 40960, [128, NEC * MYT], F32)
    at("XT6", 106496, [128, 2 * E], F32)
    at("OSB", 114688, [128, 2 * E], F32)
    at("XT6X", 122880, [128, 2 * E], F32)

    PS = nc.alloc_psum_tensor("PS", [128, 4096], F32)

    b = Builder(nc)
    for s_ in ("dS", "dG", "dA", "sV", "sS", "sT", "sG", "sCC"):
        b.mksem(s_)

    emit_program(nc, b, sb, PS, dram, out_d, sp)

    with nc.Block() as block:
        @block.sync
        def _(e):
            for f in b.ops["sync"]:
                f(e)

        @block.gpsimd
        def _(e):
            for f in b.ops["gpsimd"]:
                f(e)

        @block.vector
        def _(e):
            for f in b.ops["vector"]:
                f(e)

        @block.scalar
        def _(e):
            for f in b.ops["scalar"]:
                f(e)

        @block.tensor
        def _(e):
            for f in b.ops["tensor"]:
                f(e)

    st.close()
    return nc


def emit_program(nc, b, sb, PS, dr, out_d, sp):
    ROW = sb["ROW"]
    XNT = {0: sb["XNTA"], 1: sb["XNTB"]}

    def xnt_cols(ec, c0, c1):
        assert (c0 // MYT) == ((c1 - 1) // MYT)
        half = c0 // MYT
        t = XNT[half]
        return t[:, ec * MYT + c0 % MYT: ec * MYT + c0 % MYT + (c1 - c0)]

    b.op(V, lambda e: e.memset(sb["CST"][:, 0:1], MAGIC), "sV")
    b.op(V, lambda e: e.memset(sb["CST"][:, 1:2], 1e-5), "sV")
    b.op(V, lambda e: e.memset(sb["CST"][:, 2:3], 65536.0), "sV")
    b.op(V, lambda e: e.memset(sb["CST"][:, 3:4], -1920.0 / 4096.0), "sV")
    b.op(V, lambda e: e.memset(sb["CST"][:, 4:5], 1.0 / 4096.0), "sV")
    b.op(V, lambda e: e.memset(sb["ONES1"][:, :], 1.0), "sV")
    b.op(V, lambda e: e.memset(sb["AMX"][:, :], 0.0), "sV")
    b.op(G, lambda e: e.iota(sb["IDF"][:, :], pattern=[[1, 128]], base=0,
                             channel_multiplier=-1,
                             allow_small_or_imprecise_dtypes=True), "sG")
    b.op(G, lambda e: e.iota(sb["PIOTA"][:, :], pattern=[[0, 1]], base=0,
                             channel_multiplier=1,
                             allow_small_or_imprecise_dtypes=True), "sG")
    b.dma_g(sb["METB"][:, :], dr["meta"][0:1, :].partition_broadcast(128))
    b.wait(V, "sG")
    b.op(V, lambda e: e.tensor_scalar(sb["IND2"][:, 0:1], sb["PIOTA"][:, :],
                                      64.0, None, Alu.is_lt), "sV")
    b.op(V, lambda e: e.tensor_scalar(sb["IND2"][:, 1:2], sb["PIOTA"][:, :],
                                      64.0, None, Alu.is_ge), "sV")
    b.op(V, lambda e: e.tensor_scalar(sb["IDF"][:, :], sb["IDF"][:, :],
                                      0.0, None, Alu.is_equal), "sV")
    b.op(G, lambda e: e.iota(sb["IND16"][:, :],
                             pattern=[[2, 8], [1, 2], [0, 64]], base=0,
                             channel_multiplier=0,
                             allow_small_or_imprecise_dtypes=True), "sG")
    b.wait(V, "sG")
    b.op(V, lambda e: e.tensor_tensor(
        sb["IND16"][:, :], sb["IND16"][:, :],
        sb["PIOTA"][0:16, 0:1].broadcast_to([16, 1024]), Alu.is_equal), "sV")
    b.dma(Sy, sb["GAM"][:, :], dr["gamma8"][:, :])
    b.dma(Sy, sb["BET"][:, :], dr["beta8"][:, :])
    for wi, (nm, dst) in enumerate((("wqT", "WQTS"), ("wkT", "WKTS"),
                                    ("wvT", "WVTS"))):
        for ec in range(NEC):
            b.dmas(G, "dW%d" % wi, sb[dst][:, ec * E:(ec + 1) * E],
                   dr[nm][ec * 128:(ec + 1) * 128, :])

    def global_scale(col, key, packw, dst, qmaxv):
        b.wait(T, "sV")
        b.wait(T, "sS")
        b.op(T, lambda e: e.transpose(PS[0:packw, 1536:1664],
                                      sb["AMX"][:, col:col + packw],
                                      sb["IDF"][:, :]), "sT")
        b.wait(V, "sT")
        b.op(V, lambda e: e.tensor_reduce(ROW[0:packw, 256:257],
                                          PS[0:packw, 1536:1664], AX,
                                          Alu.max), "sV")
        b.wait(G, "sV")
        b.dma_g(sp["cc_in"][key][0:1, :].rearrange("a (p c) -> (a p) c",
                                                    p=packw),
              ROW[0:packw, 256:257])
        b.wait(G, "dG")
        b.op(G, lambda e: e.collective_compute(
            "AllReduce", Alu.max, replica_groups=[list(range(N_CORES))],
            ins=[sp["cc_in"][key].opt()], outs=[sp["cc_out"][key].opt()],
        ), "sCC")
        b.wait(G, "sCC")
        b.dma_g(ROW[0:1, 260:260 + packw], sp["cc_out"][key][0:1, :])
        b.wait(V, "dG")
        b.op(V, lambda e: e.tensor_scalar(
            ROW[0:1, 280:280 + packw], ROW[0:1, 260:260 + packw],
            1.0 / qmaxv, 1e-12, Alu.mult, Alu.max), "sV")
        b.vsync(V)
        b.op(V, lambda e: e.reciprocal(
            ROW[0:1, 288:288 + packw], ROW[0:1, 280:280 + packw]), "sV")
        b.vsync(V)
        for i in range(packw):
            b.op(V, lambda e, i=i: e.tensor_copy(
                ROW[0:1, 300 + 2 * i:301 + 2 * i],
                ROW[0:1, 280 + i:281 + i]), "sV")
            b.op(V, lambda e, i=i: e.tensor_copy(
                ROW[0:1, 301 + 2 * i:302 + 2 * i],
                ROW[0:1, 288 + i:289 + i]), "sV")
        b.wait(G, "sV")
        b.dma_g(sp["scr_row"][2:3, 0:2 * packw], ROW[0:1, 300:300 + 2 * packw])
        b.wait(G, "dG")
        b.dma_g(sb[dst][:, 0:2 * packw],
              sp["scr_row"][2:3, 0:2 * packw].partition_broadcast(128))

    def qdq_chunk(src_ap, dst_ap, s_ap, rs_ap, lo, hi, tmp_ap, war=-1):
        if war < 0:
            b.wait(Sc, "sV")
        elif war > 0:
            b.wait(Sc, "sV", war)
        b.op(Sc, lambda e: e.activation(out=tmp_ap, in_=src_ap,
                                        func=Act.Identity,
                                        bias=sb["CST"][:, 0:1],
                                        scale=rs_ap), "sS")
        b.wait(V, "sS")
        b.op(V, lambda e: e.tensor_scalar(tmp_ap, tmp_ap, MAGIC, lo,
                                          Alu.subtract, Alu.max), "sV")
        b.op(V, lambda e: e.tensor_scalar(dst_ap, tmp_ap, hi, s_ap,
                                          Alu.min, Alu.mult), "sV")
        b.wait(Sc, "sV")

    XTS, SQT, STT, XC2 = sb["XTS"], sb["SQT"], sb["STT"], sb["XC2"]
    ldsA = [b.dmas(Sy, "dXC0_%d" % qq, XC2[:, qq * 1024:(qq + 1) * 1024],
                   dr["xseqT"][0:128, qq * 1024:(qq + 1) * 1024])
            for qq in range(4)]
    amk = {}
    for tt in range(32):
        q = tt % 4
        pb = q * E
        if (tt - 4) in amk:
            b.wait(Sy, "sV", amk[tt - 4][0])
            b.wait(Sy, "sS", amk[tt - 4][1])
        ld = b.dmas(Sy, "dXT%d" % q, XTS[:, pb:pb + E],
                    dr["x_tok"][tt * 128:(tt + 1) * 128, :])
        b.wait(Sc, "dXT%d" % q, ld)
        b.wait(Sc, "sV")
        b.op(Sc, lambda e, pb=pb: e.activation(
            out=SQT[:, :], in_=XTS[:, pb:pb + E], func=Act.Square), "sS")
        b.wait(V, "dXT%d" % q, ld)
        b.op(V, lambda e, pb=pb, tt=tt: e.tensor_reduce(
            STT[:, tt:tt + 1], XTS[:, pb:pb + E], AX, Alu.add), "sV")
        b.wait(V, "sS")
        b.op(V, lambda e, tt=tt: e.tensor_reduce(
            STT[:, 32 + tt:33 + tt], SQT[:, :], AX, Alu.add), "sV")
        amk[tt] = (b.sems["sV"][1], b.sems["sS"][1])
    b.vsync(V)
    b.op(V, lambda e: e.tensor_scalar(STT[:, 0:32], STT[:, 0:32],
                                      1.0 / E, None, Alu.mult), "sV")
    b.op(V, lambda e: e.tensor_scalar(STT[:, 32:64], STT[:, 32:64],
                                      1.0 / E, None, Alu.mult), "sV")
    b.vsync(V)
    b.op(V, lambda e: e.tensor_tensor(STT[:, 64:96], STT[:, 0:32],
                                      STT[:, 0:32], Alu.mult), "sV")
    b.vsync(V)
    b.op(V, lambda e: e.tensor_tensor(STT[:, 64:96], STT[:, 32:64],
                                      STT[:, 64:96], Alu.subtract), "sV")
    b.wait(Sc, "sV")
    b.op(Sc, lambda e: e.activation(out=STT[:, 96:128], in_=STT[:, 64:96],
                                    func=Act.Sqrt, bias=sb["CST"][:, 1:2],
                                    scale=1.0), "sS")
    b.wait(V, "sS")
    b.op(V, lambda e: e.reciprocal(STT[:, 96:128], STT[:, 96:128]), "sV")
    b.vsync(V)
    b.wait(T, "sV")
    b.op(T, lambda e: e.transpose(PS[0:32, 0:128], STT[:, 0:32],
                                  sb["IDF"][:, :]), "sT")
    b.op(T, lambda e: e.transpose(PS[0:32, 512:640], STT[:, 96:128],
                                  sb["IDF"][:, :]), "sT")
    b.wait(V, "sT")
    b.op(V, lambda e: e.tensor_copy(ROW[0:32, 0:128], PS[0:32, 0:128]), "sV")
    b.op(V, lambda e: e.tensor_copy(ROW[0:32, 128:256], PS[0:32, 512:640]), "sV")
    b.wait(G, "sV")
    b.dma_g(sp["scr_row"][0:1, :].rearrange("a (p c) -> (a p) c", p=32),
          ROW[0:32, 0:128])
    b.dma_g(sp["scr_row"][1:2, :].rearrange("a (p c) -> (a p) c", p=32),
          ROW[0:32, 128:256])
    b.wait(G, "dG")
    b.dma_g(sb["MUBC"][:, :], sp["scr_row"][0:1, :].partition_broadcast(128))
    b.dma_g(sb["RSBC"][:, :], sp["scr_row"][1:2, :].partition_broadcast(128))
    b.barrier([Sy])
    nmk = {}
    for ec in range(NEC):
        xcb = sb["XC2"] if ec % 2 == 0 else sb["XC2B"]
        if (ec - 2) in nmk:
            b.wait(Sy, "sV", nmk[ec - 2])
        if ec == 0:
            lds = ldsA
        else:
            lds = [b.dmas(Sy, "dXC%d_%d" % (ec % 2, qq),
                          xcb[:, qq * 1024:(qq + 1) * 1024],
                          dr["xseqT"][ec * 128:(ec + 1) * 128,
                                      qq * 1024:(qq + 1) * 1024])
                   for qq in range(4)]
        for qq in range(4):
            b.wait(V, "dXC%d_%d" % (ec % 2, qq), lds[qq])
        b.wait(V, "dG")
        b.op(V, lambda e, xcb=xcb: e.tensor_tensor(
            xcb[:, :], xcb[:, :], sb["MUBC"][:, :], Alu.subtract), "sV")
        for half in range(2):
            b.op(V, lambda e, xcb=xcb, ec=ec, half=half: e.tensor_tensor(
                XNT[half][:, ec * MYT:(ec + 1) * MYT],
                xcb[:, half * MYT:(half + 1) * MYT],
                sb["RSBC"][:, half * MYT:(half + 1) * MYT], Alu.mult), "sV")
            b.op(V, lambda e, ec=ec, half=half: e.tensor_reduce(
                ROW[0:128, 386:387], XNT[half][:, ec * MYT:(ec + 1) * MYT],
                AX, Alu.max, apply_absolute_value=True), "sV")
            b.op(V, lambda e: e.tensor_tensor(
                sb["AMX"][:, 0:1], sb["AMX"][:, 0:1], ROW[0:128, 386:387],
                Alu.max), "sV")
        nmk[ec] = b.sems["sV"][1]
    global_scale(0, "x", 1, "SXB", 32767.0)

    b.barrier([T])
    VEV = sb["VEV"]
    vem = {}
    for tt in range(32):
        for half in range(2):
            slot = tt * 2 + half
            pt = PS[:, 2048 + (slot % 4) * 512: 2048 + (slot % 4) * 512 + 512]
            if (slot - 4) in vem:
                b.wait(T, "sV", vem[slot - 4])
            for ec in range(NEC):
                b.op(T, lambda e, tt=tt, ec=ec, pt=pt, half=half: e.matmul(
                    pt, xnt_cols(ec, tt * 128, (tt + 1) * 128),
                    sb["WVTS"][:, ec * E + half * 512:
                               ec * E + (half + 1) * 512],
                    start=(ec == 0), stop=(ec == NEC - 1)), "sT")
            b.wait(V, "sT")
            if (slot - 2) in vem:
                b.wait(V, "dSv" if "dSv" in b.sems else "sT")
            vem[slot] = b.op(V, lambda e, pt=pt, half=half: e.tensor_copy(
                VEV[:, half * 8 * 65:(half + 1) * 8 * 65]
                .rearrange("p (h c) -> p h c", c=65)[:, :, 0:64],
                pt.rearrange("p (h c) -> p h c", c=64)), "sV")
        b.op(V, lambda e: e.memset(
            VEV[:, :].rearrange("p (h c) -> p h c", c=65)[:, :, 64:65],
            1.0), "sV")
        mm = b.sems["sV"][1]
        b.wait(Sc, "sV", mm)
        b.dmas(Sc, "dSv", sp["sp_v"][tt * 128:(tt + 1) * 128, :], VEV[:, :])

    b.barrier([T])
    KF, QF = sb["KF"], sb["QF"]
    kqm = {}
    slot = 0
    for pair in range(NEC):
        for nk in range(8):
            pk = PS[:, 3072 + (slot % 2) * 512: 3072 + (slot % 2) * 512 + 512]
            if (slot - 2) in kqm:
                b.wait(T, "sV", kqm[slot - 2])
            for ec in range(NEC):
                b.op(T, lambda e, pk=pk, ec=ec, pair=pair, nk=nk: e.matmul(
                    pk, sb["WKTS"][:, ec * E + pair * 128:
                                   ec * E + (pair + 1) * 128],
                    xnt_cols(ec, nk * 512, (nk + 1) * 512),
                    start=(ec == 0), stop=(ec == NEC - 1)), "sT")
            b.wait(V, "sT")
            kqm[slot] = b.op(V, lambda e, pk=pk, pair=pair, nk=nk:
                             e.tensor_copy(
                KF[:, pair * S + nk * 512: pair * S + (nk + 1) * 512],
                pk), "sV")
            slot += 1
    for pair in range(NEC):
        for nq in range(4):
            pk = PS[:, 3072 + (slot % 2) * 512: 3072 + (slot % 2) * 512 + 512]
            if (slot - 2) in kqm:
                b.wait(T, "sV", kqm[slot - 2])
            for ec in range(NEC):
                b.op(T, lambda e, pk=pk, ec=ec, pair=pair, nq=nq: e.matmul(
                    pk, sb["WQTS"][:, ec * E + pair * 128:
                                   ec * E + (pair + 1) * 128],
                    xnt_cols(ec, nq * 512, (nq + 1) * 512),
                    start=(ec == 0), stop=(ec == NEC - 1)), "sT")
            b.wait(V, "sT")
            kqm[slot] = b.op(V, lambda e, pk=pk, pair=pair, nq=nq:
                             e.tensor_scalar(
                QF[:, pair * MYT + nq * 512: pair * MYT + (nq + 1) * 512],
                pk, 1.0 / 32.0, None, Alu.mult), "sV")
            slot += 1

    b.barrier([G])
    b.op(G, lambda e: e.iota(sb["DNE"][:, :], pattern=[[1, 4096]], base=0,
                             channel_multiplier=-1,
                             allow_small_or_imprecise_dtypes=True), "sG")
    b.op(G, lambda e: e.iota(sb["DFA"][:, :], pattern=[[1, 4096]], base=0,
                             channel_multiplier=-1,
                             allow_small_or_imprecise_dtypes=True), "sG")
    b.wait(V, "sG")
    b.wait(V, "dG")
    b.op(V, lambda e: e.tensor_scalar(sb["DFA"][:, :], sb["DFA"][:, :],
                                      sb["METB"][:, 0:1], sb["METB"][:, 1:2],
                                      Alu.mult, Alu.add), "sV")
    b.wait(Sc, "sG")
    b.wait(Sc, "sV")
    b.op(Sc, lambda e: e.activation(out=sb["DNE"][:, :], in_=sb["DNE"][:, :],
                                    func=Act.Abs, bias=sb["CST"][:, 3:4],
                                    scale=sb["CST"][:, 4:5]), "sS")

    for w in range(64):
        b.op(T, lambda e, w=w: e.matmul(
            PS[:, 0:512], KF[0:64, 0:128], QF[0:64, 0:512],
            start=True, stop=True), "sT")
    b.barrier([T, Sc, G, Sy])
    SW, PT, AVE, VSB = sb["SW"], sb["PT"], sb["AVE"], sb["VSB"]
    DNE, DFA = sb["DNE"], sb["DFA"]
    RCOL = (0, 1024, 3072)
    sc_mark, tt_mark, exp_mark, av_mark, evac_mark = {}, {}, {}, {}, {}
    spill_mark = {}

    def prevkey(it, k, back, n=32):
        kk = k - back
        return (it, kk) if kk >= 0 else (it - 1, kk + n)

    it = -1
    for head in range(H):
        pair, hh = divmod(head, 2)
        hb = hh * 64
        for qh in range(2):
            it += 1
            vb = 0
            if (it - 1, 31) in av_mark:
                b.wait(Sy, "sT", av_mark[(it - 1, 31)])
            vld = b.dmas(Sy, "dVS", VSB[:, vb:vb + 2080]
                         .rearrange("p (k c) -> p k c", c=65),
                         sp["sp_v"][:, head * 65:(head + 1) * 65]
                         .rearrange("(k p) c -> p k c", p=128))
            for step in range(34):
                k_sc, k_tt, k_av = step, step - 1, step - 2
                if k_sc < 32:
                    kc = k_sc
                    pscr = PS[:, RCOL[kc % 3]:RCOL[kc % 3] + 1024]
                    key = prevkey(it, kc, 3)
                    if key in tt_mark:
                        b.wait(T, "sV", tt_mark[key])
                    for jj in range(2):
                        b.op(T, lambda e, pscr=pscr, jj=jj, hb=hb, kc=kc,
                             qh=qh, pair=pair: e.matmul(
                                 pscr[:, jj * 512:(jj + 1) * 512],
                                 KF[hb:hb + 64, pair * S + kc * 128:
                                    pair * S + (kc + 1) * 128],
                                 QF[hb:hb + 64,
                                    pair * MYT + qh * 1024 + jj * 512:
                                    pair * MYT + qh * 1024 + (jj + 1) * 512],
                                 start=True, stop=True), "sT")
                    sc_mark[(it, kc)] = b.sems["sT"][1]
                if 0 <= k_tt < 32:
                    kc = k_tt
                    pscr = PS[:, RCOL[kc % 3]:RCOL[kc % 3] + 1024]
                    swb = SW[:, (kc % 2) * 1024:(kc % 2) * 1024 + 1024]
                    ptb = PT[:, (kc % 2) * 1024:(kc % 2) * 1024 + 1024]
                    if kc < 16:
                        doff = qh * 1024 - kc * 128 + 1920
                        dsl = DNE[:, doff:doff + 1024]
                    else:
                        doff = qh * 1024 - (kc - 16) * 128 + 1920
                        dsl = DFA[:, doff:doff + 1024]
                    b.wait(V, "sT", sc_mark[(it, kc)])
                    key = prevkey(it, kc, 2)
                    if key in exp_mark:
                        b.wait(V, "sS", exp_mark[key])
                    if kc == 0 and "dA1" in b.sems:
                        pass
                    tt_mark[(it, kc)] = b.op(
                        V, lambda e, pscr=pscr, dsl=dsl, swb=swb:
                        e.tensor_tensor(swb, pscr, dsl, Alu.mult), "sV")
                    b.wait(Sc, "sV", tt_mark[(it, kc)])
                    if key in av_mark:
                        b.wait(Sc, "sT", av_mark[key])
                    exp_mark[(it, kc)] = b.op(
                        Sc, lambda e, swb=swb, ptb=ptb: e.activation(
                            out=ptb, in_=swb, func=Act.Exp), "sS")
                if 0 <= k_av < 32:
                    kc = k_av
                    ptb = PT[:, (kc % 2) * 1024:(kc % 2) * 1024 + 1024]
                    b.wait(T, "sS", exp_mark[(it, kc)])
                    if kc == 0:
                        b.wait(T, "dVS", vld)
                        if (it - 1, "evac") in evac_mark:
                            b.wait(T, "sV", evac_mark[(it - 1, "evac")])
                    for jj in range(2):
                        b.op(T, lambda e, jj=jj, kc=kc, ptb=ptb, vb=vb:
                             e.matmul(
                                 PS[0:65, 2048 + jj * 512: 2560 + jj * 512],
                                 VSB[:, vb + kc * 65: vb + kc * 65 + 65],
                                 ptb[:, jj * 512:(jj + 1) * 512],
                                 start=(kc == 0), stop=(kc == 31)), "sT")
                    av_mark[(it, kc)] = b.sems["sT"][1]
            b.wait(V, "sT", av_mark[(it, 31)])
            if it >= 1:
                b.wait(V, "dA1")
                b.wait(V, "dA2")
            evac_mark[(it, "evac")] = b.op(
                V, lambda e: e.tensor_copy(AVE[0:65, :],
                                           PS[0:65, 2048:3072]), "sV")
            b.wait(Sc, "sV", evac_mark[(it, "evac")])
            b.dmas(Sc, "dA1", sp["sp_attn"][head * 64:(head + 1) * 64,
                                            qh * 1024:(qh + 1) * 1024],
                   AVE[0:64, :])
            b.dmas(Sc, "dA2", sp["sp_sums"][head:head + 1,
                                            qh * 1024:(qh + 1) * 1024],
                   AVE[64:65, :])

    b.barrier()
    XQT, SQ2 = sb["XQT"], sb["SQ2"]
    SQ2E = sb["SQ2E"]
    for ec in range(NEC):
        b.op(Sc, lambda e, ec=ec: e.activation(
            out=SQ2E[:, :], in_=sb["XNTA"][:, ec * MYT:(ec + 1) * MYT],
            func=Act.Identity, bias=sb["CST"][:, 0:1],
            scale=sb["SXB"][:, 1:2]), "sS")
        b.wait(V, "sS")
        b.op(V, lambda e: e.tensor_scalar(SQ2E[:, :], SQ2E[:, :], MAGIC,
                                          -32768.0, Alu.subtract,
                                          Alu.max), "sV")
        b.op(V, lambda e, ec=ec: e.tensor_scalar(
            XQT[:, ec * MYT:(ec + 1) * MYT], SQ2E[:, :], 32767.0,
            sb["SXB"][:, 0:1], Alu.min, Alu.mult), "sV")
        b.wait(Sc, "sV")

    e_s, e_v = b.sems["sS"][1], b.sems["sV"][1]
    b.wait(Sy, "sS", e_s)
    b.wait(Sy, "sV", e_v)
    ACL, BBC = sb["ACL"], sb["BBC"]
    ld = b.dma(Sy, sb["SM16"][:, :], sp["sp_sums"][:, :])
    b.wait(V, "dS")
    b.op(V, lambda e: e.reciprocal(sb["C16"][:, :], sb["SM16"][:, :]), "sV")
    b.vsync(V)
    for ec in range(NEC):
        ld = b.dma(Sy, ACL[:, ec * MYT:(ec + 1) * MYT],
                   sp["sp_attn"][ec * 128:(ec + 1) * 128, :])
        b.wait(Sc, "dS")
        b.wait(Sc, "sT")
        b.op(Sc, lambda e, ec=ec: e.activation(
            out=SQ2[:, :], in_=ACL[:, ec * MYT:(ec + 1) * MYT],
            func=Act.Square), "sS")
        b.wait(T, "dS")
        b.wait(T, "sS")
        b.wait(T, "sV")
        for np_ in range(4):
            b.op(T, lambda e, ec=ec, np_=np_: e.matmul(
                PS[0:2, np_ * 512:(np_ + 1) * 512], sb["IND2"][:, :],
                ACL[:, ec * MYT + np_ * 512: ec * MYT + (np_ + 1) * 512],
                start=True, stop=True), "sT")
            b.op(T, lambda e, np_=np_: e.matmul(
                PS[0:2, 2048 + np_ * 512: 2048 + (np_ + 1) * 512],
                sb["IND2"][:, :], SQ2[:, np_ * 512:(np_ + 1) * 512],
                start=True, stop=True), "sT")
        b.wait(V, "sT")
        b.wait(V, "dG")
        b.op(V, lambda e, ec=ec: e.tensor_copy(
            sb["S2T"][:, :], PS[0:2, 0:2048]), "sV")
        b.wait(G, "sV")
        b.dma_g(sp["sp_st"][2 * ec:2 * ec + 2, :], sb["S2T"][:, :])
        b.wait(V, "dG")
        b.op(V, lambda e, ec=ec: e.tensor_copy(
            sb["S2T"][:, :], PS[0:2, 2048:4096]), "sV")
        b.wait(G, "sV")
        b.dma_g(sp["sp_st"][16 + 2 * ec:18 + 2 * ec, :], sb["S2T"][:, :])
    b.wait(Sy, "dG")
    ldA = b.dma(Sy, sb["S16A"][:, :], sp["sp_st"][0:16, :])
    ldB = b.dma(Sy, sb["S16B"][:, :], sp["sp_st"][16:32, :])
    b.wait(V, "dS")
    b.op(V, lambda e: e.tensor_tensor(sb["S16A"][:, :], sb["S16A"][:, :],
                                      sb["C16"][:, :], Alu.mult), "sV")
    b.op(V, lambda e: e.tensor_tensor(sb["A16"][:, :], sb["C16"][:, :],
                                      sb["C16"][:, :], Alu.mult), "sV")
    b.vsync(V)
    b.op(V, lambda e: e.tensor_tensor(sb["S16B"][:, :], sb["S16B"][:, :],
                                      sb["A16"][:, :], Alu.mult), "sV")
    b.wait(T, "sV")
    for np_ in range(4):
        b.op(T, lambda e, np_=np_: e.matmul(
            PS[0:1, np_ * 512:(np_ + 1) * 512], sb["ONES1"][0:16, :],
            sb["S16A"][:, np_ * 512:(np_ + 1) * 512],
            start=True, stop=True), "sT")
        b.op(T, lambda e, np_=np_: e.matmul(
            PS[0:1, 2048 + np_ * 512: 2048 + (np_ + 1) * 512],
            sb["ONES1"][0:16, :], sb["S16B"][:, np_ * 512:(np_ + 1) * 512],
            start=True, stop=True), "sT")
    b.wait(V, "sT")
    b.op(V, lambda e: e.tensor_scalar(ROW[0:1, 0:2048], PS[0:1, 0:2048],
                                      1.0 / E, None, Alu.mult), "sV")
    b.op(V, lambda e: e.tensor_scalar(ROW[0:1, 2048:4096], PS[0:1, 2048:4096],
                                      1.0 / E, None, Alu.mult), "sV")
    mu = ROW[0:1, 0:2048]
    ex2 = ROW[0:1, 2048:4096]
    musq = sb["S16B"][0:1, 0:2048]
    var = sb["SM16"][0:1, 0:2048]
    sd = sb["S16A"][0:1, 0:2048]
    rr = sb["A16"][0:1, 0:2048]
    bb = sb["S16B"][0:1, 0:2048]
    b.vsync(V)
    b.op(V, lambda e: e.tensor_tensor(musq, mu, mu, Alu.mult), "sV")
    b.vsync(V)
    b.op(V, lambda e: e.tensor_tensor(var, ex2, musq, Alu.subtract), "sV")
    b.wait(Sc, "sV")
    b.op(Sc, lambda e: e.activation(out=sd, in_=var, func=Act.Sqrt,
                                    bias=sb["CST"][0:1, 1:2], scale=1.0), "sS")
    b.wait(V, "sS")
    b.op(V, lambda e: e.reciprocal(rr, sd), "sV")
    b.vsync(V)
    b.op(V, lambda e: e.tensor_tensor(bb, mu, rr, Alu.mult), "sV")
    b.vsync(V)
    b.wait(G, "sV")
    b.dma_g(sp["scr_row"][2:3, 0:2048], bb)
    b.dma_g(sp["scr_row"][3:4, 0:2048], rr)
    b.wait(G, "dG")
    b.dma_g(BBC[:, :], sp["scr_row"][2:3, 0:2048].partition_broadcast(128))
    b.dma_g(sb["SM16"][:, :],
          sp["scr_row"][3:4, 0:2048].partition_broadcast(16))
    b.wait(V, "dG")
    b.op(V, lambda e: e.tensor_tensor(sb["A16"][:, :], sb["C16"][:, :],
                                      sb["SM16"][:, :], Alu.mult), "sV")
    b.vsync(V)
    for ec in range(NEC):
        b.wait(T, "sV")
        for np_ in range(4):
            b.op(T, lambda e, ec=ec, np_=np_: e.matmul(
                PS[:, np_ * 512:(np_ + 1) * 512],
                sb["IND16"][:, ec * 128:(ec + 1) * 128],
                sb["A16"][:, np_ * 512:(np_ + 1) * 512],
                start=True, stop=True), "sT")
        b.wait(V, "sT")
        b.wait(Sc, "sV")
        b.op(V, lambda e, ec=ec: e.tensor_tensor(
            SQ2[:, :], ACL[:, ec * MYT:(ec + 1) * MYT], PS[:, 0:2048],
            Alu.mult), "sV")
        b.op(V, lambda e: e.tensor_tensor(SQ2[:, :], SQ2[:, :], BBC[:, :],
                                          Alu.subtract), "sV")
        b.wait(V, "dG")
        b.op(V, lambda e, ec=ec: e.tensor_scalar(
            ACL[:, ec * MYT:(ec + 1) * MYT], SQ2[:, :],
            sb["GAM"][:, ec:ec + 1], sb["BET"][:, ec:ec + 1],
            Alu.mult, Alu.add), "sV")
        b.wait(G, "sV")
        b.dma_g(sp["sp_aln"][ec * 128:(ec + 1) * 128, :],
              ACL[:, ec * MYT:(ec + 1) * MYT])

    b.barrier()

    prep_marks = {}

    def prep_ffn_weight(src, dstT, nchunks, icols, dst_stride, wna, wsc):
        nb = icols // 32
        for c in range(nchunks):
            pb = (c % 2) * icols
            key = (id(wna), c % 2)
            if key in prep_marks:
                b.wait(Sy, "sT", prep_marks[key][0])
                b.wait(Sy, "sV", prep_marks[key][1])
            elif c < 2:
                b.wait(Sy, "sT")
                b.wait(Sy, "sV")
            ld = b.dmas(Sy, "dWN%d" % (c % 2), wna[:, pb:pb + icols],
                        src[c * 128:(c + 1) * 128, :])
            b.wait(V, "dWN%d" % (c % 2), ld)
            wna3 = wna[:, pb:pb + icols].rearrange("p (nb k) -> p nb k", k=32)
            b.op(V, lambda e, wna3=wna3, nb=nb: e.tensor_reduce(
                sb["BMX"][:, 0:nb], wna3, AX, Alu.max,
                apply_absolute_value=True), "sV")
            b.vsync(V)
            b.op(V, lambda e, nb=nb: e.tensor_scalar(
                sb["BSC"][:, 0:nb], sb["BMX"][:, 0:nb], 1.0 / 7.0, 1e-12,
                Alu.mult, Alu.max), "sV")
            b.vsync(V)
            b.op(V, lambda e, nb=nb: e.reciprocal(
                sb["BRS"][:, 0:nb], sb["BSC"][:, 0:nb]), "sV")
            b.vsync(V)
            qsc = wsc[:, 0:icols]
            qsc3 = qsc.rearrange("p (nb k) -> p nb k", k=32)
            b.op(V, lambda e, wna3=wna3, qsc3=qsc3, nb=nb: e.tensor_tensor(
                qsc3, wna3,
                sb["BRS"][:, 0:nb].unsqueeze(2).broadcast_to([128, nb, 32]), Alu.mult), "sV")
            b.op(V, lambda e, qsc=qsc: e.tensor_scalar(
                qsc, qsc, MAGIC, MAGIC, Alu.add, Alu.subtract), "sV")
            b.op(V, lambda e, qsc=qsc: e.tensor_scalar(
                qsc, qsc, -8.0, 7.0, Alu.max, Alu.min), "sV")
            b.op(V, lambda e, qsc3=qsc3, wna3=wna3, nb=nb: e.tensor_tensor(
                wna3, qsc3,
                sb["BSC"][:, 0:nb].unsqueeze(2).broadcast_to([128, nb, 32]), Alu.mult), "sV")
            b.wait(T, "sV")
            for s8 in range(icols // 128):
                pp = PS[:, (s8 % 2) * 512:(s8 % 2) * 512 + 128]
                if s8 >= 2:
                    b.wait(T, "sV")
                b.op(T, lambda e, pp=pp, pb=pb, s8=s8, wna=wna: e.transpose(
                    pp, wna[:, pb + s8 * 128: pb + (s8 + 1) * 128],
                    sb["IDF"][:, :]), "sT")
                b.wait(V, "sT")
                b.op(V, lambda e, pp=pp, s8=s8, c=c: e.tensor_copy(
                    sb[dstT][:, s8 * dst_stride + c * 128:
                             s8 * dst_stride + (c + 1) * 128], pp), "sV")
            prep_marks[(id(wna), c % 2)] = (b.sems["sT"][1], b.sems["sV"][1])

    prep_ffn_weight(dr["wu"], "WUT", NIC, E, I, sb["WNA"], sb["WSC"])
    prep_ffn_weight(dr["wg"], "WGT", NIC, E, I, sb["WNA"], sb["WSC"])

    b.barrier()
    for wt, spd, amcol in (("WUT", sp["sp_up"], 1), ("WGT", sp["sp_gate"], 2)):
        for ic in range(NIC):
            base = (ic % 2) * 4 * 512
            b.wait(T, "sV")
            for ec in range(NEC):
                for nq in range(4):
                    b.op(T, lambda e, base=base, nq=nq, ec=ec, ic=ic, wt=wt:
                         e.matmul(
                             PS[:, base + nq * 512: base + (nq + 1) * 512],
                             sb[wt][:, ec * I + ic * 128:
                                    ec * I + (ic + 1) * 128],
                             sb["XQT"][:, ec * MYT + nq * 512:
                                       ec * MYT + (nq + 1) * 512],
                             start=(ec == 0), stop=(ec == NEC - 1)), "sT")
            fb = (ic % 2) * 2048
            b.wait(V, "sT")
            b.wait(V, "dF%d" % (ic % 2) if ("dF%d" % (ic % 2)) in b.sems else "sT")
            for nq in range(4):
                b.op(V, lambda e, base=base, nq=nq, fb=fb: e.tensor_copy(
                    sb["FEV"][:, fb + nq * 512: fb + (nq + 1) * 512],
                    PS[:, base + nq * 512: base + (nq + 1) * 512]), "sV")
            b.op(V, lambda e, fb=fb: e.tensor_reduce(
                ROW[0:128, 384:385], sb["FEV"][:, fb:fb + 2048], AX, Alu.max,
                apply_absolute_value=True), "sV")
            b.vsync(V)
            b.op(V, lambda e, amcol=amcol: e.tensor_tensor(
                sb["AMX"][:, amcol:amcol + 1], sb["AMX"][:, amcol:amcol + 1],
                ROW[0:128, 384:385], Alu.max), "sV")
            b.vsync(V)
            b.wait(Sc, "sV")
            b.dmas(Sc, "dF%d" % (ic % 2), spd[ic * 128:(ic + 1) * 128, :],
                   sb["FEV"][:, fb:fb + 2048])
    global_scale(1, "ug", 2, "SUGB", 32767.0)

    b.barrier()
    FG, FU, FT1 = sb["FG"], sb["FU"], sb["FT1"]
    FGQ, FSQ, FG2 = sb["FGQ"], sb["FSQ"], sb["FG2"]
    f2m = {}
    vend = {}
    for ic in range(NIC):
        par = ic % 2
        q4 = ic % 4
        fgb = (FG if q4 < 2 else sb["FGX"])[:, par * MYT:(par + 1) * MYT]
        fub = (FU if q4 < 2 else sb["FUX"])[:, par * MYT:(par + 1) * MYT]
        ft = FT1[:, par * MYT:(par + 1) * MYT]
        fq = FGQ[:, par * MYT:(par + 1) * MYT]
        fs = FSQ[:, par * MYT:(par + 1) * MYT]
        f2 = FG2[:, par * MYT:(par + 1) * MYT]
        if (ic - 4) in f2m:
            b.wait(Sy, "sS", f2m[ic - 4][0])
            b.wait(Sy, "sS", f2m[ic - 4][1])
        ldg = b.dmas(Sy, "dLG%d" % q4, fgb,
                     sp["sp_gate"][ic * 128:(ic + 1) * 128, :])
        ldu = b.dmas(Sy, "dLU%d" % q4, fub,
                     sp["sp_up"][ic * 128:(ic + 1) * 128, :])
        b.wait(Sc, "dLG%d" % q4, ldg)
        if ("dUQ%d" % par) in b.sems:
            b.wait(Sc, "dUQ%d" % par)
        qdq_chunk(fgb, fq, sb["SUGB"][:, 2:3], sb["SUGB"][:, 3:4],
                  -32768.0, 32767.0, ft, war=vend.get(ic - 2, 0))
        fg_done = b.sems["sS"][1]
        b.op(Sc, lambda e, fq=fq, fs=fs: e.activation(
            out=fs, in_=fq, func=Act.Sigmoid), "sS")
        b.op(Sc, lambda e, fs=fs, ft=ft: e.activation(
            out=ft, in_=fs, func=Act.Identity, bias=sb["CST"][:, 0:1],
            scale=sb["CST"][:, 2:3]), "sS")
        b.wait(V, "sS")
        b.op(V, lambda e, ft=ft: e.tensor_scalar(
            ft, ft, MAGIC, 0.0, Alu.subtract, Alu.max), "sV")
        b.op(V, lambda e, fs=fs, ft=ft: e.tensor_scalar(
            fs, ft, 65535.0, 1.0 / 65536.0, Alu.min, Alu.mult), "sV")
        if ("dG2%d" % par) in b.sems:
            b.wait(V, "dG2%d" % par)
        b.op(V, lambda e, f2=f2, fq=fq, fs=fs: e.tensor_tensor(
            f2, fq, fs, Alu.mult), "sV")
        b.op(V, lambda e, f2=f2: e.tensor_reduce(
            ROW[0:128, 384:385], f2, AX, Alu.max,
            apply_absolute_value=True), "sV")
        b.op(V, lambda e: e.tensor_tensor(sb["AMX"][:, 3:4], sb["AMX"][:, 3:4],
                                          ROW[0:128, 384:385], Alu.max), "sV")
        g2k = b.sems["sV"][1]
        b.wait(Sc, "sV", g2k)
        b.dmas(Sc, "dG2%d" % par, sp["sp_g2"][ic * 128:(ic + 1) * 128, :], f2)
        b.wait(Sc, "dLU%d" % q4, ldu)
        qdq_chunk(fub, fq, sb["SUGB"][:, 0:1], sb["SUGB"][:, 1:2],
                  -32768.0, 32767.0, ft, war=g2k)
        fu_done = b.sems["sS"][1]
        uqk = b.sems["sV"][1]
        b.wait(Sc, "sV", uqk)
        b.dmas(Sc, "dUQ%d" % par, sp["sp_uq"][ic * 128:(ic + 1) * 128, :], fq)
        vend[ic] = uqk
        f2m[ic] = (fg_done, fu_done)
    global_scale(3, "g2", 1, "S2B", 32767.0)

    b.barrier()
    f3m = {}
    vend3 = {}
    for ic in range(NIC):
        par = ic % 2
        q4 = ic % 4
        fgb = (FG if q4 < 2 else sb["FGX"])[:, par * MYT:(par + 1) * MYT]
        fub = (FU if q4 < 2 else sb["FUX"])[:, par * MYT:(par + 1) * MYT]
        ft = FT1[:, par * MYT:(par + 1) * MYT]
        fq = FGQ[:, par * MYT:(par + 1) * MYT]
        f2 = FG2[:, par * MYT:(par + 1) * MYT]
        if (ic - 4) in f3m:
            b.wait(Sy, "sS", f3m[ic - 4][0])
            b.wait(Sy, "sV", f3m[ic - 4][1])
        ldg = b.dmas(Sy, "dLG%d" % q4, fgb,
                     sp["sp_g2"][ic * 128:(ic + 1) * 128, :])
        ldu = b.dmas(Sy, "dLU%d" % q4, fub,
                     sp["sp_uq"][ic * 128:(ic + 1) * 128, :])
        b.wait(Sc, "dLG%d" % q4, ldg)
        qdq_chunk(fgb, fq, sb["S2B"][:, 0:1], sb["S2B"][:, 1:2],
                  -32768.0, 32767.0, ft, war=vend3.get(ic - 2, 0))
        fg_done = b.sems["sS"][1]
        b.wait(V, "dLU%d" % q4, ldu)
        if ("dO%d" % par) in b.sems:
            b.wait(V, "dO%d" % par)
        b.op(V, lambda e, f2=f2, fq=fq, fub=fub: e.tensor_tensor(
            f2, fq, fub, Alu.mult), "sV")
        fu_done = b.sems["sV"][1]
        b.op(V, lambda e, f2=f2: e.tensor_reduce(
            ROW[0:128, 384:385], f2, AX, Alu.max,
            apply_absolute_value=True), "sV")
        b.op(V, lambda e: e.tensor_tensor(sb["AMX"][:, 4:5], sb["AMX"][:, 4:5],
                                          ROW[0:128, 384:385], Alu.max), "sV")
        ok_ = b.sems["sV"][1]
        b.wait(Sc, "sV", ok_)
        b.dmas(Sc, "dO%d" % par, sp["sp_o"][ic * 128:(ic + 1) * 128, :], f2)
        vend3[ic] = ok_
        f3m[ic] = (fg_done, fu_done)
    global_scale(4, "o", 1, "SOB", 32767.0)

    b.barrier()
    prep_ffn_weight(dr["wd"], "WDT", NEC, I, E, sb["W3A"], sb["W3S"])

    b.barrier()
    FG4, FT4, OQT = sb["FG4"], sb["FT4"], sb["OQT"]
    f4m = {}
    v4 = {}
    for ic in range(NIC):
        par = ic % 2
        m3 = ic % 3
        fb4 = (FG4[:, m3 * MYT:(m3 + 1) * MYT] if m3 < 2
               else sb["FG4X"][:, 0:MYT])
        if (ic - 3) in f4m:
            b.wait(Sy, "sS", f4m[ic - 3])
        ld = b.dmas(Sy, "dLG%d" % m3, fb4,
                    sp["sp_o"][ic * 128:(ic + 1) * 128, :])
        b.wait(Sc, "dLG%d" % m3, ld)
        if (ic - 1) in v4:
            b.wait(Sc, "sV", v4[ic - 1])
        b.op(Sc, lambda e, fb4=fb4: e.activation(out=FT4[:, :], in_=fb4,
                                                 func=Act.Identity,
                                                 bias=sb["CST"][:, 0:1],
                                                 scale=sb["SOB"][:, 1:2]), "sS")
        f4m[ic] = b.sems["sS"][1]
        b.wait(V, "sS")
        b.op(V, lambda e: e.tensor_scalar(FT4[:, :], FT4[:, :], MAGIC,
                                          -32768.0, Alu.subtract,
                                          Alu.max), "sV")
        b.op(V, lambda e, ic=ic: e.tensor_scalar(
            OQT[:, ic * MYT:(ic + 1) * MYT], FT4[:, :], 32767.0,
            sb["SOB"][:, 0:1], Alu.min, Alu.mult), "sV")
        v4[ic] = b.sems["sV"][1]

    b.barrier()
    a5m = {}
    for hc in range(NEC):
        base = (hc % 2) * 4 * 512
        alcb = sb["ALC"] if hc % 2 == 0 else sb["ALCB"]
        if (hc - 2) in a5m:
            b.wait(Sy, "sV", a5m[hc - 2])
        ld = b.dmas(Sy, "dAL%d" % (hc % 2), alcb[:, :],
                    sp["sp_aln"][hc * 128:(hc + 1) * 128, :])
        if (hc - 2) in a5m:
            b.wait(T, "sV", a5m[hc - 2])
        for ic in range(NIC):
            for nq in range(4):
                b.op(T, lambda e, base=base, nq=nq, ic=ic, hc=hc: e.matmul(
                    PS[:, base + nq * 512: base + (nq + 1) * 512],
                    sb["WDT"][:, ic * E + hc * 128: ic * E + (hc + 1) * 128],
                    OQT[:, ic * MYT + nq * 512: ic * MYT + (nq + 1) * 512],
                    start=(ic == 0), stop=(ic == NIC - 1)), "sT")
        b.wait(V, "sT")
        b.wait(V, "dAL%d" % (hc % 2), ld)
        if "dAF" in b.sems:
            b.wait(V, "dAF")
        for nq in range(4):
            b.op(V, lambda e, base=base, nq=nq, alcb=alcb: e.tensor_tensor(
                sb["AFE"][:, nq * 512:(nq + 1) * 512],
                PS[:, base + nq * 512: base + (nq + 1) * 512],
                alcb[:, nq * 512:(nq + 1) * 512], Alu.add), "sV")
        a5m[hc] = b.sems["sV"][1]
        b.wait(Sc, "sV")
        b.dmas(Sc, "dAF", sp["sp_af"][hc * 128:(hc + 1) * 128, :],
               sb["AFE"][:, :])

    b.barrier()
    AFL, XT6, OSB = sb["AFL"], sb["XT6"], sb["OSB"]
    aflm = {}
    for slab in range(16):
        sname = "dAFL%d" % (slab % 4)
        if (slab - 4) in aflm:
            b.wait(Sy, sname, aflm[slab - 4])
        for hc in range(NEC):
            ld = b.dmas(Sy, sname,
                        AFL[:, hc * MYT + slab * 128:
                            hc * MYT + (slab + 1) * 128],
                        sp["sp_af"][hc * 128:(hc + 1) * 128,
                                    slab * 128:(slab + 1) * 128])
        aflm[slab] = ld
    f6m = {}
    for tt in range(16):
        par = tt % 2
        q4 = tt % 4
        pb = par * E
        base = par * 1024
        xt6b = (XT6 if q4 < 2 else sb["XT6X"])[:, pb:pb + E]
        if (tt - 4) in f6m:
            b.wait(Sy, "sV", f6m[tt - 4])
        ld = b.dmas(Sy, "dX6%d" % q4, xt6b,
                    dr["x_tok"][tt * 128:(tt + 1) * 128, :])
        b.wait(T, "dAFL%d" % (tt % 4), aflm[tt])
        if (tt - 2) in f6m:
            b.wait(T, "sV", f6m[tt - 2])
        for hc in range(NEC):
            b.op(T, lambda e, base=base, hc=hc, tt=tt: e.transpose(
                PS[:, base + hc * 128: base + (hc + 1) * 128],
                AFL[:, hc * MYT + tt * 128: hc * MYT + (tt + 1) * 128],
                sb["IDF"][:, :]), "sT")
        b.wait(V, "sT")
        b.wait(V, "dX6%d" % q4, ld)
        if ("dOU%d" % par) in b.sems:
            b.wait(V, "dOU%d" % par)
        f6m[tt] = b.op(V, lambda e, base=base, pb=pb, xt6b=xt6b:
                       e.tensor_tensor(
                           OSB[:, pb:pb + E], PS[:, base:base + E], xt6b,
                           Alu.add), "sV")
        b.wait(Sc, "sV", f6m[tt])
        b.dmas(Sc, "dOU%d" % par, out_d[tt * 128:(tt + 1) * 128, :],
               OSB[:, pb:pb + E])
    b.wait(G, "dOU0")
    b.wait(G, "dOU1")
    b.wait(G, "dG")


_NC_CACHE = {}


def kernel(L0_attn_x, L0_attn_Wq, L0_attn_Wk, L0_attn_Wv, L0_attn_ln_gamma,
           L0_attn_ln_beta, L0_ffn_w_gate, L0_ffn_w_up, L0_ffn_w_down):
    x = np.ascontiguousarray(np.asarray(L0_attn_x, np.float32))
    wq = np.asarray(L0_attn_Wq, np.float32)
    wk = np.asarray(L0_attn_Wk, np.float32)
    wv = np.asarray(L0_attn_Wv, np.float32)
    gam = np.asarray(L0_attn_ln_gamma, np.float32)
    bet = np.asarray(L0_attn_ln_beta, np.float32)
    wg = np.ascontiguousarray(np.asarray(L0_ffn_w_gate, np.float32))
    wu = np.ascontiguousarray(np.asarray(L0_ffn_w_up, np.float32))
    wd = np.ascontiguousarray(np.asarray(L0_ffn_w_down, np.float32))

    wqT = np.ascontiguousarray(wq.T)
    wkT = np.ascontiguousarray(wk.T)
    wvT = np.ascontiguousarray(wv.T)
    gamma8 = np.ascontiguousarray(gam.reshape(NEC, 128).T)
    beta8 = np.ascontiguousarray(bet.reshape(NEC, 128).T)

    if "nc" not in _NC_CACHE:
        _NC_CACHE["nc"] = build_nc()
    nc = _NC_CACHE["nc"]

    in_maps = []
    for c in range(N_CORES):
        bq, h = divmod(c, 2)
        qoff = h * MYT
        xloc = np.roll(x[bq], -qoff, axis=0)
        if h == 0:
            meta = np.array([[-1.0 / 4096.0, 3968.0 / 4096.0]], np.float32)
        else:
            meta = np.array([[1.0 / 4096.0, 1.0 - 3968.0 / 4096.0]], np.float32)
        in_maps.append({
            "xseqT": np.ascontiguousarray(xloc.T),
            "x_tok": np.ascontiguousarray(xloc),
            "wqT": wqT, "wkT": wkT, "wvT": wvT,
            "wu": wu, "wg": wg, "wd": wd,
            "gamma8": gamma8, "beta8": beta8,
            "meta": meta,
        })

    res = run_bass_kernel_spmd(nc, in_maps, core_ids=list(range(N_CORES)))
    out = np.empty((4, S, E), np.float32)
    for c in range(N_CORES):
        bq, h = divmod(c, 2)
        out[bq, h * MYT:(h + 1) * MYT, :] = res.results[c]["out"]
    return out


# version 16
# speedup vs baseline: 1.7146x; 1.0150x over previous
import numpy as np
from contextlib import ExitStack

import concourse.bass as bass
import concourse.mybir as mybir
from concourse.bass_utils import run_bass_kernel_spmd

F32 = mybir.dt.float32
BF16 = mybir.dt.bfloat16
Alu = mybir.AluOpType
Act = mybir.ActivationFunctionType
AX = mybir.AxisListType.X

MAGIC = 1.5 * 2.0**23
N_CORES = 8
S = 4096
E = 1024
H = 16
I = 3072
MYT = 2048
NEC = E // 128
NIC = I // 128

ENGS = ("sync", "gpsimd", "vector", "scalar", "tensor")
V, Sc, T, G, Sy = "vector", "scalar", "tensor", "gpsimd", "sync"


class Builder:

    def __init__(self, nc):
        self.nc = nc
        self.ops = {e: [] for e in ENGS}
        self.sems = {}

    def mksem(self, name):
        h = self.nc.semaphore(name).__enter__()
        self.sems[name] = [h, 0]

    AUTO_CHAIN = {"vector": "sV", "scalar": "sS", "gpsimd": "sG"}

    def op(self, eng, fn, inc=None, k=1):
        ch = self.AUTO_CHAIN.get(eng)
        if ch is not None:
            self.wait(eng, ch)
        if inc is None:
            self.ops[eng].append(fn)
            return None
        rec = self.sems[inc]
        rec[1] += k
        h = rec[0]
        self.ops[eng].append(lambda e, fn=fn, h=h, k=k: fn(e).then_inc(h, k))
        return rec[1]

    def wait(self, eng, sem, val=None):
        h, c = self.sems[sem]
        v = c if val is None else val
        if v > 0:
            self.ops[eng].append(lambda e, h=h, v=v: e.wait_ge(h, v))

    def dma_g(self, out, in_, **kw):
        return self.dma("gpsimd", out, in_, serial=False, **kw)

    def dmas(self, eng, sem, out, in_, **kw):
        if sem not in self.sems:
            self.mksem(sem)
        return self.op(eng, lambda e: e.dma_start(out=out, in_=in_, **kw),
                       sem, 16)

    def dma(self, eng, out, in_, serial=True, **kw):
        q = {"sync": "dS", "gpsimd": "dG", "scalar": "dA"}[eng]
        n = self.op(eng, lambda e: e.dma_start(out=out, in_=in_, **kw), q, 16)
        if serial:
            self.wait(eng, q, n)
        return n

    def vsync(self, eng):
        sem = {"vector": "sV", "scalar": "sS", "tensor": "sT",
               "gpsimd": "sG"}[eng]
        self.wait(eng, sem)

    def vfiller(self, sb):
        self.op("vector", lambda e: e.tensor_copy(sb["ROW"][0:1, 4095:4096],
                                                  sb["CST"][0:1, 0:1]), "sV")

    def barrier(self, engs=ENGS):
        snap = [(h, c) for (h, c) in self.sems.values() if c > 0]
        for eng in engs:
            for h, v in snap:
                self.ops[eng].append(lambda e, h=h, v=v: e.wait_ge(h, v))


def build_nc():
    nc = bass.Bass("TRN2", target_bir_lowering=False)

    dram = {}
    for nm, shp in [
        ("xseqT", [E, S]), ("x_tok", [S, E]),
        ("wqT", [E, E]), ("wkT", [E, E]), ("wvT", [E, E]),
        ("wu", [I, E]), ("wg", [I, E]), ("wd", [E, I]),
        ("gamma8", [128, NEC]), ("beta8", [128, NEC]),
        ("meta", [1, 2]),
    ]:
        dram[nm] = nc.dram_tensor(nm, shp, F32, kind="ExternalInput").ap()
    out_d = nc.dram_tensor("out", [MYT, E], F32, kind="ExternalOutput").ap()

    sp = {}
    for nm, shp in [
        ("sp_v", [S, H * 65]), ("sp_attn", [E, MYT]), ("sp_sums", [H, MYT]),
        ("sp_aln", [E, MYT]), ("sp_up", [I, MYT]), ("sp_gate", [I, MYT]),
        ("sp_g2", [I, MYT]), ("sp_uq", [I, MYT]), ("sp_o", [I, MYT]),
        ("sp_af", [E, MYT]), ("scr_row", [4, S]), ("sp_st", [32, MYT]),
    ]:
        sp[nm] = nc.dram_tensor(
            nm, shp, BF16 if nm == "sp_v" else F32, kind="Internal").ap()
    sp["cc_in"] = {k: nc.dram_tensor(f"cc{k}_i", [1, w], F32, kind="Internal").ap()
                   for k, w in (("x", 1), ("ug", 2), ("g2", 1), ("o", 1))}
    sp["cc_out"] = {k: nc.dram_tensor(f"cc{k}_o", [1, w], F32, kind="Internal").ap()
                    for k, w in (("x", 1), ("ug", 2), ("g2", 1), ("o", 1))}

    st = ExitStack()
    sb = {}

    def perm(nm, shp, dt):
        sb[nm] = st.enter_context(nc.sbuf_tensor(nm, shp, dt))
        return sb[nm]

    perm("CST", [128, 8], F32)
    perm("SXB", [128, 2], F32)
    perm("SUGB", [128, 4], F32)
    perm("S2B", [128, 2], F32)
    perm("SOB", [128, 2], F32)
    perm("METB", [128, 2], F32)
    perm("AMX", [128, 8], F32)
    perm("IDF", [128, 128], F32)
    perm("IND2", [128, 2], F32)
    perm("IND16", [16, 1024], F32)
    perm("ONES1", [128, 1], F32)
    perm("PIOTA", [128, 1], F32)
    perm("GAM", [128, NEC], F32)
    perm("BET", [128, NEC], F32)
    perm("ROW", [128, 4096], F32)
    perm("BMX", [128, 96], F32)
    perm("BSC", [128, 96], F32)
    perm("BRS", [128, 96], F32)

    TOP = nc.SBUF_PARTITION_SIZE_BYTES
    ZB = 24576

    def at(nm, off, shp, dt):
        end = off + int(np.prod(shp[1:])) * mybir.dt.size(dt)
        assert ZB <= off and end <= TOP, (nm, off, end)
        sb[nm] = nc.alloc_sbuf_tensor_at(nm, shp, dt, offset=off)
        return sb[nm]

    at("WVTS", 40960, [128, NEC * E], BF16)
    at("WKTS", 57344, [128, NEC * E], BF16)
    at("WQTS", 73728, [128, NEC * E], BF16)
    at("XTS", 90112, [128, 4 * E], F32)
    at("XNTA", 90112, [128, NEC * MYT], BF16)
    at("XNTB", 122880, [128, NEC * MYT], BF16)
    at("XC2", 155648, [128, S], F32)
    at("XC2B", 204800, [128, S], F32)
    at("MUBC", 172032, [128, S], F32)
    at("RSBC", 188416, [128, S], F32)
    at("XT2", 204800, [128, 2 * E], F32)
    at("SQT", 212992, [128, E], F32)
    at("STT", 217088, [128, 128], F32)
    at("VEV", 217600, [128, H * 65], BF16)
    at("KF", 155648, [128, NEC * S], BF16)
    at("QF", 40960, [128, NEC * MYT], BF16)
    at("DNE", 122880, [128, 4096], F32)
    at("DFA", 139264, [128, 4096], F32)
    at("SW", 73728, [128, 2 * 1024], F32)
    at("SQ2E", 196608, [128, MYT], F32)
    at("PT", 81920, [128, 2 * 1024], BF16)
    at("AVE", 86016, [128, 1024], F32)
    at("VSB", 221184, [128, 32 * 65], BF16)
    at("XQT", 155648, [128, NEC * MYT], BF16)
    at("SQ2", 188416, [128, MYT], F32)
    at("ACL", 40960, [128, NEC * MYT], F32)
    at("BBC", 106496, [128, MYT], F32)
    at("C16", 114688, [16, MYT], F32)
    at("A16", 122880, [16, MYT], F32)
    at("SM16", 131072, [16, MYT], F32)
    at("S16A", 139264, [16, MYT], F32)
    at("S16B", 147456, [16, MYT], F32)
    at("S2T", 196608, [2, MYT], F32)
    at("WUT", 40960, [128, NEC * I], BF16)
    at("WGT", 90112, [128, NEC * I], BF16)
    at("FEV", 139264, [128, 8 * 512], F32)
    at("WNA", 196608, [128, 2 * E], F32)
    at("WSC", 204800, [128, 2 * E], F32)
    at("FG", 40960, [128, 2 * MYT], F32)
    at("FU", 57344, [128, 2 * MYT], F32)
    at("FT1", 73728, [128, 2 * MYT], F32)
    at("FGQ", 90112, [128, 2 * MYT], F32)
    at("FSQ", 106496, [128, 2 * MYT], F32)
    at("FG2", 122880, [128, 2 * MYT], F32)
    at("FGX", 139264, [128, 2 * MYT], F32)
    at("FUX", 155648, [128, 2 * MYT], F32)
    at("FT1X", 172032, [128, 2 * MYT], F32)
    at("AMP", 188416, [128, MYT], F32)
    at("AMN", 196608, [128, MYT], F32)
    at("OQT", 40960, [128, NIC * MYT], BF16)
    at("WDT", 139264, [128, NIC * E], BF16)
    at("W3A", 188416, [128, 2 * I], F32)
    at("W3S", 212992, [128, I], F32)
    at("FG4", 188416, [128, 2 * MYT], F32)
    at("FT4", 204800, [128, MYT], F32)
    at("FG4X", 212992, [128, MYT], F32)
    at("ALC", 188416, [128, MYT], F32)
    at("AFE", 196608, [128, MYT], F32)
    at("ALCB", 204800, [128, MYT], F32)
    at("AFL", 40960, [128, NEC * MYT], F32)
    at("XT6", 106496, [128, 2 * E], F32)
    at("OSB", 114688, [128, 2 * E], F32)
    at("XT6X", 122880, [128, 2 * E], F32)

    PS = nc.alloc_psum_tensor("PS", [128, 4096], F32)

    b = Builder(nc)
    for s_ in ("dS", "dG", "dA", "sV", "sS", "sT", "sG", "sCC"):
        b.mksem(s_)

    emit_program(nc, b, sb, PS, dram, out_d, sp)

    with nc.Block() as block:
        @block.sync
        def _(e):
            for f in b.ops["sync"]:
                f(e)

        @block.gpsimd
        def _(e):
            for f in b.ops["gpsimd"]:
                f(e)

        @block.vector
        def _(e):
            for f in b.ops["vector"]:
                f(e)

        @block.scalar
        def _(e):
            for f in b.ops["scalar"]:
                f(e)

        @block.tensor
        def _(e):
            for f in b.ops["tensor"]:
                f(e)

    st.close()
    return nc


def emit_program(nc, b, sb, PS, dr, out_d, sp):
    ROW = sb["ROW"]
    XNT = {0: sb["XNTA"], 1: sb["XNTB"]}

    def xnt_cols(ec, c0, c1):
        assert (c0 // MYT) == ((c1 - 1) // MYT)
        half = c0 // MYT
        t = XNT[half]
        return t[:, ec * MYT + c0 % MYT: ec * MYT + c0 % MYT + (c1 - c0)]

    b.op(V, lambda e: e.memset(sb["CST"][:, 0:1], MAGIC), "sV")
    b.op(V, lambda e: e.memset(sb["CST"][:, 1:2], 1e-5), "sV")
    b.op(V, lambda e: e.memset(sb["CST"][:, 2:3], 65536.0), "sV")
    b.op(V, lambda e: e.memset(sb["CST"][:, 3:4], -1920.0 / 4096.0), "sV")
    b.op(V, lambda e: e.memset(sb["CST"][:, 4:5], 1.0 / 4096.0), "sV")
    b.op(V, lambda e: e.memset(sb["ONES1"][:, :], 1.0), "sV")
    b.op(V, lambda e: e.memset(sb["AMX"][:, :], 0.0), "sV")
    b.op(G, lambda e: e.iota(sb["IDF"][:, :], pattern=[[1, 128]], base=0,
                             channel_multiplier=-1,
                             allow_small_or_imprecise_dtypes=True), "sG")
    b.op(G, lambda e: e.iota(sb["PIOTA"][:, :], pattern=[[0, 1]], base=0,
                             channel_multiplier=1,
                             allow_small_or_imprecise_dtypes=True), "sG")
    b.dma_g(sb["METB"][:, :], dr["meta"][0:1, :].partition_broadcast(128))
    b.wait(V, "sG")
    b.op(V, lambda e: e.tensor_scalar(sb["IND2"][:, 0:1], sb["PIOTA"][:, :],
                                      64.0, None, Alu.is_lt), "sV")
    b.op(V, lambda e: e.tensor_scalar(sb["IND2"][:, 1:2], sb["PIOTA"][:, :],
                                      64.0, None, Alu.is_ge), "sV")
    b.op(V, lambda e: e.tensor_scalar(sb["IDF"][:, :], sb["IDF"][:, :],
                                      0.0, None, Alu.is_equal), "sV")
    b.op(G, lambda e: e.iota(sb["IND16"][:, :],
                             pattern=[[2, 8], [1, 2], [0, 64]], base=0,
                             channel_multiplier=0,
                             allow_small_or_imprecise_dtypes=True), "sG")
    b.wait(V, "sG")
    b.op(V, lambda e: e.tensor_tensor(
        sb["IND16"][:, :], sb["IND16"][:, :],
        sb["PIOTA"][0:16, 0:1].broadcast_to([16, 1024]), Alu.is_equal), "sV")
    b.dma(Sy, sb["GAM"][:, :], dr["gamma8"][:, :])
    b.dma(Sy, sb["BET"][:, :], dr["beta8"][:, :])
    for wi, (nm, dst) in enumerate((("wqT", "WQTS"), ("wkT", "WKTS"),
                                    ("wvT", "WVTS"))):
        for ec in range(NEC):
            b.dmas(G, "dW%d" % wi, sb[dst][:, ec * E:(ec + 1) * E],
                   dr[nm][ec * 128:(ec + 1) * 128, :])

    def global_scale(col, key, packw, dst, qmaxv):
        b.wait(T, "sV")
        b.wait(T, "sS")
        b.wait(T, "sG")
        b.op(T, lambda e: e.transpose(PS[0:packw, 1536:1664],
                                      sb["AMX"][:, col:col + packw],
                                      sb["IDF"][:, :]), "sT")
        b.wait(V, "sT")
        b.op(V, lambda e: e.tensor_reduce(ROW[0:packw, 256:257],
                                          PS[0:packw, 1536:1664], AX,
                                          Alu.max), "sV")
        b.wait(G, "sV")
        b.dma_g(sp["cc_in"][key][0:1, :].rearrange("a (p c) -> (a p) c",
                                                    p=packw),
              ROW[0:packw, 256:257])
        b.wait(G, "dG")
        b.op(G, lambda e: e.collective_compute(
            "AllReduce", Alu.max, replica_groups=[list(range(N_CORES))],
            ins=[sp["cc_in"][key].opt()], outs=[sp["cc_out"][key].opt()],
        ), "sCC")
        b.wait(G, "sCC")
        b.dma_g(ROW[0:1, 260:260 + packw], sp["cc_out"][key][0:1, :])
        b.wait(V, "dG")
        b.op(V, lambda e: e.tensor_scalar(
            ROW[0:1, 280:280 + packw], ROW[0:1, 260:260 + packw],
            1.0 / qmaxv, 1e-12, Alu.mult, Alu.max), "sV")
        b.vsync(V)
        b.op(V, lambda e: e.reciprocal(
            ROW[0:1, 288:288 + packw], ROW[0:1, 280:280 + packw]), "sV")
        b.vsync(V)
        for i in range(packw):
            b.op(V, lambda e, i=i: e.tensor_copy(
                ROW[0:1, 300 + 2 * i:301 + 2 * i],
                ROW[0:1, 280 + i:281 + i]), "sV")
            b.op(V, lambda e, i=i: e.tensor_copy(
                ROW[0:1, 301 + 2 * i:302 + 2 * i],
                ROW[0:1, 288 + i:289 + i]), "sV")
        b.wait(G, "sV")
        b.dma_g(sp["scr_row"][2:3, 0:2 * packw], ROW[0:1, 300:300 + 2 * packw])
        b.wait(G, "dG")
        b.dma_g(sb[dst][:, 0:2 * packw],
              sp["scr_row"][2:3, 0:2 * packw].partition_broadcast(128))

    def qdq_chunk(src_ap, dst_ap, s_ap, rs_ap, lo, hi, tmp_ap, war=-1):
        if war < 0:
            b.wait(Sc, "sV")
        elif war > 0:
            b.wait(Sc, "sV", war)
        b.op(Sc, lambda e: e.activation(out=tmp_ap, in_=src_ap,
                                        func=Act.Identity,
                                        bias=sb["CST"][:, 0:1],
                                        scale=rs_ap), "sS")
        b.wait(V, "sS")
        b.op(V, lambda e: e.tensor_scalar(tmp_ap, tmp_ap, MAGIC, lo,
                                          Alu.subtract, Alu.max), "sV")
        b.op(V, lambda e: e.tensor_scalar(dst_ap, tmp_ap, hi, s_ap,
                                          Alu.min, Alu.mult), "sV")
        b.wait(Sc, "sV")

    XTS, SQT, STT, XC2 = sb["XTS"], sb["SQT"], sb["STT"], sb["XC2"]
    ldsA = [b.dmas(Sy, "dXC0_%d" % qq, XC2[:, qq * 1024:(qq + 1) * 1024],
                   dr["xseqT"][0:128, qq * 1024:(qq + 1) * 1024])
            for qq in range(4)]
    amk = {}
    for tt in range(32):
        q = tt % 4
        pb = q * E
        if (tt - 4) in amk:
            b.wait(Sy, "sV", amk[tt - 4][0])
            b.wait(Sy, "sS", amk[tt - 4][1])
        ld = b.dmas(Sy, "dXT%d" % q, XTS[:, pb:pb + E],
                    dr["x_tok"][tt * 128:(tt + 1) * 128, :])
        b.wait(Sc, "dXT%d" % q, ld)
        b.wait(Sc, "sV")
        b.op(Sc, lambda e, pb=pb: e.activation(
            out=SQT[:, :], in_=XTS[:, pb:pb + E], func=Act.Square), "sS")
        b.wait(V, "dXT%d" % q, ld)
        b.op(V, lambda e, pb=pb, tt=tt: e.tensor_reduce(
            STT[:, tt:tt + 1], XTS[:, pb:pb + E], AX, Alu.add), "sV")
        b.wait(V, "sS")
        b.op(V, lambda e, tt=tt: e.tensor_reduce(
            STT[:, 32 + tt:33 + tt], SQT[:, :], AX, Alu.add), "sV")
        amk[tt] = (b.sems["sV"][1], b.sems["sS"][1])
    b.vsync(V)
    b.op(V, lambda e: e.tensor_scalar(STT[:, 0:32], STT[:, 0:32],
                                      1.0 / E, None, Alu.mult), "sV")
    b.op(V, lambda e: e.tensor_scalar(STT[:, 32:64], STT[:, 32:64],
                                      1.0 / E, None, Alu.mult), "sV")
    b.vsync(V)
    b.op(V, lambda e: e.tensor_tensor(STT[:, 64:96], STT[:, 0:32],
                                      STT[:, 0:32], Alu.mult), "sV")
    b.vsync(V)
    b.op(V, lambda e: e.tensor_tensor(STT[:, 64:96], STT[:, 32:64],
                                      STT[:, 64:96], Alu.subtract), "sV")
    b.wait(Sc, "sV")
    b.op(Sc, lambda e: e.activation(out=STT[:, 96:128], in_=STT[:, 64:96],
                                    func=Act.Sqrt, bias=sb["CST"][:, 1:2],
                                    scale=1.0), "sS")
    b.wait(V, "sS")
    b.op(V, lambda e: e.reciprocal(STT[:, 96:128], STT[:, 96:128]), "sV")
    b.vsync(V)
    b.wait(T, "sV")
    b.op(T, lambda e: e.transpose(PS[0:32, 0:128], STT[:, 0:32],
                                  sb["IDF"][:, :]), "sT")
    b.op(T, lambda e: e.transpose(PS[0:32, 512:640], STT[:, 96:128],
                                  sb["IDF"][:, :]), "sT")
    b.wait(V, "sT")
    b.op(V, lambda e: e.tensor_copy(ROW[0:32, 0:128], PS[0:32, 0:128]), "sV")
    b.op(V, lambda e: e.tensor_copy(ROW[0:32, 128:256], PS[0:32, 512:640]), "sV")
    b.wait(G, "sV")
    b.dma_g(sp["scr_row"][0:1, :].rearrange("a (p c) -> (a p) c", p=32),
          ROW[0:32, 0:128])
    b.dma_g(sp["scr_row"][1:2, :].rearrange("a (p c) -> (a p) c", p=32),
          ROW[0:32, 128:256])
    b.wait(G, "dG")
    b.dma_g(sb["MUBC"][:, :], sp["scr_row"][0:1, :].partition_broadcast(128))
    b.dma_g(sb["RSBC"][:, :], sp["scr_row"][1:2, :].partition_broadcast(128))
    b.barrier([Sy])
    nmk = {}
    for ec in range(NEC):
        xcb = sb["XC2"] if ec % 2 == 0 else sb["XC2B"]
        if (ec - 2) in nmk:
            b.wait(Sy, "sV", nmk[ec - 2])
        if ec == 0:
            lds = ldsA
        else:
            lds = [b.dmas(Sy, "dXC%d_%d" % (ec % 2, qq),
                          xcb[:, qq * 1024:(qq + 1) * 1024],
                          dr["xseqT"][ec * 128:(ec + 1) * 128,
                                      qq * 1024:(qq + 1) * 1024])
                   for qq in range(4)]
        for qq in range(4):
            b.wait(V, "dXC%d_%d" % (ec % 2, qq), lds[qq])
        b.wait(V, "dG")
        b.op(V, lambda e, xcb=xcb: e.tensor_tensor(
            xcb[:, :], xcb[:, :], sb["MUBC"][:, :], Alu.subtract), "sV")
        for half in range(2):
            b.op(V, lambda e, xcb=xcb, ec=ec, half=half: e.tensor_tensor(
                XNT[half][:, ec * MYT:(ec + 1) * MYT],
                xcb[:, half * MYT:(half + 1) * MYT],
                sb["RSBC"][:, half * MYT:(half + 1) * MYT], Alu.mult), "sV")
            b.op(V, lambda e, ec=ec, half=half: e.tensor_reduce(
                ROW[0:128, 386:387], XNT[half][:, ec * MYT:(ec + 1) * MYT],
                AX, Alu.max, apply_absolute_value=True), "sV")
            b.op(V, lambda e: e.tensor_tensor(
                sb["AMX"][:, 0:1], sb["AMX"][:, 0:1], ROW[0:128, 386:387],
                Alu.max), "sV")
        nmk[ec] = b.sems["sV"][1]
    global_scale(0, "x", 1, "SXB", 32767.0)

    b.barrier([T])
    VEV = sb["VEV"]
    vem = {}
    for tt in range(32):
        for half in range(2):
            slot = tt * 2 + half
            pt = PS[:, 2048 + (slot % 4) * 512: 2048 + (slot % 4) * 512 + 512]
            if (slot - 4) in vem:
                b.wait(T, "sV", vem[slot - 4])
            for ec in range(NEC):
                b.op(T, lambda e, tt=tt, ec=ec, pt=pt, half=half: e.matmul(
                    pt, xnt_cols(ec, tt * 128, (tt + 1) * 128),
                    sb["WVTS"][:, ec * E + half * 512:
                               ec * E + (half + 1) * 512],
                    start=(ec == 0), stop=(ec == NEC - 1)), "sT")
            b.wait(V, "sT")
            if (slot - 2) in vem:
                b.wait(V, "dSv" if "dSv" in b.sems else "sT")
            vem[slot] = b.op(V, lambda e, pt=pt, half=half: e.tensor_copy(
                VEV[:, half * 8 * 65:(half + 1) * 8 * 65]
                .rearrange("p (h c) -> p h c", c=65)[:, :, 0:64],
                pt.rearrange("p (h c) -> p h c", c=64)), "sV")
        b.op(V, lambda e: e.memset(
            VEV[:, :].rearrange("p (h c) -> p h c", c=65)[:, :, 64:65],
            1.0), "sV")
        mm = b.sems["sV"][1]
        b.wait(Sc, "sV", mm)
        b.dmas(Sc, "dSv", sp["sp_v"][tt * 128:(tt + 1) * 128, :], VEV[:, :])

    b.barrier([T])
    KF, QF = sb["KF"], sb["QF"]
    kqm = {}
    slot = 0
    for pair in range(NEC):
        for nk in range(8):
            pk = PS[:, 3072 + (slot % 2) * 512: 3072 + (slot % 2) * 512 + 512]
            if (slot - 2) in kqm:
                b.wait(T, "sV", kqm[slot - 2])
            for ec in range(NEC):
                b.op(T, lambda e, pk=pk, ec=ec, pair=pair, nk=nk: e.matmul(
                    pk, sb["WKTS"][:, ec * E + pair * 128:
                                   ec * E + (pair + 1) * 128],
                    xnt_cols(ec, nk * 512, (nk + 1) * 512),
                    start=(ec == 0), stop=(ec == NEC - 1)), "sT")
            b.wait(V, "sT")
            kqm[slot] = b.op(V, lambda e, pk=pk, pair=pair, nk=nk:
                             e.tensor_copy(
                KF[:, pair * S + nk * 512: pair * S + (nk + 1) * 512],
                pk), "sV")
            slot += 1
    for pair in range(NEC):
        for nq in range(4):
            pk = PS[:, 3072 + (slot % 2) * 512: 3072 + (slot % 2) * 512 + 512]
            if (slot - 2) in kqm:
                b.wait(T, "sV", kqm[slot - 2])
            for ec in range(NEC):
                b.op(T, lambda e, pk=pk, ec=ec, pair=pair, nq=nq: e.matmul(
                    pk, sb["WQTS"][:, ec * E + pair * 128:
                                   ec * E + (pair + 1) * 128],
                    xnt_cols(ec, nq * 512, (nq + 1) * 512),
                    start=(ec == 0), stop=(ec == NEC - 1)), "sT")
            b.wait(V, "sT")
            kqm[slot] = b.op(V, lambda e, pk=pk, pair=pair, nq=nq:
                             e.tensor_scalar(
                QF[:, pair * MYT + nq * 512: pair * MYT + (nq + 1) * 512],
                pk, 1.0 / 32.0, None, Alu.mult), "sV")
            slot += 1

    b.barrier([G])
    b.op(G, lambda e: e.iota(sb["DNE"][:, :], pattern=[[1, 4096]], base=0,
                             channel_multiplier=-1,
                             allow_small_or_imprecise_dtypes=True), "sG")
    b.op(G, lambda e: e.iota(sb["DFA"][:, :], pattern=[[1, 4096]], base=0,
                             channel_multiplier=-1,
                             allow_small_or_imprecise_dtypes=True), "sG")
    b.wait(V, "sG")
    b.wait(V, "dG")
    b.op(V, lambda e: e.tensor_scalar(sb["DFA"][:, :], sb["DFA"][:, :],
                                      sb["METB"][:, 0:1], sb["METB"][:, 1:2],
                                      Alu.mult, Alu.add), "sV")
    b.wait(Sc, "sG")
    b.wait(Sc, "sV")
    b.op(Sc, lambda e: e.activation(out=sb["DNE"][:, :], in_=sb["DNE"][:, :],
                                    func=Act.Abs, bias=sb["CST"][:, 3:4],
                                    scale=sb["CST"][:, 4:5]), "sS")

    for w in range(64):
        b.op(T, lambda e, w=w: e.matmul(
            PS[:, 0:512], KF[0:64, 0:128], QF[0:64, 0:512],
            start=True, stop=True), "sT")
    b.barrier([T, Sc, G, Sy])
    SW, PT, AVE, VSB = sb["SW"], sb["PT"], sb["AVE"], sb["VSB"]
    DNE, DFA = sb["DNE"], sb["DFA"]
    RCOL = (0, 1024, 3072)
    sc_mark, tt_mark, exp_mark, av_mark, evac_mark = {}, {}, {}, {}, {}
    spill_mark = {}

    def prevkey(it, k, back, n=32):
        kk = k - back
        return (it, kk) if kk >= 0 else (it - 1, kk + n)

    it = -1
    for head in range(H):
        pair, hh = divmod(head, 2)
        hb = hh * 64
        for qh in range(2):
            it += 1
            vb = 0
            if (it - 1, 31) in av_mark:
                b.wait(Sy, "sT", av_mark[(it - 1, 31)])
            vld = b.dmas(Sy, "dVS", VSB[:, vb:vb + 2080]
                         .rearrange("p (k c) -> p k c", c=65),
                         sp["sp_v"][:, head * 65:(head + 1) * 65]
                         .rearrange("(k p) c -> p k c", p=128))
            for step in range(34):
                k_sc, k_tt, k_av = step, step - 1, step - 2
                if k_sc < 32:
                    kc = k_sc
                    pscr = PS[:, RCOL[kc % 3]:RCOL[kc % 3] + 1024]
                    key = prevkey(it, kc, 3)
                    if key in tt_mark:
                        b.wait(T, "sV", tt_mark[key])
                    for jj in range(2):
                        b.op(T, lambda e, pscr=pscr, jj=jj, hb=hb, kc=kc,
                             qh=qh, pair=pair: e.matmul(
                                 pscr[:, jj * 512:(jj + 1) * 512],
                                 KF[hb:hb + 64, pair * S + kc * 128:
                                    pair * S + (kc + 1) * 128],
                                 QF[hb:hb + 64,
                                    pair * MYT + qh * 1024 + jj * 512:
                                    pair * MYT + qh * 1024 + (jj + 1) * 512],
                                 start=True, stop=True), "sT")
                    sc_mark[(it, kc)] = b.sems["sT"][1]
                if 0 <= k_tt < 32:
                    kc = k_tt
                    pscr = PS[:, RCOL[kc % 3]:RCOL[kc % 3] + 1024]
                    swb = SW[:, (kc % 2) * 1024:(kc % 2) * 1024 + 1024]
                    ptb = PT[:, (kc % 2) * 1024:(kc % 2) * 1024 + 1024]
                    if kc < 16:
                        doff = qh * 1024 - kc * 128 + 1920
                        dsl = DNE[:, doff:doff + 1024]
                    else:
                        doff = qh * 1024 - (kc - 16) * 128 + 1920
                        dsl = DFA[:, doff:doff + 1024]
                    b.wait(V, "sT", sc_mark[(it, kc)])
                    key = prevkey(it, kc, 2)
                    if key in exp_mark:
                        b.wait(V, "sS", exp_mark[key])
                    if kc == 0 and "dA1" in b.sems:
                        pass
                    tt_mark[(it, kc)] = b.op(
                        V, lambda e, pscr=pscr, dsl=dsl, swb=swb:
                        e.tensor_tensor(swb, pscr, dsl, Alu.mult), "sV")
                    b.wait(Sc, "sV", tt_mark[(it, kc)])
                    if key in av_mark:
                        b.wait(Sc, "sT", av_mark[key])
                    exp_mark[(it, kc)] = b.op(
                        Sc, lambda e, swb=swb, ptb=ptb: e.activation(
                            out=ptb, in_=swb, func=Act.Exp), "sS")
                if 0 <= k_av < 32:
                    kc = k_av
                    ptb = PT[:, (kc % 2) * 1024:(kc % 2) * 1024 + 1024]
                    b.wait(T, "sS", exp_mark[(it, kc)])
                    if kc == 0:
                        b.wait(T, "dVS", vld)
                        if (it - 1, "evac") in evac_mark:
                            b.wait(T, "sV", evac_mark[(it - 1, "evac")])
                    for jj in range(2):
                        b.op(T, lambda e, jj=jj, kc=kc, ptb=ptb, vb=vb:
                             e.matmul(
                                 PS[0:65, 2048 + jj * 512: 2560 + jj * 512],
                                 VSB[:, vb + kc * 65: vb + kc * 65 + 65],
                                 ptb[:, jj * 512:(jj + 1) * 512],
                                 start=(kc == 0), stop=(kc == 31)), "sT")
                    av_mark[(it, kc)] = b.sems["sT"][1]
            b.wait(V, "sT", av_mark[(it, 31)])
            if it >= 1:
                b.wait(V, "dA1")
                b.wait(V, "dA2")
            evac_mark[(it, "evac")] = b.op(
                V, lambda e: e.tensor_copy(AVE[0:65, :],
                                           PS[0:65, 2048:3072]), "sV")
            b.wait(Sc, "sV", evac_mark[(it, "evac")])
            b.dmas(Sc, "dA1", sp["sp_attn"][head * 64:(head + 1) * 64,
                                            qh * 1024:(qh + 1) * 1024],
                   AVE[0:64, :])
            b.dmas(Sc, "dA2", sp["sp_sums"][head:head + 1,
                                            qh * 1024:(qh + 1) * 1024],
                   AVE[64:65, :])

    b.barrier()
    XQT, SQ2 = sb["XQT"], sb["SQ2"]
    SQ2E = sb["SQ2E"]
    for ec in range(NEC):
        b.op(Sc, lambda e, ec=ec: e.activation(
            out=SQ2E[:, :], in_=sb["XNTA"][:, ec * MYT:(ec + 1) * MYT],
            func=Act.Identity, bias=sb["CST"][:, 0:1],
            scale=sb["SXB"][:, 1:2]), "sS")
        b.wait(V, "sS")
        b.op(V, lambda e: e.tensor_scalar(SQ2E[:, :], SQ2E[:, :], MAGIC,
                                          -32768.0, Alu.subtract,
                                          Alu.max), "sV")
        b.op(V, lambda e, ec=ec: e.tensor_scalar(
            XQT[:, ec * MYT:(ec + 1) * MYT], SQ2E[:, :], 32767.0,
            sb["SXB"][:, 0:1], Alu.min, Alu.mult), "sV")
        b.wait(Sc, "sV")

    e_s, e_v = b.sems["sS"][1], b.sems["sV"][1]
    b.wait(Sy, "sS", e_s)
    b.wait(Sy, "sV", e_v)
    ACL, BBC = sb["ACL"], sb["BBC"]
    ld = b.dma(Sy, sb["SM16"][:, :], sp["sp_sums"][:, :])
    b.wait(V, "dS")
    b.op(V, lambda e: e.reciprocal(sb["C16"][:, :], sb["SM16"][:, :]), "sV")
    b.vsync(V)
    for ec in range(NEC):
        ld = b.dma(Sy, ACL[:, ec * MYT:(ec + 1) * MYT],
                   sp["sp_attn"][ec * 128:(ec + 1) * 128, :])
        b.wait(Sc, "dS")
        b.wait(Sc, "sT")
        b.op(Sc, lambda e, ec=ec: e.activation(
            out=SQ2[:, :], in_=ACL[:, ec * MYT:(ec + 1) * MYT],
            func=Act.Square), "sS")
        b.wait(T, "dS")
        b.wait(T, "sS")
        b.wait(T, "sV")
        for np_ in range(4):
            b.op(T, lambda e, ec=ec, np_=np_: e.matmul(
                PS[0:2, np_ * 512:(np_ + 1) * 512], sb["IND2"][:, :],
                ACL[:, ec * MYT + np_ * 512: ec * MYT + (np_ + 1) * 512],
                start=True, stop=True), "sT")
            b.op(T, lambda e, np_=np_: e.matmul(
                PS[0:2, 2048 + np_ * 512: 2048 + (np_ + 1) * 512],
                sb["IND2"][:, :], SQ2[:, np_ * 512:(np_ + 1) * 512],
                start=True, stop=True), "sT")
        b.wait(V, "sT")
        b.wait(V, "dG")
        b.op(V, lambda e, ec=ec: e.tensor_copy(
            sb["S2T"][:, :], PS[0:2, 0:2048]), "sV")
        b.wait(G, "sV")
        b.dma_g(sp["sp_st"][2 * ec:2 * ec + 2, :], sb["S2T"][:, :])
        b.wait(V, "dG")
        b.op(V, lambda e, ec=ec: e.tensor_copy(
            sb["S2T"][:, :], PS[0:2, 2048:4096]), "sV")
        b.wait(G, "sV")
        b.dma_g(sp["sp_st"][16 + 2 * ec:18 + 2 * ec, :], sb["S2T"][:, :])
    b.wait(Sy, "dG")
    ldA = b.dma(Sy, sb["S16A"][:, :], sp["sp_st"][0:16, :])
    ldB = b.dma(Sy, sb["S16B"][:, :], sp["sp_st"][16:32, :])
    b.wait(V, "dS")
    b.op(V, lambda e: e.tensor_tensor(sb["S16A"][:, :], sb["S16A"][:, :],
                                      sb["C16"][:, :], Alu.mult), "sV")
    b.op(V, lambda e: e.tensor_tensor(sb["A16"][:, :], sb["C16"][:, :],
                                      sb["C16"][:, :], Alu.mult), "sV")
    b.vsync(V)
    b.op(V, lambda e: e.tensor_tensor(sb["S16B"][:, :], sb["S16B"][:, :],
                                      sb["A16"][:, :], Alu.mult), "sV")
    b.wait(T, "sV")
    for np_ in range(4):
        b.op(T, lambda e, np_=np_: e.matmul(
            PS[0:1, np_ * 512:(np_ + 1) * 512], sb["ONES1"][0:16, :],
            sb["S16A"][:, np_ * 512:(np_ + 1) * 512],
            start=True, stop=True), "sT")
        b.op(T, lambda e, np_=np_: e.matmul(
            PS[0:1, 2048 + np_ * 512: 2048 + (np_ + 1) * 512],
            sb["ONES1"][0:16, :], sb["S16B"][:, np_ * 512:(np_ + 1) * 512],
            start=True, stop=True), "sT")
    b.wait(V, "sT")
    b.op(V, lambda e: e.tensor_scalar(ROW[0:1, 0:2048], PS[0:1, 0:2048],
                                      1.0 / E, None, Alu.mult), "sV")
    b.op(V, lambda e: e.tensor_scalar(ROW[0:1, 2048:4096], PS[0:1, 2048:4096],
                                      1.0 / E, None, Alu.mult), "sV")
    mu = ROW[0:1, 0:2048]
    ex2 = ROW[0:1, 2048:4096]
    musq = sb["S16B"][0:1, 0:2048]
    var = sb["SM16"][0:1, 0:2048]
    sd = sb["S16A"][0:1, 0:2048]
    rr = sb["A16"][0:1, 0:2048]
    bb = sb["S16B"][0:1, 0:2048]
    b.vsync(V)
    b.op(V, lambda e: e.tensor_tensor(musq, mu, mu, Alu.mult), "sV")
    b.vsync(V)
    b.op(V, lambda e: e.tensor_tensor(var, ex2, musq, Alu.subtract), "sV")
    b.wait(Sc, "sV")
    b.op(Sc, lambda e: e.activation(out=sd, in_=var, func=Act.Sqrt,
                                    bias=sb["CST"][0:1, 1:2], scale=1.0), "sS")
    b.wait(V, "sS")
    b.op(V, lambda e: e.reciprocal(rr, sd), "sV")
    b.vsync(V)
    b.op(V, lambda e: e.tensor_tensor(bb, mu, rr, Alu.mult), "sV")
    b.vsync(V)
    b.wait(G, "sV")
    b.dma_g(sp["scr_row"][2:3, 0:2048], bb)
    b.dma_g(sp["scr_row"][3:4, 0:2048], rr)
    b.wait(G, "dG")
    b.dma_g(BBC[:, :], sp["scr_row"][2:3, 0:2048].partition_broadcast(128))
    b.dma_g(sb["SM16"][:, :],
          sp["scr_row"][3:4, 0:2048].partition_broadcast(16))
    b.wait(V, "dG")
    b.op(V, lambda e: e.tensor_tensor(sb["A16"][:, :], sb["C16"][:, :],
                                      sb["SM16"][:, :], Alu.mult), "sV")
    b.vsync(V)
    for ec in range(NEC):
        b.wait(T, "sV")
        for np_ in range(4):
            b.op(T, lambda e, ec=ec, np_=np_: e.matmul(
                PS[:, np_ * 512:(np_ + 1) * 512],
                sb["IND16"][:, ec * 128:(ec + 1) * 128],
                sb["A16"][:, np_ * 512:(np_ + 1) * 512],
                start=True, stop=True), "sT")
        b.wait(V, "sT")
        b.wait(Sc, "sV")
        b.op(V, lambda e, ec=ec: e.tensor_tensor(
            SQ2[:, :], ACL[:, ec * MYT:(ec + 1) * MYT], PS[:, 0:2048],
            Alu.mult), "sV")
        b.op(V, lambda e: e.tensor_tensor(SQ2[:, :], SQ2[:, :], BBC[:, :],
                                          Alu.subtract), "sV")
        b.wait(V, "dG")
        b.op(V, lambda e, ec=ec: e.tensor_scalar(
            ACL[:, ec * MYT:(ec + 1) * MYT], SQ2[:, :],
            sb["GAM"][:, ec:ec + 1], sb["BET"][:, ec:ec + 1],
            Alu.mult, Alu.add), "sV")
        b.wait(G, "sV")
        b.dma_g(sp["sp_aln"][ec * 128:(ec + 1) * 128, :],
              ACL[:, ec * MYT:(ec + 1) * MYT])

    b.barrier()

    prep_marks = {}

    def prep_ffn_weight(src, dstT, nchunks, icols, dst_stride, wna, wsc):
        nb = icols // 32
        for c in range(nchunks):
            pb = (c % 2) * icols
            key = (id(wna), c % 2)
            if key in prep_marks:
                b.wait(Sy, "sT", prep_marks[key][0])
                b.wait(Sy, "sV", prep_marks[key][1])
            elif c < 2:
                b.wait(Sy, "sT")
                b.wait(Sy, "sV")
            ld = b.dmas(Sy, "dWN%d" % (c % 2), wna[:, pb:pb + icols],
                        src[c * 128:(c + 1) * 128, :])
            b.wait(V, "dWN%d" % (c % 2), ld)
            wna3 = wna[:, pb:pb + icols].rearrange("p (nb k) -> p nb k", k=32)
            b.op(V, lambda e, wna3=wna3, nb=nb: e.tensor_reduce(
                sb["BMX"][:, 0:nb], wna3, AX, Alu.max,
                apply_absolute_value=True), "sV")
            b.vsync(V)
            b.op(V, lambda e, nb=nb: e.tensor_scalar(
                sb["BSC"][:, 0:nb], sb["BMX"][:, 0:nb], 1.0 / 7.0, 1e-12,
                Alu.mult, Alu.max), "sV")
            b.vsync(V)
            b.op(V, lambda e, nb=nb: e.reciprocal(
                sb["BRS"][:, 0:nb], sb["BSC"][:, 0:nb]), "sV")
            b.vsync(V)
            qsc = wsc[:, 0:icols]
            qsc3 = qsc.rearrange("p (nb k) -> p nb k", k=32)
            b.op(V, lambda e, wna3=wna3, qsc3=qsc3, nb=nb: e.tensor_tensor(
                qsc3, wna3,
                sb["BRS"][:, 0:nb].unsqueeze(2).broadcast_to([128, nb, 32]), Alu.mult), "sV")
            b.op(V, lambda e, qsc=qsc: e.tensor_scalar(
                qsc, qsc, MAGIC, MAGIC, Alu.add, Alu.subtract), "sV")
            b.op(V, lambda e, qsc=qsc: e.tensor_scalar(
                qsc, qsc, -8.0, 7.0, Alu.max, Alu.min), "sV")
            b.op(V, lambda e, qsc3=qsc3, wna3=wna3, nb=nb: e.tensor_tensor(
                wna3, qsc3,
                sb["BSC"][:, 0:nb].unsqueeze(2).broadcast_to([128, nb, 32]), Alu.mult), "sV")
            b.wait(T, "sV")
            for s8 in range(icols // 128):
                pp = PS[:, (s8 % 2) * 512:(s8 % 2) * 512 + 128]
                if s8 >= 2:
                    b.wait(T, "sV")
                b.op(T, lambda e, pp=pp, pb=pb, s8=s8, wna=wna: e.transpose(
                    pp, wna[:, pb + s8 * 128: pb + (s8 + 1) * 128],
                    sb["IDF"][:, :]), "sT")
                b.wait(V, "sT")
                b.op(V, lambda e, pp=pp, s8=s8, c=c: e.tensor_copy(
                    sb[dstT][:, s8 * dst_stride + c * 128:
                             s8 * dst_stride + (c + 1) * 128], pp), "sV")
            prep_marks[(id(wna), c % 2)] = (b.sems["sT"][1], b.sems["sV"][1])

    prep_ffn_weight(dr["wu"], "WUT", NIC, E, I, sb["WNA"], sb["WSC"])
    prep_ffn_weight(dr["wg"], "WGT", NIC, E, I, sb["WNA"], sb["WSC"])

    b.barrier()
    for wt, spd, amcol in (("WUT", sp["sp_up"], 1), ("WGT", sp["sp_gate"], 2)):
        for ic in range(NIC):
            base = (ic % 2) * 4 * 512
            b.wait(T, "sV")
            for ec in range(NEC):
                for nq in range(4):
                    b.op(T, lambda e, base=base, nq=nq, ec=ec, ic=ic, wt=wt:
                         e.matmul(
                             PS[:, base + nq * 512: base + (nq + 1) * 512],
                             sb[wt][:, ec * I + ic * 128:
                                    ec * I + (ic + 1) * 128],
                             sb["XQT"][:, ec * MYT + nq * 512:
                                       ec * MYT + (nq + 1) * 512],
                             start=(ec == 0), stop=(ec == NEC - 1)), "sT")
            fb = (ic % 2) * 2048
            b.wait(V, "sT")
            b.wait(V, "dF%d" % (ic % 2) if ("dF%d" % (ic % 2)) in b.sems else "sT")
            for nq in range(4):
                b.op(V, lambda e, base=base, nq=nq, fb=fb: e.tensor_copy(
                    sb["FEV"][:, fb + nq * 512: fb + (nq + 1) * 512],
                    PS[:, base + nq * 512: base + (nq + 1) * 512]), "sV")
            b.op(V, lambda e, fb=fb: e.tensor_reduce(
                ROW[0:128, 384:385], sb["FEV"][:, fb:fb + 2048], AX, Alu.max,
                apply_absolute_value=True), "sV")
            b.vsync(V)
            b.op(V, lambda e, amcol=amcol: e.tensor_tensor(
                sb["AMX"][:, amcol:amcol + 1], sb["AMX"][:, amcol:amcol + 1],
                ROW[0:128, 384:385], Alu.max), "sV")
            b.vsync(V)
            b.wait(Sc, "sV")
            b.dmas(Sc, "dF%d" % (ic % 2), spd[ic * 128:(ic + 1) * 128, :],
                   sb["FEV"][:, fb:fb + 2048])
    global_scale(1, "ug", 2, "SUGB", 32767.0)

    b.barrier()
    FG, FU, FT1 = sb["FG"], sb["FU"], sb["FT1"]
    FGQ, FSQ, FG2 = sb["FGQ"], sb["FSQ"], sb["FG2"]
    f2m = {}
    vend = {}
    for ic in range(NIC):
        par = ic % 2
        q4 = ic % 4
        fgb = (FG if q4 < 2 else sb["FGX"])[:, par * MYT:(par + 1) * MYT]
        fub = (FU if q4 < 2 else sb["FUX"])[:, par * MYT:(par + 1) * MYT]
        ft = FT1[:, par * MYT:(par + 1) * MYT]
        fq = FGQ[:, par * MYT:(par + 1) * MYT]
        fs = FSQ[:, par * MYT:(par + 1) * MYT]
        f2 = FG2[:, par * MYT:(par + 1) * MYT]
        if (ic - 4) in f2m:
            b.wait(Sy, "sS", f2m[ic - 4][0])
            b.wait(Sy, "sS", f2m[ic - 4][1])
        ldg = b.dmas(Sy, "dLG%d" % q4, fgb,
                     sp["sp_gate"][ic * 128:(ic + 1) * 128, :])
        ldu = b.dmas(Sy, "dLU%d" % q4, fub,
                     sp["sp_up"][ic * 128:(ic + 1) * 128, :])
        b.wait(Sc, "dLG%d" % q4, ldg)
        if ("dUQ%d" % par) in b.sems:
            b.wait(Sc, "dUQ%d" % par)
        qdq_chunk(fgb, fq, sb["SUGB"][:, 2:3], sb["SUGB"][:, 3:4],
                  -32768.0, 32767.0, ft, war=vend.get(ic - 2, 0))
        fg_done = b.sems["sS"][1]
        b.op(Sc, lambda e, fq=fq, fs=fs: e.activation(
            out=fs, in_=fq, func=Act.Sigmoid), "sS")
        b.op(Sc, lambda e, fs=fs, ft=ft: e.activation(
            out=ft, in_=fs, func=Act.Identity, bias=sb["CST"][:, 0:1],
            scale=sb["CST"][:, 2:3]), "sS")
        b.wait(V, "sS")
        b.op(V, lambda e, ft=ft: e.tensor_scalar(
            ft, ft, MAGIC, 0.0, Alu.subtract, Alu.max), "sV")
        b.op(V, lambda e, fs=fs, ft=ft: e.tensor_scalar(
            fs, ft, 65535.0, 1.0 / 65536.0, Alu.min, Alu.mult), "sV")
        if ("dG2%d" % par) in b.sems:
            b.wait(V, "dG2%d" % par)
        b.op(V, lambda e, f2=f2, fq=fq, fs=fs: e.tensor_tensor(
            f2, fq, fs, Alu.mult), "sV")
        f2k = b.sems["sV"][1]
        b.op(V, lambda e, f2=f2: e.tensor_reduce(
            ROW[0:128, 384:385], f2, AX, Alu.max,
            apply_absolute_value=True), "sV")
        b.op(V, lambda e: e.tensor_tensor(sb["AMX"][:, 3:4], sb["AMX"][:, 3:4],
                                          ROW[0:128, 384:385], Alu.max), "sV")
        ftx = sb["FT1X"][:, par * MYT:(par + 1) * MYT]
        b.wait(Sc, "dLU%d" % q4, ldu)
        uw = vend.get(ic - 2, 0)
        if uw > 0:
            b.wait(Sc, "sV", uw)
        b.op(Sc, lambda e, fub=fub, ftx=ftx: e.activation(
            out=ftx, in_=fub, func=Act.Identity, bias=sb["CST"][:, 0:1],
            scale=sb["SUGB"][:, 1:2]), "sS")
        fu_done = b.sems["sS"][1]
        b.wait(Sc, "sV", f2k)
        b.dmas(Sc, "dG2%d" % par, sp["sp_g2"][ic * 128:(ic + 1) * 128, :], f2)
        b.wait(V, "sS")
        b.op(V, lambda e, ftx=ftx: e.tensor_scalar(
            ftx, ftx, MAGIC, -32768.0, Alu.subtract, Alu.max), "sV")
        b.op(V, lambda e, fq=fq, ftx=ftx: e.tensor_scalar(
            fq, ftx, 32767.0, sb["SUGB"][:, 0:1], Alu.min, Alu.mult), "sV")
        uqk = b.sems["sV"][1]
        b.wait(Sc, "sV", uqk)
        b.dmas(Sc, "dUQ%d" % par, sp["sp_uq"][ic * 128:(ic + 1) * 128, :], fq)
        vend[ic] = uqk
        f2m[ic] = (fg_done, fu_done)
    global_scale(3, "g2", 1, "S2B", 32767.0)

    b.barrier()
    f3m = {}
    vend3 = {}
    for ic in range(NIC):
        par = ic % 2
        q4 = ic % 4
        fgb = (FG if q4 < 2 else sb["FGX"])[:, par * MYT:(par + 1) * MYT]
        fub = (FU if q4 < 2 else sb["FUX"])[:, par * MYT:(par + 1) * MYT]
        ft = FT1[:, par * MYT:(par + 1) * MYT]
        fq = FGQ[:, par * MYT:(par + 1) * MYT]
        f2 = FG2[:, par * MYT:(par + 1) * MYT]
        if (ic - 4) in f3m:
            b.wait(Sy, "sS", f3m[ic - 4][0])
            b.wait(Sy, "sV", f3m[ic - 4][1])
        ldg = b.dmas(Sy, "dLG%d" % q4, fgb,
                     sp["sp_g2"][ic * 128:(ic + 1) * 128, :])
        ldu = b.dmas(Sy, "dLU%d" % q4, fub,
                     sp["sp_uq"][ic * 128:(ic + 1) * 128, :])
        b.wait(Sc, "dLG%d" % q4, ldg)
        qdq_chunk(fgb, fq, sb["S2B"][:, 0:1], sb["S2B"][:, 1:2],
                  -32768.0, 32767.0, ft, war=vend3.get(ic - 2, 0))
        fg_done = b.sems["sS"][1]
        b.wait(V, "dLU%d" % q4, ldu)
        if ("dO%d" % par) in b.sems:
            b.wait(V, "dO%d" % par)
        b.op(V, lambda e, f2=f2, fq=fq, fub=fub: e.tensor_tensor(
            f2, fq, fub, Alu.mult), "sV")
        fu_done = b.sems["sV"][1]
        b.op(V, lambda e, f2=f2: e.tensor_reduce(
            ROW[0:128, 384:385], f2, AX, Alu.max,
            apply_absolute_value=True), "sV")
        b.op(V, lambda e: e.tensor_tensor(sb["AMX"][:, 4:5], sb["AMX"][:, 4:5],
                                          ROW[0:128, 384:385], Alu.max), "sV")
        b.wait(Sc, "sV", fu_done)
        b.dmas(Sc, "dO%d" % par, sp["sp_o"][ic * 128:(ic + 1) * 128, :], f2)
        vend3[ic] = fu_done
        f3m[ic] = (fg_done, fu_done)
    global_scale(4, "o", 1, "SOB", 32767.0)

    b.barrier()
    prep_ffn_weight(dr["wd"], "WDT", NEC, I, E, sb["W3A"], sb["W3S"])

    b.barrier()
    FG4, FT4, OQT = sb["FG4"], sb["FT4"], sb["OQT"]
    f4m = {}
    v4 = {}
    for ic in range(NIC):
        par = ic % 2
        m3 = ic % 3
        fb4 = (FG4[:, m3 * MYT:(m3 + 1) * MYT] if m3 < 2
               else sb["FG4X"][:, 0:MYT])
        if (ic - 3) in f4m:
            b.wait(Sy, "sS", f4m[ic - 3])
        ld = b.dmas(Sy, "dLG%d" % m3, fb4,
                    sp["sp_o"][ic * 128:(ic + 1) * 128, :])
        b.wait(Sc, "dLG%d" % m3, ld)
        if (ic - 1) in v4:
            b.wait(Sc, "sV", v4[ic - 1])
        b.op(Sc, lambda e, fb4=fb4: e.activation(out=FT4[:, :], in_=fb4,
                                                 func=Act.Identity,
                                                 bias=sb["CST"][:, 0:1],
                                                 scale=sb["SOB"][:, 1:2]), "sS")
        f4m[ic] = b.sems["sS"][1]
        b.wait(V, "sS")
        b.op(V, lambda e: e.tensor_scalar(FT4[:, :], FT4[:, :], MAGIC,
                                          -32768.0, Alu.subtract,
                                          Alu.max), "sV")
        b.op(V, lambda e, ic=ic: e.tensor_scalar(
            OQT[:, ic * MYT:(ic + 1) * MYT], FT4[:, :], 32767.0,
            sb["SOB"][:, 0:1], Alu.min, Alu.mult), "sV")
        v4[ic] = b.sems["sV"][1]

    b.barrier()
    a5m = {}
    for hc in range(NEC):
        base = (hc % 2) * 4 * 512
        alcb = sb["ALC"] if hc % 2 == 0 else sb["ALCB"]
        if (hc - 2) in a5m:
            b.wait(Sy, "sV", a5m[hc - 2])
        ld = b.dmas(Sy, "dAL%d" % (hc % 2), alcb[:, :],
                    sp["sp_aln"][hc * 128:(hc + 1) * 128, :])
        if (hc - 2) in a5m:
            b.wait(T, "sV", a5m[hc - 2])
        for ic in range(NIC):
            for nq in range(4):
                b.op(T, lambda e, base=base, nq=nq, ic=ic, hc=hc: e.matmul(
                    PS[:, base + nq * 512: base + (nq + 1) * 512],
                    sb["WDT"][:, ic * E + hc * 128: ic * E + (hc + 1) * 128],
                    OQT[:, ic * MYT + nq * 512: ic * MYT + (nq + 1) * 512],
                    start=(ic == 0), stop=(ic == NIC - 1)), "sT")
        b.wait(V, "sT")
        b.wait(V, "dAL%d" % (hc % 2), ld)
        if "dAF" in b.sems:
            b.wait(V, "dAF")
        for nq in range(4):
            b.op(V, lambda e, base=base, nq=nq, alcb=alcb: e.tensor_tensor(
                sb["AFE"][:, nq * 512:(nq + 1) * 512],
                PS[:, base + nq * 512: base + (nq + 1) * 512],
                alcb[:, nq * 512:(nq + 1) * 512], Alu.add), "sV")
        a5m[hc] = b.sems["sV"][1]
        b.wait(Sc, "sV")
        b.dmas(Sc, "dAF", sp["sp_af"][hc * 128:(hc + 1) * 128, :],
               sb["AFE"][:, :])

    b.barrier()
    AFL, XT6, OSB = sb["AFL"], sb["XT6"], sb["OSB"]
    aflm = {}
    for slab in range(16):
        sname = "dAFL%d" % (slab % 4)
        if (slab - 4) in aflm:
            b.wait(Sy, sname, aflm[slab - 4])
        for hc in range(NEC):
            ld = b.dmas(Sy, sname,
                        AFL[:, hc * MYT + slab * 128:
                            hc * MYT + (slab + 1) * 128],
                        sp["sp_af"][hc * 128:(hc + 1) * 128,
                                    slab * 128:(slab + 1) * 128])
        aflm[slab] = ld
    f6m = {}
    for tt in range(16):
        par = tt % 2
        q4 = tt % 4
        pb = par * E
        base = par * 1024
        xt6b = (XT6 if q4 < 2 else sb["XT6X"])[:, pb:pb + E]
        if (tt - 4) in f6m:
            b.wait(Sy, "sV", f6m[tt - 4])
        ld = b.dmas(Sy, "dX6%d" % q4, xt6b,
                    dr["x_tok"][tt * 128:(tt + 1) * 128, :])
        b.wait(T, "dAFL%d" % (tt % 4), aflm[tt])
        if (tt - 2) in f6m:
            b.wait(T, "sV", f6m[tt - 2])
        for hc in range(NEC):
            b.op(T, lambda e, base=base, hc=hc, tt=tt: e.transpose(
                PS[:, base + hc * 128: base + (hc + 1) * 128],
                AFL[:, hc * MYT + tt * 128: hc * MYT + (tt + 1) * 128],
                sb["IDF"][:, :]), "sT")
        b.wait(V, "sT")
        b.wait(V, "dX6%d" % q4, ld)
        if ("dOU%d" % par) in b.sems:
            b.wait(V, "dOU%d" % par)
        f6m[tt] = b.op(V, lambda e, base=base, pb=pb, xt6b=xt6b:
                       e.tensor_tensor(
                           OSB[:, pb:pb + E], PS[:, base:base + E], xt6b,
                           Alu.add), "sV")
        b.wait(Sc, "sV", f6m[tt])
        b.dmas(Sc, "dOU%d" % par, out_d[tt * 128:(tt + 1) * 128, :],
               OSB[:, pb:pb + E])
    b.wait(G, "dOU0")
    b.wait(G, "dOU1")
    b.wait(G, "dG")


_NC_CACHE = {}


def kernel(L0_attn_x, L0_attn_Wq, L0_attn_Wk, L0_attn_Wv, L0_attn_ln_gamma,
           L0_attn_ln_beta, L0_ffn_w_gate, L0_ffn_w_up, L0_ffn_w_down):
    x = np.ascontiguousarray(np.asarray(L0_attn_x, np.float32))
    wq = np.asarray(L0_attn_Wq, np.float32)
    wk = np.asarray(L0_attn_Wk, np.float32)
    wv = np.asarray(L0_attn_Wv, np.float32)
    gam = np.asarray(L0_attn_ln_gamma, np.float32)
    bet = np.asarray(L0_attn_ln_beta, np.float32)
    wg = np.ascontiguousarray(np.asarray(L0_ffn_w_gate, np.float32))
    wu = np.ascontiguousarray(np.asarray(L0_ffn_w_up, np.float32))
    wd = np.ascontiguousarray(np.asarray(L0_ffn_w_down, np.float32))

    wqT = np.ascontiguousarray(wq.T)
    wkT = np.ascontiguousarray(wk.T)
    wvT = np.ascontiguousarray(wv.T)
    gamma8 = np.ascontiguousarray(gam.reshape(NEC, 128).T)
    beta8 = np.ascontiguousarray(bet.reshape(NEC, 128).T)

    if "nc" not in _NC_CACHE:
        _NC_CACHE["nc"] = build_nc()
    nc = _NC_CACHE["nc"]

    in_maps = []
    for c in range(N_CORES):
        bq, h = divmod(c, 2)
        qoff = h * MYT
        xloc = np.roll(x[bq], -qoff, axis=0)
        if h == 0:
            meta = np.array([[-1.0 / 4096.0, 3968.0 / 4096.0]], np.float32)
        else:
            meta = np.array([[1.0 / 4096.0, 1.0 - 3968.0 / 4096.0]], np.float32)
        in_maps.append({
            "xseqT": np.ascontiguousarray(xloc.T),
            "x_tok": np.ascontiguousarray(xloc),
            "wqT": wqT, "wkT": wkT, "wvT": wvT,
            "wu": wu, "wg": wg, "wd": wd,
            "gamma8": gamma8, "beta8": beta8,
            "meta": meta,
        })

    res = run_bass_kernel_spmd(nc, in_maps, core_ids=list(range(N_CORES)))
    out = np.empty((4, S, E), np.float32)
    for c in range(N_CORES):
        bq, h = divmod(c, 2)
        out[bq, h * MYT:(h + 1) * MYT, :] = res.results[c]["out"]
    return out
